# version 15
# speedup vs baseline: 2.0410x; 1.0292x over previous
import numpy as np
import ml_dtypes
from contextlib import ExitStack
import concourse.bass as bass
import concourse.mybir as mybir
from concourse.bass_utils import run_bass_kernel_spmd

F32 = mybir.dt.float32
BF16 = mybir.dt.bfloat16
AF = mybir.ActivationFunctionType
ALU = mybir.AluOpType
AX = mybir.AxisListType
AP = bass.AP

B, T, D, H, NK, KVE = 2, 16384, 1024, 16, 64, 65
TC, TILE, CH = 4096, 512, 128
NT = TC // TILE
NCHT = TILE // CH
NCH = TC // CH
NDC = D // 128
NPAIR = H // 2
FFN, IFF = 4096, 2816
NFC = FFN // 128
NIC = IFF // 128
LN_EPS = 1e-5
EPS_DEN = 1e-6
LAM_MIN = 1e-8
QMAX = 127.0
MAGIC = 12582912.0
PAY_SEC = H * NK * KVE
PAY = 2 * PAY_SEC + 32
NCORE = 8
PHASE_LIMIT = None
POS = [(hh % 2) * 8 + hh // 2 for hh in range(16)]
HORDER = [0] * 16
for _hh in range(16):
    HORDER[POS[_hh]] = _hh
P1_CUT = None
LNJS = None
ENGS = ("tensor", "vector", "scalar", "gpsimd", "sync")


class Bld:

    SEMNAMES = ("tensor", "vector", "scalar", "gpsimd", "sync", "gdma", "cc",
                "syncb")
    OWNER = {"tensor": "tensor", "vector": "vector", "scalar": "scalar",
             "gpsimd": "gpsimd", "sync": "sync", "gdma": "gpsimd", "cc": "gpsimd",
             "syncb": "sync"}

    NPOOL = {"sync": 48, "gpsimd": 16}

    def __init__(self, nc):
        self.nc = nc
        self.sems = {s: nc.alloc_semaphore(name=f"sem_{s}") for s in self.SEMNAMES}
        self.count = {s: 0 for s in self.SEMNAMES}
        self.groups = {e: [] for e in ENGS}
        self.pool = {}
        self.pool_count = {}
        for eng, npool in self.NPOOL.items():
            for k in range(npool):
                s = f"dp_{eng}_{k}"
                self.pool[s] = nc.alloc_semaphore(name=s)
                self.pool_count[s] = 0
                self.count[s] = 0
        self.sems.update(self.pool)
        self.pool_idx = {e: 0 for e in self.NPOOL}
        self.prev_flush_counts = {}
        self.last_dma = None

    def _add(self, eng, semname, fn, after, inc, is_async=False):
        self.count[semname] += inc
        self.groups[eng].append((fn, tuple(after), inc, semname))
        return (semname, self.count[semname], is_async)

    def t(self, fn, after=()):
        return self._add("tensor", "tensor", fn, after, 1)

    def v(self, fn, after=()):
        return self._add("vector", "vector", fn, after, 1)

    def s(self, fn, after=()):
        return self._add("scalar", "scalar", fn, after, 1)

    def g(self, fn, after=()):
        return self._add("gpsimd", "gpsimd", fn, after, 1)

    def dma(self, eng, fn, n=1, after=()):
        k = self.pool_idx[eng]
        self.pool_idx[eng] = (k + 1) % self.NPOOL[eng]
        semname = f"dp_{eng}_{k}"
        drain = self.count[semname]

        def wrap(e, sem, fn=fn, drain=drain):
            if drain:
                e.wait_ge(sem, drain)
            fn(e, sem)
        tok = self._add(eng, semname, wrap, after, 16 * n, is_async=True)
        self.last_dma = tok
        return tok

    def g_async(self, fn, after=()):
        return self._add("gpsimd", "cc", fn, after, 1, is_async=True)

    def fence(self, queue="sync"):
        return self.last_dma

    def quiesce(self, semname="sync"):
        return ("syncb", 0, True)

    def flush(self, name=None):
        nc = self.nc
        groups = {e: list(self.groups[e]) for e in ENGS}
        self.groups = {e: [] for e in ENGS}
        if not any(groups.values()):
            return
        qprev = {s: v for s, v in self.prev_flush_counts.items() if v > 0}
        self.prev_flush_counts = {s: self.count[s] for s in self.pool}
        with nc.Block(name) as blk:
            def mk(ename, glist):
                def body(eng):
                    lastw = {}
                    for qs, qv in qprev.items():
                        eng.wait_ge(self.sems[qs], qv)
                        lastw[qs] = qv
                    for fn, after, inc, semname in glist:
                        for tok in after:
                            ps, val, is_async = tok
                            if ename == "tensor" and self.OWNER.get(ps) == "tensor" \
                                    and not is_async:
                                continue
                            if lastw.get(ps, -1) >= val:
                                continue
                            eng.wait_ge(self.sems[ps], val)
                            lastw[ps] = val
                        mysem = self.sems[semname]
                        if inc >= 16:
                            fn(eng, mysem)
                        else:
                            last = fn(eng)
                            (last if last is not None else eng.nop()).then_inc(mysem, 1)
                return body
            for ename in ENGS:
                if groups[ename]:
                    getattr(blk, ename)(mk(ename, groups[ename]))


def _consts():
    c = {}
    c["id_f"] = np.eye(128, dtype=np.float32)
    c["id_b"] = np.eye(128, dtype=ml_dtypes.bfloat16)
    c["ones_row_f"] = np.ones((1, 128), np.float32)
    c["ones_col_f"] = np.ones((128, 1), np.float32)
    c["ones16_b"] = np.ones((16, 128), ml_dtypes.bfloat16)
    c["ones16_f"] = np.ones((16, 128), np.float32)
    sp = np.zeros((16, NPAIR * 128), np.float32)
    for g in range(NPAIR):
        sp[2 * g, g * 128: g * 128 + 64] = 1.0
        sp[2 * g + 1, g * 128 + 64: (g + 1) * 128] = 1.0
    c["sel_pair_b"] = sp.astype(ml_dtypes.bfloat16)
    c["sel_pair_f"] = sp
    sh = np.zeros((16, 128), np.float32)
    for h in range(16):
        sh[h, 0:64] = 1.0 if h % 2 == 0 else 0.0
        sh[h, 64:128] = 1.0 if h % 2 == 1 else 0.0
    c["sel_half_f"] = sh
    pm = np.zeros((16, NPAIR), np.float32)
    for h in range(16):
        pm[h, h // 2] = 1.0
    c["pair_mask_f"] = pm
    hm = np.zeros((16, 16 * 128), np.float32)
    for p_ in range(16):
        hm[HORDER[p_], p_ * 128:(p_ + 1) * 128] = 1.0
    c["head_mask_f"] = hm.astype(ml_dtypes.bfloat16)
    return c


def bcv(ap2d, reps):
    base = ap2d
    a = list(base.ap)
    new = [list(a[0]), [0, reps]] + [list(x) for x in a[1:]]
    return AP(base.tensor, base.offset, new)


def bccol(ap_col, reps):
    a = list(ap_col.ap)
    return AP(ap_col.tensor, ap_col.offset, [list(a[0]), [0, reps]])


def build_nc(dbg_names=()):
    nc = bass.Bass("TRN2", target_bir_lowering=False, detect_race_conditions=False)
    bld = Bld(nc)
    Cst = _consts()
    T_, V_, S_, G_, DMA_ = bld.t, bld.v, bld.s, bld.g, bld.dma

    x_in = nc.dram_tensor("x_in", [TC, D], F32, kind="ExternalInput")
    win = {}
    for nm, shp in [
        ("L0_attn_q_w", [D, D]), ("L0_attn_k_w", [D, D]), ("L0_attn_v_w", [D, D]),
        ("L0_attn_o_w", [D, D]), ("L0_attn_dec_w", [D, H]),
        ("L0_attn_f1_w", [D, FFN]), ("L0_attn_f2_w", [FFN, D]),
        ("L0_ffn_w_gate", [IFF, D]), ("L0_ffn_w_up", [IFF, D]), ("L0_ffn_w_down", [D, IFF]),
        ("L0_attn_q_b", [D]), ("L0_attn_k_b", [D]), ("L0_attn_v_b", [D]),
        ("L0_attn_o_b", [D]), ("L0_attn_dec_b", [H]), ("L0_attn_f1_b", [FFN]),
        ("L0_attn_f2_b", [D]),
        ("L0_attn_ln1_w", [D]), ("L0_attn_ln1_b", [D]),
        ("L0_attn_ln2_w", [D]), ("L0_attn_ln2_b", [D]),
        ("L0_ffn_clip_x", [1]), ("L0_ffn_clip_down", [1]),
        ("cc_pf", [8, 8]), ("cc_pb", [8, 8]), ("cc_mf", [8, 1]), ("cc_mb", [8, 1]),
    ]:
        win[nm] = nc.dram_tensor(nm, shp, F32, kind="ExternalInput")
    out_t = nc.dram_tensor("out", [TC, D], F32, kind="ExternalOutput")

    dbg = {}
    _dbg_shapes = {
        "xn_fm": ([D, TC], F32), "x2e_fm": ([D, TC], BF16), "phiq": ([D, TC], BF16),
        "phik": ([D, TC], BF16),
        "lam_d": ([H, TC], F32), "llam_d": ([H, TC], F32), "vtm_d": ([TC, H * KVE], BF16),
        "sf_d": ([128, NPAIR * KVE], F32), "sb_d": ([128, NPAIR * KVE], F32),
        "sin_d": ([128, NPAIR * KVE], F32), "rin_d": ([128, NPAIR * KVE], F32),
        "m1_d": ([1, 16], F32), "attn_d": ([D, TC], BF16), "x1_d": ([D, TC], F32),
        "tot_d": ([D, TC], F32), "xq_d": ([D, TC], BF16), "h2_d": ([IFF, TC], F32),
        "m2_d": ([1, 16], F32),
    }
    for nmm in dbg_names:
        shp, dt_ = _dbg_shapes[nmm]
        dbg[nmm] = nc.dram_tensor("dbg_" + nmm, shp, dt_, kind="ExternalOutput")

    xn_s = nc.dram_tensor("xn_s", [D, TC], F32, kind="Internal")
    qc_s = nc.dram_tensor("qc_s", [NCH, 128, NPAIR * CH], BF16, kind="Internal")
    qe_s = nc.dram_tensor("qe_s", [NCH, 128, NPAIR * CH], BF16, kind="Internal")
    part_s = nc.dram_tensor("part_s", [NCH, CH, H * KVE], BF16, kind="Internal")
    x1_s = nc.dram_tensor("x1_s", [D, TC], F32, kind="Internal")
    xq_s = nc.dram_tensor("xq_s", [D, TC], BF16, kind="Internal")
    tot_s = nc.dram_tensor("tot_s", [D, TC], F32, kind="Internal")
    h2_s = nc.dram_tensor("h2_s", [IFF, TC], F32, kind="Internal")
    wgT_s = nc.dram_tensor("wgT_s", [D, IFF], BF16, kind="Internal")
    wuT_s = nc.dram_tensor("wuT_s", [D, IFF], BF16, kind="Internal")
    wdT_s = nc.dram_tensor("wdT_s", [IFF, D], BF16, kind="Internal")
    kcv_s = nc.dram_tensor("kcv_s", [NCH, 128, NPAIR * KVE], BF16, kind="Internal")
    ag_in = nc.dram_tensor("ag_in", [1, PAY], F32, kind="Internal")
    ag_out = nc.dram_tensor("ag_out", [8, PAY], F32, kind="Internal")
    ag2_in = nc.dram_tensor("ag2_in", [1, 16], F32, kind="Internal")
    ag2_out = nc.dram_tensor("ag2_out", [8, 16], F32, kind="Internal")

    const_t = {k: nc.inline_tensor(np.ascontiguousarray(v), "c_" + k) for k, v in Cst.items()}

    es_persist = ExitStack()
    es_kcv = ExitStack()
    es_ph = ExitStack()

    def SB(name, shape, dtype, es):
        return es.enter_context(nc.sbuf_tensor(name, shape, dtype))

    def PS(name, es, shape=(128, 512)):
        return es.enter_context(nc.psum_tensor(name, list(shape), F32))

    P = es_persist
    id_f = SB("id_f", [128, 128], F32, P)
    id_b = SB("id_b", [128, 128], BF16, P)
    ones_row = SB("ones_row", [1, 128], F32, P)
    ones_col = SB("ones_col", [128, 1], F32, P)
    ones16_b = SB("ones16_b", [16, 128], BF16, P)
    ones16_f = SB("ones16_f", [16, 128], F32, P)
    sel_pair_b = SB("sel_pair_b", [16, NPAIR * 128], BF16, P)
    sel_half = SB("sel_half", [16, 128], F32, P)
    pair_mask = SB("pair_mask", [16, NPAIR], F32, P)
    ccpf = SB("ccpf", [8, 8], F32, P)
    ccpb = SB("ccpb", [8, 8], F32, P)
    ccmf = SB("ccmf", [8, 1], F32, P)
    ccmb = SB("ccmb", [8, 1], F32, P)

    lcend_st = SB("lcend_st", [16, NCH], F32, P)
    lpre_st = SB("lpre_st", [16, NCH], F32, P)
    ltot_c = SB("ltot_c", [16, 1], F32, P)
    mx_col = SB("mx_col", [128, 1], F32, P)
    mh_col = SB("mh_col", [128, 1], F32, P)
    colA = SB("colA", [128, 64], F32, P)
    rowA = SB("rowA", [1, 64], F32, P)
    s16A = SB("s16A", [16, 64], F32, P)
    rs1_col = SB("rs1_col", [128, 1], F32, P)
    sc1_col = SB("sc1_col", [128, 1], F32, P)
    rs2_col = SB("rs2_col", [128, 1], F32, P)
    sc2_col = SB("sc2_col", [128, 1], F32, P)
    sg_sc = SB("sg_sc", [128, NIC], F32, P)
    su_sc = SB("su_sc", [128, NIC], F32, P)
    sd_sc = SB("sd_sc", [128, NDC], F32, P)
    g2c = SB("g2c", [128, NDC], F32, P)
    b2cb = SB("b2cb", [128, NDC], BF16, P)
    obC = SB("obC", [128, NDC], F32, P)
    f2bC = SB("f2bC", [128, NDC], F32, P)
    clipc = SB("clipc", [1, 2], F32, P)
    rowsP = SB("rowsP", [1, D], F32, P)
    cstT = SB("cstT", [128, 8], F32, P)
    _cvals = [LN_EPS, MAGIC, -1.0, -0.5, 0.0, 1.0]
    for _i, _v in enumerate(_cvals):
        nc.const_aps.register(F32, _v, cstT[:, _i:_i + 1])
    declc = SB("declc", [128, NPAIR], F32, P)
    declp = SB("declp", [128, NPAIR], F32, P)
    s16B = SB("s16B", [16, 16], F32, P)

    kcvs = SB("kcvs", [128, 2, NPAIR * KVE], BF16, es_kcv)
    S_st = SB("S_st", [128, NPAIR * KVE], F32, es_kcv)
    Sb_st = SB("Sb_st", [128, NPAIR * KVE], F32, es_kcv)
    Sin_st = SB("Sin_st", [128, NPAIR * KVE], F32, es_kcv)
    Rin_st = SB("Rin_st", [128, NPAIR * KVE], F32, es_kcv)
    R_st = SB("R_st", [128, NPAIR * KVE], F32, es_kcv)
    Tmp_st = SB("Tmp_st", [128, NPAIR * KVE], F32, es_kcv)

    es1 = ExitStack()
    wq = SB("wq", [128, NDC, D], BF16, es1)
    wk = SB("wk", [128, NDC, D], BF16, es1)
    wv = SB("wv", [128, NDC, D], BF16, es1)
    wdec = SB("wdec", [128, NDC, H], BF16, es1)
    qbC = SB("qbC", [128, NDC], F32, es1)
    kbC = SB("kbC", [128, NDC], F32, es1)
    qb1C = SB("qb1C", [128, NDC], F32, es1)
    kb1C = SB("kb1C", [128, NDC], F32, es1)
    vbtm = SB("vbtm", [128, D], BF16, es1)
    decbc = SB("decbc", [16, 1], F32, es1)
    g1c = SB("g1c", [128, NDC], F32, es1)
    b1cb = SB("b1cb", [128, NDC], BF16, es1)
    vbeC = SB("vbeC", [128, NDC], F32, es1)
    xtm = [SB("xtm0", [128, NCHT, D], F32, es1)]
    xn_tm = SB("xn_tm", [128, D], F32, es1)
    x2e_tm = SB("x2e_tm", [128, D], F32, es1)
    xnfm = SB("xnfm", [128, NDC, CH], F32, es1)
    x2e = SB("x2e", [128, NDC, 2, TILE], BF16, es1)
    phiq = SB("phiq", [128, NDC, 2, TILE], BF16, es1)
    phik = SB("phik", [128, NDC, 2, TILE], BF16, es1)
    tbuf1 = SB("tbuf1", [128, TILE], F32, es1)
    lam_t = SB("lam_t", [16, 2, TILE], F32, es1)
    llam_t = lam_t
    vtm = SB("vtm", [128, 2, H * KVE], BF16, es1)
    lc_f = SB("lc_f", [16, CH], F32, es1)
    ctil_f = SB("ctil_f", [16, CH], F32, es1)
    e_f = SB("e_f", [16, CH], F32, es1)
    lctm = SB("lctm", [128, 16], F32, es1)
    ctil_b = SB("ctil_b", [16, CH], BF16, es1)
    e_b = SB("e_b", [16, CH], BF16, es1)
    ctm = SB("ctm", [128, 16], F32, es1)
    etm = SB("etm", [128, 16], F32, es1)
    drhs = SB("drhs", [16, 16 * CH], BF16, es1)
    D_sb = SB("D_sb", [128, H * CH], BF16, es1)
    scD = SB("scD", [128, H * CH], BF16, es1)
    keB = SB("keB", [128, H * NK], BF16, es1)
    kcB = SB("kcB", [128, H * NK], BF16, es1)
    phikf = SB("phikf", [128, NPAIR * CH], F32, es1)
    head_mask = SB("head_mask", [16, 16 * 128], BF16, es1)
    Sb16 = SB("Sb16", [128, NPAIR * KVE], BF16, es1)
    qcB = SB("qcB", [128, 2, NPAIR * CH], BF16, es1)
    qeB = SB("qeB", [128, 2, NPAIR * CH], BF16, es1)
    partB = SB("partB", [128, 2, H * KVE], BF16, es1)
    psA = [PS(f"psA{i}", es1) for i in range(8)]

    def p0_consts(eng, sem):
        for dst, src in [(id_f, "id_f"), (id_b, "id_b"), (ones_row, "ones_row_f"),
                         (ones_col, "ones_col_f"),
                         (ones16_b, "ones16_b"), (ones16_f, "ones16_f"),
                         (sel_pair_b, "sel_pair_b"),
                         (sel_half, "sel_half_f"),
                         (pair_mask, "pair_mask_f"), (head_mask, "head_mask_f")]:
            eng.dma_start(out=dst[:, :], in_=const_t[src].ap()).then_inc(sem, 16)
        for dst, src in [(ccpf, "cc_pf"), (ccpb, "cc_pb"), (ccmf, "cc_mf"), (ccmb, "cc_mb")]:
            eng.dma_start(out=dst[:, :], in_=win[src].ap()).then_inc(sem, 16)
    tok_c0 = DMA_("sync", p0_consts, n=14)

    def p0_wload(eng, sem):
        for dst, src in [(wq, "L0_attn_q_w"), (wk, "L0_attn_k_w"), (wv, "L0_attn_v_w")]:
            eng.dma_start(out=dst[:, :, :],
                          in_=AP(win[src], 0, [[D, 128], [128 * D, NDC], [1, D]])
                          ).then_inc(sem, 16)
        eng.dma_start(out=wdec[:, :, :],
                      in_=AP(win["L0_attn_dec_w"], 0, [[H, 128], [128 * H, NDC], [1, H]])
                      ).then_inc(sem, 16)
    tok_w0 = DMA_("gpsimd", p0_wload, n=4)

    def _setconsts(e):
        last = None
        for _i, _v in enumerate(_cvals):
            last = e.memset(cstT[:, _i:_i + 1], _v)
        return last
    tok_cst = G_(_setconsts)
    tok_z = G_(lambda e: (e.memset(S_st[:, :], 0.0), e.memset(Sb_st[:, :], 0.0),
                          e.memset(ltot_c[:, :], 0.0), e.memset(mx_col[:, :], 0.0),
                          e.memset(mh_col[:, :], 0.0), e.memset(Sin_st[:, :], 0.0),
                          e.memset(Rin_st[:, :], 0.0), e.memset(Sb16[:, :], 0.0))[-1])

    chain = {"rc": tok_c0}

    def row_to_cols(name, dstC, nchunks, col0=0):
        prev = chain["rc"]

        def ld(eng, sem):
            n = nchunks * 128
            eng.dma_start(out=rowsP[0:1, 0:n],
                          in_=AP(win[name], 0, [[n, 1], [1, n]])).then_inc(sem, 16)
        tl = DMA_("sync", ld, n=1, after=[prev])
        tts = []
        for c in range(nchunks):
            tts.append(T_(lambda e, c=c: e.transpose(psA[7][:, c:c + 1],
                                                     rowsP[0:1, c * 128:(c + 1) * 128],
                                                     id_f[0:1, 0:1]),
                          after=[tl, prev] + ([tts[-1]] if tts else [])))
        tv = V_(lambda e: e.tensor_copy(dstC[:, col0:col0 + nchunks], psA[7][:, 0:nchunks]),
                after=[tts[-1]])
        chain["rc"] = tv
        return tv

    tok_g1 = row_to_cols("L0_attn_ln1_w", g1c, NDC)
    tok_b1 = row_to_cols("L0_attn_ln1_b", colA, NDC)
    tok_b1b = V_(lambda e: e.tensor_copy(b1cb[:, :], colA[:, 0:NDC]), after=[tok_b1])
    tok_g2 = row_to_cols("L0_attn_ln2_w", g2c, NDC)
    tok_b2 = row_to_cols("L0_attn_ln2_b", colA, NDC)
    tok_b2b = V_(lambda e: e.tensor_copy(b2cb[:, :], colA[:, 0:NDC]), after=[tok_b2])
    tok_ob = row_to_cols("L0_attn_o_b", obC, NDC)
    tok_f2bc = row_to_cols("L0_attn_f2_b", f2bC, NDC)

    def fold_bias(wsb, bname, dstC, deps):
        prev = deps
        toks = []
        for cp in range(NDC):
            mms = []
            for c in range(NDC):
                mms.append(T_(lambda e, c=c, cp=cp, wsb=wsb: e.matmul(
                    psA[6][:, cp:cp + 1], wsb[:, c, cp * 128:(cp + 1) * 128],
                    b1cb[:, c:c + 1], start=(c == 0), stop=(c == NDC - 1)),
                    after=(prev if cp == 0 and c == 0 else []) + ([mms[-1]] if mms else ([toks[-1]] if toks else []))))
            toks.append(mms[-1])
        tb = row_to_cols(bname, dstC, NDC)
        return V_(lambda e, dstC=dstC: e.tensor_add(dstC[:, 0:NDC], dstC[:, 0:NDC],
                                                    psA[6][:, 0:NDC]),
                  after=[toks[-1], tb])

    tok_qb = fold_bias(wq, "L0_attn_q_b", qbC, [tok_w0, tok_b1b])
    tok_kb = fold_bias(wk, "L0_attn_k_b", kbC, [tok_qb])
    tok_vb = fold_bias(wv, "L0_attn_v_b", vbeC, [tok_kb])
    tok_qb1 = V_(lambda e: e.tensor_scalar(qb1C[:, :], qbC[:, 0:NDC], 1.0, None, ALU.add),
                 after=[tok_qb])
    tok_kb1 = V_(lambda e: e.tensor_scalar(kb1C[:, :], kbC[:, 0:NDC], 1.0, None, ALU.add),
                 after=[tok_kb])

    decmms = []
    for c in range(NDC):
        decmms.append(T_(lambda e, c=c: e.matmul(
            psA[5][0:16, 0:1], wdec[:, c, :], b1cb[:, c:c + 1],
            start=(c == 0), stop=(c == NDC - 1)),
            after=([tok_w0, tok_b1b, tok_vb] if c == 0 else []) + ([decmms[-1]] if decmms else [])))

    def ld_decb(eng, sem):
        eng.dma_start(out=rowA[0:1, 0:16],
                      in_=AP(win["L0_attn_dec_b"], 0, [[16, 1], [1, 16]])).then_inc(sem, 16)
    tok_decb = DMA_("sync", ld_decb, n=1, after=[chain["rc"]])
    tok_decT = T_(lambda e: e.transpose(psA[5][0:16, 1:2], rowA[0:1, 0:16], id_f[0:1, 0:1]),
                  after=[tok_decb, decmms[-1]])
    tok_decc0 = V_(lambda e: e.tensor_copy(s16B[:, 3:4], psA[5][0:16, 0:1]), after=[tok_decT])
    tok_decc = V_(lambda e: e.tensor_add(decbc[:, :], s16B[:, 3:4], psA[5][0:16, 1:2]),
                  after=[tok_decc0])

    tok_vbrow = None
    for rnd in range(2):
        vbT = []
        for c4 in range(4):
            c = rnd * 4 + c4
            vbT.append(T_(lambda e, c=c, c4=c4: e.transpose(
                psA[4][0:1, c4 * 128:(c4 + 1) * 128], vbeC[:, c:c + 1], id_f[:, :]),
                after=[tok_vb, tok_decc] + ([vbT[-1]] if vbT else ([tok_vbrow] if tok_vbrow else []))))
        tok_vbrow = V_(lambda e, rnd=rnd: e.tensor_copy(
            rowsP[0:1, rnd * 512:(rnd + 1) * 512], psA[4][0:1, 0:512]),
            after=[vbT[-1], chain["rc"]])
    chain["rc"] = tok_vbrow
    vb2 = []
    for c4 in range(2):
        mm = []
        for c in range(4):
            mm.append(T_(lambda e, c4=c4, c=c: e.matmul(
                psA[3][:, c * 128:(c + 1) * 128], ones_row[:, :],
                rowsP[0:1, (c4 * 4 + c) * 128:(c4 * 4 + c + 1) * 128], start=True, stop=True),
                after=[tok_vbrow] + ([vb2[-1]] if vb2 else []) + ([mm[-1]] if mm else [])))
        vb2.append(V_(lambda e, c4=c4: e.tensor_copy(
            vbtm[:, c4 * 512:(c4 + 1) * 512], psA[3][:, :]), after=[mm[-1]]))
    tok_vbtm = vb2[-1]

    sc_toks = []
    for wsb, dep in [(wq, tok_qb1), (wk, tok_kb1), (wv, tok_vbtm), (wdec, tok_decc)]:
        ncols = wsb.shape[2]
        for c in range(NDC):
            sc_toks.append(V_(lambda e, wsb=wsb, c=c, ncols=ncols: e.tensor_scalar(
                wsb[:, c, 0:ncols], wsb[:, c, 0:ncols], g1c[:, c:c + 1], None, ALU.mult),
                after=[dep, tok_g1]))
    tok_wsc = sc_toks[-1]

    def ld_clip(eng, sem):
        eng.dma_start(out=rowA[0:1, 20:21],
                      in_=AP(win["L0_ffn_clip_x"], 0, [[1, 1], [1, 1]])).then_inc(sem, 16)
        eng.dma_start(out=rowA[0:1, 21:22],
                      in_=AP(win["L0_ffn_clip_down"], 0, [[1, 1], [1, 1]])).then_inc(sem, 16)
    tok_clipl = DMA_("sync", ld_clip, n=2, after=[chain["rc"]])
    tok_clip = S_(lambda e: e.activation(clipc[0:1, 0:2], rowA[0:1, 20:22], AF.Sigmoid),
                  after=[tok_clipl])
    tok_vone = G_(lambda e: e.memset(
        AP(vtm, 64, [[2 * H * KVE, 128], [KVE, 2 * H], [1, 1]]), 1.0), after=[tok_z])

    bld.flush("p0")
    if PHASE_LIMIT == "p0":
        return nc

    st1 = {"xtm": [], "chunk_tail": [], "b7": [], "xnst": [], "ln": {},
           "tll": {}, "vg": {}, "ln_last": []}
    stq = {}
    hbm_st = {}

    def emit_ldx(t):
        def ld_x(eng, sem, t=t):
            eng.dma_start(out=xtm[0][:, :, :],
                          in_=AP(x_in, t * TILE * D, [[D, 128], [128 * D, NCHT], [1, D]])
                          ).then_inc(sem, 16)
        tok = DMA_("sync", ld_x, n=1, after=st1["xtm"])
        st1["xtm"] = []
        st1["ln"][t] = {"x": tok, "xn_j": [], "x2e_j": []}

    def prep_ln_part(t, j):
        st = st1["ln"][t]
        tok_x = st["x"]
        tok_xn_j = st["xn_j"]
        tok_x2e_j = st["x2e_j"]
        sl = t % 2
        xj = xtm[0][:, j, :]
        cb = (j % 2) * 24
        tm = V_(lambda e, xj=xj, cb=cb: e.tensor_reduce(colA[:, cb:cb + 1], xj, AX.X, ALU.add),
                after=[tok_x] + (st1["ln_last"] if j == 0 else []))
        ts1 = S_(lambda e, xj=xj, cb=cb: e.activation(x2e_tm[:, :], xj, AF.Square,
                                                      accum_out=colA[:, cb + 1:cb + 2]),
                 after=[tok_x] + (st1["ln_last"] if j == 0 else [])
                 + (tok_x2e_j[-1:] if tok_x2e_j else []))

        def colops(e, cb=cb):
            e.tensor_scalar(colA[:, cb + 2:cb + 3], colA[:, cb:cb + 1], 1.0 / D, None, ALU.mult)
            e.tensor_scalar(colA[:, cb + 3:cb + 4], colA[:, cb + 1:cb + 2], 1.0 / D, None, ALU.mult)
            e.drain()
            e.tensor_mul(colA[:, cb + 4:cb + 5], colA[:, cb + 2:cb + 3], colA[:, cb + 2:cb + 3])
            e.drain()
            return e.tensor_sub(colA[:, cb + 5:cb + 6], colA[:, cb + 3:cb + 4], colA[:, cb + 4:cb + 5])
        tv1 = V_(colops, after=[tm, ts1])
        tsq = S_(lambda e, cb=cb: e.activation(colA[:, cb + 6:cb + 7], colA[:, cb + 5:cb + 6],
                                               AF.Sqrt, bias=LN_EPS), after=[tv1])

        def colops2(e, cb=cb):
            e.reciprocal(colA[:, cb + 7:cb + 8], colA[:, cb + 6:cb + 7])
            e.drain()
            e.tensor_mul(colA[:, cb + 8:cb + 9], colA[:, cb + 2:cb + 3], colA[:, cb + 7:cb + 8])
            e.drain()
            return e.tensor_scalar(colA[:, cb + 9:cb + 10], colA[:, cb + 8:cb + 9],
                                   -1.0, None, ALU.mult)
        tv2 = V_(colops2, after=[tsq])
        ta = S_(lambda e, xj=xj, cb=cb: e.activation(xn_tm[:, :], xj, AF.Identity,
                                                     bias=colA[:, cb + 9:cb + 10],
                                                     scale=colA[:, cb + 7:cb + 8]),
                after=[tv2])
        st1["xtm"].append(ta)
        tmx = V_(lambda e, cb=cb: (
            e.tensor_reduce(colA[:, cb + 10:cb + 11], xn_tm[:, :], AX.X, ALU.max,
                            apply_absolute_value=True),
            e.drain(),
            e.tensor_max(mx_col[:, :], mx_col[:, :], colA[:, cb + 10:cb + 11]))[-1],
            after=[ta])
        tm1 = V_(lambda e, cb=cb: e.tensor_reduce(colA[:, cb + 11:cb + 12], xn_tm[:, :],
                                                  AX.X, ALU.add), after=[ta])
        ts2 = S_(lambda e, cb=cb: e.activation(x2e_tm[:, :], xn_tm[:, :], AF.Square,
                                               accum_out=colA[:, cb + 12:cb + 13]),
                 after=[ta])

        def colops3(e, cb=cb):
            e.tensor_scalar(colA[:, cb + 13:cb + 14], colA[:, cb + 11:cb + 12], 1.0 / D, None, ALU.mult)
            e.tensor_scalar(colA[:, cb + 14:cb + 15], colA[:, cb + 12:cb + 13], 1.0 / D, None, ALU.mult)
            e.drain()
            e.tensor_mul(colA[:, cb + 15:cb + 16], colA[:, cb + 13:cb + 14], colA[:, cb + 13:cb + 14])
            e.drain()
            return e.tensor_sub(colA[:, cb + 16:cb + 17], colA[:, cb + 14:cb + 15], colA[:, cb + 15:cb + 16])
        tv3 = V_(colops3, after=[tm1, ts2])
        tsq2 = S_(lambda e, cb=cb: e.activation(colA[:, cb + 17:cb + 18], colA[:, cb + 16:cb + 17],
                                                AF.Sqrt, bias=LN_EPS), after=[tv3])

        def colops4(e, cb=cb):
            e.reciprocal(colA[:, cb + 18:cb + 19], colA[:, cb + 17:cb + 18])
            e.drain()
            e.tensor_mul(colA[:, cb + 19:cb + 20], colA[:, cb + 13:cb + 14], colA[:, cb + 18:cb + 19])
            e.drain()
            return e.tensor_scalar(colA[:, cb + 20:cb + 21], colA[:, cb + 19:cb + 20],
                                   -1.0, None, ALU.mult)
        tv4 = V_(colops4, after=[tsq2])
        ta2 = S_(lambda e, cb=cb: e.activation(x2e_tm[:, :], xn_tm[:, :], AF.Identity,
                                               bias=colA[:, cb + 20:cb + 21],
                                               scale=colA[:, cb + 18:cb + 19]),
                 after=[tv4])
        prevT = []
        for c in range(NDC):
            tT = T_(lambda e, c=c: e.transpose(
                psA[7][:, 0:128], xn_tm[:, c * 128:(c + 1) * 128], id_f[:, :]),
                after=[ta, tmx, ts2] + prevT
                + (st1["b7"] if (j == 0 and c == 0) else []))
            tE = S_(lambda e, c=c, j=j: e.copy(
                xnfm[:, c, 0:128], psA[7][:, 0:128]),
                after=[tT] + (tok_xn_j[-1:] if (c == 0 and tok_xn_j) else [])
                + (st1["xnst"] if (j == 0 and c == 0) else []))
            prevT = [tE]

        def st_xn_j(eng, sem, t=t, j=j):
            eng.dma_start(out=AP(xn_s, t * TILE + j * 128,
                                 [[TC, 128], [128 * TC, NDC], [1, 128]]),
                          in_=xnfm[:, :, :]).then_inc(sem, 16)
        DMA_("sync", st_xn_j, n=1, after=prevT)
        tok_xn_j.append(bld.fence("sync"))
        if "xn_fm" in dbg:
            DMA_("sync", lambda eng, sem, t=t, j=j: eng.dma_start(
                out=AP(dbg["xn_fm"], t * TILE + j * 128,
                       [[TC, 128], [128 * TC, NDC], [1, 128]]),
                in_=xnfm[:, :, :]).then_inc(sem, 16), n=1, after=prevT)
        for c in range(NDC):
            tT = T_(lambda e, c=c: e.transpose(
                psA[7][:, 0:128], x2e_tm[:, c * 128:(c + 1) * 128], id_f[:, :]),
                after=[ta2] + prevT)
            tE = S_(lambda e, c=c, j=j, sl=sl: e.copy(
                x2e[:, c, sl, j * 128:(j + 1) * 128], psA[7][:, 0:128]),
                after=[tT]
                + (st1["chunk_tail"] if (j == 0 and c == 0) else []))
            prevT = [tE]
        tok_x2e_j.append(prevT[0])
        if j == NCHT - 1:
            st1["ln_last"] = [prevT[0]]
            st1["xnst"] = [tok_xn_j[-1]]

    def prep_tail(t):
        st = st1["ln"][t]
        sl = t % 2
        x2e_done = st["x2e_j"][-1]

        def proj_phi(wsb, bC, b1C, dstP, deps):
            outs = []
            tvs = []
            for co in range(NDC):
                mm = []
                for c in range(NDC):
                    mm.append(T_(lambda e, c=c, co=co, wsb=wsb, sl=sl: e.matmul(
                        psA[7][:, :], wsb[:, c, co * 128:(co + 1) * 128], x2e[:, c, sl, :],
                        start=(c == 0), stop=(c == NDC - 1)),
                        after=((deps + [tok_wsc]) if (co == 0 and c == 0) else [])
                        + ([mm[-1]] if mm else ([outs[-1]] if outs else []))))
                tv = V_(lambda e, co=co, bC=bC: e.tensor_scalar(
                    tbuf1[:, :], psA[7][:, :], bC[:, co:co + 1], 0.0, ALU.add, ALU.min),
                    after=[mm[-1]])
                tvs.append(tv)
                tsx = S_(lambda e: e.activation(tbuf1[:, :], tbuf1[:, :], AF.Exp),
                         after=[tv] + ([outs[-1]] if outs else []))
                tph = V_(lambda e, co=co, dstP=dstP, b1C=b1C, sl=sl: e.scalar_tensor_tensor(
                    dstP[:, co, sl, :], psA[7][:, :], b1C[:, co:co + 1], tbuf1[:, :],
                    ALU.add, ALU.max), after=[tsx])
                outs.append(tph)
            return outs, tvs
        qt, qtv = proj_phi(wq, qbC, qb1C, phiq, [x2e_done])
        kt, ktv = proj_phi(wk, kbC, kb1C, phik, [x2e_done, qt[-1]])
        if "phiq" in dbg:
            DMA_("sync", lambda eng, sem, t=t, sl=sl: eng.dma_start(
                out=AP(dbg["phiq"], t * TILE, [[TC, 128], [128 * TC, NDC], [1, TILE]]),
                in_=phiq[:, :, sl, :]).then_inc(sem, 16), n=1, after=[qt[-1]])
        if "phik" in dbg:
            DMA_("sync", lambda eng, sem, t=t, sl=sl: eng.dma_start(
                out=AP(dbg["phik"], t * TILE, [[TC, 128], [128 * TC, NDC], [1, TILE]]),
                in_=phik[:, :, sl, :]).then_inc(sem, 16), n=1, after=[kt[-1]])

        dmm = []
        for c in range(NDC):
            dmm.append(T_(lambda e, c=c, sl=sl: e.matmul(
                psA[7][0:16, 0:512], wdec[:, c, :], x2e[:, c, sl, :],
                start=(c == 0), stop=(c == NDC - 1)),
                after=(([x2e_done, ktv[-1], kt[-1]]) if c == 0 else []) + ([dmm[-1]] if dmm else [])))
        tlam = S_(lambda e, sl=sl: e.activation(lam_t[:, sl, :], psA[7][0:16, 0:512], AF.Sigmoid,
                                                bias=decbc[:, 0:1]), after=[dmm[-1]])
        tlam2 = V_(lambda e, sl=sl: e.tensor_scalar(lam_t[:, sl, :], lam_t[:, sl, :],
                                                    LAM_MIN, None, ALU.max),
                   after=[tlam])
        if "lam_d" in dbg:
            tdl = DMA_("sync", lambda eng, sem, t=t, sl=sl: eng.dma_start(
                out=AP(dbg["lam_d"], t * TILE, [[TC, 16], [1, TILE]]),
                in_=lam_t[:, sl, :]).then_inc(sem, 16), n=1, after=[tlam2])
            tlam2 = tdl
        tll = S_(lambda e, sl=sl: e.activation(llam_t[:, sl, :], lam_t[:, sl, :], AF.Ln),
                 after=[tlam2])
        st1["b7"] = [tlam]
        st1["tll"][t] = tll

    def p1_chunk(t, cc, ch, prev, vgate):
        vslot = cc % 2
        tsl = t % 2
        vtoks = []
        for half in range(2):
            bank = psA[6]
            mm = []
            for c in range(NDC):
                dep = []
                if half == 0 and c == 0:
                    dep = list(prev) + vgate.get(ch - 2, [])
                mm.append(T_(lambda e, c=c, cc=cc, half=half, bank=bank, tsl=tsl: e.matmul(
                    bank[:, :], x2e[:, c, tsl, cc * 128:(cc + 1) * 128],
                    wv[:, c, half * 512:(half + 1) * 512],
                    start=(c == 0), stop=(c == NDC - 1)),
                    after=dep + ([mm[-1]] if mm else (vtoks[-1:] if vtoks else []))))
            ev = V_(lambda e, vslot=vslot, half=half, bank=bank: e.tensor_add(
                AP(vtm, vslot * H * KVE + half * 8 * KVE,
                   [[2 * H * KVE, 128], [KVE, 8], [1, 64]]),
                bank[:, :].rearrange("p (a b) -> p a b", a=8),
                vbtm[:, half * 512:(half + 1) * 512].rearrange("p (a b) -> p a b", a=8)),
                after=[mm[-1]])
            vtoks.append(ev)
        vdone = vtoks[-1]
        if "vtm_d" in dbg:
            DMA_("sync", lambda eng, sem, t=t, cc=cc, vslot=vslot: eng.dma_start(
                out=AP(dbg["vtm_d"], (t * NCHT + cc) * CH * H * KVE,
                       [[H * KVE, 128], [1, H * KVE]]),
                in_=vtm[:, vslot, :]).then_inc(sem, 16), n=1, after=[vdone])
        tlc = V_(lambda e, tsl=tsl: e.tensor_tensor_scan(
            lc_f[:, :], ones16_f[:, 0:CH], llam_t[:, tsl, cc * CH:(cc + 1) * CH],
            0.0, ALU.mult, ALU.add), after=prev)

        def lbk(e):
            e.tensor_copy(lcend_st[:, ch:ch + 1], lc_f[:, CH - 1:CH])
            e.tensor_copy(lpre_st[:, ch:ch + 1], ltot_c[:, :])
            e.drain()
            return e.tensor_add(ltot_c[:, :], ltot_c[:, :], lc_f[:, CH - 1:CH])
        tlbk = V_(lbk, after=[tlc])
        tce = S_(lambda e: e.activation(ctil_f[:, :], lc_f[:, :], AF.Exp), after=[tlc])
        tceb = V_(lambda e: e.tensor_copy(ctil_b[:, :], ctil_f[:, :]), after=[tce])
        tef = V_(lambda e: e.tensor_single_scalar(e_f[:, :], lc_f[:, :],
                                                  lc_f[:, CH - 1:CH], ALU.subtract),
                 after=[tlc])
        tee = S_(lambda e: e.activation(e_f[:, :], e_f[:, :], AF.Exp, scale=-1.0), after=[tef])
        teeb = V_(lambda e: e.tensor_copy(e_b[:, :], e_f[:, :]), after=[tee])
        tLt = T_(lambda e: e.transpose(psA[6][:, 0:16], lc_f[:, :], id_f[0:16, 0:16]),
                 after=[tlc] + prev)
        tCt = T_(lambda e: e.transpose(psA[6][:, 16:32], ctil_f[:, :], id_f[0:16, 0:16]),
                 after=[tce])
        tEt = T_(lambda e: e.transpose(psA[6][:, 32:48], e_f[:, :], id_f[0:16, 0:16]),
                 after=[tee])
        tLe = V_(lambda e: (e.tensor_copy(lctm[:, :], psA[6][:, 0:16]),
                            e.tensor_copy(ctm[:, :], psA[6][:, 16:32]),
                            e.tensor_copy(etm[:, :], psA[6][:, 32:48]))[-1],
                 after=[tLt, tCt, tEt])
        tdc1 = S_(lambda e: e.activation(s16B[:, 1:2], lcend_st[:, ch:ch + 1], AF.Exp),
                  after=[tlbk])
        tdc2 = S_(lambda e: e.activation(s16B[:, 2:3], lpre_st[:, ch:ch + 1], AF.Exp),
                  after=[tlbk])
        tdm1 = V_(lambda e: e.tensor_mul(s16B[:, 8:16], bccol(s16B[:, 1:2], NPAIR),
                                         pair_mask[:, :]), after=[tdc1])
        tdm2 = V_(lambda e: e.tensor_mul(s16A[:, 32:40], bccol(s16B[:, 2:3], NPAIR),
                                         pair_mask[:, :]), after=[tdc2])
        tdb1 = T_(lambda e: e.matmul(psA[6][:, 48:56], sel_half[:, :], s16B[:, 8:16],
                                     start=True, stop=True), after=[tdm1, tLe])
        tdb2 = T_(lambda e: e.matmul(psA[6][:, 56:64], sel_half[:, :], s16A[:, 32:40],
                                     start=True, stop=True), after=[tdm2])
        tdc = V_(lambda e: (e.tensor_copy(declc[:, :], psA[6][:, 48:56]),
                            e.tensor_copy(declp[:, :], psA[6][:, 56:64]))[-1],
                 after=[tdb1, tdb2])

        trh = V_(lambda e: e.tensor_mul(
            drhs[:, :].rearrange("p (a b) -> p a b", a=16),
            bcv(lc_f[:, :], 16), head_mask[:, :].rearrange("p (a b) -> p a b", a=16)),
            after=[tlc])
        dtoks = []
        prevD = [tdc]
        for bh in range(4):
            bank = psA[6]
            tb = T_(lambda e, bh=bh, bank=bank: e.matmul(
                bank[:, 64:576] if False else bank[:, 0:512],
                ones16_b[:, 0:128], drhs[:, bh * 512:(bh + 1) * 512],
                start=True, stop=True), after=[trh] + prevD)
            tsub = V_(lambda e, bh=bh, bank=bank: e.tensor_sub(
                D_sb[:, bh * 512:(bh + 1) * 512],
                AP(lctm, HORDER[bh * 4], [[16, 128], [2, 4], [0, CH]]),
                bank[:, 0:512].rearrange("p (a b) -> p a b", a=4)),
                after=[tb, tLe])
            tab = S_(lambda e, bh=bh: e.activation(D_sb[:, bh * 512:(bh + 1) * 512],
                                                   D_sb[:, bh * 512:(bh + 1) * 512], AF.Abs),
                     after=[tsub])
            tex = S_(lambda e, bh=bh: e.activation(D_sb[:, bh * 512:(bh + 1) * 512],
                                                   D_sb[:, bh * 512:(bh + 1) * 512], AF.Exp,
                                                   scale=-1.0), after=[tab])
            dtoks.append(tex)
            prevD = [tsub]

        if P1_CUT == "lc":
            return [dtoks[-1], tLe, tdc, vdone] + prev
        sc_toks = []
        sc_bank_tok = [None] * 4
        for hh in range(H):
            g, half = hh // 2, hh % 2
            pos = POS[hh]
            bank = psA[2 + pos // 4]
            tk = T_(lambda e, hh=hh, g=g, half=half, bank=bank, pos=pos, tsl=tsl: e.matmul(
                bank[:, (pos % 4) * 128:(pos % 4 + 1) * 128],
                phik[half * 64:(half + 1) * 64, g, tsl, cc * CH:(cc + 1) * CH],
                phiq[half * 64:(half + 1) * 64, g, tsl, cc * CH:(cc + 1) * CH],
                start=True, stop=True),
                after=([dtoks[-1]] + prev) if hh == 0 else [])
            sc_toks.append(tk)
            sc_bank_tok[pos // 4] = tk
        if P1_CUT == "sconly0":
            return [sc_toks[-1], tLe, tdc, dtoks[-1]] + prev
        if P1_CUT == "sconly":
            pvx = []
            for bq in range(4):
                pvx.append(V_(lambda e, bq=bq: e.tensor_copy(
                    scD[:, bq * 512:(bq + 1) * 512], psA[2 + bq][:, :]),
                    after=[sc_toks[bq * 4 + 3]]))
            return [pvx[-1], tLe, tdc, dtoks[-1]] + prev
        sd_toks = []
        for bh in range(4):
            sd_toks.append(V_(lambda e, bh=bh: e.tensor_mul(
                scD[:, bh * 512:(bh + 1) * 512], psA[2 + bh][:, :],
                D_sb[:, bh * 512:(bh + 1) * 512]),
                after=[sc_bank_tok[bh], sc_toks[-1], dtoks[bh]]))
        if P1_CUT == "sd":
            return [sd_toks[-1], tLe, tdc] + prev

        tkf = V_(lambda e, tsl=tsl: e.tensor_copy(
            phikf[:, :].rearrange("p (a b) -> p a b", a=NPAIR),
            phik[:, :, tsl, cc * CH:(cc + 1) * CH]), after=[sc_toks[-1]])
        kT = []
        prevKT = [sc_toks[-1], tkf] + list(prev)
        for gq in range(2):
            tTs = []
            for g4 in range(4):
                g = gq * 4 + g4
                tTs.append(T_(lambda e, g=g, g4=g4: e.transpose(
                    psA[g4 % 2][:, (g4 // 2) * 128:(g4 // 2) * 128 + 128],
                    phikf[:, g * CH:(g + 1) * CH], id_f[:, :]),
                    after=(prevKT if g4 == 0 else [])))
            evs = []
            for g4 in range(4):
                g = gq * 4 + g4
                for half in range(2):
                    hh = 2 * g + half
                    evs.append(V_(lambda e, g4=g4, half=half, hh=hh: (
                        e.tensor_scalar(keB[:, hh * NK:(hh + 1) * NK],
                                        psA[g4 % 2][:, (g4 // 2) * 128 + half * 64:
                                                     (g4 // 2) * 128 + half * 64 + 64],
                                        etm[:, hh:hh + 1], None, ALU.mult),
                        e.tensor_scalar(kcB[:, hh * NK:(hh + 1) * NK],
                                        psA[g4 % 2][:, (g4 // 2) * 128 + half * 64:
                                                     (g4 // 2) * 128 + half * 64 + 64],
                                        ctm[:, hh:hh + 1], None, ALU.mult))[-1],
                        after=([tTs[-1], tLe] if g4 == 0 and half == 0 else [])))
            prevKT = [evs[-1]]
            kT += tTs + evs
        kdone = kT[-1]

        if P1_CUT == "scores":
            return [sd_toks[-1], kdone, tLe, tdc] + prev
        qtk = []
        prevQ = [dtoks[-1], tceb, teeb, tdc]
        for gq in range(2):
            mmq = []
            for g4 in range(4):
                g = gq * 4 + g4
                mmq.append(T_(lambda e, g=g, g4=g4: e.matmul(
                    psA[6][:, g4 * 128:(g4 + 1) * 128],
                    sel_pair_b[:, g * 128:(g + 1) * 128],
                    ctil_b[:, :], start=True, stop=True),
                    after=(prevQ if g4 == 0 else [])))
            tqc = V_(lambda e, gq=gq, sl=ch % 2, tsl=tsl: e.tensor_mul(
                qcB[:, sl, gq * 4 * CH:(gq + 1) * 4 * CH].rearrange(
                    "p (a b) -> p a b", a=4),
                phiq[:, gq * 4:(gq + 1) * 4, tsl, cc * CH:(cc + 1) * CH],
                psA[6][:, 0:512].rearrange("p (a b) -> p a b", a=4)),
                after=[mmq[-1]]
                + ([stq[("qc", ch - 2)]] if (gq == 0 and ("qc", ch - 2) in stq) else []))
            mme = []
            for g4 in range(4):
                g = gq * 4 + g4
                mme.append(T_(lambda e, g=g, g4=g4: e.matmul(
                    psA[6][:, g4 * 128:(g4 + 1) * 128],
                    sel_pair_b[:, g * 128:(g + 1) * 128],
                    e_b[:, :], start=True, stop=True),
                    after=([tqc] if g4 == 0 else [])))
            tqe = V_(lambda e, gq=gq, sl=ch % 2, tsl=tsl: e.tensor_mul(
                qeB[:, sl, gq * 4 * CH:(gq + 1) * 4 * CH].rearrange(
                    "p (a b) -> p a b", a=4),
                phiq[:, gq * 4:(gq + 1) * 4, tsl, cc * CH:(cc + 1) * CH],
                psA[6][:, 0:512].rearrange("p (a b) -> p a b", a=4)),
                after=[mme[-1]])
            prevQ = [tqe]
            qtk += [tqc, tqe]
        qdone = qtk[-1]

        def st_qc(eng, sem, sl=ch % 2):
            eng.dma_start(out=AP(qc_s, ch * 128 * NPAIR * CH, [[NPAIR * CH, 128], [1, NPAIR * CH]]),
                          in_=qcB[:, sl, :]).then_inc(sem, 16)
            eng.dma_start(out=AP(qe_s, ch * 128 * NPAIR * CH, [[NPAIR * CH, 128], [1, NPAIR * CH]]),
                          in_=qeB[:, sl, :]).then_inc(sem, 16)
        DMA_("sync", st_qc, n=2, after=[qdone])
        tok_qst = bld.fence("sync")
        stq[("qc", ch)] = tok_qst

        if P1_CUT == "qc":
            return [qdone, tok_qst, kdone, sd_toks[-1], tdc]
        av_toks = []
        av_bank_tok = [None] * 4
        for hh in range(H):
            g, half = hh // 2, hh % 2
            pos = POS[hh]
            bank = psA[2 + pos // 4]
            m1 = T_(lambda e, hh=hh, bank=bank, pos=pos: e.matmul(
                bank[:, (pos % 4) * KVE:(pos % 4) * KVE + KVE],
                scD[:, pos * CH:(pos + 1) * CH],
                vtm[:, cc % 2, hh * KVE:(hh + 1) * KVE],
                start=True, stop=False),
                after=([sd_toks[pos // 4], qdone, vdone] if hh == 0 else [sd_toks[pos // 4]]))
            m2 = T_(lambda e, hh=hh, g=g, half=half, bank=bank, pos=pos, sl=ch % 2: e.matmul(
                bank[:, (pos % 4) * KVE:(pos % 4) * KVE + KVE],
                qcB[half * 64:(half + 1) * 64, sl, g * CH:(g + 1) * CH],
                Sb16[half * 64:(half + 1) * 64, g * KVE:(g + 1) * KVE],
                start=False, stop=True), after=[m1])
            av_toks.append(m2)
            av_bank_tok[pos // 4] = m2
        pv = []
        for bq in range(4):
            pv.append(V_(lambda e, bq=bq, sl=ch % 2: e.tensor_copy(
                partB[:, sl, bq * 4 * KVE:(bq + 1) * 4 * KVE], psA[2 + bq][:, 0:4 * KVE]),
                after=[av_bank_tok[bq], av_toks[-1]]
                + ([stq[("part", ch - 2)]] if ("part", ch - 2) in stq else [])))

        def st_part(eng, sem, sl=ch % 2):
            eng.dma_start(out=AP(part_s, ch * CH * H * KVE, [[H * KVE, 128], [1, H * KVE]]),
                          in_=partB[:, sl, :]).then_inc(sem, 16)
        DMA_("sync", st_part, n=1, after=pv)
        tok_pst = bld.fence("sync")
        stq[("part", ch)] = tok_pst

        if P1_CUT == "av":
            return [tok_pst, qdone, tok_qst, kdone, tdc, pv[-1]]
        kev_toks = []
        for hh in range(H):
            half = hh % 2
            bank = psA[hh // 8]
            slot = (hh % 8) // 2
            kev_toks.append(T_(lambda e, hh=hh, half=half, bank=bank, slot=slot: e.matmul(
                bank[half * 64:half * 64 + 64, slot * KVE:slot * KVE + KVE],
                keB[:, hh * NK:(hh + 1) * NK],
                vtm[:, cc % 2, hh * KVE:(hh + 1) * KVE],
                start=True, stop=True, tile_position=(0, half * 64)),
                after=([kdone, av_toks[-1]] if hh == 0 else [])))
        sup = []
        for g in range(NPAIR):
            bank = psA[g // 4]
            slot = g % 4
            sup.append(V_(lambda e, g=g, bank=bank, slot=slot: (
                e.tensor_scalar(Tmp_st[:, g * KVE:(g + 1) * KVE],
                                S_st[:, g * KVE:(g + 1) * KVE], declc[:, g:g + 1], None, ALU.mult),
                e.drain(),
                e.tensor_add(S_st[:, g * KVE:(g + 1) * KVE],
                             Tmp_st[:, g * KVE:(g + 1) * KVE],
                             bank[:, slot * KVE:slot * KVE + KVE]),
                e.drain(),
                e.tensor_copy(Sb16[:, g * KVE:(g + 1) * KVE],
                              S_st[:, g * KVE:(g + 1) * KVE]))[-1],
                after=[kev_toks[7 if g < 4 else 15], tdc, av_toks[-1]]))
        kcv_toks = []
        for hh in range(H):
            half = hh % 2
            bank = psA[hh // 8]
            slot = (hh % 8) // 2
            kcv_toks.append(T_(lambda e, hh=hh, half=half, bank=bank, slot=slot: e.matmul(
                bank[half * 64:half * 64 + 64, slot * KVE:slot * KVE + KVE],
                kcB[:, hh * NK:(hh + 1) * NK],
                vtm[:, cc % 2, hh * KVE:(hh + 1) * KVE],
                start=True, stop=True, tile_position=(0, half * 64)),
                after=([sup[-1]] if hh == 0 else [])))
        kce = []
        for g in range(NPAIR):
            bank = psA[g // 4]
            slot = g % 4
            kce.append(V_(lambda e, g=g, ch=ch, bank=bank, slot=slot: (
                e.tensor_copy(kcvs[:, ch % 2, g * KVE:(g + 1) * KVE],
                              bank[:, slot * KVE:slot * KVE + KVE]),
                e.tensor_scalar(Tmp_st[:, g * KVE:(g + 1) * KVE],
                                bank[:, slot * KVE:slot * KVE + KVE],
                                declp[:, g:g + 1], None, ALU.mult),
                e.drain(),
                e.tensor_add(Sb_st[:, g * KVE:(g + 1) * KVE],
                             Sb_st[:, g * KVE:(g + 1) * KVE],
                             Tmp_st[:, g * KVE:(g + 1) * KVE]))[-1],
                after=[kcv_toks[7 if g < 4 else 15], tdc, sup[-1] if g == 0 else kce[-1]]
                + (stq.get(("kcv", ch - 2), []) if g == 0 else [])))

        def st_kcv(eng, sem, ch=ch):
            eng.dma_start(out=AP(kcv_s, ch * 128 * NPAIR * KVE,
                                 [[NPAIR * KVE, 128], [1, NPAIR * KVE]]),
                          in_=kcvs[:, ch % 2, :]).then_inc(sem, 16)
        DMA_("sync", st_kcv, n=1, after=[kce[-1]])
        stq[("kcv", ch)] = [bld.fence("sync")]
        vgate[ch] = [kev_toks[-1], kcv_toks[-1]]
        return [sup[-1], kce[-1], pv[-1], qdone]

    st1["ln_last"] = [tok_wsc, tok_clip, tok_vone]
    emit_ldx(0)
    for j in range(NCHT):
        prep_ln_part(0, j)
    prep_tail(0)
    for t in range(NT):
        if t + 1 < NT:
            emit_ldx(t + 1)
        prev = [st1["tll"][t]] + st1["chunk_tail"]
        for cc in range(NCHT):
            prev = p1_chunk(t, cc, t * NCHT + cc, prev, st1["vg"])
            if t + 1 < NT:
                prep_ln_part(t + 1, cc)
        if t + 1 < NT:
            prep_tail(t + 1)
        st1["chunk_tail"] = list(prev)
    gate = prev
    p1_end = list(gate)
    if "sf_d" in dbg:
        DMA_("sync", lambda eng, sem: eng.dma_start(
            out=dbg["sf_d"].ap(), in_=S_st[:, :]).then_inc(sem, 16), n=1, after=p1_end)
    if "sb_d" in dbg:
        DMA_("sync", lambda eng, sem: eng.dma_start(
            out=dbg["sb_d"].ap(), in_=Sb_st[:, :]).then_inc(sem, 16), n=1, after=p1_end)

    q_p1 = bld.quiesce("sync")
    bld.flush("p1")
    if PHASE_LIMIT == "p1":
        return nc
    es1.close()

    es_x = ExitStack()
    psX = [PS(f"psX{i}", es_x) for i in range(4)]
    rowX = SB("rowX", [1, 192], F32, es_x)
    tmxT = T_(lambda e: e.transpose(psX[0][0:1, 0:128], mx_col[:, :], id_f[:, :]))
    tmxE = V_(lambda e: (e.tensor_copy(rowX[0:1, 0:128], psX[0][0:1, 0:128]),
                         e.memset(rowA[0:1, 31:46], 0.0),
                         e.drain(),
                         e.tensor_reduce(rowA[0:1, 30:31], rowX[0:1, 0:128], AX.X, ALU.max))[-1],
              after=[tmxT])

    def st_ag(eng, sem):
        for half in range(2):
            eng.dma_start(out=AP(ag_in, half * 4160,
                                 [[65, 64], [2 * 4160, NPAIR], [1, 65]]),
                          in_=S_st[half * 64:half * 64 + 64, :].rearrange("p (a b) -> p a b", a=NPAIR)
                          ).then_inc(sem, 16)
            eng.dma_start(out=AP(ag_in, PAY_SEC + half * 4160,
                                 [[65, 64], [2 * 4160, NPAIR], [1, 65]]),
                          in_=Sb_st[half * 64:half * 64 + 64, :].rearrange("p (a b) -> p a b", a=NPAIR)
                          ).then_inc(sem, 16)
        eng.dma_start(out=AP(ag_in, 2 * PAY_SEC, [[1, 16], [1, 1]]),
                      in_=ltot_c[:, :]).then_inc(sem, 16)
        eng.dma_start(out=AP(ag_in, 2 * PAY_SEC + 16, [[1, 16], [1, 1]]),
                      in_=rowA[0:1, 30:46]).then_inc(sem, 16)
    tok_ag_in = DMA_("gpsimd", st_ag, n=6, after=[tmxE])

    tok_cc = bld.g_async(lambda e: e.collective_compute(
        "AllGather", ALU.bypass, replica_groups=[list(range(8))],
        ins=[ag_in.ap().opt()], outs=[ag_out.ap().opt()]),
        after=[tok_ag_in])

    wprep = [SB(f"wprep{i}", [128, IFF], F32, es_x) for i in range(2)]
    wtT = SB("wtT", [128, 2, NIC * 128], BF16, es_x)
    wpc = SB("wpc", [128, 8], F32, es_x)

    blocks = []
    for wname, wT_dst, sc_dst, nrows in [
            ("L0_ffn_w_gate", wgT_s, sg_sc, IFF),
            ("L0_ffn_w_up", wuT_s, su_sc, IFF),
            ("L0_ffn_w_down", wdT_s, sd_sc, D)]:
        for r in range(nrows // 128):
            blocks.append((wname, wT_dst, sc_dst, nrows, r))

    def emit_ld(idx):
        wname, _, _, _, r = blocks[idx]
        ncols = win[wname].shape[1]
        buf = wprep[idx % 2]

        def ldw(eng, sem, r=r, buf=buf, wname=wname, ncols=ncols):
            eng.dma_start(out=buf[:, 0:ncols],
                          in_=AP(win[wname], r * 128 * ncols, [[ncols, 128], [1, ncols]])
                          ).then_inc(sem, 16)
        return DMA_("sync", ldw, n=1, after=pw_st["buf"].get(idx % 2, []))

    pw_st = {"trd": None, "bank": {}, "buf": {}, "wst": {}}
    pend_ld = {0: emit_ld(0), 1: emit_ld(1)}
    pw = []
    for idx, (wname, wT_dst, sc_dst, nrows, r) in enumerate(blocks):
        ncols = win[wname].shape[1]
        nbl = ncols // 128
        buf = wprep[idx % 2]
        tl = pend_ld.pop(idx)
        tmax = V_(lambda e, buf=buf, ncols=ncols: (
            e.tensor_reduce(wpc[:, 0:1], buf[:, 0:ncols], AX.X, ALU.max,
                            apply_absolute_value=True),
            e.drain(),
            e.tensor_scalar(wpc[:, 1:2], wpc[:, 0:1], 1.0 / QMAX, None, ALU.mult),
            e.drain(),
            e.reciprocal(wpc[:, 2:3], wpc[:, 1:2]))[-1],
            after=[tl] + ([pw_st["trd"]] if pw_st["trd"] else []))
        tsc = V_(lambda e, r=r, sc_dst=sc_dst: e.tensor_copy(sc_dst[:, r:r + 1], wpc[:, 1:2]),
                 after=[tmax])
        trd = S_(lambda e, buf=buf, ncols=ncols: e.activation(
            buf[:, 0:ncols], buf[:, 0:ncols], AF.Identity,
            bias=MAGIC, scale=wpc[:, 2:3]), after=[tmax, tsc])
        tcl = V_(lambda e, buf=buf, ncols=ncols: (
            e.tensor_scalar(buf[:, 0:ncols], buf[:, 0:ncols], MAGIC, QMAX,
                            ALU.subtract, ALU.min),
            e.tensor_scalar(buf[:, 0:ncols], buf[:, 0:ncols], -QMAX - 1.0, None, ALU.max))[-1],
            after=[trd])
        pw_st["trd"] = trd
        tT2 = []
        for c in range(nbl):
            bk = c % 4
            bank = psX[bk]
            tt = T_(lambda e, bank=bank, c=c, buf=buf: e.transpose(
                bank[:, 0:128], buf[:, c * 128:(c + 1) * 128], id_f[:, :]),
                after=[tcl] + (([tT2[c - 4]] if c >= 4 else pw_st["bank"].get(bk, []))))
            te = S_(lambda e, c=c, bank=bank, sl=idx % 2: e.copy(
                wtT[:, sl, c * 128:(c + 1) * 128], bank[:, 0:128]),
                after=[tt] + ([pw_st["wst"][idx % 2]] if (c == 0 and idx % 2 in pw_st["wst"]) else []))
            tT2.append(te)
            pw_st["bank"][bk] = [te]
        pw_st["buf"][idx % 2] = [tT2[-1]]
        if idx + 2 < len(blocks):
            pend_ld[idx + 2] = emit_ld(idx + 2)

        def stw(eng, sem, r=r, wT_dst=wT_dst, nbl=nbl, nrows=nrows, sl=idx % 2):
            eng.dma_start(out=AP(wT_dst, r * 128,
                                 [[nrows, 128], [128 * nrows, nbl], [1, 128]]),
                          in_=wtT[:, sl, 0:nbl * 128].rearrange("p (a b) -> p a b", a=nbl)
                          ).then_inc(sem, 16)
        pw = [DMA_("sync", stw, n=1, after=[tT2[-1]])]
        pw_st["wst"][idx % 2] = pw[0]

    tok_ccw = G_(lambda e: e.nop(), after=[tok_cc, pw[-1]])
    q_x1 = bld.quiesce("sync")
    es_x.close()
    bld.flush("x1")
    if PHASE_LIMIT == "x1":
        return nc

    es2 = ExitStack()
    sel_pair_f = SB("sel_pair_f", [16, NPAIR * 128], F32, es2)
    SinC = SB("SinC", [128, NPAIR * KVE], BF16, es2)
    Rb16 = SB("Rb16", [128, NPAIR * KVE], BF16, es2)
    wo = SB("wo", [128, NDC, D], BF16, es2)
    scr2 = SB("scr2", [128, TILE], F32, es2)
    LtG = SB("LtG", [8, 16], F32, es2)
    mG = SB("mG", [8, 1], F32, es2)
    Wf8 = SB("Wf8", [8, 16], F32, es2)
    Wb8 = SB("Wb8", [8, 16], F32, es2)
    WfT = SB("WfT", [16, 8], F32, es2)
    WbT = SB("WbT", [16, 8], F32, es2)
    Wexp = SB("Wexp", [128, NPAIR * 8], F32, es2)
    Wexpb = SB("Wexpb", [128, NPAIR * 8], F32, es2)
    peerb = SB("peerb", [128, NPAIR * KVE], F32, es2)
    xnfm2 = SB("xnfm2", [128, NDC, TILE], F32, es2)
    qcL = SB("qcL", [128, NCHT, NPAIR * CH], BF16, es2)
    qeL = SB("qeL", [128, NCHT, NPAIR * CH], BF16, es2)
    partL = SB("partL", [128, NCHT, H * KVE], BF16, es2)
    totB = SB("totB", [128, H * KVE], F32, es2)
    atm = SB("atm", [128, H * NK], F32, es2)
    attnfm = SB("attnfm", [128, NDC, TILE], BF16, es2)
    x1b = SB("x1b", [128, NDC, 2, TILE], F32, es2)
    kcvL = SB("kcvL", [128, 2, NPAIR * KVE], BF16, es2)
    xqb = SB("xqb", [128, NDC, 2, TILE], BF16, es2)
    psB = [PS(f"psB{i}", es2) for i in range(8)]

    def p2_wo(eng, sem):
        eng.dma_start(out=wo[:, :, :],
                      in_=AP(win["L0_attn_o_w"], 0, [[D, 128], [128 * D, NDC], [1, D]])
                      ).then_inc(sem, 16)
    tok_wo = DMA_("gpsimd", p2_wo, n=1)

    def p2_spf(eng, sem):
        eng.dma_start(out=sel_pair_f[:, :], in_=const_t["sel_pair_f"].ap()).then_inc(sem, 16)
    tok_spf = DMA_("sync", p2_spf, n=1)

    def p2_gather_small(eng, sem):
        eng.dma_start(out=LtG[:, :], in_=AP(ag_out, 2 * PAY_SEC, [[PAY, 8], [1, 16]])
                      ).then_inc(sem, 16)
        eng.dma_start(out=mG[:, :], in_=AP(ag_out, 2 * PAY_SEC + 16, [[PAY, 8], [1, 1]]),
                      allow_slow_non_contiguous=True).then_inc(sem, 16)
    tok_gs = DMA_("sync", p2_gather_small, n=2, after=[tok_ccw])

    tm1T = T_(lambda e: e.transpose(psB[0][0:1, 0:8], mG[:, :], id_f[0:8, 0:8]),
              after=[tok_gs])
    tm1 = V_(lambda e: (e.tensor_copy(rowA[0:1, 40:48], psB[0][0:1, 0:8]),
                        e.drain(),
                        e.tensor_reduce(rowA[0:1, 24:25], rowA[0:1, 40:48], AX.X, ALU.max),
                        e.drain(),
                        e.tensor_mul(rowA[0:1, 25:26], rowA[0:1, 24:25], clipc[0:1, 0:1]),
                        e.drain(),
                        e.tensor_scalar(rowA[0:1, 26:27], rowA[0:1, 25:26], 1.0 / QMAX, None, ALU.mult),
                        e.drain(),
                        e.reciprocal(rowA[0:1, 27:28], rowA[0:1, 26:27]))[-1],
             after=[tm1T])
    if "m1_d" in dbg:
        DMA_("sync", lambda eng, sem: eng.dma_start(
            out=dbg["m1_d"].ap(), in_=rowA[0:1, 24:40]).then_inc(sem, 16), n=1, after=[tm1])
    tb1 = T_(lambda e: (e.matmul(psB[1][:, 0:1], ones_row[:, :], rowA[0:1, 26:27],
                                 start=True, stop=True),
                        e.matmul(psB[1][:, 1:2], ones_row[:, :], rowA[0:1, 27:28],
                                 start=True, stop=True))[-1], after=[tm1])
    tb1e = V_(lambda e: (e.tensor_copy(sc1_col[:, :], psB[1][:, 0:1]),
                         e.tensor_copy(rs1_col[:, :], psB[1][:, 1:2]))[-1], after=[tb1])

    tWf = T_(lambda e: (e.matmul(psB[2][0:8, 0:16], ccpf[:, :], LtG[:, :], start=True, stop=True),
                        e.matmul(psB[2][0:8, 16:32], ccpb[:, :], LtG[:, :], start=True, stop=True))[-1],
             after=[tok_gs, tb1])
    tWfe = S_(lambda e: (e.activation(Wf8[:, :], psB[2][0:8, 0:16], AF.Exp),
                         e.activation(Wb8[:, :], psB[2][0:8, 16:32], AF.Exp))[-1], after=[tWf])
    tWfm = V_(lambda e: (e.tensor_scalar(Wf8[:, :], Wf8[:, :], ccmf[:, 0:1], None, ALU.mult),
                         e.tensor_scalar(Wb8[:, :], Wb8[:, :], ccmb[:, 0:1], None, ALU.mult))[-1],
              after=[tWfe])
    tWT = T_(lambda e: (e.transpose(psB[3][0:16, 0:8], Wf8[:, :], id_f[0:8, 0:8]),
                        e.transpose(psB[3][0:16, 8:16], Wb8[:, :], id_f[0:8, 0:8]))[-1],
             after=[tWfm])
    tWTe = V_(lambda e: (e.tensor_copy(WfT[:, :], psB[3][0:16, 0:8]),
                         e.tensor_copy(WbT[:, :], psB[3][0:16, 8:16]))[-1], after=[tWT])
    wx = []
    for g in range(NPAIR):
        wx.append(T_(lambda e, g=g: (
            e.matmul(psB[4][:, g * 8:(g + 1) * 8], sel_pair_f[:, g * 128:(g + 1) * 128],
                     WfT[:, :], start=True, stop=True),
            e.matmul(psB[4][:, 64 + g * 8:64 + (g + 1) * 8],
                     sel_pair_f[:, g * 128:(g + 1) * 128], WbT[:, :], start=True, stop=True))[-1],
            after=[tWTe, tok_spf] + (wx[-1:] if wx else [])))
    tWx = V_(lambda e: (e.tensor_copy(Wexp[:, :], psB[4][:, 0:64]),
                        e.tensor_copy(Wexpb[:, :], psB[4][:, 64:128]))[-1], after=[wx[-1]])

    comb = [tWx]
    for peer in range(8):
        def ld_peer(eng, sem, peer=peer):
            for half in range(2):
                eng.dma_start(out=peerb[half * 64:half * 64 + 64, :].rearrange(
                    "p (a b) -> p a b", a=NPAIR),
                    in_=AP(ag_out, peer * PAY + half * 4160,
                           [[65, 64], [2 * 4160, NPAIR], [1, 65]])).then_inc(sem, 16)
        tlp = DMA_("sync", ld_peer, n=2, after=comb[-1:])
        ops = []
        for g in range(NPAIR):
            ops.append(V_(lambda e, g=g, peer=peer: (
                e.tensor_scalar(Tmp_st[:, g * KVE:(g + 1) * KVE],
                                peerb[:, g * KVE:(g + 1) * KVE],
                                Wexp[:, g * 8 + peer:g * 8 + peer + 1], None, ALU.mult),
                e.drain(),
                e.tensor_add(Sin_st[:, g * KVE:(g + 1) * KVE],
                             Sin_st[:, g * KVE:(g + 1) * KVE],
                             Tmp_st[:, g * KVE:(g + 1) * KVE]))[-1],
                after=[tlp, tWx]))

        def ld_peerb(eng, sem, peer=peer):
            for half in range(2):
                eng.dma_start(out=peerb[half * 64:half * 64 + 64, :].rearrange(
                    "p (a b) -> p a b", a=NPAIR),
                    in_=AP(ag_out, peer * PAY + PAY_SEC + half * 4160,
                           [[65, 64], [2 * 4160, NPAIR], [1, 65]])).then_inc(sem, 16)
        tlpb = DMA_("sync", ld_peerb, n=2, after=[ops[-1]])
        for g in range(NPAIR):
            ops.append(V_(lambda e, g=g, peer=peer: (
                e.tensor_scalar(Tmp_st[:, g * KVE:(g + 1) * KVE],
                                peerb[:, g * KVE:(g + 1) * KVE],
                                Wexpb[:, g * 8 + peer:g * 8 + peer + 1], None, ALU.mult),
                e.drain(),
                e.tensor_add(Rin_st[:, g * KVE:(g + 1) * KVE],
                             Rin_st[:, g * KVE:(g + 1) * KVE],
                             Tmp_st[:, g * KVE:(g + 1) * KVE]))[-1],
                after=[tlpb]))
        comb = [ops[-1]]
    tok_comb = comb[-1]
    tok_Rinit = V_(lambda e: (e.tensor_copy(R_st[:, :], Rin_st[:, :]),
                              e.tensor_copy(Rb16[:, :], Rin_st[:, :]))[-1], after=[tok_comb])
    if "sin_d" in dbg:
        DMA_("sync", lambda eng, sem: eng.dma_start(
            out=dbg["sin_d"].ap(), in_=Sin_st[:, :]).then_inc(sem, 16), n=1, after=[tok_comb])
    if "rin_d" in dbg:
        DMA_("sync", lambda eng, sem: eng.dma_start(
            out=dbg["rin_d"].ap(), in_=Rin_st[:, :]).then_inc(sem, 16), n=1, after=[tok_comb])

    def p2_tile(t, gate):
        def ld_t(eng, sem):
            eng.dma_start(out=xnfm2[:, :, :],
                          in_=AP(xn_s, t * TILE, [[TC, 128], [128 * TC, NDC], [1, TILE]])
                          ).then_inc(sem, 16)
            eng.dma_start(out=qcL[:, :, :],
                          in_=AP(qc_s, t * NCHT * 128 * NPAIR * CH,
                                 [[NPAIR * CH, 128], [128 * NPAIR * CH, NCHT], [1, NPAIR * CH]])
                          ).then_inc(sem, 16)
            eng.dma_start(out=qeL[:, :, :],
                          in_=AP(qe_s, t * NCHT * 128 * NPAIR * CH,
                                 [[NPAIR * CH, 128], [128 * NPAIR * CH, NCHT], [1, NPAIR * CH]])
                          ).then_inc(sem, 16)
            eng.dma_start(out=partL[:, :, :],
                          in_=AP(part_s, t * NCHT * CH * H * KVE,
                                 [[H * KVE, 128], [128 * H * KVE, NCHT], [1, H * KVE]])
                          ).then_inc(sem, 16)
        tok_l = DMA_("sync", ld_t, n=4, after=gate + [q_p1])

        last = list(gate)
        for cc in range(NCHT - 1, -1, -1):
            ch = t * NCHT + cc
            td1 = S_(lambda e, ch=ch: (
                e.activation(s16B[:, 1:2], lcend_st[:, ch:ch + 1], AF.Exp),
                e.activation(s16B[:, 2:3], lpre_st[:, ch:ch + 1], AF.Exp))[-1], after=last)
            td2 = V_(lambda e: (
                e.tensor_mul(s16B[:, 8:16], bccol(s16B[:, 1:2], NPAIR), pair_mask[:, :]),
                e.tensor_mul(s16A[:, 32:40], bccol(s16B[:, 2:3], NPAIR), pair_mask[:, :]))[-1],
                after=[td1])
            td3 = T_(lambda e: (
                e.matmul(psB[5][:, 0:8], sel_half[:, :], s16B[:, 8:16], start=True, stop=True),
                e.matmul(psB[5][:, 8:16], sel_half[:, :], s16A[:, 32:40], start=True, stop=True))[-1],
                after=[td2] + last)
            td4 = V_(lambda e: (e.tensor_copy(declc[:, :], psB[5][:, 0:8]),
                                e.tensor_copy(declp[:, :], psB[5][:, 8:16]))[-1], after=[td3])
            sic = []
            for g in range(NPAIR):
                sic.append(V_(lambda e, g=g: e.tensor_scalar(
                    SinC[:, g * KVE:(g + 1) * KVE], Sin_st[:, g * KVE:(g + 1) * KVE],
                    declp[:, g:g + 1], None, ALU.mult), after=[td4, tok_Rinit]))
            mmz = []
            mm_bank_tok = [None] * 4
            for hh in range(H):
                g, half = hh // 2, hh % 2
                pos = POS[hh]
                bank = psB[pos // 4]
                ma = T_(lambda e, hh=hh, g=g, half=half, bank=bank, pos=pos, cc=cc: e.matmul(
                    bank[:, (pos % 4) * KVE:(pos % 4) * KVE + KVE],
                    qeL[half * 64:half * 64 + 64, cc, g * CH:(g + 1) * CH],
                    Rb16[half * 64:half * 64 + 64, g * KVE:(g + 1) * KVE],
                    start=True, stop=False),
                    after=([sic[-1], tok_l] + last) if hh == 0 else [])
                mb = T_(lambda e, hh=hh, g=g, half=half, bank=bank, pos=pos, cc=cc: e.matmul(
                    bank[:, (pos % 4) * KVE:(pos % 4) * KVE + KVE],
                    qcL[half * 64:half * 64 + 64, cc, g * CH:(g + 1) * CH],
                    SinC[half * 64:half * 64 + 64, g * KVE:(g + 1) * KVE],
                    start=False, stop=True), after=[ma])
                mmz.append(mb)
                mm_bank_tok[pos // 4] = mb
            pa = []
            for bq in range(4):
                pa.append(V_(lambda e, bq=bq, cc=cc: e.tensor_add(
                    totB[:, bq * 4 * KVE:(bq + 1) * 4 * KVE],
                    psB[bq][:, 0:4 * KVE],
                    partL[:, cc, bq * 4 * KVE:(bq + 1) * 4 * KVE]),
                    after=[mm_bank_tok[bq], mmz[-1], tok_l]))
            dn = []
            for hh in range(H):
                pos = POS[hh]
                dn.append(V_(lambda e, hh=hh, pos=pos: (
                    e.tensor_scalar(colA[:, 40:41], totB[:, pos * KVE + NK:pos * KVE + NK + 1],
                                    EPS_DEN, None, ALU.max),
                    e.drain(),
                    e.reciprocal(colA[:, 41:42], colA[:, 40:41]),
                    e.drain(),
                    e.tensor_scalar(atm[:, hh * NK:(hh + 1) * NK],
                                    totB[:, pos * KVE:pos * KVE + NK],
                                    colA[:, 41:42], None, ALU.mult))[-1],
                    after=[pa[pos // 4]] + (dn[-1:] if dn else [])))
            ru = []
            for g in range(NPAIR):
                ru.append(V_(lambda e, g=g, ch=ch: (
                    e.tensor_scalar(Tmp_st[:, g * KVE:(g + 1) * KVE],
                                    R_st[:, g * KVE:(g + 1) * KVE],
                                    declc[:, g:g + 1], None, ALU.mult),
                    e.drain(),
                    e.tensor_add(R_st[:, g * KVE:(g + 1) * KVE],
                                 Tmp_st[:, g * KVE:(g + 1) * KVE],
                                 kcvL[:, ch % 2, g * KVE:(g + 1) * KVE]),
                    e.drain(),
                    e.tensor_copy(Rb16[:, g * KVE:(g + 1) * KVE],
                                  R_st[:, g * KVE:(g + 1) * KVE]))[-1],
                    after=[td4, tok_Rinit, mmz[-1], st2kl[ch]] + (ru[-1:] if ru else [])))
            if ch - 2 >= 0:
                emit_kl(ch - 2, [ru[-1]])
            at = []
            for g in range(NPAIR):
                bank = psB[6]
                tt = T_(lambda e, g=g, bank=bank: e.transpose(
                    bank[:, (g % 4) * 128:(g % 4) * 128 + 128],
                    atm[:, g * 128:(g + 1) * 128], id_f[:, :]),
                    after=[dn[2 * g + 1]] + ([at[-1]] if at else [mmz[-1]]))
                te = S_(lambda e, g=g, cc=cc, bank=bank: e.copy(
                    attnfm[:, g, cc * CH:(cc + 1) * CH], bank[:, (g % 4) * 128:(g % 4) * 128 + 128]),
                    after=[tt])
                at.append(te)
            last = [ru[-1], at[-1], dn[-1]]
        attn_done = last[1]
        if "attn_d" in dbg:
            DMA_("sync", lambda eng, sem, t=t: eng.dma_start(
                out=AP(dbg["attn_d"], t * TILE, [[TC, 128], [128 * TC, NDC], [1, TILE]]),
                in_=AP(attnfm, 0, [[NDC * TILE, 128], [TILE, NDC], [1, TILE]])).then_inc(sem, 16),
                n=1, after=last)

        oT = []
        for co in range(NDC):
            bank = psB[7]
            mm = []
            for c in range(NDC):
                mm.append(T_(lambda e, c=c, co=co, bank=bank: e.matmul(
                    bank[:, :], wo[:, c, co * 128:(co + 1) * 128], attnfm[:, c, :],
                    start=(c == 0), stop=(c == NDC - 1)),
                    after=(([tok_wo, attn_done] + last) if (co == 0 and c == 0) else [])
                    + ([mm[-1]] if mm else (oT[-1:] if oT else []))))
            ev = V_(lambda e, co=co, bank=bank, sl=t % 2: e.scalar_tensor_tensor(
                x1b[:, co, sl, :], bank[:, :], obC[:, co:co + 1], xnfm2[:, co, :],
                ALU.add, ALU.add), after=[mm[-1], tok_l]
                + ([stq[("x1", t + 2)]] if (co == 0 and ("x1", t + 2) in stq) else []))
            oT.append(ev)

        def st_x1(eng, sem, sl=t % 2):
            eng.dma_start(out=AP(x1_s, t * TILE, [[TC, 128], [128 * TC, NDC], [1, TILE]]),
                          in_=x1b[:, :, sl, :]).then_inc(sem, 16)
        DMA_("sync", st_x1, n=1, after=oT)
        tok_x1st = bld.fence("sync")
        stq[("x1", t)] = tok_x1st
        if "x1_d" in dbg:
            DMA_("sync", lambda eng, sem, t=t: eng.dma_start(
                out=AP(dbg["x1_d"], t * TILE, [[TC, 128], [128 * TC, NDC], [1, TILE]]),
                in_=x1b[:, :, t % 2, :]).then_inc(sem, 16), n=1, after=oT)

        xq_toks = []
        for co in range(NDC):
            t1 = S_(lambda e, co=co: e.activation(scr2[:, :],
                                                  xnfm2[:, co, :], AF.Identity,
                                                  bias=MAGIC, scale=rs1_col[:, 0:1]),
                    after=[tok_l, tb1e] + ([xq_toks[-1]] if xq_toks else []))
            t2 = V_(lambda e, co=co, sl=t % 2: (
                e.tensor_scalar(scr2[:, :], scr2[:, :], MAGIC, QMAX,
                                ALU.subtract, ALU.min),
                e.tensor_scalar(xqb[:, co, sl, :], scr2[:, :], -QMAX - 1.0, None, ALU.max))[-1],
                after=[t1]
                + ([stq[("xq", t + 2)]] if (co == 0 and ("xq", t + 2) in stq) else []))
            xq_toks.append(t2)

        def st_xq(eng, sem, sl=t % 2):
            eng.dma_start(out=AP(xq_s, t * TILE, [[TC, 128], [128 * TC, NDC], [1, TILE]]),
                          in_=xqb[:, :, sl, :]).then_inc(sem, 16)
        DMA_("sync", st_xq, n=1, after=[xq_toks[-1]])
        tok_xqst = bld.fence("sync")
        stq[("xq", t)] = tok_xqst
        if "xq_d" in dbg:
            DMA_("sync", lambda eng, sem, t=t: eng.dma_start(
                out=AP(dbg["xq_d"], t * TILE, [[TC, 128], [128 * TC, NDC], [1, TILE]]),
                in_=xqb[:, :, t % 2, :]).then_inc(sem, 16),
                n=1, after=[xq_toks[-1]])
        return [oT[-1], xq_toks[-1]] + last

    st2kl = {}

    def emit_kl(ch, after):
        def ld_kl(eng, sem, ch=ch):
            eng.dma_start(out=kcvL[:, ch % 2, :],
                          in_=AP(kcv_s, ch * 128 * NPAIR * KVE,
                                 [[NPAIR * KVE, 128], [1, NPAIR * KVE]])).then_inc(sem, 16)
        st2kl[ch] = DMA_("sync", ld_kl, n=1, after=after)

    emit_kl(NCH - 1, [])
    emit_kl(NCH - 2, [])
    gate2 = [tok_Rinit, tok_comb, tb1e]
    for t in range(NT - 1, -1, -1):
        gate2 = p2_tile(t, gate2)
    q_p2 = bld.quiesce("sync")
    bld.flush("p2")
    if PHASE_LIMIT == "p2":
        return nc
    es2.close()
    es_kcv.close()

    es4 = ExitStack()
    wgT = SB("wgT", [128, NDC, IFF], BF16, es4)
    wuT = SB("wuT", [128, NDC, IFF], BF16, es4)
    xqL = [SB(f"xqL{i}", [128, NDC, TILE], BF16, es4) for i in range(2)]
    silb = SB("silb", [128, TILE], F32, es4)
    h2b = [SB(f"h2b{i}", [128, TILE], F32, es4) for i in range(4)]
    sgv = SB("sgv", [128, NIC], F32, es4)
    suv = SB("suv", [128, NIC], F32, es4)
    psC = [PS(f"psC{i}", es4) for i in range(4)]

    def p4_w(eng, sem):
        eng.dma_start(out=wgT[:, :, :],
                      in_=AP(wgT_s, 0, [[IFF, 128], [128 * IFF, NDC], [1, IFF]])).then_inc(sem, 16)
        eng.dma_start(out=wuT[:, :, :],
                      in_=AP(wuT_s, 0, [[IFF, 128], [128 * IFF, NDC], [1, IFF]])).then_inc(sem, 16)
    tok_w4 = DMA_("gpsimd", p4_w, n=2, after=[q_x1])
    tok_sv = V_(lambda e: (e.tensor_scalar(sgv[:, :], sg_sc[:, 0:NIC], sc1_col[:, 0:1], None, ALU.mult),
                           e.tensor_scalar(suv[:, :], su_sc[:, 0:NIC], sc1_col[:, 0:1], None, ALU.mult))[-1])

    def p4_tile(t, gate):
        xq_ = xqL[t % 2]

        def ld_xq(eng, sem):
            eng.dma_start(out=xq_[:, :, :],
                          in_=AP(xq_s, t * TILE, [[TC, 128], [128 * TC, NDC], [1, TILE]])
                          ).then_inc(sem, 16)
        tl = DMA_("sync", ld_xq, n=1, after=gate + [q_p2])
        sil_toks, h2_toks, mx_toks, st_toks, gmm_toks, umm_toks = [], [], [], [], [], []
        for ic in range(NIC):
            h2slot = (t * NIC + ic) % 4
            bg = psC[ic % 2]
            bu = psC[2 + ic % 2]
            mmg = []
            for c in range(NDC):
                dep = []
                if ic == 0 and c == 0:
                    dep = [tok_w4, tl] + gate
                elif c == 0 and ic >= 2:
                    dep = [sil_toks[ic - 2]]
                mmg.append(T_(lambda e, c=c, ic=ic, bg=bg, xq_=xq_: e.matmul(
                    bg[:, :], wgT[:, c, ic * 128:(ic + 1) * 128], xq_[:, c, :],
                    start=(c == 0), stop=(c == NDC - 1)),
                    after=dep + ([mmg[-1]] if mmg else [])))
            gmm_toks.append(mmg[-1])
            mmu = []
            for c in range(NDC):
                dep = [mmg[-1]] if c == 0 else []
                if c == 0 and ic >= 2:
                    dep = dep + [h2_toks[ic - 2]]
                mmu.append(T_(lambda e, c=c, ic=ic, bu=bu, xq_=xq_: e.matmul(
                    bu[:, :], wuT[:, c, ic * 128:(ic + 1) * 128], xq_[:, c, :],
                    start=(c == 0), stop=(c == NDC - 1)),
                    after=dep + ([mmu[-1]] if mmu else [])))
            umm_toks.append(mmu[-1])
            tsil = S_(lambda e, ic=ic, bg=bg: e.activation(
                silb[:, :], bg[:, :], AF.Silu, scale=sgv[:, ic:ic + 1]),
                after=[mmg[-1], tok_sv] + ([h2_toks[-1]] if h2_toks else []))
            sil_toks.append(tsil)
            h2x = h2b[h2slot]
            dep_h2 = [mmu[-1], tsil] + stq.get(("h2s", h2slot), [])
            th2 = V_(lambda e, ic=ic, bu=bu, h2x=h2x: e.scalar_tensor_tensor(
                h2x[:, :], bu[:, :], suv[:, ic:ic + 1], silb[:, :], ALU.mult, ALU.mult),
                after=dep_h2)
            h2_toks.append(th2)
            tmx2 = V_(lambda e, h2x=h2x: (
                e.tensor_reduce(colA[:, 44:45], h2x[:, :], AX.X, ALU.max,
                                apply_absolute_value=True),
                e.drain(),
                e.tensor_max(mh_col[:, :], mh_col[:, :], colA[:, 44:45]))[-1],
                after=[th2])
            mx_toks.append(tmx2)

            def st_h2(eng, sem, ic=ic, t=t, h2x=h2x):
                eng.dma_start(out=AP(h2_s, ic * 128 * TC + t * TILE, [[TC, 128], [1, TILE]]),
                              in_=h2x[:, :]).then_inc(sem, 16)
            DMA_("sync", st_h2, n=1, after=[th2])
            tst = bld.fence("sync")
            stq[("h2s", h2slot)] = [tst, tmx2]
            if "h2_d" in dbg:
                DMA_("sync", lambda eng, sem, ic=ic, t=t, h2x=h2x: eng.dma_start(
                    out=AP(dbg["h2_d"], ic * 128 * TC + t * TILE, [[TC, 128], [1, TILE]]),
                    in_=h2x[:, :]).then_inc(sem, 16), n=1, after=[th2])
            st_toks.append(tst)
        return [st_toks[-1], st_toks[-2], mx_toks[-1], mx_toks[-2], h2_toks[-1]]

    gate4 = [tok_sv]
    for t in range(NT):
        gate4 = p4_tile(t, gate4)
    row4 = SB("row4", [1, 192], F32, es4)
    tm2T = T_(lambda e: e.transpose(psC[0][0:1, 0:128], mh_col[:, :], id_f[:, :]), after=gate4)
    tm2E = V_(lambda e: (e.tensor_copy(row4[0:1, 0:128], psC[0][0:1, 0:128]),
                         e.memset(rowA[0:1, 29:44], 0.0),
                         e.drain(),
                         e.tensor_reduce(rowA[0:1, 28:29], row4[0:1, 0:128], AX.X, ALU.max))[-1],
              after=[tm2T] + gate4)

    def st_ag2(eng, sem):
        eng.dma_start(out=AP(ag2_in, 0, [[1, 16], [1, 1]]), in_=rowA[0:1, 28:44]).then_inc(sem, 16)
    tok_ag2 = DMA_("gpsimd", st_ag2, n=1, after=[tm2E])
    tok_cc2 = bld.g_async(lambda e: e.collective_compute(
        "AllGather", ALU.bypass, replica_groups=[list(range(8))],
        ins=[ag2_in.ap().opt()], outs=[ag2_out.ap().opt()]), after=[tok_ag2])
    q_p4 = bld.quiesce("sync")
    bld.flush("p4")
    if PHASE_LIMIT == "p4":
        return nc
    es4.close()

    es3 = ExitStack()
    wf1 = SB("wf1", [128, NDC, FFN], BF16, es3)
    wf2 = SB("wf2", [128, NFC, D], BF16, es3)
    f1bC = SB("f1bC", [128, NFC], F32, es3)
    x1L = SB("x1L", [128, NDC, TILE], F32, es3)
    x2b = SB("x2b", [128, NDC, TILE], BF16, es3)
    h1b = SB("h1b", [128, NFC, TILE], BF16, es3)
    totc = [SB(f"totc{i}", [128, TILE], F32, es3) for i in range(4)]
    sqs3 = SB("sqs3", [128, TILE], F32, es3)
    row3 = SB("row3", [1, TILE], F32, es3)
    psD = [PS(f"psD{i}", es3) for i in range(8)]

    def p3_w(eng, sem):
        eng.dma_start(out=wf1[:, :, :],
                      in_=AP(win["L0_attn_f1_w"], 0, [[FFN, 128], [128 * FFN, NDC], [1, FFN]])
                      ).then_inc(sem, 16)
        eng.dma_start(out=wf2[:, :, :],
                      in_=AP(win["L0_attn_f2_w"], 0, [[D, 128], [128 * D, NFC], [1, D]])
                      ).then_inc(sem, 16)
    tok_w3 = DMA_("gpsimd", p3_w, n=2)
    fmm = []
    for cp in range(NFC):
        mms = []
        for c in range(NDC):
            mms.append(T_(lambda e, c=c, cp=cp: e.matmul(
                psD[6][:, cp:cp + 1], wf1[:, c, cp * 128:(cp + 1) * 128],
                b2cb[:, c:c + 1], start=(c == 0), stop=(c == NDC - 1)),
                after=([tok_w3] if (cp == 0 and c == 0) else []) + ([mms[-1]] if mms else ([fmm[-1]] if fmm else []))))
        fmm.append(mms[-1])
    f1bc_toks = []
    for rnd in range(4):
        def ldf1b(eng, sem, rnd=rnd):
            eng.dma_start(out=rowsP[0:1, 0:1024],
                          in_=AP(win["L0_attn_f1_b"], rnd * 1024, [[1024, 1], [1, 1024]])
                          ).then_inc(sem, 16)
        tl = DMA_("sync", ldf1b, n=1, after=f1bc_toks[-1:] if f1bc_toks else [fmm[-1]])
        tts = []
        for c in range(8):
            tts.append(T_(lambda e, c=c, rnd=rnd: e.transpose(
                psD[7][:, rnd * 8 + c:rnd * 8 + c + 1],
                rowsP[0:1, c * 128:(c + 1) * 128], id_f[0:1, 0:1]),
                after=[tl, fmm[-1]] + (tts[-1:] if tts else (f1bc_toks[-1:] if f1bc_toks else []))))
        f1bc_toks.append(tts[-1])
    tf1b0 = V_(lambda e: e.tensor_copy(f1bC[:, :], psD[7][:, 0:NFC]),
               after=[f1bc_toks[-1], fmm[-1]])
    tf1b = V_(lambda e: e.tensor_add(f1bC[:, :], f1bC[:, :], psD[6][:, 0:NFC]),
              after=[tf1b0])
    bld.flush("p3head")

    def p3_tile(t, gate):
        def ld_x1(eng, sem):
            eng.dma_start(out=x1L[:, :, :],
                          in_=AP(x1_s, t * TILE, [[TC, 128], [128 * TC, NDC], [1, TILE]])
                          ).then_inc(sem, 16)
        tl = DMA_("sync", ld_x1, n=1, after=gate + [q_p2])
        stat_toks = []
        for c in range(NDC):
            stat_toks.append(T_(lambda e, c=c: e.matmul(
                psD[4][0:1, 0:512], ones_col[:, 0:1], x1L[:, c, :],
                start=(c == 0), stop=(c == NDC - 1)),
                after=([tl] + gate) if c == 0 else []))
        sq_toks = []
        for c in range(NDC):
            tq = S_(lambda e, c=c: e.activation(sqs3[:, :], x1L[:, c, :], AF.Square),
                    after=[tl] + (sq_toks[-1:] if sq_toks else []))
            tqm = T_(lambda e, c=c: e.matmul(
                psD[5][0:1, 0:512], ones_col[:, 0:1], sqs3[:, :],
                start=(c == 0), stop=(c == NDC - 1)),
                after=[tq, stat_toks[-1]] + (sq_toks[-1:] if sq_toks else []))
            sq_toks.append(tqm)
        trow = V_(lambda e: (
            e.tensor_scalar(row3[0:1, 0:512], psD[4][0:1, 0:512], 1.0 / D, None, ALU.mult),
            e.tensor_scalar(rowsP[0:1, 0:512], psD[5][0:1, 0:512], 1.0 / D, None, ALU.mult),
            e.tensor_mul(rowsP[0:1, 512:1024], row3[0:1, 0:512], row3[0:1, 0:512]),
            e.tensor_sub(rowsP[0:1, 0:512], rowsP[0:1, 0:512], rowsP[0:1, 512:1024]))[-1],
            after=[sq_toks[-1], stat_toks[-1]])
        trl = S_(lambda e: (e.activation(rowsP[0:1, 0:512], rowsP[0:1, 0:512], AF.Ln, bias=LN_EPS),
                            e.activation(rowsP[0:1, 0:512], rowsP[0:1, 0:512], AF.Exp, scale=-0.5))[-1],
                 after=[trow])
        trm = V_(lambda e: e.tensor_mul(rowsP[0:1, 512:1024], row3[0:1, 0:512], rowsP[0:1, 0:512]),
                 after=[trl])
        tgq = T_(lambda e: (e.matmul(psD[4][:, :], ones_row[:, :], rowsP[0:1, 0:512],
                                     start=True, stop=True),
                            e.matmul(psD[5][:, :], ones_row[:, :], rowsP[0:1, 512:1024],
                                     start=True, stop=True))[-1], after=[trm])
        x2toks = []
        for c in range(NDC):
            x2toks.append(V_(lambda e, c=c: (
                e.tensor_mul(sqs3[:, :], x1L[:, c, :], psD[4][:, :]),
                e.tensor_sub(x2b[:, c, :], sqs3[:, :], psD[5][:, :]))[-1],
                after=[tgq] + (x2toks[-1:] if x2toks else [])))
        x2done = x2toks[-1]
        h1toks = []
        for fo in range(NFC):
            bank = psD[fo % 2]
            mm = []
            for c in range(NDC):
                mm.append(T_(lambda e, c=c, fo=fo, bank=bank: e.matmul(
                    bank[:, :], wf1[:, c, fo * 128:(fo + 1) * 128], x2b[:, c, :],
                    start=(c == 0), stop=(c == NDC - 1)),
                    after=([x2done] if (fo == 0 and c == 0) else [])
                    + ([mm[-1]] if mm else (h1toks[-2:-1] if len(h1toks) >= 2 else []))))
            te = S_(lambda e, fo=fo, bank=bank: e.activation(
                h1b[:, fo, :], bank[:, :], AF.Gelu, bias=f1bC[:, fo:fo + 1]),
                after=[mm[-1]])
            h1toks.append(te)
        h1done = h1toks[-1]
        f2toks = []
        for co in range(NDC):
            mm = []
            for fc in range(NFC):
                mm.append(T_(lambda e, fc=fc, co=co: e.matmul(
                    psD[co][:, :], wf2[:, fc, co * 128:(co + 1) * 128], h1b[:, fc, :],
                    start=(fc == 0), stop=(fc == NFC - 1)),
                    after=([h1done] if (co == 0 and fc == 0) else []) + ([mm[-1]] if mm else (f2toks[-2:-1] if len(f2toks) >= 2 else []))))
            ev = V_(lambda e, co=co: e.scalar_tensor_tensor(
                totc[co % 4][:, :], psD[co][:, :], f2bC[:, co:co + 1], x1L[:, co, :],
                ALU.add, ALU.add),
                after=[mm[-1]] + stq.get(("tots", co % 4), []))

            def st_tot(eng, sem, co=co, t=t):
                eng.dma_start(out=AP(tot_s, co * 128 * TC + t * TILE, [[TC, 128], [1, TILE]]),
                              in_=totc[co % 4][:, :]).then_inc(sem, 16)
            DMA_("sync", st_tot, n=1, after=[ev])
            td = bld.fence("sync")
            stq[("tots", co % 4)] = [td]
            if "tot_d" in dbg:
                DMA_("sync", lambda eng, sem, co=co, t=t: eng.dma_start(
                    out=AP(dbg["tot_d"], co * 128 * TC + t * TILE, [[TC, 128], [1, TILE]]),
                    in_=totc[co % 4][:, :]).then_inc(sem, 16), n=1, after=[ev])
            f2toks += [ev, td]
        return [f2toks[-1], f2toks[-2], h1done]

    gate3 = [tf1b]
    f1sc2 = []
    for c in range(NDC):
        f1sc2.append(V_(lambda e, c=c: e.tensor_scalar(
            wf1[:, c, :], wf1[:, c, :], g2c[:, c:c + 1], None, ALU.mult),
            after=[tf1b] + (f1sc2[-1:] if f1sc2 else [])))
    gate3 = [f1sc2[-1]]
    for t in range(NT):
        gate3 = p3_tile(t, gate3)
    q_p3 = bld.quiesce("sync")
    bld.flush("p3")
    if PHASE_LIMIT == "p3":
        return nc
    es3.close()

    es5 = ExitStack()
    wdT = SB("wdT", [128, NIC, D], BF16, es5)
    h2L = [SB(f"h2L{i}", [128, TILE], F32, es5) for i in range(4)]
    h2q = [SB(f"h2q{i}", [128, TILE], BF16, es5) for i in range(4)]
    totLs = [SB(f"totL{i}", [128, NDC, TILE], F32, es5) for i in range(2)]
    fintm = SB("fintm", [128, NCHT, D], F32, es5)
    xrl = [SB(f"xrl{i}", [128, NCHT, D], F32, es5) for i in range(2)]
    outb = SB("outb", [128, NCHT, D], F32, es5)
    psE = [PS(f"psE{i}", es5) for i in range(8)]

    def p5_w(eng, sem):
        eng.dma_start(out=wdT[:, :, :],
                      in_=AP(wdT_s, 0, [[D, 128], [128 * D, NIC], [1, D]])).then_inc(sem, 16)
    tok_w5 = DMA_("gpsimd", p5_w, n=1, after=[q_x1])
    tok_cc2w = G_(lambda e: e.nop(), after=[tok_cc2])

    def ld_m2(eng, sem):
        eng.dma_start(out=colA[0:8, 45:46],
                      in_=AP(ag2_out, 0, [[16, 8], [1, 1]]),
                      allow_slow_non_contiguous=True).then_inc(sem, 16)
    tok_m2l = DMA_("sync", ld_m2, n=1, after=[tok_cc2w])
    tm2T2 = T_(lambda e: e.transpose(psE[0][0:1, 0:8], colA[0:8, 45:46], id_f[0:8, 0:8]),
               after=[tok_m2l])
    tm2f = V_(lambda e: (e.tensor_copy(rowA[0:1, 40:48], psE[0][0:1, 0:8]),
                         e.drain(),
                         e.tensor_reduce(rowA[0:1, 29:30], rowA[0:1, 40:48], AX.X, ALU.max),
                         e.drain(),
                         e.tensor_mul(rowA[0:1, 30:31], rowA[0:1, 29:30], clipc[0:1, 1:2]),
                         e.drain(),
                         e.tensor_scalar(rowA[0:1, 31:32], rowA[0:1, 30:31], 1.0 / QMAX, None, ALU.mult),
                         e.drain(),
                         e.reciprocal(rowA[0:1, 33:34], rowA[0:1, 31:32]))[-1],
              after=[tm2T2])
    if "m2_d" in dbg:
        DMA_("sync", lambda eng, sem: eng.dma_start(
            out=dbg["m2_d"].ap(), in_=rowA[0:1, 24:40]).then_inc(sem, 16), n=1, after=[tm2f])
    tb2x = T_(lambda e: (e.matmul(psE[1][:, 0:1], ones_row[:, :], rowA[0:1, 31:32],
                                  start=True, stop=True),
                         e.matmul(psE[1][:, 1:2], ones_row[:, :], rowA[0:1, 33:34],
                                  start=True, stop=True))[-1], after=[tm2f])
    tb2e = V_(lambda e: (e.tensor_copy(sc2_col[:, :], psE[1][:, 0:1]),
                         e.tensor_copy(rs2_col[:, :], psE[1][:, 1:2]))[-1], after=[tb2x])
    tsdv = V_(lambda e: e.tensor_scalar(sd_sc[:, 0:NDC], sd_sc[:, 0:NDC], sc2_col[:, 0:1],
                                        None, ALU.mult), after=[tb2e])

    st5 = {
        "h2slot": [[] for _ in range(4)],
        "totslot": [[] for _ in range(2)],
        "bank": [[] for _ in range(NDC)],
        "fintm": [],
        "outb": [],
        "xr": [[] for _ in range(2)],
    }

    def p5_tile(t, gate):
        tsl = t % 2
        totL = totLs[tsl]

        def ld_tot(eng, sem):
            eng.dma_start(out=totL[:, :, :],
                          in_=AP(tot_s, t * TILE, [[TC, 128], [128 * TC, NDC], [1, TILE]])
                          ).then_inc(sem, 16)
        tlt = DMA_("sync", ld_tot, n=1,
                   after=st5["totslot"][tsl] + (gate if t == 0 else []) + [q_p3])
        xr = xrl[t % 2]

        def ld_x(eng, sem):
            eng.dma_start(out=xr[:, :, :],
                          in_=AP(x_in, t * TILE * D, [[D, 128], [128 * D, NCHT], [1, D]])
                          ).then_inc(sem, 16)
        tlx = DMA_("sync", ld_x, n=1, after=st5["xr"][t % 2] + (gate if t == 0 else []))

        ic_toks = []
        fin = [None] * NDC
        for ic in range(NIC):
            slot = (t * NIC + ic) % 4
            hb = h2L[slot]
            hq = h2q[slot]

            def ld_h2(eng, sem, ic=ic, t=t, hb=hb):
                eng.dma_start(out=hb[:, :],
                              in_=AP(h2_s, ic * 128 * TC + t * TILE, [[TC, 128], [1, TILE]])
                              ).then_inc(sem, 16)
            tlh = DMA_("sync", ld_h2, n=1,
                       after=st5["h2slot"][slot] + [q_p4]
                       + ([tb2e] if t == 0 and ic == 0 else []))
            tq1 = S_(lambda e, hb=hb: e.activation(hb[:, :], hb[:, :], AF.Identity,
                                                   bias=MAGIC, scale=rs2_col[:, 0:1]),
                     after=[tlh, tb2e])
            tq2 = V_(lambda e, hb=hb, hq=hq: (
                e.tensor_scalar(hb[:, :], hb[:, :], MAGIC, QMAX, ALU.subtract, ALU.min),
                e.tensor_scalar(hq[:, :], hb[:, :], -QMAX - 1.0, None, ALU.max))[-1],
                after=[tq1])
            mmd = []
            for co in range(NDC):
                mmd.append(T_(lambda e, co=co, ic=ic, hq=hq: e.matmul(
                    psE[co][:, :], wdT[:, ic, co * 128:(co + 1) * 128], hq[:, :],
                    start=(ic == 0), stop=(ic == NIC - 1)),
                    after=[tq2, tok_w5]
                    + (st5["bank"][co] if ic == 0 else [])
                    + ([mmd[-1]] if mmd else (ic_toks[-1:] if ic_toks else []))))
            st5["h2slot"][slot] = [mmd[-1], tq2]
            ic_toks += [tq2, mmd[-1]]
        for co in range(NDC):
            ev = V_(lambda e, co=co, totL=totL: e.scalar_tensor_tensor(
                totL[:, co, :], psE[co][:, :], sd_sc[:, co:co + 1], totL[:, co, :],
                ALU.mult, ALU.add), after=[ic_toks[-1], tsdv, tlt])
            fin[co] = ev
        ttk = []
        for jj in range(NCHT):
            for co in range(NDC):
                bank = psE[co % 2]
                tt = T_(lambda e, co=co, jj=jj, bank=bank, totL=totL: e.transpose(
                    bank[:, 0:128], totL[:, co, jj * 128:(jj + 1) * 128], id_f[:, :]),
                    after=[fin[co]] + (ttk[-1:] if ttk else [fin[-1]]))
                te = S_(lambda e, co=co, jj=jj, bank=bank: e.copy(
                    fintm[:, jj, co * 128:(co + 1) * 128], bank[:, 0:128]),
                    after=[tt] + st5["fintm"])
                ttk.append(te)
        te_all = ttk
        for co in range(NDC):
            st5["bank"][co] = [ttk[-1]]
        st5["totslot"][tsl] = [ttk[-1]]
        tadd = V_(lambda e: e.tensor_add(outb[:, :, :], fintm[:, :, :], xr[:, :, :]),
                  after=te_all[-1:] + [tlx] + st5["outb"])
        st5["fintm"] = [tadd]
        st5["xr"][t % 2] = [tadd]

        def st_out(eng, sem, t=t):
            eng.dma_start(out=AP(out_t, t * TILE * D, [[D, 128], [128 * D, NCHT], [1, D]]),
                          in_=outb[:, :, :]).then_inc(sem, 16)
        tso = DMA_("sync", st_out, n=1, after=[tadd])
        st5["outb"] = [bld.fence("sync")]
        return [tso, tadd]

    gate5 = [tsdv]
    for t in range(NT):
        gate5 = p5_tile(t, gate5)
    V_(lambda e: e.nop(), after=gate5)
    bld.flush("p5")
    es5.close()
    es_persist.close()
    return nc


def kernel(**inputs):
    x_full = np.ascontiguousarray(inputs["L0_attn_x"])
    nc = build_nc()
    in_maps = []
    for cid in range(NCORE):
        b, t0 = cid // 4, (cid % 4) * TC
        m = {"x_in": np.ascontiguousarray(x_full[b, t0:t0 + TC])}
        for k, v in inputs.items():
            if k == "L0_attn_x":
                continue
            m[k] = np.ascontiguousarray(np.asarray(v))
        my = cid % 4
        pf = np.zeros((8, 8), np.float32)
        pb = np.zeros((8, 8), np.float32)
        mf = np.zeros((8, 1), np.float32)
        mb = np.zeros((8, 1), np.float32)
        grp = [4 * (cid // 4) + i for i in range(4)]
        for i, c2 in enumerate(grp):
            if i < my:
                mf[c2, 0] = 1.0
                for j in range(i + 1, my):
                    pf[grp[j], c2] = 1.0
            if i > my:
                mb[c2, 0] = 1.0
                for j in range(my + 1, i):
                    pb[grp[j], c2] = 1.0
        m["cc_pf"], m["cc_pb"], m["cc_mf"], m["cc_mb"] = pf, pb, mf, mb
        in_maps.append(m)
    res = run_bass_kernel_spmd(nc, in_maps, core_ids=list(range(NCORE)))
    out = np.empty((B, T, D), np.float32)
    for cid in range(NCORE):
        b, t0 = cid // 4, (cid % 4) * TC
        out[b, t0:t0 + TC] = res.results[cid]["out"]
    return out


# version 16
# speedup vs baseline: 2.1559x; 1.0563x over previous
import numpy as np
import ml_dtypes
from contextlib import ExitStack
import concourse.bass as bass
import concourse.mybir as mybir
from concourse.bass_utils import run_bass_kernel_spmd

F32 = mybir.dt.float32
BF16 = mybir.dt.bfloat16
AF = mybir.ActivationFunctionType
ALU = mybir.AluOpType
AX = mybir.AxisListType
AP = bass.AP

B, T, D, H, NK, KVE = 2, 16384, 1024, 16, 64, 65
TC, TILE, CH = 4096, 512, 128
NT = TC // TILE
NCHT = TILE // CH
NCH = TC // CH
NDC = D // 128
NPAIR = H // 2
FFN, IFF = 4096, 2816
NFC = FFN // 128
NIC = IFF // 128
LN_EPS = 1e-5
EPS_DEN = 1e-6
LAM_MIN = 1e-8
QMAX = 127.0
MAGIC = 12582912.0
PAY_SEC = H * NK * KVE
PAY = 2 * PAY_SEC + 32
NCORE = 8
PHASE_LIMIT = None
POS = [(hh % 2) * 8 + hh // 2 for hh in range(16)]
HORDER = [0] * 16
for _hh in range(16):
    HORDER[POS[_hh]] = _hh
P1_CUT = None
LNJS = None
ENGS = ("tensor", "vector", "scalar", "gpsimd", "sync")


class Bld:

    SEMNAMES = ("tensor", "vector", "scalar", "gpsimd", "sync", "gdma", "cc",
                "syncb")
    OWNER = {"tensor": "tensor", "vector": "vector", "scalar": "scalar",
             "gpsimd": "gpsimd", "sync": "sync", "gdma": "gpsimd", "cc": "gpsimd",
             "syncb": "sync"}

    NPOOL = {"sync": 48, "gpsimd": 16}

    def __init__(self, nc):
        self.nc = nc
        self.sems = {s: nc.alloc_semaphore(name=f"sem_{s}") for s in self.SEMNAMES}
        self.count = {s: 0 for s in self.SEMNAMES}
        self.groups = {e: [] for e in ENGS}
        self.pool = {}
        self.pool_count = {}
        for eng, npool in self.NPOOL.items():
            for k in range(npool):
                s = f"dp_{eng}_{k}"
                self.pool[s] = nc.alloc_semaphore(name=s)
                self.pool_count[s] = 0
                self.count[s] = 0
        self.sems.update(self.pool)
        self.pool_idx = {e: 0 for e in self.NPOOL}
        self.prev_flush_counts = {}
        self.last_dma = None

    def _add(self, eng, semname, fn, after, inc, is_async=False):
        self.count[semname] += inc
        self.groups[eng].append((fn, tuple(after), inc, semname))
        return (semname, self.count[semname], is_async)

    def t(self, fn, after=()):
        return self._add("tensor", "tensor", fn, after, 1)

    def v(self, fn, after=()):
        return self._add("vector", "vector", fn, after, 1)

    def s(self, fn, after=()):
        return self._add("scalar", "scalar", fn, after, 1)

    def g(self, fn, after=()):
        return self._add("gpsimd", "gpsimd", fn, after, 1)

    def dma(self, eng, fn, n=1, after=()):
        k = self.pool_idx[eng]
        self.pool_idx[eng] = (k + 1) % self.NPOOL[eng]
        semname = f"dp_{eng}_{k}"
        drain = self.count[semname]

        def wrap(e, sem, fn=fn, drain=drain):
            if drain:
                e.wait_ge(sem, drain)
            fn(e, sem)
        tok = self._add(eng, semname, wrap, after, 16 * n, is_async=True)
        self.last_dma = tok
        return tok

    def g_async(self, fn, after=()):
        return self._add("gpsimd", "cc", fn, after, 1, is_async=True)

    def fence(self, queue="sync"):
        return self.last_dma

    def quiesce(self, semname="sync"):
        return ("syncb", 0, True)

    def flush(self, name=None):
        nc = self.nc
        groups = {e: list(self.groups[e]) for e in ENGS}
        self.groups = {e: [] for e in ENGS}
        if not any(groups.values()):
            return
        qprev = {s: v for s, v in self.prev_flush_counts.items() if v > 0}
        self.prev_flush_counts = {s: self.count[s] for s in self.pool}
        with nc.Block(name) as blk:
            def mk(ename, glist):
                def body(eng):
                    lastw = {}
                    for qs, qv in qprev.items():
                        eng.wait_ge(self.sems[qs], qv)
                        lastw[qs] = qv
                    for fn, after, inc, semname in glist:
                        for tok in after:
                            ps, val, is_async = tok
                            if ename == "tensor" and self.OWNER.get(ps) == "tensor" \
                                    and not is_async:
                                continue
                            if lastw.get(ps, -1) >= val:
                                continue
                            eng.wait_ge(self.sems[ps], val)
                            lastw[ps] = val
                        mysem = self.sems[semname]
                        if inc >= 16:
                            fn(eng, mysem)
                        else:
                            last = fn(eng)
                            (last if last is not None else eng.nop()).then_inc(mysem, 1)
                return body
            for ename in ENGS:
                if groups[ename]:
                    getattr(blk, ename)(mk(ename, groups[ename]))


def _consts():
    c = {}
    c["id_f"] = np.eye(128, dtype=np.float32)
    c["id_b"] = np.eye(128, dtype=ml_dtypes.bfloat16)
    c["ones_row_f"] = np.ones((1, 128), np.float32)
    c["ones_col_f"] = np.ones((128, 1), np.float32)
    c["ones16_b"] = np.ones((16, 128), ml_dtypes.bfloat16)
    c["ones16_f"] = np.ones((16, 128), np.float32)
    sp = np.zeros((16, NPAIR * 128), np.float32)
    for g in range(NPAIR):
        sp[2 * g, g * 128: g * 128 + 64] = 1.0
        sp[2 * g + 1, g * 128 + 64: (g + 1) * 128] = 1.0
    c["sel_pair_b"] = sp.astype(ml_dtypes.bfloat16)
    c["sel_pair_f"] = sp
    sh = np.zeros((16, 128), np.float32)
    for h in range(16):
        sh[h, 0:64] = 1.0 if h % 2 == 0 else 0.0
        sh[h, 64:128] = 1.0 if h % 2 == 1 else 0.0
    c["sel_half_f"] = sh
    pm = np.zeros((16, NPAIR), np.float32)
    for h in range(16):
        pm[h, h // 2] = 1.0
    c["pair_mask_f"] = pm
    hm = np.zeros((16, 16 * 128), np.float32)
    for p_ in range(16):
        hm[HORDER[p_], p_ * 128:(p_ + 1) * 128] = 1.0
    c["head_mask_f"] = hm.astype(ml_dtypes.bfloat16)
    return c


def bcv(ap2d, reps):
    base = ap2d
    a = list(base.ap)
    new = [list(a[0]), [0, reps]] + [list(x) for x in a[1:]]
    return AP(base.tensor, base.offset, new)


def bccol(ap_col, reps):
    a = list(ap_col.ap)
    return AP(ap_col.tensor, ap_col.offset, [list(a[0]), [0, reps]])


def build_nc(dbg_names=()):
    nc = bass.Bass("TRN2", target_bir_lowering=False, detect_race_conditions=False)
    bld = Bld(nc)
    Cst = _consts()
    T_, V_, S_, G_, DMA_ = bld.t, bld.v, bld.s, bld.g, bld.dma

    x_in = nc.dram_tensor("x_in", [TC, D], F32, kind="ExternalInput")
    win = {}
    for nm, shp in [
        ("L0_attn_q_w", [D, D]), ("L0_attn_k_w", [D, D]), ("L0_attn_v_w", [D, D]),
        ("L0_attn_o_w", [D, D]), ("L0_attn_dec_w", [D, H]),
        ("L0_attn_f1_w", [D, FFN]), ("L0_attn_f2_w", [FFN, D]),
        ("L0_ffn_w_gate", [IFF, D]), ("L0_ffn_w_up", [IFF, D]), ("L0_ffn_w_down", [D, IFF]),
        ("L0_attn_q_b", [D]), ("L0_attn_k_b", [D]), ("L0_attn_v_b", [D]),
        ("L0_attn_o_b", [D]), ("L0_attn_dec_b", [H]), ("L0_attn_f1_b", [FFN]),
        ("L0_attn_f2_b", [D]),
        ("L0_attn_ln1_w", [D]), ("L0_attn_ln1_b", [D]),
        ("L0_attn_ln2_w", [D]), ("L0_attn_ln2_b", [D]),
        ("L0_ffn_clip_x", [1]), ("L0_ffn_clip_down", [1]),
        ("cc_pf", [8, 8]), ("cc_pb", [8, 8]), ("cc_mf", [8, 1]), ("cc_mb", [8, 1]),
    ]:
        win[nm] = nc.dram_tensor(nm, shp, F32, kind="ExternalInput")
    out_t = nc.dram_tensor("out", [TC, D], F32, kind="ExternalOutput")

    dbg = {}
    _dbg_shapes = {
        "xn_fm": ([D, TC], F32), "x2e_fm": ([D, TC], BF16), "phiq": ([D, TC], BF16),
        "phik": ([D, TC], BF16),
        "lam_d": ([H, TC], F32), "llam_d": ([H, TC], F32), "vtm_d": ([TC, H * KVE], BF16),
        "sf_d": ([128, NPAIR * KVE], F32), "sb_d": ([128, NPAIR * KVE], F32),
        "sin_d": ([128, NPAIR * KVE], F32), "rin_d": ([128, NPAIR * KVE], F32),
        "m1_d": ([1, 16], F32), "attn_d": ([D, TC], BF16), "x1_d": ([D, TC], F32),
        "tot_d": ([D, TC], F32), "xq_d": ([D, TC], BF16), "h2_d": ([IFF, TC], F32),
        "m2_d": ([1, 16], F32),
    }
    for nmm in dbg_names:
        shp, dt_ = _dbg_shapes[nmm]
        dbg[nmm] = nc.dram_tensor("dbg_" + nmm, shp, dt_, kind="ExternalOutput")

    xn_s = nc.dram_tensor("xn_s", [D, TC], F32, kind="Internal")
    qc_s = nc.dram_tensor("qc_s", [NCH, 128, NPAIR * CH], BF16, kind="Internal")
    qe_s = nc.dram_tensor("qe_s", [NCH, 128, NPAIR * CH], BF16, kind="Internal")
    part_s = nc.dram_tensor("part_s", [NCH, CH, H * KVE], BF16, kind="Internal")
    x1_s = nc.dram_tensor("x1_s", [D, TC], F32, kind="Internal")
    xq_s = nc.dram_tensor("xq_s", [D, TC], BF16, kind="Internal")
    tot_s = nc.dram_tensor("tot_s", [D, TC], F32, kind="Internal")
    h2_s = nc.dram_tensor("h2_s", [IFF, TC], F32, kind="Internal")
    wgT_s = nc.dram_tensor("wgT_s", [D, IFF], BF16, kind="Internal")
    wuT_s = nc.dram_tensor("wuT_s", [D, IFF], BF16, kind="Internal")
    wdT_s = nc.dram_tensor("wdT_s", [IFF, D], BF16, kind="Internal")
    kcv_s = nc.dram_tensor("kcv_s", [NCH, 128, NPAIR * KVE], BF16, kind="Internal")
    ag_in = nc.dram_tensor("ag_in", [1, PAY], F32, kind="Internal")
    ag_out = nc.dram_tensor("ag_out", [8, PAY], F32, kind="Internal")
    ag2_in = nc.dram_tensor("ag2_in", [1, 16], F32, kind="Internal")
    ag2_out = nc.dram_tensor("ag2_out", [8, 16], F32, kind="Internal")

    const_t = {k: nc.inline_tensor(np.ascontiguousarray(v), "c_" + k) for k, v in Cst.items()}

    es_persist = ExitStack()
    es_kcv = ExitStack()
    es_ph = ExitStack()

    def SB(name, shape, dtype, es):
        return es.enter_context(nc.sbuf_tensor(name, shape, dtype))

    def PS(name, es, shape=(128, 512)):
        return es.enter_context(nc.psum_tensor(name, list(shape), F32))

    P = es_persist
    id_f = SB("id_f", [128, 128], F32, P)
    id_b = SB("id_b", [128, 128], BF16, P)
    ones_row = SB("ones_row", [1, 128], F32, P)
    ones_col = SB("ones_col", [128, 1], F32, P)
    ones16_b = SB("ones16_b", [16, 128], BF16, P)
    ones16_f = SB("ones16_f", [16, 128], F32, P)
    sel_pair_b = SB("sel_pair_b", [16, NPAIR * 128], BF16, P)
    sel_half = SB("sel_half", [16, 128], F32, P)
    pair_mask = SB("pair_mask", [16, NPAIR], F32, P)
    ccpf = SB("ccpf", [8, 8], F32, P)
    ccpb = SB("ccpb", [8, 8], F32, P)
    ccmf = SB("ccmf", [8, 1], F32, P)
    ccmb = SB("ccmb", [8, 1], F32, P)

    lcend_st = SB("lcend_st", [16, NCH], F32, P)
    lpre_st = SB("lpre_st", [16, NCH], F32, P)
    ltot_c = SB("ltot_c", [16, 1], F32, P)
    mx_col = SB("mx_col", [128, 1], F32, P)
    mh_col = SB("mh_col", [128, 1], F32, P)
    colA = SB("colA", [128, 64], F32, P)
    rowA = SB("rowA", [1, 64], F32, P)
    s16A = SB("s16A", [16, 64], F32, P)
    rs1_col = SB("rs1_col", [128, 1], F32, P)
    sc1_col = SB("sc1_col", [128, 1], F32, P)
    rs2_col = SB("rs2_col", [128, 1], F32, P)
    sc2_col = SB("sc2_col", [128, 1], F32, P)
    sg_sc = SB("sg_sc", [128, NIC], F32, P)
    su_sc = SB("su_sc", [128, NIC], F32, P)
    sd_sc = SB("sd_sc", [128, NDC], F32, P)
    g2c = SB("g2c", [128, NDC], F32, P)
    b2cb = SB("b2cb", [128, NDC], BF16, P)
    obC = SB("obC", [128, NDC], F32, P)
    f2bC = SB("f2bC", [128, NDC], F32, P)
    clipc = SB("clipc", [1, 2], F32, P)
    rowsP = SB("rowsP", [1, D], F32, P)
    cstT = SB("cstT", [128, 8], F32, P)
    _cvals = [LN_EPS, MAGIC, -1.0, -0.5, 0.0, 1.0]
    for _i, _v in enumerate(_cvals):
        nc.const_aps.register(F32, _v, cstT[:, _i:_i + 1])
    declc = SB("declc", [128, NPAIR], F32, P)
    declp = SB("declp", [128, NPAIR], F32, P)
    s16B = SB("s16B", [16, 16], F32, P)

    kcvs = SB("kcvs", [128, 2, NPAIR * KVE], BF16, es_kcv)
    S_st = SB("S_st", [128, NPAIR * KVE], F32, es_kcv)
    Sb_st = SB("Sb_st", [128, NPAIR * KVE], F32, es_kcv)
    Sin_st = SB("Sin_st", [128, NPAIR * KVE], F32, es_kcv)
    Rin_st = SB("Rin_st", [128, NPAIR * KVE], F32, es_kcv)
    R_st = SB("R_st", [128, NPAIR * KVE], F32, es_kcv)
    Tmp_st = SB("Tmp_st", [128, NPAIR * KVE], F32, es_kcv)

    es1 = ExitStack()
    wq = SB("wq", [128, NDC, D], BF16, es1)
    wk = SB("wk", [128, NDC, D], BF16, es1)
    wv = SB("wv", [128, NDC, D], BF16, es1)
    wdec = SB("wdec", [128, NDC, H], BF16, es1)
    qbC = SB("qbC", [128, NDC], F32, es1)
    kbC = SB("kbC", [128, NDC], F32, es1)
    qb1C = SB("qb1C", [128, NDC], F32, es1)
    kb1C = SB("kb1C", [128, NDC], F32, es1)
    vbtm = SB("vbtm", [128, D], BF16, es1)
    decbc = SB("decbc", [16, 1], F32, es1)
    g1c = SB("g1c", [128, NDC], F32, es1)
    b1cb = SB("b1cb", [128, NDC], BF16, es1)
    vbeC = SB("vbeC", [128, NDC], F32, es1)
    xtm = [SB("xtm0", [128, NCHT, D], F32, es1)]
    xn_tm = SB("xn_tm", [128, D], F32, es1)
    x2e_tm = SB("x2e_tm", [128, D], F32, es1)
    xnfm = SB("xnfm", [128, NDC, CH], F32, es1)
    x2e = SB("x2e", [128, NDC, 2, TILE], BF16, es1)
    phiq = SB("phiq", [128, NDC, 2, TILE], BF16, es1)
    phik = SB("phik", [128, NDC, 2, TILE], BF16, es1)
    tbuf1 = SB("tbuf1", [128, TILE], F32, es1)
    lam_t = SB("lam_t", [16, 2, TILE], F32, es1)
    llam_t = lam_t
    vtm = SB("vtm", [128, 2, H * KVE], BF16, es1)
    lc_f = SB("lc_f", [16, CH], F32, es1)
    ctil_f = SB("ctil_f", [16, CH], F32, es1)
    e_f = SB("e_f", [16, CH], F32, es1)
    lctm = SB("lctm", [128, 16], F32, es1)
    ctil_b = SB("ctil_b", [16, CH], BF16, es1)
    e_b = SB("e_b", [16, CH], BF16, es1)
    ctm = SB("ctm", [128, 16], F32, es1)
    etm = SB("etm", [128, 16], F32, es1)
    drhs = SB("drhs", [16, 16 * CH], BF16, es1)
    D_sb = SB("D_sb", [128, H * CH], BF16, es1)
    scD = SB("scD", [128, H * CH], BF16, es1)
    keB = SB("keB", [128, H * NK], BF16, es1)
    kcB = SB("kcB", [128, H * NK], BF16, es1)
    phikf = SB("phikf", [128, NPAIR * CH], F32, es1)
    head_mask = SB("head_mask", [16, 16 * 128], BF16, es1)
    Sb16 = SB("Sb16", [128, NPAIR * KVE], BF16, es1)
    qcB = SB("qcB", [128, 2, NPAIR * CH], BF16, es1)
    qeB = SB("qeB", [128, 2, NPAIR * CH], BF16, es1)
    partB = SB("partB", [128, 2, H * KVE], BF16, es1)
    psA = [PS(f"psA{i}", es1) for i in range(8)]

    def p0_consts(eng, sem):
        for dst, src in [(id_f, "id_f"), (id_b, "id_b"), (ones_row, "ones_row_f"),
                         (ones_col, "ones_col_f"),
                         (ones16_b, "ones16_b"), (ones16_f, "ones16_f"),
                         (sel_pair_b, "sel_pair_b"),
                         (sel_half, "sel_half_f"),
                         (pair_mask, "pair_mask_f"), (head_mask, "head_mask_f")]:
            eng.dma_start(out=dst[:, :], in_=const_t[src].ap()).then_inc(sem, 16)
        for dst, src in [(ccpf, "cc_pf"), (ccpb, "cc_pb"), (ccmf, "cc_mf"), (ccmb, "cc_mb")]:
            eng.dma_start(out=dst[:, :], in_=win[src].ap()).then_inc(sem, 16)
    tok_c0 = DMA_("sync", p0_consts, n=14)

    def p0_wload(eng, sem):
        for dst, src in [(wq, "L0_attn_q_w"), (wk, "L0_attn_k_w"), (wv, "L0_attn_v_w")]:
            eng.dma_start(out=dst[:, :, :],
                          in_=AP(win[src], 0, [[D, 128], [128 * D, NDC], [1, D]])
                          ).then_inc(sem, 16)
        eng.dma_start(out=wdec[:, :, :],
                      in_=AP(win["L0_attn_dec_w"], 0, [[H, 128], [128 * H, NDC], [1, H]])
                      ).then_inc(sem, 16)
    tok_w0 = DMA_("gpsimd", p0_wload, n=4)

    def _setconsts(e):
        last = None
        for _i, _v in enumerate(_cvals):
            last = e.memset(cstT[:, _i:_i + 1], _v)
        return last
    tok_cst = G_(_setconsts)
    tok_z = G_(lambda e: (e.memset(S_st[:, :], 0.0), e.memset(Sb_st[:, :], 0.0),
                          e.memset(ltot_c[:, :], 0.0), e.memset(mx_col[:, :], 0.0),
                          e.memset(mh_col[:, :], 0.0), e.memset(Sin_st[:, :], 0.0),
                          e.memset(Rin_st[:, :], 0.0), e.memset(Sb16[:, :], 0.0))[-1])

    chain = {"rc": tok_c0}

    def row_to_cols(name, dstC, nchunks, col0=0):
        prev = chain["rc"]

        def ld(eng, sem):
            n = nchunks * 128
            eng.dma_start(out=rowsP[0:1, 0:n],
                          in_=AP(win[name], 0, [[n, 1], [1, n]])).then_inc(sem, 16)
        tl = DMA_("sync", ld, n=1, after=[prev])
        tts = []
        for c in range(nchunks):
            tts.append(T_(lambda e, c=c: e.transpose(psA[7][:, c:c + 1],
                                                     rowsP[0:1, c * 128:(c + 1) * 128],
                                                     id_f[0:1, 0:1]),
                          after=[tl, prev] + ([tts[-1]] if tts else [])))
        tv = V_(lambda e: e.tensor_copy(dstC[:, col0:col0 + nchunks], psA[7][:, 0:nchunks]),
                after=[tts[-1]])
        chain["rc"] = tv
        return tv

    tok_g1 = row_to_cols("L0_attn_ln1_w", g1c, NDC)
    tok_b1 = row_to_cols("L0_attn_ln1_b", colA, NDC)
    tok_b1b = V_(lambda e: e.tensor_copy(b1cb[:, :], colA[:, 0:NDC]), after=[tok_b1])
    tok_g2 = row_to_cols("L0_attn_ln2_w", g2c, NDC)
    tok_b2 = row_to_cols("L0_attn_ln2_b", colA, NDC)
    tok_b2b = V_(lambda e: e.tensor_copy(b2cb[:, :], colA[:, 0:NDC]), after=[tok_b2])
    tok_ob = row_to_cols("L0_attn_o_b", obC, NDC)
    tok_f2bc = row_to_cols("L0_attn_f2_b", f2bC, NDC)

    def fold_bias(wsb, bname, dstC, deps):
        prev = deps
        toks = []
        for cp in range(NDC):
            mms = []
            for c in range(NDC):
                mms.append(T_(lambda e, c=c, cp=cp, wsb=wsb: e.matmul(
                    psA[6][:, cp:cp + 1], wsb[:, c, cp * 128:(cp + 1) * 128],
                    b1cb[:, c:c + 1], start=(c == 0), stop=(c == NDC - 1)),
                    after=(prev if cp == 0 and c == 0 else []) + ([mms[-1]] if mms else ([toks[-1]] if toks else []))))
            toks.append(mms[-1])
        tb = row_to_cols(bname, dstC, NDC)
        return V_(lambda e, dstC=dstC: e.tensor_add(dstC[:, 0:NDC], dstC[:, 0:NDC],
                                                    psA[6][:, 0:NDC]),
                  after=[toks[-1], tb])

    tok_qb = fold_bias(wq, "L0_attn_q_b", qbC, [tok_w0, tok_b1b])
    tok_kb = fold_bias(wk, "L0_attn_k_b", kbC, [tok_qb])
    tok_vb = fold_bias(wv, "L0_attn_v_b", vbeC, [tok_kb])
    tok_qb1 = V_(lambda e: e.tensor_scalar(qb1C[:, :], qbC[:, 0:NDC], 1.0, None, ALU.add),
                 after=[tok_qb])
    tok_kb1 = V_(lambda e: e.tensor_scalar(kb1C[:, :], kbC[:, 0:NDC], 1.0, None, ALU.add),
                 after=[tok_kb])

    decmms = []
    for c in range(NDC):
        decmms.append(T_(lambda e, c=c: e.matmul(
            psA[5][0:16, 0:1], wdec[:, c, :], b1cb[:, c:c + 1],
            start=(c == 0), stop=(c == NDC - 1)),
            after=([tok_w0, tok_b1b, tok_vb] if c == 0 else []) + ([decmms[-1]] if decmms else [])))

    def ld_decb(eng, sem):
        eng.dma_start(out=rowA[0:1, 0:16],
                      in_=AP(win["L0_attn_dec_b"], 0, [[16, 1], [1, 16]])).then_inc(sem, 16)
    tok_decb = DMA_("sync", ld_decb, n=1, after=[chain["rc"]])
    tok_decT = T_(lambda e: e.transpose(psA[5][0:16, 1:2], rowA[0:1, 0:16], id_f[0:1, 0:1]),
                  after=[tok_decb, decmms[-1]])
    tok_decc0 = V_(lambda e: e.tensor_copy(s16B[:, 3:4], psA[5][0:16, 0:1]), after=[tok_decT])
    tok_decc = V_(lambda e: e.tensor_add(decbc[:, :], s16B[:, 3:4], psA[5][0:16, 1:2]),
                  after=[tok_decc0])

    tok_vbrow = None
    for rnd in range(2):
        vbT = []
        for c4 in range(4):
            c = rnd * 4 + c4
            vbT.append(T_(lambda e, c=c, c4=c4: e.transpose(
                psA[4][0:1, c4 * 128:(c4 + 1) * 128], vbeC[:, c:c + 1], id_f[:, :]),
                after=[tok_vb, tok_decc] + ([vbT[-1]] if vbT else ([tok_vbrow] if tok_vbrow else []))))
        tok_vbrow = V_(lambda e, rnd=rnd: e.tensor_copy(
            rowsP[0:1, rnd * 512:(rnd + 1) * 512], psA[4][0:1, 0:512]),
            after=[vbT[-1], chain["rc"]])
    chain["rc"] = tok_vbrow
    vb2 = []
    for c4 in range(2):
        mm = []
        for c in range(4):
            mm.append(T_(lambda e, c4=c4, c=c: e.matmul(
                psA[3][:, c * 128:(c + 1) * 128], ones_row[:, :],
                rowsP[0:1, (c4 * 4 + c) * 128:(c4 * 4 + c + 1) * 128], start=True, stop=True),
                after=[tok_vbrow] + ([vb2[-1]] if vb2 else []) + ([mm[-1]] if mm else [])))
        vb2.append(V_(lambda e, c4=c4: e.tensor_copy(
            vbtm[:, c4 * 512:(c4 + 1) * 512], psA[3][:, :]), after=[mm[-1]]))
    tok_vbtm = vb2[-1]

    sc_toks = []
    for wsb, dep in [(wq, tok_qb1), (wk, tok_kb1), (wv, tok_vbtm), (wdec, tok_decc)]:
        ncols = wsb.shape[2]
        for c in range(NDC):
            sc_toks.append(V_(lambda e, wsb=wsb, c=c, ncols=ncols: e.tensor_scalar(
                wsb[:, c, 0:ncols], wsb[:, c, 0:ncols], g1c[:, c:c + 1], None, ALU.mult),
                after=[dep, tok_g1]))
    tok_wsc = sc_toks[-1]

    def ld_clip(eng, sem):
        eng.dma_start(out=rowA[0:1, 20:21],
                      in_=AP(win["L0_ffn_clip_x"], 0, [[1, 1], [1, 1]])).then_inc(sem, 16)
        eng.dma_start(out=rowA[0:1, 21:22],
                      in_=AP(win["L0_ffn_clip_down"], 0, [[1, 1], [1, 1]])).then_inc(sem, 16)
    tok_clipl = DMA_("sync", ld_clip, n=2, after=[chain["rc"]])
    tok_clip = S_(lambda e: e.activation(clipc[0:1, 0:2], rowA[0:1, 20:22], AF.Sigmoid),
                  after=[tok_clipl])
    tok_vone = G_(lambda e: e.memset(
        AP(vtm, 64, [[2 * H * KVE, 128], [KVE, 2 * H], [1, 1]]), 1.0), after=[tok_z])

    bld.flush("p0")
    if PHASE_LIMIT == "p0":
        return nc

    st1 = {"xtm": [], "chunk_tail": [], "b7": [], "xnst": [], "ln": {},
           "tll": {}, "vg": {}, "ln_last": []}
    stq = {}
    hbm_st = {}

    def emit_ldx(t):
        def ld_x(eng, sem, t=t):
            eng.dma_start(out=xtm[0][:, :, :],
                          in_=AP(x_in, t * TILE * D, [[D, 128], [128 * D, NCHT], [1, D]])
                          ).then_inc(sem, 16)
        tok = DMA_("sync", ld_x, n=1, after=st1["xtm"])
        st1["xtm"] = []
        st1["ln"][t] = {"x": tok, "xn_j": [], "x2e_j": []}

    def prep_ln_part(t, j):
        st = st1["ln"][t]
        tok_x = st["x"]
        tok_xn_j = st["xn_j"]
        tok_x2e_j = st["x2e_j"]
        sl = t % 2
        xj = xtm[0][:, j, :]
        cb = (j % 2) * 24
        tm = V_(lambda e, xj=xj, cb=cb: e.tensor_reduce(colA[:, cb:cb + 1], xj, AX.X, ALU.add),
                after=[tok_x] + (st1["ln_last"] if j == 0 else []))
        ts1 = S_(lambda e, xj=xj, cb=cb: e.activation(x2e_tm[:, :], xj, AF.Square,
                                                      accum_out=colA[:, cb + 1:cb + 2]),
                 after=[tok_x] + (st1["ln_last"] if j == 0 else [])
                 + (tok_x2e_j[-1:] if tok_x2e_j else []))

        def colops(e, cb=cb):
            e.tensor_scalar(colA[:, cb + 2:cb + 3], colA[:, cb:cb + 1], 1.0 / D, None, ALU.mult)
            e.tensor_scalar(colA[:, cb + 3:cb + 4], colA[:, cb + 1:cb + 2], 1.0 / D, None, ALU.mult)
            e.drain()
            e.tensor_mul(colA[:, cb + 4:cb + 5], colA[:, cb + 2:cb + 3], colA[:, cb + 2:cb + 3])
            e.drain()
            return e.tensor_sub(colA[:, cb + 5:cb + 6], colA[:, cb + 3:cb + 4], colA[:, cb + 4:cb + 5])
        tv1 = V_(colops, after=[tm, ts1])
        tsq = S_(lambda e, cb=cb: e.activation(colA[:, cb + 6:cb + 7], colA[:, cb + 5:cb + 6],
                                               AF.Sqrt, bias=LN_EPS), after=[tv1])

        def colops2(e, cb=cb):
            e.reciprocal(colA[:, cb + 7:cb + 8], colA[:, cb + 6:cb + 7])
            e.drain()
            e.tensor_mul(colA[:, cb + 8:cb + 9], colA[:, cb + 2:cb + 3], colA[:, cb + 7:cb + 8])
            e.drain()
            return e.tensor_scalar(colA[:, cb + 9:cb + 10], colA[:, cb + 8:cb + 9],
                                   -1.0, None, ALU.mult)
        tv2 = V_(colops2, after=[tsq])
        ta = S_(lambda e, xj=xj, cb=cb: e.activation(xn_tm[:, :], xj, AF.Identity,
                                                     bias=colA[:, cb + 9:cb + 10],
                                                     scale=colA[:, cb + 7:cb + 8]),
                after=[tv2])
        st1["xtm"].append(ta)
        tmx = V_(lambda e, cb=cb: (
            e.tensor_reduce(colA[:, cb + 10:cb + 11], xn_tm[:, :], AX.X, ALU.max,
                            apply_absolute_value=True),
            e.drain(),
            e.tensor_max(mx_col[:, :], mx_col[:, :], colA[:, cb + 10:cb + 11]))[-1],
            after=[ta])
        tm1 = V_(lambda e, cb=cb: e.tensor_reduce(colA[:, cb + 11:cb + 12], xn_tm[:, :],
                                                  AX.X, ALU.add), after=[ta])
        ts2 = S_(lambda e, cb=cb: e.activation(x2e_tm[:, :], xn_tm[:, :], AF.Square,
                                               accum_out=colA[:, cb + 12:cb + 13]),
                 after=[ta])

        def colops3(e, cb=cb):
            e.tensor_scalar(colA[:, cb + 13:cb + 14], colA[:, cb + 11:cb + 12], 1.0 / D, None, ALU.mult)
            e.tensor_scalar(colA[:, cb + 14:cb + 15], colA[:, cb + 12:cb + 13], 1.0 / D, None, ALU.mult)
            e.drain()
            e.tensor_mul(colA[:, cb + 15:cb + 16], colA[:, cb + 13:cb + 14], colA[:, cb + 13:cb + 14])
            e.drain()
            return e.tensor_sub(colA[:, cb + 16:cb + 17], colA[:, cb + 14:cb + 15], colA[:, cb + 15:cb + 16])
        tv3 = V_(colops3, after=[tm1, ts2])
        tsq2 = S_(lambda e, cb=cb: e.activation(colA[:, cb + 17:cb + 18], colA[:, cb + 16:cb + 17],
                                                AF.Sqrt, bias=LN_EPS), after=[tv3])

        def colops4(e, cb=cb):
            e.reciprocal(colA[:, cb + 18:cb + 19], colA[:, cb + 17:cb + 18])
            e.drain()
            e.tensor_mul(colA[:, cb + 19:cb + 20], colA[:, cb + 13:cb + 14], colA[:, cb + 18:cb + 19])
            e.drain()
            return e.tensor_scalar(colA[:, cb + 20:cb + 21], colA[:, cb + 19:cb + 20],
                                   -1.0, None, ALU.mult)
        tv4 = V_(colops4, after=[tsq2])
        ta2 = S_(lambda e, cb=cb: e.activation(x2e_tm[:, :], xn_tm[:, :], AF.Identity,
                                               bias=colA[:, cb + 20:cb + 21],
                                               scale=colA[:, cb + 18:cb + 19]),
                 after=[tv4])
        prevT = []
        for c in range(NDC):
            tT = T_(lambda e, c=c: e.transpose(
                psA[7][:, 0:128], xn_tm[:, c * 128:(c + 1) * 128], id_f[:, :]),
                after=[ta, tmx, ts2] + prevT
                + (st1["b7"] if (j == 0 and c == 0) else []))
            tE = S_(lambda e, c=c, j=j: e.copy(
                xnfm[:, c, 0:128], psA[7][:, 0:128]),
                after=[tT] + (tok_xn_j[-1:] if (c == 0 and tok_xn_j) else [])
                + (st1["xnst"] if (j == 0 and c == 0) else []))
            prevT = [tE]

        def st_xn_j(eng, sem, t=t, j=j):
            eng.dma_start(out=AP(xn_s, t * TILE + j * 128,
                                 [[TC, 128], [128 * TC, NDC], [1, 128]]),
                          in_=xnfm[:, :, :]).then_inc(sem, 16)
        DMA_("sync", st_xn_j, n=1, after=prevT)
        tok_xn_j.append(bld.fence("sync"))
        if "xn_fm" in dbg:
            DMA_("sync", lambda eng, sem, t=t, j=j: eng.dma_start(
                out=AP(dbg["xn_fm"], t * TILE + j * 128,
                       [[TC, 128], [128 * TC, NDC], [1, 128]]),
                in_=xnfm[:, :, :]).then_inc(sem, 16), n=1, after=prevT)
        for c in range(NDC):
            tT = T_(lambda e, c=c: e.transpose(
                psA[7][:, 0:128], x2e_tm[:, c * 128:(c + 1) * 128], id_f[:, :]),
                after=[ta2] + prevT)
            tE = S_(lambda e, c=c, j=j, sl=sl: e.copy(
                x2e[:, c, sl, j * 128:(j + 1) * 128], psA[7][:, 0:128]),
                after=[tT]
                + (st1["chunk_tail"] if (j == 0 and c == 0) else []))
            prevT = [tE]
        tok_x2e_j.append(prevT[0])
        if j == NCHT - 1:
            st1["ln_last"] = [prevT[0]]
            st1["xnst"] = [tok_xn_j[-1]]

    def prep_tail(t, b6gate):
        st = st1["ln"][t]
        sl = t % 2
        x2e_done = st["x2e_j"][-1]
        pcnt = {"i": 0, "first6": True, "first7": True}

        def proj_phi(wsb, bC, b1C, dstP, deps, prev_outs):
            outs = list(prev_outs)
            for co in range(NDC):
                bank = psA[6 + pcnt["i"] % 2]
                use6 = pcnt["i"] % 2 == 0
                pcnt["i"] += 1
                gate6 = []
                if use6 and pcnt["first6"]:
                    gate6 = list(b6gate)
                    pcnt["first6"] = False
                elif not use6 and pcnt["first7"]:
                    gate6 = [x2e_done]
                    pcnt["first7"] = False
                mm = []
                for c in range(NDC):
                    mm.append(T_(lambda e, c=c, co=co, wsb=wsb, sl=sl, bank=bank: e.matmul(
                        bank[:, :], wsb[:, c, co * 128:(co + 1) * 128], x2e[:, c, sl, :],
                        start=(c == 0), stop=(c == NDC - 1)),
                        after=((deps + [tok_wsc]) if (co == 0 and c == 0) else [])
                        + gate6
                        + ([mm[-1]] if mm else ([outs[-2]] if len(outs) >= 2 else []))))
                tv = V_(lambda e, co=co, bC=bC, bank=bank: e.tensor_scalar(
                    tbuf1[:, :], bank[:, :], bC[:, co:co + 1], 0.0, ALU.add, ALU.min),
                    after=[mm[-1]])
                tsx = S_(lambda e: e.activation(tbuf1[:, :], tbuf1[:, :], AF.Exp),
                         after=[tv] + ([outs[-1]] if outs else []))
                tph = V_(lambda e, co=co, dstP=dstP, b1C=b1C, sl=sl, bank=bank:
                         e.scalar_tensor_tensor(
                    dstP[:, co, sl, :], bank[:, :], b1C[:, co:co + 1], tbuf1[:, :],
                    ALU.add, ALU.max), after=[tsx])
                outs.append(tph)
            return outs[len(prev_outs):]
        qt = proj_phi(wq, qbC, qb1C, phiq, [x2e_done], [])
        kt = proj_phi(wk, kbC, kb1C, phik, [x2e_done], qt)
        if "phiq" in dbg:
            DMA_("sync", lambda eng, sem, t=t, sl=sl: eng.dma_start(
                out=AP(dbg["phiq"], t * TILE, [[TC, 128], [128 * TC, NDC], [1, TILE]]),
                in_=phiq[:, :, sl, :]).then_inc(sem, 16), n=1, after=[qt[-1]])
        if "phik" in dbg:
            DMA_("sync", lambda eng, sem, t=t, sl=sl: eng.dma_start(
                out=AP(dbg["phik"], t * TILE, [[TC, 128], [128 * TC, NDC], [1, TILE]]),
                in_=phik[:, :, sl, :]).then_inc(sem, 16), n=1, after=[kt[-1]])

        dmm = []
        for c in range(NDC):
            dmm.append(T_(lambda e, c=c, sl=sl: e.matmul(
                psA[7][0:16, 0:512], wdec[:, c, :], x2e[:, c, sl, :],
                start=(c == 0), stop=(c == NDC - 1)),
                after=(([x2e_done, kt[-1], kt[-2]]) if c == 0 else []) + ([dmm[-1]] if dmm else [])))
        tlam = S_(lambda e, sl=sl: e.activation(lam_t[:, sl, :], psA[7][0:16, 0:512], AF.Sigmoid,
                                                bias=decbc[:, 0:1]), after=[dmm[-1]])
        tlam2 = V_(lambda e, sl=sl: e.tensor_scalar(lam_t[:, sl, :], lam_t[:, sl, :],
                                                    LAM_MIN, None, ALU.max),
                   after=[tlam])
        if "lam_d" in dbg:
            tdl = DMA_("sync", lambda eng, sem, t=t, sl=sl: eng.dma_start(
                out=AP(dbg["lam_d"], t * TILE, [[TC, 16], [1, TILE]]),
                in_=lam_t[:, sl, :]).then_inc(sem, 16), n=1, after=[tlam2])
            tlam2 = tdl
        tll = S_(lambda e, sl=sl: e.activation(llam_t[:, sl, :], lam_t[:, sl, :], AF.Ln),
                 after=[tlam2])
        st1["b7"] = [tlam]
        st1["tll"][t] = tll

    def p1_chunk(t, cc, ch, prev, vgate):
        vslot = cc % 2
        tsl = t % 2
        vtoks = []
        for half in range(2):
            bank = psA[6]
            mm = []
            for c in range(NDC):
                dep = []
                if half == 0 and c == 0:
                    dep = list(prev) + vgate.get(ch - 2, [])
                mm.append(T_(lambda e, c=c, cc=cc, half=half, bank=bank, tsl=tsl: e.matmul(
                    bank[:, :], x2e[:, c, tsl, cc * 128:(cc + 1) * 128],
                    wv[:, c, half * 512:(half + 1) * 512],
                    start=(c == 0), stop=(c == NDC - 1)),
                    after=dep + ([mm[-1]] if mm else (vtoks[-1:] if vtoks else []))))
            ev = V_(lambda e, vslot=vslot, half=half, bank=bank: e.tensor_add(
                AP(vtm, vslot * H * KVE + half * 8 * KVE,
                   [[2 * H * KVE, 128], [KVE, 8], [1, 64]]),
                bank[:, :].rearrange("p (a b) -> p a b", a=8),
                vbtm[:, half * 512:(half + 1) * 512].rearrange("p (a b) -> p a b", a=8)),
                after=[mm[-1]])
            vtoks.append(ev)
        vdone = vtoks[-1]
        if "vtm_d" in dbg:
            DMA_("sync", lambda eng, sem, t=t, cc=cc, vslot=vslot: eng.dma_start(
                out=AP(dbg["vtm_d"], (t * NCHT + cc) * CH * H * KVE,
                       [[H * KVE, 128], [1, H * KVE]]),
                in_=vtm[:, vslot, :]).then_inc(sem, 16), n=1, after=[vdone])
        tlc = V_(lambda e, tsl=tsl: e.tensor_tensor_scan(
            lc_f[:, :], ones16_f[:, 0:CH], llam_t[:, tsl, cc * CH:(cc + 1) * CH],
            0.0, ALU.mult, ALU.add), after=prev)

        def lbk(e):
            e.tensor_copy(lcend_st[:, ch:ch + 1], lc_f[:, CH - 1:CH])
            e.tensor_copy(lpre_st[:, ch:ch + 1], ltot_c[:, :])
            e.drain()
            return e.tensor_add(ltot_c[:, :], ltot_c[:, :], lc_f[:, CH - 1:CH])
        tlbk = V_(lbk, after=[tlc])
        tce = S_(lambda e: e.activation(ctil_f[:, :], lc_f[:, :], AF.Exp), after=[tlc])
        tceb = V_(lambda e: e.tensor_copy(ctil_b[:, :], ctil_f[:, :]), after=[tce])
        tef = V_(lambda e: e.tensor_single_scalar(e_f[:, :], lc_f[:, :],
                                                  lc_f[:, CH - 1:CH], ALU.subtract),
                 after=[tlc])
        tee = S_(lambda e: e.activation(e_f[:, :], e_f[:, :], AF.Exp, scale=-1.0), after=[tef])
        teeb = V_(lambda e: e.tensor_copy(e_b[:, :], e_f[:, :]), after=[tee])
        tLt = T_(lambda e: e.transpose(psA[6][:, 0:16], lc_f[:, :], id_f[0:16, 0:16]),
                 after=[tlc] + prev)
        tCt = T_(lambda e: e.transpose(psA[6][:, 16:32], ctil_f[:, :], id_f[0:16, 0:16]),
                 after=[tce])
        tEt = T_(lambda e: e.transpose(psA[6][:, 32:48], e_f[:, :], id_f[0:16, 0:16]),
                 after=[tee])
        tLe = V_(lambda e: (e.tensor_copy(lctm[:, :], psA[6][:, 0:16]),
                            e.tensor_copy(ctm[:, :], psA[6][:, 16:32]),
                            e.tensor_copy(etm[:, :], psA[6][:, 32:48]))[-1],
                 after=[tLt, tCt, tEt])
        tdc1 = S_(lambda e: e.activation(s16B[:, 1:2], lcend_st[:, ch:ch + 1], AF.Exp),
                  after=[tlbk])
        tdc2 = S_(lambda e: e.activation(s16B[:, 2:3], lpre_st[:, ch:ch + 1], AF.Exp),
                  after=[tlbk])
        tdm1 = V_(lambda e: e.tensor_mul(s16B[:, 8:16], bccol(s16B[:, 1:2], NPAIR),
                                         pair_mask[:, :]), after=[tdc1])
        tdm2 = V_(lambda e: e.tensor_mul(s16A[:, 32:40], bccol(s16B[:, 2:3], NPAIR),
                                         pair_mask[:, :]), after=[tdc2])
        tdb1 = T_(lambda e: e.matmul(psA[6][:, 48:56], sel_half[:, :], s16B[:, 8:16],
                                     start=True, stop=True), after=[tdm1, tLe])
        tdb2 = T_(lambda e: e.matmul(psA[6][:, 56:64], sel_half[:, :], s16A[:, 32:40],
                                     start=True, stop=True), after=[tdm2])
        tdc = V_(lambda e: (e.tensor_copy(declc[:, :], psA[6][:, 48:56]),
                            e.tensor_copy(declp[:, :], psA[6][:, 56:64]))[-1],
                 after=[tdb1, tdb2])

        trh = V_(lambda e: e.tensor_mul(
            drhs[:, :].rearrange("p (a b) -> p a b", a=16),
            bcv(lc_f[:, :], 16), head_mask[:, :].rearrange("p (a b) -> p a b", a=16)),
            after=[tlc])
        dtoks = []
        prevD = [tdc]
        for bh in range(4):
            bank = psA[6]
            tb = T_(lambda e, bh=bh, bank=bank: e.matmul(
                bank[:, 64:576] if False else bank[:, 0:512],
                ones16_b[:, 0:128], drhs[:, bh * 512:(bh + 1) * 512],
                start=True, stop=True), after=[trh] + prevD)
            tsub = V_(lambda e, bh=bh, bank=bank: e.tensor_sub(
                D_sb[:, bh * 512:(bh + 1) * 512],
                AP(lctm, HORDER[bh * 4], [[16, 128], [2, 4], [0, CH]]),
                bank[:, 0:512].rearrange("p (a b) -> p a b", a=4)),
                after=[tb, tLe])
            tab = S_(lambda e, bh=bh: e.activation(D_sb[:, bh * 512:(bh + 1) * 512],
                                                   D_sb[:, bh * 512:(bh + 1) * 512], AF.Abs),
                     after=[tsub])
            tex = S_(lambda e, bh=bh: e.activation(D_sb[:, bh * 512:(bh + 1) * 512],
                                                   D_sb[:, bh * 512:(bh + 1) * 512], AF.Exp,
                                                   scale=-1.0), after=[tab])
            dtoks.append(tex)
            prevD = [tsub]

        if P1_CUT == "lc":
            return [dtoks[-1], tLe, tdc, vdone] + prev
        sc_toks = []
        sc_bank_tok = [None] * 4
        for hh in range(H):
            g, half = hh // 2, hh % 2
            pos = POS[hh]
            bank = psA[2 + pos // 4]
            tk = T_(lambda e, hh=hh, g=g, half=half, bank=bank, pos=pos, tsl=tsl: e.matmul(
                bank[:, (pos % 4) * 128:(pos % 4 + 1) * 128],
                phik[half * 64:(half + 1) * 64, g, tsl, cc * CH:(cc + 1) * CH],
                phiq[half * 64:(half + 1) * 64, g, tsl, cc * CH:(cc + 1) * CH],
                start=True, stop=True),
                after=([dtoks[-1]] + prev) if hh == 0 else [])
            sc_toks.append(tk)
            sc_bank_tok[pos // 4] = tk
        if P1_CUT == "sconly0":
            return [sc_toks[-1], tLe, tdc, dtoks[-1]] + prev
        if P1_CUT == "sconly":
            pvx = []
            for bq in range(4):
                pvx.append(V_(lambda e, bq=bq: e.tensor_copy(
                    scD[:, bq * 512:(bq + 1) * 512], psA[2 + bq][:, :]),
                    after=[sc_toks[bq * 4 + 3]]))
            return [pvx[-1], tLe, tdc, dtoks[-1]] + prev
        sd_toks = []
        for bh in range(4):
            sd_toks.append(V_(lambda e, bh=bh: e.tensor_mul(
                scD[:, bh * 512:(bh + 1) * 512], psA[2 + bh][:, :],
                D_sb[:, bh * 512:(bh + 1) * 512]),
                after=[sc_bank_tok[bh], sc_toks[-1], dtoks[bh]]))
        if P1_CUT == "sd":
            return [sd_toks[-1], tLe, tdc] + prev

        tkf = V_(lambda e, tsl=tsl: e.tensor_copy(
            phikf[:, :].rearrange("p (a b) -> p a b", a=NPAIR),
            phik[:, :, tsl, cc * CH:(cc + 1) * CH]), after=[sc_toks[-1]])
        kT = []
        prevKT = [sc_toks[-1], tkf] + list(prev)
        for gq in range(2):
            tTs = []
            for g4 in range(4):
                g = gq * 4 + g4
                tTs.append(T_(lambda e, g=g, g4=g4: e.transpose(
                    psA[g4 % 2][:, (g4 // 2) * 128:(g4 // 2) * 128 + 128],
                    phikf[:, g * CH:(g + 1) * CH], id_f[:, :]),
                    after=(prevKT if g4 == 0 else [])))
            evs = []
            for g4 in range(4):
                g = gq * 4 + g4
                for half in range(2):
                    hh = 2 * g + half
                    evs.append(V_(lambda e, g4=g4, half=half, hh=hh: (
                        e.tensor_scalar(keB[:, hh * NK:(hh + 1) * NK],
                                        psA[g4 % 2][:, (g4 // 2) * 128 + half * 64:
                                                     (g4 // 2) * 128 + half * 64 + 64],
                                        etm[:, hh:hh + 1], None, ALU.mult),
                        e.tensor_scalar(kcB[:, hh * NK:(hh + 1) * NK],
                                        psA[g4 % 2][:, (g4 // 2) * 128 + half * 64:
                                                     (g4 // 2) * 128 + half * 64 + 64],
                                        ctm[:, hh:hh + 1], None, ALU.mult))[-1],
                        after=([tTs[-1], tLe] if g4 == 0 and half == 0 else [])))
            prevKT = [evs[-1]]
            kT += tTs + evs
        kdone = kT[-1]

        if P1_CUT == "scores":
            return [sd_toks[-1], kdone, tLe, tdc] + prev
        qtk = []
        prevQ = [dtoks[-1], tceb, teeb, tdc]
        for gq in range(2):
            mmq = []
            for g4 in range(4):
                g = gq * 4 + g4
                mmq.append(T_(lambda e, g=g, g4=g4: e.matmul(
                    psA[6][:, g4 * 128:(g4 + 1) * 128],
                    sel_pair_b[:, g * 128:(g + 1) * 128],
                    ctil_b[:, :], start=True, stop=True),
                    after=(prevQ if g4 == 0 else [])))
            tqc = V_(lambda e, gq=gq, sl=ch % 2, tsl=tsl: e.tensor_mul(
                qcB[:, sl, gq * 4 * CH:(gq + 1) * 4 * CH].rearrange(
                    "p (a b) -> p a b", a=4),
                phiq[:, gq * 4:(gq + 1) * 4, tsl, cc * CH:(cc + 1) * CH],
                psA[6][:, 0:512].rearrange("p (a b) -> p a b", a=4)),
                after=[mmq[-1]]
                + ([stq[("qc", ch - 2)]] if (gq == 0 and ("qc", ch - 2) in stq) else []))
            mme = []
            for g4 in range(4):
                g = gq * 4 + g4
                mme.append(T_(lambda e, g=g, g4=g4: e.matmul(
                    psA[6][:, g4 * 128:(g4 + 1) * 128],
                    sel_pair_b[:, g * 128:(g + 1) * 128],
                    e_b[:, :], start=True, stop=True),
                    after=([tqc] if g4 == 0 else [])))
            tqe = V_(lambda e, gq=gq, sl=ch % 2, tsl=tsl: e.tensor_mul(
                qeB[:, sl, gq * 4 * CH:(gq + 1) * 4 * CH].rearrange(
                    "p (a b) -> p a b", a=4),
                phiq[:, gq * 4:(gq + 1) * 4, tsl, cc * CH:(cc + 1) * CH],
                psA[6][:, 0:512].rearrange("p (a b) -> p a b", a=4)),
                after=[mme[-1]])
            prevQ = [tqe]
            qtk += [tqc, tqe]
        qdone = qtk[-1]

        def st_qc(eng, sem, sl=ch % 2):
            eng.dma_start(out=AP(qc_s, ch * 128 * NPAIR * CH, [[NPAIR * CH, 128], [1, NPAIR * CH]]),
                          in_=qcB[:, sl, :]).then_inc(sem, 16)
            eng.dma_start(out=AP(qe_s, ch * 128 * NPAIR * CH, [[NPAIR * CH, 128], [1, NPAIR * CH]]),
                          in_=qeB[:, sl, :]).then_inc(sem, 16)
        DMA_("sync", st_qc, n=2, after=[qdone])
        tok_qst = bld.fence("sync")
        stq[("qc", ch)] = tok_qst

        if P1_CUT == "qc":
            return [qdone, tok_qst, kdone, sd_toks[-1], tdc]
        av_toks = []
        av_bank_tok = [None] * 4
        for hh in range(H):
            g, half = hh // 2, hh % 2
            pos = POS[hh]
            bank = psA[2 + pos // 4]
            m1 = T_(lambda e, hh=hh, bank=bank, pos=pos: e.matmul(
                bank[:, (pos % 4) * KVE:(pos % 4) * KVE + KVE],
                scD[:, pos * CH:(pos + 1) * CH],
                vtm[:, cc % 2, hh * KVE:(hh + 1) * KVE],
                start=True, stop=False),
                after=([sd_toks[pos // 4], qdone, vdone] if hh == 0 else [sd_toks[pos // 4]]))
            m2 = T_(lambda e, hh=hh, g=g, half=half, bank=bank, pos=pos, sl=ch % 2: e.matmul(
                bank[:, (pos % 4) * KVE:(pos % 4) * KVE + KVE],
                qcB[half * 64:(half + 1) * 64, sl, g * CH:(g + 1) * CH],
                Sb16[half * 64:(half + 1) * 64, g * KVE:(g + 1) * KVE],
                start=False, stop=True), after=[m1])
            av_toks.append(m2)
            av_bank_tok[pos // 4] = m2
        pv = []
        for bq in range(4):
            pv.append(V_(lambda e, bq=bq, sl=ch % 2: e.tensor_copy(
                partB[:, sl, bq * 4 * KVE:(bq + 1) * 4 * KVE], psA[2 + bq][:, 0:4 * KVE]),
                after=[av_bank_tok[bq], av_toks[-1]]
                + ([stq[("part", ch - 2)]] if ("part", ch - 2) in stq else [])))

        def st_part(eng, sem, sl=ch % 2):
            eng.dma_start(out=AP(part_s, ch * CH * H * KVE, [[H * KVE, 128], [1, H * KVE]]),
                          in_=partB[:, sl, :]).then_inc(sem, 16)
        DMA_("sync", st_part, n=1, after=pv)
        tok_pst = bld.fence("sync")
        stq[("part", ch)] = tok_pst

        if P1_CUT == "av":
            return [tok_pst, qdone, tok_qst, kdone, tdc, pv[-1]]
        kev_toks = []
        for hh in range(H):
            half = hh % 2
            bank = psA[hh // 8]
            slot = (hh % 8) // 2
            kev_toks.append(T_(lambda e, hh=hh, half=half, bank=bank, slot=slot: e.matmul(
                bank[half * 64:half * 64 + 64, slot * KVE:slot * KVE + KVE],
                keB[:, hh * NK:(hh + 1) * NK],
                vtm[:, cc % 2, hh * KVE:(hh + 1) * KVE],
                start=True, stop=True, tile_position=(0, half * 64)),
                after=([kdone, av_toks[-1]] if hh == 0 else [])))
        sup = []
        for g in range(NPAIR):
            bank = psA[g // 4]
            slot = g % 4
            sup.append(V_(lambda e, g=g, bank=bank, slot=slot: (
                e.tensor_scalar(Tmp_st[:, g * KVE:(g + 1) * KVE],
                                S_st[:, g * KVE:(g + 1) * KVE], declc[:, g:g + 1], None, ALU.mult),
                e.drain(),
                e.tensor_add(S_st[:, g * KVE:(g + 1) * KVE],
                             Tmp_st[:, g * KVE:(g + 1) * KVE],
                             bank[:, slot * KVE:slot * KVE + KVE]),
                e.drain(),
                e.tensor_copy(Sb16[:, g * KVE:(g + 1) * KVE],
                              S_st[:, g * KVE:(g + 1) * KVE]))[-1],
                after=[kev_toks[7 if g < 4 else 15], tdc, av_toks[-1]]))
        kcv_toks = []
        for hh in range(H):
            half = hh % 2
            bank = psA[hh // 8]
            slot = (hh % 8) // 2
            kcv_toks.append(T_(lambda e, hh=hh, half=half, bank=bank, slot=slot: e.matmul(
                bank[half * 64:half * 64 + 64, slot * KVE:slot * KVE + KVE],
                kcB[:, hh * NK:(hh + 1) * NK],
                vtm[:, cc % 2, hh * KVE:(hh + 1) * KVE],
                start=True, stop=True, tile_position=(0, half * 64)),
                after=([sup[-1]] if hh == 0 else [])))
        kce = []
        for g in range(NPAIR):
            bank = psA[g // 4]
            slot = g % 4
            kce.append(V_(lambda e, g=g, ch=ch, bank=bank, slot=slot: (
                e.tensor_copy(kcvs[:, ch % 2, g * KVE:(g + 1) * KVE],
                              bank[:, slot * KVE:slot * KVE + KVE]),
                e.tensor_scalar(Tmp_st[:, g * KVE:(g + 1) * KVE],
                                bank[:, slot * KVE:slot * KVE + KVE],
                                declp[:, g:g + 1], None, ALU.mult),
                e.drain(),
                e.tensor_add(Sb_st[:, g * KVE:(g + 1) * KVE],
                             Sb_st[:, g * KVE:(g + 1) * KVE],
                             Tmp_st[:, g * KVE:(g + 1) * KVE]))[-1],
                after=[kcv_toks[7 if g < 4 else 15], tdc, sup[-1] if g == 0 else kce[-1]]
                + (stq.get(("kcv", ch - 2), []) if g == 0 else [])))

        def st_kcv(eng, sem, ch=ch):
            eng.dma_start(out=AP(kcv_s, ch * 128 * NPAIR * KVE,
                                 [[NPAIR * KVE, 128], [1, NPAIR * KVE]]),
                          in_=kcvs[:, ch % 2, :]).then_inc(sem, 16)
        DMA_("sync", st_kcv, n=1, after=[kce[-1]])
        stq[("kcv", ch)] = [bld.fence("sync")]
        vgate[ch] = [kev_toks[-1], kcv_toks[-1]]
        return [sup[-1], kce[-1], pv[-1], qdone]

    st1["ln_last"] = [tok_wsc, tok_clip, tok_vone]
    emit_ldx(0)
    for j in range(NCHT):
        prep_ln_part(0, j)
    prep_tail(0, [tok_vone])
    for t in range(NT):
        if t + 1 < NT:
            emit_ldx(t + 1)
        prev = [st1["tll"][t]] + st1["chunk_tail"]
        for cc in range(NCHT):
            prev = p1_chunk(t, cc, t * NCHT + cc, prev, st1["vg"])
            if t + 1 < NT:
                prep_ln_part(t + 1, cc)
        if t + 1 < NT:
            prep_tail(t + 1, list(prev))
        st1["chunk_tail"] = list(prev)
    gate = prev
    p1_end = list(gate)
    if "sf_d" in dbg:
        DMA_("sync", lambda eng, sem: eng.dma_start(
            out=dbg["sf_d"].ap(), in_=S_st[:, :]).then_inc(sem, 16), n=1, after=p1_end)
    if "sb_d" in dbg:
        DMA_("sync", lambda eng, sem: eng.dma_start(
            out=dbg["sb_d"].ap(), in_=Sb_st[:, :]).then_inc(sem, 16), n=1, after=p1_end)

    q_p1 = bld.quiesce("sync")
    bld.flush("p1")
    if PHASE_LIMIT == "p1":
        return nc
    es1.close()

    es_x = ExitStack()
    psX = [PS(f"psX{i}", es_x) for i in range(4)]
    rowX = SB("rowX", [1, 192], F32, es_x)
    tmxT = T_(lambda e: e.transpose(psX[0][0:1, 0:128], mx_col[:, :], id_f[:, :]))
    tmxE = V_(lambda e: (e.tensor_copy(rowX[0:1, 0:128], psX[0][0:1, 0:128]),
                         e.memset(rowA[0:1, 31:46], 0.0),
                         e.drain(),
                         e.tensor_reduce(rowA[0:1, 30:31], rowX[0:1, 0:128], AX.X, ALU.max))[-1],
              after=[tmxT])

    def st_ag(eng, sem):
        for half in range(2):
            eng.dma_start(out=AP(ag_in, half * 4160,
                                 [[65, 64], [2 * 4160, NPAIR], [1, 65]]),
                          in_=S_st[half * 64:half * 64 + 64, :].rearrange("p (a b) -> p a b", a=NPAIR)
                          ).then_inc(sem, 16)
            eng.dma_start(out=AP(ag_in, PAY_SEC + half * 4160,
                                 [[65, 64], [2 * 4160, NPAIR], [1, 65]]),
                          in_=Sb_st[half * 64:half * 64 + 64, :].rearrange("p (a b) -> p a b", a=NPAIR)
                          ).then_inc(sem, 16)
        eng.dma_start(out=AP(ag_in, 2 * PAY_SEC, [[1, 16], [1, 1]]),
                      in_=ltot_c[:, :]).then_inc(sem, 16)
        eng.dma_start(out=AP(ag_in, 2 * PAY_SEC + 16, [[1, 16], [1, 1]]),
                      in_=rowA[0:1, 30:46]).then_inc(sem, 16)
    tok_ag_in = DMA_("gpsimd", st_ag, n=6, after=[tmxE])

    tok_cc = bld.g_async(lambda e: e.collective_compute(
        "AllGather", ALU.bypass, replica_groups=[list(range(8))],
        ins=[ag_in.ap().opt()], outs=[ag_out.ap().opt()]),
        after=[tok_ag_in])

    wprep = [SB(f"wprep{i}", [128, IFF], F32, es_x) for i in range(2)]
    wtT = SB("wtT", [128, 2, NIC * 128], BF16, es_x)
    wpc = SB("wpc", [128, 8], F32, es_x)

    blocks = []
    for wname, wT_dst, sc_dst, nrows in [
            ("L0_ffn_w_gate", wgT_s, sg_sc, IFF),
            ("L0_ffn_w_up", wuT_s, su_sc, IFF),
            ("L0_ffn_w_down", wdT_s, sd_sc, D)]:
        for r in range(nrows // 128):
            blocks.append((wname, wT_dst, sc_dst, nrows, r))

    def emit_ld(idx):
        wname, _, _, _, r = blocks[idx]
        ncols = win[wname].shape[1]
        buf = wprep[idx % 2]

        def ldw(eng, sem, r=r, buf=buf, wname=wname, ncols=ncols):
            eng.dma_start(out=buf[:, 0:ncols],
                          in_=AP(win[wname], r * 128 * ncols, [[ncols, 128], [1, ncols]])
                          ).then_inc(sem, 16)
        return DMA_("sync", ldw, n=1, after=pw_st["buf"].get(idx % 2, []))

    pw_st = {"trd": None, "bank": {}, "buf": {}, "wst": {}}
    pend_ld = {0: emit_ld(0), 1: emit_ld(1)}
    pw = []
    for idx, (wname, wT_dst, sc_dst, nrows, r) in enumerate(blocks):
        ncols = win[wname].shape[1]
        nbl = ncols // 128
        buf = wprep[idx % 2]
        tl = pend_ld.pop(idx)
        tmax = V_(lambda e, buf=buf, ncols=ncols: (
            e.tensor_reduce(wpc[:, 0:1], buf[:, 0:ncols], AX.X, ALU.max,
                            apply_absolute_value=True),
            e.drain(),
            e.tensor_scalar(wpc[:, 1:2], wpc[:, 0:1], 1.0 / QMAX, None, ALU.mult),
            e.drain(),
            e.reciprocal(wpc[:, 2:3], wpc[:, 1:2]))[-1],
            after=[tl] + ([pw_st["trd"]] if pw_st["trd"] else []))
        tsc = V_(lambda e, r=r, sc_dst=sc_dst: e.tensor_copy(sc_dst[:, r:r + 1], wpc[:, 1:2]),
                 after=[tmax])
        trd = S_(lambda e, buf=buf, ncols=ncols: e.activation(
            buf[:, 0:ncols], buf[:, 0:ncols], AF.Identity,
            bias=MAGIC, scale=wpc[:, 2:3]), after=[tmax, tsc])
        tcl = V_(lambda e, buf=buf, ncols=ncols: (
            e.tensor_scalar(buf[:, 0:ncols], buf[:, 0:ncols], MAGIC, QMAX,
                            ALU.subtract, ALU.min),
            e.tensor_scalar(buf[:, 0:ncols], buf[:, 0:ncols], -QMAX - 1.0, None, ALU.max))[-1],
            after=[trd])
        pw_st["trd"] = trd
        tT2 = []
        for c in range(nbl):
            bk = c % 4
            bank = psX[bk]
            tt = T_(lambda e, bank=bank, c=c, buf=buf: e.transpose(
                bank[:, 0:128], buf[:, c * 128:(c + 1) * 128], id_f[:, :]),
                after=[tcl] + (([tT2[c - 4]] if c >= 4 else pw_st["bank"].get(bk, []))))
            te = S_(lambda e, c=c, bank=bank, sl=idx % 2: e.copy(
                wtT[:, sl, c * 128:(c + 1) * 128], bank[:, 0:128]),
                after=[tt] + ([pw_st["wst"][idx % 2]] if (c == 0 and idx % 2 in pw_st["wst"]) else []))
            tT2.append(te)
            pw_st["bank"][bk] = [te]
        pw_st["buf"][idx % 2] = [tT2[-1]]
        if idx + 2 < len(blocks):
            pend_ld[idx + 2] = emit_ld(idx + 2)

        def stw(eng, sem, r=r, wT_dst=wT_dst, nbl=nbl, nrows=nrows, sl=idx % 2):
            eng.dma_start(out=AP(wT_dst, r * 128,
                                 [[nrows, 128], [128 * nrows, nbl], [1, 128]]),
                          in_=wtT[:, sl, 0:nbl * 128].rearrange("p (a b) -> p a b", a=nbl)
                          ).then_inc(sem, 16)
        pw = [DMA_("sync", stw, n=1, after=[tT2[-1]])]
        pw_st["wst"][idx % 2] = pw[0]

    tok_ccw = G_(lambda e: e.nop(), after=[tok_cc, pw[-1]])
    q_x1 = bld.quiesce("sync")
    es_x.close()
    bld.flush("x1")
    if PHASE_LIMIT == "x1":
        return nc

    es2 = ExitStack()
    sel_pair_f = SB("sel_pair_f", [16, NPAIR * 128], F32, es2)
    SinC = SB("SinC", [128, NPAIR * KVE], BF16, es2)
    Rb16 = SB("Rb16", [128, NPAIR * KVE], BF16, es2)
    wo = SB("wo", [128, NDC, D], BF16, es2)
    scr2 = SB("scr2", [128, TILE], F32, es2)
    LtG = SB("LtG", [8, 16], F32, es2)
    mG = SB("mG", [8, 1], F32, es2)
    Wf8 = SB("Wf8", [8, 16], F32, es2)
    Wb8 = SB("Wb8", [8, 16], F32, es2)
    WfT = SB("WfT", [16, 8], F32, es2)
    WbT = SB("WbT", [16, 8], F32, es2)
    Wexp = SB("Wexp", [128, NPAIR * 8], F32, es2)
    Wexpb = SB("Wexpb", [128, NPAIR * 8], F32, es2)
    peerb = SB("peerb", [128, NPAIR * KVE], F32, es2)
    xnfm2 = SB("xnfm2", [128, NDC, TILE], F32, es2)
    qcL = SB("qcL", [128, NCHT, NPAIR * CH], BF16, es2)
    qeL = SB("qeL", [128, NCHT, NPAIR * CH], BF16, es2)
    partL = SB("partL", [128, NCHT, H * KVE], BF16, es2)
    totB = SB("totB", [128, H * KVE], F32, es2)
    atm = SB("atm", [128, H * NK], F32, es2)
    attnfm = SB("attnfm", [128, NDC, TILE], BF16, es2)
    x1b = SB("x1b", [128, NDC, 2, TILE], F32, es2)
    kcvL = SB("kcvL", [128, 2, NPAIR * KVE], BF16, es2)
    xqb = SB("xqb", [128, NDC, 2, TILE], BF16, es2)
    psB = [PS(f"psB{i}", es2) for i in range(8)]

    def p2_wo(eng, sem):
        eng.dma_start(out=wo[:, :, :],
                      in_=AP(win["L0_attn_o_w"], 0, [[D, 128], [128 * D, NDC], [1, D]])
                      ).then_inc(sem, 16)
    tok_wo = DMA_("gpsimd", p2_wo, n=1)

    def p2_spf(eng, sem):
        eng.dma_start(out=sel_pair_f[:, :], in_=const_t["sel_pair_f"].ap()).then_inc(sem, 16)
    tok_spf = DMA_("sync", p2_spf, n=1)

    def p2_gather_small(eng, sem):
        eng.dma_start(out=LtG[:, :], in_=AP(ag_out, 2 * PAY_SEC, [[PAY, 8], [1, 16]])
                      ).then_inc(sem, 16)
        eng.dma_start(out=mG[:, :], in_=AP(ag_out, 2 * PAY_SEC + 16, [[PAY, 8], [1, 1]]),
                      allow_slow_non_contiguous=True).then_inc(sem, 16)
    tok_gs = DMA_("sync", p2_gather_small, n=2, after=[tok_ccw])

    tm1T = T_(lambda e: e.transpose(psB[0][0:1, 0:8], mG[:, :], id_f[0:8, 0:8]),
              after=[tok_gs])
    tm1 = V_(lambda e: (e.tensor_copy(rowA[0:1, 40:48], psB[0][0:1, 0:8]),
                        e.drain(),
                        e.tensor_reduce(rowA[0:1, 24:25], rowA[0:1, 40:48], AX.X, ALU.max),
                        e.drain(),
                        e.tensor_mul(rowA[0:1, 25:26], rowA[0:1, 24:25], clipc[0:1, 0:1]),
                        e.drain(),
                        e.tensor_scalar(rowA[0:1, 26:27], rowA[0:1, 25:26], 1.0 / QMAX, None, ALU.mult),
                        e.drain(),
                        e.reciprocal(rowA[0:1, 27:28], rowA[0:1, 26:27]))[-1],
             after=[tm1T])
    if "m1_d" in dbg:
        DMA_("sync", lambda eng, sem: eng.dma_start(
            out=dbg["m1_d"].ap(), in_=rowA[0:1, 24:40]).then_inc(sem, 16), n=1, after=[tm1])
    tb1 = T_(lambda e: (e.matmul(psB[1][:, 0:1], ones_row[:, :], rowA[0:1, 26:27],
                                 start=True, stop=True),
                        e.matmul(psB[1][:, 1:2], ones_row[:, :], rowA[0:1, 27:28],
                                 start=True, stop=True))[-1], after=[tm1])
    tb1e = V_(lambda e: (e.tensor_copy(sc1_col[:, :], psB[1][:, 0:1]),
                         e.tensor_copy(rs1_col[:, :], psB[1][:, 1:2]))[-1], after=[tb1])

    tWf = T_(lambda e: (e.matmul(psB[2][0:8, 0:16], ccpf[:, :], LtG[:, :], start=True, stop=True),
                        e.matmul(psB[2][0:8, 16:32], ccpb[:, :], LtG[:, :], start=True, stop=True))[-1],
             after=[tok_gs, tb1])
    tWfe = S_(lambda e: (e.activation(Wf8[:, :], psB[2][0:8, 0:16], AF.Exp),
                         e.activation(Wb8[:, :], psB[2][0:8, 16:32], AF.Exp))[-1], after=[tWf])
    tWfm = V_(lambda e: (e.tensor_scalar(Wf8[:, :], Wf8[:, :], ccmf[:, 0:1], None, ALU.mult),
                         e.tensor_scalar(Wb8[:, :], Wb8[:, :], ccmb[:, 0:1], None, ALU.mult))[-1],
              after=[tWfe])
    tWT = T_(lambda e: (e.transpose(psB[3][0:16, 0:8], Wf8[:, :], id_f[0:8, 0:8]),
                        e.transpose(psB[3][0:16, 8:16], Wb8[:, :], id_f[0:8, 0:8]))[-1],
             after=[tWfm])
    tWTe = V_(lambda e: (e.tensor_copy(WfT[:, :], psB[3][0:16, 0:8]),
                         e.tensor_copy(WbT[:, :], psB[3][0:16, 8:16]))[-1], after=[tWT])
    wx = []
    for g in range(NPAIR):
        wx.append(T_(lambda e, g=g: (
            e.matmul(psB[4][:, g * 8:(g + 1) * 8], sel_pair_f[:, g * 128:(g + 1) * 128],
                     WfT[:, :], start=True, stop=True),
            e.matmul(psB[4][:, 64 + g * 8:64 + (g + 1) * 8],
                     sel_pair_f[:, g * 128:(g + 1) * 128], WbT[:, :], start=True, stop=True))[-1],
            after=[tWTe, tok_spf] + (wx[-1:] if wx else [])))
    tWx = V_(lambda e: (e.tensor_copy(Wexp[:, :], psB[4][:, 0:64]),
                        e.tensor_copy(Wexpb[:, :], psB[4][:, 64:128]))[-1], after=[wx[-1]])

    comb = [tWx]
    for peer in range(8):
        def ld_peer(eng, sem, peer=peer):
            for half in range(2):
                eng.dma_start(out=peerb[half * 64:half * 64 + 64, :].rearrange(
                    "p (a b) -> p a b", a=NPAIR),
                    in_=AP(ag_out, peer * PAY + half * 4160,
                           [[65, 64], [2 * 4160, NPAIR], [1, 65]])).then_inc(sem, 16)
        tlp = DMA_("sync", ld_peer, n=2, after=comb[-1:])
        ops = []
        for g in range(NPAIR):
            ops.append(V_(lambda e, g=g, peer=peer: (
                e.tensor_scalar(Tmp_st[:, g * KVE:(g + 1) * KVE],
                                peerb[:, g * KVE:(g + 1) * KVE],
                                Wexp[:, g * 8 + peer:g * 8 + peer + 1], None, ALU.mult),
                e.drain(),
                e.tensor_add(Sin_st[:, g * KVE:(g + 1) * KVE],
                             Sin_st[:, g * KVE:(g + 1) * KVE],
                             Tmp_st[:, g * KVE:(g + 1) * KVE]))[-1],
                after=[tlp, tWx]))

        def ld_peerb(eng, sem, peer=peer):
            for half in range(2):
                eng.dma_start(out=peerb[half * 64:half * 64 + 64, :].rearrange(
                    "p (a b) -> p a b", a=NPAIR),
                    in_=AP(ag_out, peer * PAY + PAY_SEC + half * 4160,
                           [[65, 64], [2 * 4160, NPAIR], [1, 65]])).then_inc(sem, 16)
        tlpb = DMA_("sync", ld_peerb, n=2, after=[ops[-1]])
        for g in range(NPAIR):
            ops.append(V_(lambda e, g=g, peer=peer: (
                e.tensor_scalar(Tmp_st[:, g * KVE:(g + 1) * KVE],
                                peerb[:, g * KVE:(g + 1) * KVE],
                                Wexpb[:, g * 8 + peer:g * 8 + peer + 1], None, ALU.mult),
                e.drain(),
                e.tensor_add(Rin_st[:, g * KVE:(g + 1) * KVE],
                             Rin_st[:, g * KVE:(g + 1) * KVE],
                             Tmp_st[:, g * KVE:(g + 1) * KVE]))[-1],
                after=[tlpb]))
        comb = [ops[-1]]
    tok_comb = comb[-1]
    tok_Rinit = V_(lambda e: (e.tensor_copy(R_st[:, :], Rin_st[:, :]),
                              e.tensor_copy(Rb16[:, :], Rin_st[:, :]))[-1], after=[tok_comb])
    if "sin_d" in dbg:
        DMA_("sync", lambda eng, sem: eng.dma_start(
            out=dbg["sin_d"].ap(), in_=Sin_st[:, :]).then_inc(sem, 16), n=1, after=[tok_comb])
    if "rin_d" in dbg:
        DMA_("sync", lambda eng, sem: eng.dma_start(
            out=dbg["rin_d"].ap(), in_=Rin_st[:, :]).then_inc(sem, 16), n=1, after=[tok_comb])

    def p2_tile(t, gate):
        def ld_t(eng, sem):
            eng.dma_start(out=xnfm2[:, :, :],
                          in_=AP(xn_s, t * TILE, [[TC, 128], [128 * TC, NDC], [1, TILE]])
                          ).then_inc(sem, 16)
            eng.dma_start(out=qcL[:, :, :],
                          in_=AP(qc_s, t * NCHT * 128 * NPAIR * CH,
                                 [[NPAIR * CH, 128], [128 * NPAIR * CH, NCHT], [1, NPAIR * CH]])
                          ).then_inc(sem, 16)
            eng.dma_start(out=qeL[:, :, :],
                          in_=AP(qe_s, t * NCHT * 128 * NPAIR * CH,
                                 [[NPAIR * CH, 128], [128 * NPAIR * CH, NCHT], [1, NPAIR * CH]])
                          ).then_inc(sem, 16)
            eng.dma_start(out=partL[:, :, :],
                          in_=AP(part_s, t * NCHT * CH * H * KVE,
                                 [[H * KVE, 128], [128 * H * KVE, NCHT], [1, H * KVE]])
                          ).then_inc(sem, 16)
        tok_l = DMA_("sync", ld_t, n=4, after=gate + [q_p1])

        last = list(gate)
        for cc in range(NCHT - 1, -1, -1):
            ch = t * NCHT + cc
            td1 = S_(lambda e, ch=ch: (
                e.activation(s16B[:, 1:2], lcend_st[:, ch:ch + 1], AF.Exp),
                e.activation(s16B[:, 2:3], lpre_st[:, ch:ch + 1], AF.Exp))[-1], after=last)
            td2 = V_(lambda e: (
                e.tensor_mul(s16B[:, 8:16], bccol(s16B[:, 1:2], NPAIR), pair_mask[:, :]),
                e.tensor_mul(s16A[:, 32:40], bccol(s16B[:, 2:3], NPAIR), pair_mask[:, :]))[-1],
                after=[td1])
            td3 = T_(lambda e: (
                e.matmul(psB[5][:, 0:8], sel_half[:, :], s16B[:, 8:16], start=True, stop=True),
                e.matmul(psB[5][:, 8:16], sel_half[:, :], s16A[:, 32:40], start=True, stop=True))[-1],
                after=[td2] + last)
            td4 = V_(lambda e: (e.tensor_copy(declc[:, :], psB[5][:, 0:8]),
                                e.tensor_copy(declp[:, :], psB[5][:, 8:16]))[-1], after=[td3])
            sic = []
            for g in range(NPAIR):
                sic.append(V_(lambda e, g=g: e.tensor_scalar(
                    SinC[:, g * KVE:(g + 1) * KVE], Sin_st[:, g * KVE:(g + 1) * KVE],
                    declp[:, g:g + 1], None, ALU.mult), after=[td4, tok_Rinit]))
            mmz = []
            mm_bank_tok = [None] * 4
            for hh in range(H):
                g, half = hh // 2, hh % 2
                pos = POS[hh]
                bank = psB[pos // 4]
                ma = T_(lambda e, hh=hh, g=g, half=half, bank=bank, pos=pos, cc=cc: e.matmul(
                    bank[:, (pos % 4) * KVE:(pos % 4) * KVE + KVE],
                    qeL[half * 64:half * 64 + 64, cc, g * CH:(g + 1) * CH],
                    Rb16[half * 64:half * 64 + 64, g * KVE:(g + 1) * KVE],
                    start=True, stop=False),
                    after=([sic[-1], tok_l] + last) if hh == 0 else [])
                mb = T_(lambda e, hh=hh, g=g, half=half, bank=bank, pos=pos, cc=cc: e.matmul(
                    bank[:, (pos % 4) * KVE:(pos % 4) * KVE + KVE],
                    qcL[half * 64:half * 64 + 64, cc, g * CH:(g + 1) * CH],
                    SinC[half * 64:half * 64 + 64, g * KVE:(g + 1) * KVE],
                    start=False, stop=True), after=[ma])
                mmz.append(mb)
                mm_bank_tok[pos // 4] = mb
            pa = []
            for bq in range(4):
                pa.append(V_(lambda e, bq=bq, cc=cc: e.tensor_add(
                    totB[:, bq * 4 * KVE:(bq + 1) * 4 * KVE],
                    psB[bq][:, 0:4 * KVE],
                    partL[:, cc, bq * 4 * KVE:(bq + 1) * 4 * KVE]),
                    after=[mm_bank_tok[bq], mmz[-1], tok_l]))
            dn = []
            for hh in range(H):
                pos = POS[hh]
                dn.append(V_(lambda e, hh=hh, pos=pos: (
                    e.tensor_scalar(colA[:, 40:41], totB[:, pos * KVE + NK:pos * KVE + NK + 1],
                                    EPS_DEN, None, ALU.max),
                    e.drain(),
                    e.reciprocal(colA[:, 41:42], colA[:, 40:41]),
                    e.drain(),
                    e.tensor_scalar(atm[:, hh * NK:(hh + 1) * NK],
                                    totB[:, pos * KVE:pos * KVE + NK],
                                    colA[:, 41:42], None, ALU.mult))[-1],
                    after=[pa[pos // 4]] + (dn[-1:] if dn else [])))
            ru = []
            for g in range(NPAIR):
                ru.append(V_(lambda e, g=g, ch=ch: (
                    e.tensor_scalar(Tmp_st[:, g * KVE:(g + 1) * KVE],
                                    R_st[:, g * KVE:(g + 1) * KVE],
                                    declc[:, g:g + 1], None, ALU.mult),
                    e.drain(),
                    e.tensor_add(R_st[:, g * KVE:(g + 1) * KVE],
                                 Tmp_st[:, g * KVE:(g + 1) * KVE],
                                 kcvL[:, ch % 2, g * KVE:(g + 1) * KVE]),
                    e.drain(),
                    e.tensor_copy(Rb16[:, g * KVE:(g + 1) * KVE],
                                  R_st[:, g * KVE:(g + 1) * KVE]))[-1],
                    after=[td4, tok_Rinit, mmz[-1], st2kl[ch]] + (ru[-1:] if ru else [])))
            if ch - 2 >= 0:
                emit_kl(ch - 2, [ru[-1]])
            at = []
            for g in range(NPAIR):
                bank = psB[6]
                tt = T_(lambda e, g=g, bank=bank: e.transpose(
                    bank[:, (g % 4) * 128:(g % 4) * 128 + 128],
                    atm[:, g * 128:(g + 1) * 128], id_f[:, :]),
                    after=[dn[2 * g + 1]] + ([at[-1]] if at else [mmz[-1]]))
                te = S_(lambda e, g=g, cc=cc, bank=bank: e.copy(
                    attnfm[:, g, cc * CH:(cc + 1) * CH], bank[:, (g % 4) * 128:(g % 4) * 128 + 128]),
                    after=[tt])
                at.append(te)
            last = [ru[-1], at[-1], dn[-1]]
        attn_done = last[1]
        if "attn_d" in dbg:
            DMA_("sync", lambda eng, sem, t=t: eng.dma_start(
                out=AP(dbg["attn_d"], t * TILE, [[TC, 128], [128 * TC, NDC], [1, TILE]]),
                in_=AP(attnfm, 0, [[NDC * TILE, 128], [TILE, NDC], [1, TILE]])).then_inc(sem, 16),
                n=1, after=last)

        oT = []
        for co in range(NDC):
            bank = psB[7]
            mm = []
            for c in range(NDC):
                mm.append(T_(lambda e, c=c, co=co, bank=bank: e.matmul(
                    bank[:, :], wo[:, c, co * 128:(co + 1) * 128], attnfm[:, c, :],
                    start=(c == 0), stop=(c == NDC - 1)),
                    after=(([tok_wo, attn_done] + last) if (co == 0 and c == 0) else [])
                    + ([mm[-1]] if mm else (oT[-1:] if oT else []))))
            ev = V_(lambda e, co=co, bank=bank, sl=t % 2: e.scalar_tensor_tensor(
                x1b[:, co, sl, :], bank[:, :], obC[:, co:co + 1], xnfm2[:, co, :],
                ALU.add, ALU.add), after=[mm[-1], tok_l]
                + ([stq[("x1", t + 2)]] if (co == 0 and ("x1", t + 2) in stq) else []))
            oT.append(ev)

        def st_x1(eng, sem, sl=t % 2):
            eng.dma_start(out=AP(x1_s, t * TILE, [[TC, 128], [128 * TC, NDC], [1, TILE]]),
                          in_=x1b[:, :, sl, :]).then_inc(sem, 16)
        DMA_("sync", st_x1, n=1, after=oT)
        tok_x1st = bld.fence("sync")
        stq[("x1", t)] = tok_x1st
        if "x1_d" in dbg:
            DMA_("sync", lambda eng, sem, t=t: eng.dma_start(
                out=AP(dbg["x1_d"], t * TILE, [[TC, 128], [128 * TC, NDC], [1, TILE]]),
                in_=x1b[:, :, t % 2, :]).then_inc(sem, 16), n=1, after=oT)

        xq_toks = []
        for co in range(NDC):
            t1 = S_(lambda e, co=co: e.activation(scr2[:, :],
                                                  xnfm2[:, co, :], AF.Identity,
                                                  bias=MAGIC, scale=rs1_col[:, 0:1]),
                    after=[tok_l, tb1e] + ([xq_toks[-1]] if xq_toks else []))
            t2 = V_(lambda e, co=co, sl=t % 2: (
                e.tensor_scalar(scr2[:, :], scr2[:, :], MAGIC, QMAX,
                                ALU.subtract, ALU.min),
                e.tensor_scalar(xqb[:, co, sl, :], scr2[:, :], -QMAX - 1.0, None, ALU.max))[-1],
                after=[t1]
                + ([stq[("xq", t + 2)]] if (co == 0 and ("xq", t + 2) in stq) else []))
            xq_toks.append(t2)

        def st_xq(eng, sem, sl=t % 2):
            eng.dma_start(out=AP(xq_s, t * TILE, [[TC, 128], [128 * TC, NDC], [1, TILE]]),
                          in_=xqb[:, :, sl, :]).then_inc(sem, 16)
        DMA_("sync", st_xq, n=1, after=[xq_toks[-1]])
        tok_xqst = bld.fence("sync")
        stq[("xq", t)] = tok_xqst
        if "xq_d" in dbg:
            DMA_("sync", lambda eng, sem, t=t: eng.dma_start(
                out=AP(dbg["xq_d"], t * TILE, [[TC, 128], [128 * TC, NDC], [1, TILE]]),
                in_=xqb[:, :, t % 2, :]).then_inc(sem, 16),
                n=1, after=[xq_toks[-1]])
        return [oT[-1], xq_toks[-1]] + last

    st2kl = {}

    def emit_kl(ch, after):
        def ld_kl(eng, sem, ch=ch):
            eng.dma_start(out=kcvL[:, ch % 2, :],
                          in_=AP(kcv_s, ch * 128 * NPAIR * KVE,
                                 [[NPAIR * KVE, 128], [1, NPAIR * KVE]])).then_inc(sem, 16)
        st2kl[ch] = DMA_("sync", ld_kl, n=1, after=after)

    emit_kl(NCH - 1, [])
    emit_kl(NCH - 2, [])
    gate2 = [tok_Rinit, tok_comb, tb1e]
    for t in range(NT - 1, -1, -1):
        gate2 = p2_tile(t, gate2)
    q_p2 = bld.quiesce("sync")
    bld.flush("p2")
    if PHASE_LIMIT == "p2":
        return nc
    es2.close()
    es_kcv.close()

    es4 = ExitStack()
    wgT = SB("wgT", [128, NDC, IFF], BF16, es4)
    wuT = SB("wuT", [128, NDC, IFF], BF16, es4)
    xqL = [SB(f"xqL{i}", [128, NDC, TILE], BF16, es4) for i in range(2)]
    silb = SB("silb", [128, TILE], F32, es4)
    h2b = [SB(f"h2b{i}", [128, TILE], F32, es4) for i in range(4)]
    sgv = SB("sgv", [128, NIC], F32, es4)
    suv = SB("suv", [128, NIC], F32, es4)
    psC = [PS(f"psC{i}", es4) for i in range(4)]

    def p4_w(eng, sem):
        eng.dma_start(out=wgT[:, :, :],
                      in_=AP(wgT_s, 0, [[IFF, 128], [128 * IFF, NDC], [1, IFF]])).then_inc(sem, 16)
        eng.dma_start(out=wuT[:, :, :],
                      in_=AP(wuT_s, 0, [[IFF, 128], [128 * IFF, NDC], [1, IFF]])).then_inc(sem, 16)
    tok_w4 = DMA_("gpsimd", p4_w, n=2, after=[q_x1])
    tok_sv = V_(lambda e: (e.tensor_scalar(sgv[:, :], sg_sc[:, 0:NIC], sc1_col[:, 0:1], None, ALU.mult),
                           e.tensor_scalar(suv[:, :], su_sc[:, 0:NIC], sc1_col[:, 0:1], None, ALU.mult))[-1])

    def p4_tile(t, gate):
        xq_ = xqL[t % 2]

        def ld_xq(eng, sem):
            eng.dma_start(out=xq_[:, :, :],
                          in_=AP(xq_s, t * TILE, [[TC, 128], [128 * TC, NDC], [1, TILE]])
                          ).then_inc(sem, 16)
        tl = DMA_("sync", ld_xq, n=1, after=gate + [q_p2])
        sil_toks, h2_toks, mx_toks, st_toks, gmm_toks, umm_toks = [], [], [], [], [], []
        for ic in range(NIC):
            h2slot = (t * NIC + ic) % 4
            bg = psC[ic % 2]
            bu = psC[2 + ic % 2]
            mmg = []
            for c in range(NDC):
                dep = []
                if ic == 0 and c == 0:
                    dep = [tok_w4, tl] + gate
                elif c == 0 and ic >= 2:
                    dep = [sil_toks[ic - 2]]
                mmg.append(T_(lambda e, c=c, ic=ic, bg=bg, xq_=xq_: e.matmul(
                    bg[:, :], wgT[:, c, ic * 128:(ic + 1) * 128], xq_[:, c, :],
                    start=(c == 0), stop=(c == NDC - 1)),
                    after=dep + ([mmg[-1]] if mmg else [])))
            gmm_toks.append(mmg[-1])
            mmu = []
            for c in range(NDC):
                dep = [mmg[-1]] if c == 0 else []
                if c == 0 and ic >= 2:
                    dep = dep + [h2_toks[ic - 2]]
                mmu.append(T_(lambda e, c=c, ic=ic, bu=bu, xq_=xq_: e.matmul(
                    bu[:, :], wuT[:, c, ic * 128:(ic + 1) * 128], xq_[:, c, :],
                    start=(c == 0), stop=(c == NDC - 1)),
                    after=dep + ([mmu[-1]] if mmu else [])))
            umm_toks.append(mmu[-1])
            tsil = S_(lambda e, ic=ic, bg=bg: e.activation(
                silb[:, :], bg[:, :], AF.Silu, scale=sgv[:, ic:ic + 1]),
                after=[mmg[-1], tok_sv] + ([h2_toks[-1]] if h2_toks else []))
            sil_toks.append(tsil)
            h2x = h2b[h2slot]
            dep_h2 = [mmu[-1], tsil] + stq.get(("h2s", h2slot), [])
            th2 = V_(lambda e, ic=ic, bu=bu, h2x=h2x: e.scalar_tensor_tensor(
                h2x[:, :], bu[:, :], suv[:, ic:ic + 1], silb[:, :], ALU.mult, ALU.mult),
                after=dep_h2)
            h2_toks.append(th2)
            tmx2 = V_(lambda e, h2x=h2x: (
                e.tensor_reduce(colA[:, 44:45], h2x[:, :], AX.X, ALU.max,
                                apply_absolute_value=True),
                e.drain(),
                e.tensor_max(mh_col[:, :], mh_col[:, :], colA[:, 44:45]))[-1],
                after=[th2])
            mx_toks.append(tmx2)

            def st_h2(eng, sem, ic=ic, t=t, h2x=h2x):
                eng.dma_start(out=AP(h2_s, ic * 128 * TC + t * TILE, [[TC, 128], [1, TILE]]),
                              in_=h2x[:, :]).then_inc(sem, 16)
            DMA_("sync", st_h2, n=1, after=[th2])
            tst = bld.fence("sync")
            stq[("h2s", h2slot)] = [tst, tmx2]
            if "h2_d" in dbg:
                DMA_("sync", lambda eng, sem, ic=ic, t=t, h2x=h2x: eng.dma_start(
                    out=AP(dbg["h2_d"], ic * 128 * TC + t * TILE, [[TC, 128], [1, TILE]]),
                    in_=h2x[:, :]).then_inc(sem, 16), n=1, after=[th2])
            st_toks.append(tst)
        return [st_toks[-1], st_toks[-2], mx_toks[-1], mx_toks[-2], h2_toks[-1]]

    gate4 = [tok_sv]
    for t in range(NT):
        gate4 = p4_tile(t, gate4)
    row4 = SB("row4", [1, 192], F32, es4)
    tm2T = T_(lambda e: e.transpose(psC[0][0:1, 0:128], mh_col[:, :], id_f[:, :]), after=gate4)
    tm2E = V_(lambda e: (e.tensor_copy(row4[0:1, 0:128], psC[0][0:1, 0:128]),
                         e.memset(rowA[0:1, 29:44], 0.0),
                         e.drain(),
                         e.tensor_reduce(rowA[0:1, 28:29], row4[0:1, 0:128], AX.X, ALU.max))[-1],
              after=[tm2T] + gate4)

    def st_ag2(eng, sem):
        eng.dma_start(out=AP(ag2_in, 0, [[1, 16], [1, 1]]), in_=rowA[0:1, 28:44]).then_inc(sem, 16)
    tok_ag2 = DMA_("gpsimd", st_ag2, n=1, after=[tm2E])
    tok_cc2 = bld.g_async(lambda e: e.collective_compute(
        "AllGather", ALU.bypass, replica_groups=[list(range(8))],
        ins=[ag2_in.ap().opt()], outs=[ag2_out.ap().opt()]), after=[tok_ag2])
    q_p4 = bld.quiesce("sync")
    bld.flush("p4")
    if PHASE_LIMIT == "p4":
        return nc
    es4.close()

    es3 = ExitStack()
    wf1 = SB("wf1", [128, NDC, FFN], BF16, es3)
    wf2 = SB("wf2", [128, NFC, D], BF16, es3)
    f1bC = SB("f1bC", [128, NFC], F32, es3)
    x1L = SB("x1L", [128, NDC, TILE], F32, es3)
    x2b = SB("x2b", [128, NDC, TILE], BF16, es3)
    h1b = SB("h1b", [128, NFC, TILE], BF16, es3)
    totc = [SB(f"totc{i}", [128, TILE], F32, es3) for i in range(4)]
    sqs3 = SB("sqs3", [128, TILE], F32, es3)
    row3 = SB("row3", [1, TILE], F32, es3)
    psD = [PS(f"psD{i}", es3) for i in range(8)]

    def p3_w(eng, sem):
        eng.dma_start(out=wf1[:, :, :],
                      in_=AP(win["L0_attn_f1_w"], 0, [[FFN, 128], [128 * FFN, NDC], [1, FFN]])
                      ).then_inc(sem, 16)
        eng.dma_start(out=wf2[:, :, :],
                      in_=AP(win["L0_attn_f2_w"], 0, [[D, 128], [128 * D, NFC], [1, D]])
                      ).then_inc(sem, 16)
    tok_w3 = DMA_("gpsimd", p3_w, n=2)
    fmm = []
    for cp in range(NFC):
        mms = []
        for c in range(NDC):
            mms.append(T_(lambda e, c=c, cp=cp: e.matmul(
                psD[6][:, cp:cp + 1], wf1[:, c, cp * 128:(cp + 1) * 128],
                b2cb[:, c:c + 1], start=(c == 0), stop=(c == NDC - 1)),
                after=([tok_w3] if (cp == 0 and c == 0) else []) + ([mms[-1]] if mms else ([fmm[-1]] if fmm else []))))
        fmm.append(mms[-1])
    f1bc_toks = []
    for rnd in range(4):
        def ldf1b(eng, sem, rnd=rnd):
            eng.dma_start(out=rowsP[0:1, 0:1024],
                          in_=AP(win["L0_attn_f1_b"], rnd * 1024, [[1024, 1], [1, 1024]])
                          ).then_inc(sem, 16)
        tl = DMA_("sync", ldf1b, n=1, after=f1bc_toks[-1:] if f1bc_toks else [fmm[-1]])
        tts = []
        for c in range(8):
            tts.append(T_(lambda e, c=c, rnd=rnd: e.transpose(
                psD[7][:, rnd * 8 + c:rnd * 8 + c + 1],
                rowsP[0:1, c * 128:(c + 1) * 128], id_f[0:1, 0:1]),
                after=[tl, fmm[-1]] + (tts[-1:] if tts else (f1bc_toks[-1:] if f1bc_toks else []))))
        f1bc_toks.append(tts[-1])
    tf1b0 = V_(lambda e: e.tensor_copy(f1bC[:, :], psD[7][:, 0:NFC]),
               after=[f1bc_toks[-1], fmm[-1]])
    tf1b = V_(lambda e: e.tensor_add(f1bC[:, :], f1bC[:, :], psD[6][:, 0:NFC]),
              after=[tf1b0])
    bld.flush("p3head")

    def p3_tile(t, gate):
        def ld_x1(eng, sem):
            eng.dma_start(out=x1L[:, :, :],
                          in_=AP(x1_s, t * TILE, [[TC, 128], [128 * TC, NDC], [1, TILE]])
                          ).then_inc(sem, 16)
        tl = DMA_("sync", ld_x1, n=1, after=gate + [q_p2])
        stat_toks = []
        for c in range(NDC):
            stat_toks.append(T_(lambda e, c=c: e.matmul(
                psD[4][0:1, 0:512], ones_col[:, 0:1], x1L[:, c, :],
                start=(c == 0), stop=(c == NDC - 1)),
                after=([tl] + gate) if c == 0 else []))
        sq_toks = []
        for c in range(NDC):
            tq = S_(lambda e, c=c: e.activation(sqs3[:, :], x1L[:, c, :], AF.Square),
                    after=[tl] + (sq_toks[-1:] if sq_toks else []))
            tqm = T_(lambda e, c=c: e.matmul(
                psD[5][0:1, 0:512], ones_col[:, 0:1], sqs3[:, :],
                start=(c == 0), stop=(c == NDC - 1)),
                after=[tq, stat_toks[-1]] + (sq_toks[-1:] if sq_toks else []))
            sq_toks.append(tqm)
        trow = V_(lambda e: (
            e.tensor_scalar(row3[0:1, 0:512], psD[4][0:1, 0:512], 1.0 / D, None, ALU.mult),
            e.tensor_scalar(rowsP[0:1, 0:512], psD[5][0:1, 0:512], 1.0 / D, None, ALU.mult),
            e.tensor_mul(rowsP[0:1, 512:1024], row3[0:1, 0:512], row3[0:1, 0:512]),
            e.tensor_sub(rowsP[0:1, 0:512], rowsP[0:1, 0:512], rowsP[0:1, 512:1024]))[-1],
            after=[sq_toks[-1], stat_toks[-1]])
        trl = S_(lambda e: (e.activation(rowsP[0:1, 0:512], rowsP[0:1, 0:512], AF.Ln, bias=LN_EPS),
                            e.activation(rowsP[0:1, 0:512], rowsP[0:1, 0:512], AF.Exp, scale=-0.5))[-1],
                 after=[trow])
        trm = V_(lambda e: e.tensor_mul(rowsP[0:1, 512:1024], row3[0:1, 0:512], rowsP[0:1, 0:512]),
                 after=[trl])
        tgq = T_(lambda e: (e.matmul(psD[4][:, :], ones_row[:, :], rowsP[0:1, 0:512],
                                     start=True, stop=True),
                            e.matmul(psD[5][:, :], ones_row[:, :], rowsP[0:1, 512:1024],
                                     start=True, stop=True))[-1], after=[trm])
        x2toks = []
        for c in range(NDC):
            x2toks.append(V_(lambda e, c=c: (
                e.tensor_mul(sqs3[:, :], x1L[:, c, :], psD[4][:, :]),
                e.tensor_sub(x2b[:, c, :], sqs3[:, :], psD[5][:, :]))[-1],
                after=[tgq] + (x2toks[-1:] if x2toks else [])))
        x2done = x2toks[-1]
        h1toks = []
        for fo in range(NFC):
            bank = psD[fo % 2]
            mm = []
            for c in range(NDC):
                mm.append(T_(lambda e, c=c, fo=fo, bank=bank: e.matmul(
                    bank[:, :], wf1[:, c, fo * 128:(fo + 1) * 128], x2b[:, c, :],
                    start=(c == 0), stop=(c == NDC - 1)),
                    after=([x2done] if (fo == 0 and c == 0) else [])
                    + ([mm[-1]] if mm else (h1toks[-2:-1] if len(h1toks) >= 2 else []))))
            te = S_(lambda e, fo=fo, bank=bank: e.activation(
                h1b[:, fo, :], bank[:, :], AF.Gelu, bias=f1bC[:, fo:fo + 1]),
                after=[mm[-1]])
            h1toks.append(te)
        h1done = h1toks[-1]
        f2toks = []
        for co in range(NDC):
            mm = []
            for fc in range(NFC):
                mm.append(T_(lambda e, fc=fc, co=co: e.matmul(
                    psD[co][:, :], wf2[:, fc, co * 128:(co + 1) * 128], h1b[:, fc, :],
                    start=(fc == 0), stop=(fc == NFC - 1)),
                    after=([h1done] if (co == 0 and fc == 0) else []) + ([mm[-1]] if mm else (f2toks[-2:-1] if len(f2toks) >= 2 else []))))
            ev = V_(lambda e, co=co: e.scalar_tensor_tensor(
                totc[co % 4][:, :], psD[co][:, :], f2bC[:, co:co + 1], x1L[:, co, :],
                ALU.add, ALU.add),
                after=[mm[-1]] + stq.get(("tots", co % 4), []))

            def st_tot(eng, sem, co=co, t=t):
                eng.dma_start(out=AP(tot_s, co * 128 * TC + t * TILE, [[TC, 128], [1, TILE]]),
                              in_=totc[co % 4][:, :]).then_inc(sem, 16)
            DMA_("sync", st_tot, n=1, after=[ev])
            td = bld.fence("sync")
            stq[("tots", co % 4)] = [td]
            if "tot_d" in dbg:
                DMA_("sync", lambda eng, sem, co=co, t=t: eng.dma_start(
                    out=AP(dbg["tot_d"], co * 128 * TC + t * TILE, [[TC, 128], [1, TILE]]),
                    in_=totc[co % 4][:, :]).then_inc(sem, 16), n=1, after=[ev])
            f2toks += [ev, td]
        return [f2toks[-1], f2toks[-2], h1done]

    gate3 = [tf1b]
    f1sc2 = []
    for c in range(NDC):
        f1sc2.append(V_(lambda e, c=c: e.tensor_scalar(
            wf1[:, c, :], wf1[:, c, :], g2c[:, c:c + 1], None, ALU.mult),
            after=[tf1b] + (f1sc2[-1:] if f1sc2 else [])))
    gate3 = [f1sc2[-1]]
    for t in range(NT):
        gate3 = p3_tile(t, gate3)
    q_p3 = bld.quiesce("sync")
    bld.flush("p3")
    if PHASE_LIMIT == "p3":
        return nc
    es3.close()

    es5 = ExitStack()
    wdT = SB("wdT", [128, NIC, D], BF16, es5)
    h2L = [SB(f"h2L{i}", [128, TILE], F32, es5) for i in range(4)]
    h2q = [SB(f"h2q{i}", [128, TILE], BF16, es5) for i in range(4)]
    totLs = [SB(f"totL{i}", [128, NDC, TILE], F32, es5) for i in range(2)]
    fintm = SB("fintm", [128, NCHT, D], F32, es5)
    xrl = [SB(f"xrl{i}", [128, NCHT, D], F32, es5) for i in range(2)]
    outb = SB("outb", [128, NCHT, D], F32, es5)
    psE = [PS(f"psE{i}", es5) for i in range(8)]

    def p5_w(eng, sem):
        eng.dma_start(out=wdT[:, :, :],
                      in_=AP(wdT_s, 0, [[D, 128], [128 * D, NIC], [1, D]])).then_inc(sem, 16)
    tok_w5 = DMA_("gpsimd", p5_w, n=1, after=[q_x1])
    tok_cc2w = G_(lambda e: e.nop(), after=[tok_cc2])

    def ld_m2(eng, sem):
        eng.dma_start(out=colA[0:8, 45:46],
                      in_=AP(ag2_out, 0, [[16, 8], [1, 1]]),
                      allow_slow_non_contiguous=True).then_inc(sem, 16)
    tok_m2l = DMA_("sync", ld_m2, n=1, after=[tok_cc2w])
    tm2T2 = T_(lambda e: e.transpose(psE[0][0:1, 0:8], colA[0:8, 45:46], id_f[0:8, 0:8]),
               after=[tok_m2l])
    tm2f = V_(lambda e: (e.tensor_copy(rowA[0:1, 40:48], psE[0][0:1, 0:8]),
                         e.drain(),
                         e.tensor_reduce(rowA[0:1, 29:30], rowA[0:1, 40:48], AX.X, ALU.max),
                         e.drain(),
                         e.tensor_mul(rowA[0:1, 30:31], rowA[0:1, 29:30], clipc[0:1, 1:2]),
                         e.drain(),
                         e.tensor_scalar(rowA[0:1, 31:32], rowA[0:1, 30:31], 1.0 / QMAX, None, ALU.mult),
                         e.drain(),
                         e.reciprocal(rowA[0:1, 33:34], rowA[0:1, 31:32]))[-1],
              after=[tm2T2])
    if "m2_d" in dbg:
        DMA_("sync", lambda eng, sem: eng.dma_start(
            out=dbg["m2_d"].ap(), in_=rowA[0:1, 24:40]).then_inc(sem, 16), n=1, after=[tm2f])
    tb2x = T_(lambda e: (e.matmul(psE[1][:, 0:1], ones_row[:, :], rowA[0:1, 31:32],
                                  start=True, stop=True),
                         e.matmul(psE[1][:, 1:2], ones_row[:, :], rowA[0:1, 33:34],
                                  start=True, stop=True))[-1], after=[tm2f])
    tb2e = V_(lambda e: (e.tensor_copy(sc2_col[:, :], psE[1][:, 0:1]),
                         e.tensor_copy(rs2_col[:, :], psE[1][:, 1:2]))[-1], after=[tb2x])
    tsdv = V_(lambda e: e.tensor_scalar(sd_sc[:, 0:NDC], sd_sc[:, 0:NDC], sc2_col[:, 0:1],
                                        None, ALU.mult), after=[tb2e])

    st5 = {
        "h2slot": [[] for _ in range(4)],
        "totslot": [[] for _ in range(2)],
        "bank": [[] for _ in range(NDC)],
        "fintm": [],
        "outb": [],
        "xr": [[] for _ in range(2)],
    }

    def p5_tile(t, gate):
        tsl = t % 2
        totL = totLs[tsl]

        def ld_tot(eng, sem):
            eng.dma_start(out=totL[:, :, :],
                          in_=AP(tot_s, t * TILE, [[TC, 128], [128 * TC, NDC], [1, TILE]])
                          ).then_inc(sem, 16)
        tlt = DMA_("sync", ld_tot, n=1,
                   after=st5["totslot"][tsl] + (gate if t == 0 else []) + [q_p3])
        xr = xrl[t % 2]

        def ld_x(eng, sem):
            eng.dma_start(out=xr[:, :, :],
                          in_=AP(x_in, t * TILE * D, [[D, 128], [128 * D, NCHT], [1, D]])
                          ).then_inc(sem, 16)
        tlx = DMA_("sync", ld_x, n=1, after=st5["xr"][t % 2] + (gate if t == 0 else []))

        ic_toks = []
        fin = [None] * NDC
        for ic in range(NIC):
            slot = (t * NIC + ic) % 4
            hb = h2L[slot]
            hq = h2q[slot]

            def ld_h2(eng, sem, ic=ic, t=t, hb=hb):
                eng.dma_start(out=hb[:, :],
                              in_=AP(h2_s, ic * 128 * TC + t * TILE, [[TC, 128], [1, TILE]])
                              ).then_inc(sem, 16)
            tlh = DMA_("sync", ld_h2, n=1,
                       after=st5["h2slot"][slot] + [q_p4]
                       + ([tb2e] if t == 0 and ic == 0 else []))
            tq1 = S_(lambda e, hb=hb: e.activation(hb[:, :], hb[:, :], AF.Identity,
                                                   bias=MAGIC, scale=rs2_col[:, 0:1]),
                     after=[tlh, tb2e])
            tq2 = V_(lambda e, hb=hb, hq=hq: (
                e.tensor_scalar(hb[:, :], hb[:, :], MAGIC, QMAX, ALU.subtract, ALU.min),
                e.tensor_scalar(hq[:, :], hb[:, :], -QMAX - 1.0, None, ALU.max))[-1],
                after=[tq1])
            mmd = []
            for co in range(NDC):
                mmd.append(T_(lambda e, co=co, ic=ic, hq=hq: e.matmul(
                    psE[co][:, :], wdT[:, ic, co * 128:(co + 1) * 128], hq[:, :],
                    start=(ic == 0), stop=(ic == NIC - 1)),
                    after=[tq2, tok_w5]
                    + (st5["bank"][co] if ic == 0 else [])
                    + ([mmd[-1]] if mmd else (ic_toks[-1:] if ic_toks else []))))
            st5["h2slot"][slot] = [mmd[-1], tq2]
            ic_toks += [tq2, mmd[-1]]
        for co in range(NDC):
            ev = V_(lambda e, co=co, totL=totL: e.scalar_tensor_tensor(
                totL[:, co, :], psE[co][:, :], sd_sc[:, co:co + 1], totL[:, co, :],
                ALU.mult, ALU.add), after=[ic_toks[-1], tsdv, tlt])
            fin[co] = ev
        ttk = []
        for jj in range(NCHT):
            for co in range(NDC):
                bank = psE[co % 2]
                tt = T_(lambda e, co=co, jj=jj, bank=bank, totL=totL: e.transpose(
                    bank[:, 0:128], totL[:, co, jj * 128:(jj + 1) * 128], id_f[:, :]),
                    after=[fin[co]] + (ttk[-1:] if ttk else [fin[-1]]))
                te = S_(lambda e, co=co, jj=jj, bank=bank: e.copy(
                    fintm[:, jj, co * 128:(co + 1) * 128], bank[:, 0:128]),
                    after=[tt] + st5["fintm"])
                ttk.append(te)
        te_all = ttk
        for co in range(NDC):
            st5["bank"][co] = [ttk[-1]]
        st5["totslot"][tsl] = [ttk[-1]]
        tadd = V_(lambda e: e.tensor_add(outb[:, :, :], fintm[:, :, :], xr[:, :, :]),
                  after=te_all[-1:] + [tlx] + st5["outb"])
        st5["fintm"] = [tadd]
        st5["xr"][t % 2] = [tadd]

        def st_out(eng, sem, t=t):
            eng.dma_start(out=AP(out_t, t * TILE * D, [[D, 128], [128 * D, NCHT], [1, D]]),
                          in_=outb[:, :, :]).then_inc(sem, 16)
        tso = DMA_("sync", st_out, n=1, after=[tadd])
        st5["outb"] = [bld.fence("sync")]
        return [tso, tadd]

    gate5 = [tsdv]
    for t in range(NT):
        gate5 = p5_tile(t, gate5)
    V_(lambda e: e.nop(), after=gate5)
    bld.flush("p5")
    es5.close()
    es_persist.close()
    return nc


def kernel(**inputs):
    x_full = np.ascontiguousarray(inputs["L0_attn_x"])
    nc = build_nc()
    in_maps = []
    for cid in range(NCORE):
        b, t0 = cid // 4, (cid % 4) * TC
        m = {"x_in": np.ascontiguousarray(x_full[b, t0:t0 + TC])}
        for k, v in inputs.items():
            if k == "L0_attn_x":
                continue
            m[k] = np.ascontiguousarray(np.asarray(v))
        my = cid % 4
        pf = np.zeros((8, 8), np.float32)
        pb = np.zeros((8, 8), np.float32)
        mf = np.zeros((8, 1), np.float32)
        mb = np.zeros((8, 1), np.float32)
        grp = [4 * (cid // 4) + i for i in range(4)]
        for i, c2 in enumerate(grp):
            if i < my:
                mf[c2, 0] = 1.0
                for j in range(i + 1, my):
                    pf[grp[j], c2] = 1.0
            if i > my:
                mb[c2, 0] = 1.0
                for j in range(my + 1, i):
                    pb[grp[j], c2] = 1.0
        m["cc_pf"], m["cc_pb"], m["cc_mf"], m["cc_mb"] = pf, pb, mf, mb
        in_maps.append(m)
    res = run_bass_kernel_spmd(nc, in_maps, core_ids=list(range(NCORE)))
    out = np.empty((B, T, D), np.float32)
    for cid in range(NCORE):
        b, t0 = cid // 4, (cid % 4) * TC
        out[b, t0:t0 + TC] = res.results[cid]["out"]
    return out


# version 17
# speedup vs baseline: 2.1941x; 1.0177x over previous
import numpy as np
import ml_dtypes
from contextlib import ExitStack
import concourse.bass as bass
import concourse.mybir as mybir
from concourse.bass_utils import run_bass_kernel_spmd

F32 = mybir.dt.float32
BF16 = mybir.dt.bfloat16
AF = mybir.ActivationFunctionType
ALU = mybir.AluOpType
AX = mybir.AxisListType
AP = bass.AP

B, T, D, H, NK, KVE = 2, 16384, 1024, 16, 64, 65
TC, TILE, CH = 4096, 512, 128
NT = TC // TILE
NCHT = TILE // CH
NCH = TC // CH
NDC = D // 128
NPAIR = H // 2
FFN, IFF = 4096, 2816
NFC = FFN // 128
NIC = IFF // 128
LN_EPS = 1e-5
EPS_DEN = 1e-6
LAM_MIN = 1e-8
QMAX = 127.0
MAGIC = 12582912.0
PAY_SEC = H * NK * KVE
PAY = 2 * PAY_SEC + 32
NCORE = 8
PHASE_LIMIT = None
POS = [(hh % 2) * 8 + hh // 2 for hh in range(16)]
HORDER = [0] * 16
for _hh in range(16):
    HORDER[POS[_hh]] = _hh
P1_CUT = None
LNJS = None
ENGS = ("tensor", "vector", "scalar", "gpsimd", "sync")


class Bld:

    SEMNAMES = ("tensor", "vector", "scalar", "gpsimd", "sync", "gdma", "cc",
                "syncb")
    OWNER = {"tensor": "tensor", "vector": "vector", "scalar": "scalar",
             "gpsimd": "gpsimd", "sync": "sync", "gdma": "gpsimd", "cc": "gpsimd",
             "syncb": "sync"}

    NPOOL = {"sync": 48, "gpsimd": 16}

    def __init__(self, nc):
        self.nc = nc
        self.sems = {s: nc.alloc_semaphore(name=f"sem_{s}") for s in self.SEMNAMES}
        self.count = {s: 0 for s in self.SEMNAMES}
        self.groups = {e: [] for e in ENGS}
        self.pool = {}
        self.pool_count = {}
        for eng, npool in self.NPOOL.items():
            for k in range(npool):
                s = f"dp_{eng}_{k}"
                self.pool[s] = nc.alloc_semaphore(name=s)
                self.pool_count[s] = 0
                self.count[s] = 0
        self.sems.update(self.pool)
        self.pool_idx = {e: 0 for e in self.NPOOL}
        self.prev_flush_counts = {}
        self.last_dma = None

    def _add(self, eng, semname, fn, after, inc, is_async=False):
        self.count[semname] += inc
        self.groups[eng].append((fn, tuple(after), inc, semname))
        return (semname, self.count[semname], is_async)

    def t(self, fn, after=()):
        return self._add("tensor", "tensor", fn, after, 1)

    def v(self, fn, after=()):
        return self._add("vector", "vector", fn, after, 1)

    def s(self, fn, after=()):
        return self._add("scalar", "scalar", fn, after, 1)

    def g(self, fn, after=()):
        return self._add("gpsimd", "gpsimd", fn, after, 1)

    def dma(self, eng, fn, n=1, after=()):
        k = self.pool_idx[eng]
        self.pool_idx[eng] = (k + 1) % self.NPOOL[eng]
        semname = f"dp_{eng}_{k}"
        drain = self.count[semname]

        def wrap(e, sem, fn=fn, drain=drain):
            if drain:
                e.wait_ge(sem, drain)
            fn(e, sem)
        tok = self._add(eng, semname, wrap, after, 16 * n, is_async=True)
        self.last_dma = tok
        return tok

    def g_async(self, fn, after=()):
        return self._add("gpsimd", "cc", fn, after, 1, is_async=True)

    def fence(self, queue="sync"):
        return self.last_dma

    def quiesce(self, semname="sync"):
        return ("syncb", 0, True)

    def flush(self, name=None):
        nc = self.nc
        groups = {e: list(self.groups[e]) for e in ENGS}
        self.groups = {e: [] for e in ENGS}
        if not any(groups.values()):
            return
        qprev = {s: v for s, v in self.prev_flush_counts.items() if v > 0}
        self.prev_flush_counts = {s: self.count[s] for s in self.pool}
        with nc.Block(name) as blk:
            def mk(ename, glist):
                def body(eng):
                    lastw = {}
                    for qs, qv in qprev.items():
                        eng.wait_ge(self.sems[qs], qv)
                        lastw[qs] = qv
                    for fn, after, inc, semname in glist:
                        for tok in after:
                            ps, val, is_async = tok
                            if ename == "tensor" and self.OWNER.get(ps) == "tensor" \
                                    and not is_async:
                                continue
                            if lastw.get(ps, -1) >= val:
                                continue
                            eng.wait_ge(self.sems[ps], val)
                            lastw[ps] = val
                        mysem = self.sems[semname]
                        if inc >= 16:
                            fn(eng, mysem)
                        else:
                            last = fn(eng)
                            (last if last is not None else eng.nop()).then_inc(mysem, 1)
                return body
            for ename in ENGS:
                if groups[ename]:
                    getattr(blk, ename)(mk(ename, groups[ename]))


def _consts():
    c = {}
    c["id_f"] = np.eye(128, dtype=np.float32)
    c["id_b"] = np.eye(128, dtype=ml_dtypes.bfloat16)
    c["ones_row_f"] = np.ones((1, 128), np.float32)
    c["ones_col_f"] = np.ones((128, 1), np.float32)
    c["ones16_b"] = np.ones((16, 128), ml_dtypes.bfloat16)
    c["ones16_f"] = np.ones((16, 128), np.float32)
    sp = np.zeros((16, NPAIR * 128), np.float32)
    for g in range(NPAIR):
        sp[2 * g, g * 128: g * 128 + 64] = 1.0
        sp[2 * g + 1, g * 128 + 64: (g + 1) * 128] = 1.0
    c["sel_pair_b"] = sp.astype(ml_dtypes.bfloat16)
    c["sel_pair_f"] = sp
    sh = np.zeros((16, 128), np.float32)
    for h in range(16):
        sh[h, 0:64] = 1.0 if h % 2 == 0 else 0.0
        sh[h, 64:128] = 1.0 if h % 2 == 1 else 0.0
    c["sel_half_f"] = sh
    pm = np.zeros((16, NPAIR), np.float32)
    for h in range(16):
        pm[h, h // 2] = 1.0
    c["pair_mask_f"] = pm
    hm = np.zeros((16, 16 * 128), np.float32)
    for p_ in range(16):
        hm[HORDER[p_], p_ * 128:(p_ + 1) * 128] = 1.0
    c["head_mask_f"] = hm.astype(ml_dtypes.bfloat16)
    return c


def bcv(ap2d, reps):
    base = ap2d
    a = list(base.ap)
    new = [list(a[0]), [0, reps]] + [list(x) for x in a[1:]]
    return AP(base.tensor, base.offset, new)


def bccol(ap_col, reps):
    a = list(ap_col.ap)
    return AP(ap_col.tensor, ap_col.offset, [list(a[0]), [0, reps]])


def build_nc(dbg_names=()):
    nc = bass.Bass("TRN2", target_bir_lowering=False, detect_race_conditions=False)
    bld = Bld(nc)
    Cst = _consts()
    T_, V_, S_, G_, DMA_ = bld.t, bld.v, bld.s, bld.g, bld.dma

    x_in = nc.dram_tensor("x_in", [TC, D], F32, kind="ExternalInput")
    win = {}
    for nm, shp in [
        ("L0_attn_q_w", [D, D]), ("L0_attn_k_w", [D, D]), ("L0_attn_v_w", [D, D]),
        ("L0_attn_o_w", [D, D]), ("L0_attn_dec_w", [D, H]),
        ("L0_attn_f1_w", [D, FFN]), ("L0_attn_f2_w", [FFN, D]),
        ("L0_ffn_w_gate", [IFF, D]), ("L0_ffn_w_up", [IFF, D]), ("L0_ffn_w_down", [D, IFF]),
        ("L0_attn_q_b", [D]), ("L0_attn_k_b", [D]), ("L0_attn_v_b", [D]),
        ("L0_attn_o_b", [D]), ("L0_attn_dec_b", [H]), ("L0_attn_f1_b", [FFN]),
        ("L0_attn_f2_b", [D]),
        ("L0_attn_ln1_w", [D]), ("L0_attn_ln1_b", [D]),
        ("L0_attn_ln2_w", [D]), ("L0_attn_ln2_b", [D]),
        ("L0_ffn_clip_x", [1]), ("L0_ffn_clip_down", [1]),
        ("cc_pf", [8, 8]), ("cc_pb", [8, 8]), ("cc_mf", [8, 1]), ("cc_mb", [8, 1]),
    ]:
        win[nm] = nc.dram_tensor(nm, shp, F32, kind="ExternalInput")
    out_t = nc.dram_tensor("out", [TC, D], F32, kind="ExternalOutput")

    dbg = {}
    _dbg_shapes = {
        "xn_fm": ([D, TC], F32), "x2e_fm": ([D, TC], BF16), "phiq": ([D, TC], BF16),
        "phik": ([D, TC], BF16),
        "lam_d": ([H, TC], F32), "llam_d": ([H, TC], F32), "vtm_d": ([TC, H * KVE], BF16),
        "sf_d": ([128, NPAIR * KVE], F32), "sb_d": ([128, NPAIR * KVE], F32),
        "sin_d": ([128, NPAIR * KVE], F32), "rin_d": ([128, NPAIR * KVE], F32),
        "m1_d": ([1, 16], F32), "attn_d": ([D, TC], BF16), "x1_d": ([D, TC], F32),
        "tot_d": ([D, TC], F32), "xq_d": ([D, TC], BF16), "h2_d": ([IFF, TC], F32),
        "m2_d": ([1, 16], F32),
    }
    for nmm in dbg_names:
        shp, dt_ = _dbg_shapes[nmm]
        dbg[nmm] = nc.dram_tensor("dbg_" + nmm, shp, dt_, kind="ExternalOutput")

    xn_s = nc.dram_tensor("xn_s", [D, TC], F32, kind="Internal")
    qc_s = nc.dram_tensor("qc_s", [NCH, 128, NPAIR * CH], BF16, kind="Internal")
    qe_s = nc.dram_tensor("qe_s", [NCH, 128, NPAIR * CH], BF16, kind="Internal")
    part_s = nc.dram_tensor("part_s", [NCH, CH, H * KVE], BF16, kind="Internal")
    x1_s = nc.dram_tensor("x1_s", [D, TC], F32, kind="Internal")
    xq_s = nc.dram_tensor("xq_s", [D, TC], BF16, kind="Internal")
    tot_s = nc.dram_tensor("tot_s", [D, TC], F32, kind="Internal")
    h2_s = nc.dram_tensor("h2_s", [IFF, TC], F32, kind="Internal")
    wgT_s = nc.dram_tensor("wgT_s", [D, IFF], BF16, kind="Internal")
    wuT_s = nc.dram_tensor("wuT_s", [D, IFF], BF16, kind="Internal")
    wdT_s = nc.dram_tensor("wdT_s", [IFF, D], BF16, kind="Internal")
    kcv_s = nc.dram_tensor("kcv_s", [NCH, 128, NPAIR * KVE], BF16, kind="Internal")
    ag_in = nc.dram_tensor("ag_in", [1, PAY], F32, kind="Internal")
    ag_out = nc.dram_tensor("ag_out", [8, PAY], F32, kind="Internal")
    ag2_in = nc.dram_tensor("ag2_in", [1, 16], F32, kind="Internal")
    ag2_out = nc.dram_tensor("ag2_out", [8, 16], F32, kind="Internal")

    const_t = {k: nc.inline_tensor(np.ascontiguousarray(v), "c_" + k) for k, v in Cst.items()}

    es_persist = ExitStack()
    es_kcv = ExitStack()
    es_ph = ExitStack()

    def SB(name, shape, dtype, es):
        return es.enter_context(nc.sbuf_tensor(name, shape, dtype))

    def PS(name, es, shape=(128, 512)):
        return es.enter_context(nc.psum_tensor(name, list(shape), F32))

    P = es_persist
    id_f = SB("id_f", [128, 128], F32, P)
    id_b = SB("id_b", [128, 128], BF16, P)
    ones_row = SB("ones_row", [1, 128], F32, P)
    ones_col = SB("ones_col", [128, 1], F32, P)
    ones16_b = SB("ones16_b", [16, 128], BF16, P)
    ones16_f = SB("ones16_f", [16, 128], F32, P)
    sel_pair_b = SB("sel_pair_b", [16, NPAIR * 128], BF16, P)
    sel_half = SB("sel_half", [16, 128], F32, P)
    pair_mask = SB("pair_mask", [16, NPAIR], F32, P)
    ccpf = SB("ccpf", [8, 8], F32, P)
    ccpb = SB("ccpb", [8, 8], F32, P)
    ccmf = SB("ccmf", [8, 1], F32, P)
    ccmb = SB("ccmb", [8, 1], F32, P)

    lcend_st = SB("lcend_st", [16, NCH], F32, P)
    lpre_st = SB("lpre_st", [16, NCH], F32, P)
    ltot_c = SB("ltot_c", [16, 1], F32, P)
    mx_col = SB("mx_col", [128, 1], F32, P)
    mh_col = SB("mh_col", [128, 1], F32, P)
    colA = SB("colA", [128, 64], F32, P)
    rowA = SB("rowA", [1, 64], F32, P)
    s16A = SB("s16A", [16, 64], F32, P)
    rs1_col = SB("rs1_col", [128, 1], F32, P)
    sc1_col = SB("sc1_col", [128, 1], F32, P)
    rs2_col = SB("rs2_col", [128, 1], F32, P)
    sc2_col = SB("sc2_col", [128, 1], F32, P)
    sg_sc = SB("sg_sc", [128, NIC], F32, P)
    su_sc = SB("su_sc", [128, NIC], F32, P)
    sd_sc = SB("sd_sc", [128, NDC], F32, P)
    g2c = SB("g2c", [128, NDC], F32, P)
    b2cb = SB("b2cb", [128, NDC], BF16, P)
    obC = SB("obC", [128, NDC], F32, P)
    f2bC = SB("f2bC", [128, NDC], F32, P)
    clipc = SB("clipc", [1, 2], F32, P)
    rowsP = SB("rowsP", [1, D], F32, P)
    cstT = SB("cstT", [128, 8], F32, P)
    _cvals = [LN_EPS, MAGIC, -1.0, -0.5, 0.0, 1.0]
    for _i, _v in enumerate(_cvals):
        nc.const_aps.register(F32, _v, cstT[:, _i:_i + 1])
    declc = SB("declc", [128, NPAIR], F32, P)
    declp = SB("declp", [128, NPAIR], F32, P)
    s16B = SB("s16B", [16, 16], F32, P)

    kcvs = SB("kcvs", [128, 2, NPAIR * KVE], BF16, es_kcv)
    S_st = SB("S_st", [128, NPAIR * KVE], F32, es_kcv)
    Sb_st = SB("Sb_st", [128, NPAIR * KVE], F32, es_kcv)
    Sin_st = SB("Sin_st", [128, NPAIR * KVE], F32, es_kcv)
    Rin_st = SB("Rin_st", [128, NPAIR * KVE], F32, es_kcv)
    R_st = SB("R_st", [128, NPAIR * KVE], F32, es_kcv)
    Tmp_st = SB("Tmp_st", [128, NPAIR * KVE], F32, es_kcv)

    es1 = ExitStack()
    wq = SB("wq", [128, NDC, D], BF16, es1)
    wk = SB("wk", [128, NDC, D], BF16, es1)
    wv = SB("wv", [128, NDC, D], BF16, es1)
    wdec = SB("wdec", [128, NDC, H], BF16, es1)
    qbC = SB("qbC", [128, NDC], F32, es1)
    kbC = SB("kbC", [128, NDC], F32, es1)
    qb1C = SB("qb1C", [128, NDC], F32, es1)
    kb1C = SB("kb1C", [128, NDC], F32, es1)
    vbtm = SB("vbtm", [128, D], BF16, es1)
    decbc = SB("decbc", [16, 1], F32, es1)
    g1c = SB("g1c", [128, NDC], F32, es1)
    b1cb = SB("b1cb", [128, NDC], BF16, es1)
    vbeC = SB("vbeC", [128, NDC], F32, es1)
    xtm = [SB("xtm0", [128, NCHT, D], F32, es1)]
    xn_tm = SB("xn_tm", [128, D], F32, es1)
    x2e_tm = SB("x2e_tm", [128, D], F32, es1)
    xnfm = SB("xnfm", [128, NDC, CH], F32, es1)
    x2e = SB("x2e", [128, NDC, 2, TILE], BF16, es1)
    phiq = SB("phiq", [128, NDC, 2, TILE], BF16, es1)
    phik = SB("phik", [128, NDC, 2, TILE], BF16, es1)
    tbuf1 = SB("tbuf1", [128, TILE], F32, es1)
    lam_t = SB("lam_t", [16, 2, TILE], F32, es1)
    llam_t = lam_t
    vtm = SB("vtm", [128, 2, H * KVE], BF16, es1)
    lc_f = SB("lc_f", [16, CH], F32, es1)
    ctil_f = SB("ctil_f", [16, CH], F32, es1)
    e_f = SB("e_f", [16, CH], F32, es1)
    lctm = SB("lctm", [128, 16], F32, es1)
    ctil_b = SB("ctil_b", [16, CH], BF16, es1)
    e_b = SB("e_b", [16, CH], BF16, es1)
    ctm = SB("ctm", [128, 16], F32, es1)
    etm = SB("etm", [128, 16], F32, es1)
    drhs = SB("drhs", [16, 16 * CH], BF16, es1)
    D_sb = SB("D_sb", [128, H * CH], BF16, es1)
    scD = SB("scD", [128, H * CH], BF16, es1)
    keB = SB("keB", [128, H * NK], BF16, es1)
    kcB = SB("kcB", [128, H * NK], BF16, es1)
    phikf = SB("phikf", [128, NPAIR * CH], F32, es1)
    head_mask = SB("head_mask", [16, 16 * 128], BF16, es1)
    Sb16 = SB("Sb16", [128, NPAIR * KVE], BF16, es1)
    qcB = SB("qcB", [128, 2, NPAIR * CH], BF16, es1)
    qeB = SB("qeB", [128, 2, NPAIR * CH], BF16, es1)
    partB = SB("partB", [128, 2, H * KVE], BF16, es1)
    psA = [PS(f"psA{i}", es1) for i in range(8)]

    def p0_consts(eng, sem):
        for dst, src in [(id_f, "id_f"), (id_b, "id_b"), (ones_row, "ones_row_f"),
                         (ones_col, "ones_col_f"),
                         (ones16_b, "ones16_b"), (ones16_f, "ones16_f"),
                         (sel_pair_b, "sel_pair_b"),
                         (sel_half, "sel_half_f"),
                         (pair_mask, "pair_mask_f"), (head_mask, "head_mask_f")]:
            eng.dma_start(out=dst[:, :], in_=const_t[src].ap()).then_inc(sem, 16)
        for dst, src in [(ccpf, "cc_pf"), (ccpb, "cc_pb"), (ccmf, "cc_mf"), (ccmb, "cc_mb")]:
            eng.dma_start(out=dst[:, :], in_=win[src].ap()).then_inc(sem, 16)
    tok_c0 = DMA_("sync", p0_consts, n=14)

    def p0_wload(eng, sem):
        for dst, src in [(wq, "L0_attn_q_w"), (wk, "L0_attn_k_w"), (wv, "L0_attn_v_w")]:
            eng.dma_start(out=dst[:, :, :],
                          in_=AP(win[src], 0, [[D, 128], [128 * D, NDC], [1, D]])
                          ).then_inc(sem, 16)
        eng.dma_start(out=wdec[:, :, :],
                      in_=AP(win["L0_attn_dec_w"], 0, [[H, 128], [128 * H, NDC], [1, H]])
                      ).then_inc(sem, 16)
    tok_w0 = DMA_("gpsimd", p0_wload, n=4)

    def _setconsts(e):
        last = None
        for _i, _v in enumerate(_cvals):
            last = e.memset(cstT[:, _i:_i + 1], _v)
        return last
    tok_cst = G_(_setconsts)
    tok_z = G_(lambda e: (e.memset(S_st[:, :], 0.0), e.memset(Sb_st[:, :], 0.0),
                          e.memset(ltot_c[:, :], 0.0), e.memset(mx_col[:, :], 0.0),
                          e.memset(mh_col[:, :], 0.0), e.memset(Sin_st[:, :], 0.0),
                          e.memset(Rin_st[:, :], 0.0), e.memset(Sb16[:, :], 0.0))[-1])

    chain = {"rc": tok_c0}

    def row_to_cols(name, dstC, nchunks, col0=0):
        prev = chain["rc"]

        def ld(eng, sem):
            n = nchunks * 128
            eng.dma_start(out=rowsP[0:1, 0:n],
                          in_=AP(win[name], 0, [[n, 1], [1, n]])).then_inc(sem, 16)
        tl = DMA_("sync", ld, n=1, after=[prev])
        tts = []
        for c in range(nchunks):
            tts.append(T_(lambda e, c=c: e.transpose(psA[7][:, c:c + 1],
                                                     rowsP[0:1, c * 128:(c + 1) * 128],
                                                     id_f[0:1, 0:1]),
                          after=[tl, prev] + ([tts[-1]] if tts else [])))
        tv = V_(lambda e: e.tensor_copy(dstC[:, col0:col0 + nchunks], psA[7][:, 0:nchunks]),
                after=[tts[-1]])
        chain["rc"] = tv
        return tv

    tok_g1 = row_to_cols("L0_attn_ln1_w", g1c, NDC)
    tok_b1 = row_to_cols("L0_attn_ln1_b", colA, NDC)
    tok_b1b = V_(lambda e: e.tensor_copy(b1cb[:, :], colA[:, 0:NDC]), after=[tok_b1])
    tok_g2 = row_to_cols("L0_attn_ln2_w", g2c, NDC)
    tok_b2 = row_to_cols("L0_attn_ln2_b", colA, NDC)
    tok_b2b = V_(lambda e: e.tensor_copy(b2cb[:, :], colA[:, 0:NDC]), after=[tok_b2])
    tok_ob = row_to_cols("L0_attn_o_b", obC, NDC)
    tok_f2bc = row_to_cols("L0_attn_f2_b", f2bC, NDC)

    def fold_bias(wsb, bname, dstC, deps):
        prev = deps
        toks = []
        for cp in range(NDC):
            mms = []
            for c in range(NDC):
                mms.append(T_(lambda e, c=c, cp=cp, wsb=wsb: e.matmul(
                    psA[6][:, cp:cp + 1], wsb[:, c, cp * 128:(cp + 1) * 128],
                    b1cb[:, c:c + 1], start=(c == 0), stop=(c == NDC - 1)),
                    after=(prev if cp == 0 and c == 0 else []) + ([mms[-1]] if mms else ([toks[-1]] if toks else []))))
            toks.append(mms[-1])
        tb = row_to_cols(bname, dstC, NDC)
        return V_(lambda e, dstC=dstC: e.tensor_add(dstC[:, 0:NDC], dstC[:, 0:NDC],
                                                    psA[6][:, 0:NDC]),
                  after=[toks[-1], tb])

    tok_qb = fold_bias(wq, "L0_attn_q_b", qbC, [tok_w0, tok_b1b])
    tok_kb = fold_bias(wk, "L0_attn_k_b", kbC, [tok_qb])
    tok_vb = fold_bias(wv, "L0_attn_v_b", vbeC, [tok_kb])
    tok_qb1 = V_(lambda e: e.tensor_scalar(qb1C[:, :], qbC[:, 0:NDC], 1.0, None, ALU.add),
                 after=[tok_qb])
    tok_kb1 = V_(lambda e: e.tensor_scalar(kb1C[:, :], kbC[:, 0:NDC], 1.0, None, ALU.add),
                 after=[tok_kb])

    decmms = []
    for c in range(NDC):
        decmms.append(T_(lambda e, c=c: e.matmul(
            psA[5][0:16, 0:1], wdec[:, c, :], b1cb[:, c:c + 1],
            start=(c == 0), stop=(c == NDC - 1)),
            after=([tok_w0, tok_b1b, tok_vb] if c == 0 else []) + ([decmms[-1]] if decmms else [])))

    def ld_decb(eng, sem):
        eng.dma_start(out=rowA[0:1, 0:16],
                      in_=AP(win["L0_attn_dec_b"], 0, [[16, 1], [1, 16]])).then_inc(sem, 16)
    tok_decb = DMA_("sync", ld_decb, n=1, after=[chain["rc"]])
    tok_decT = T_(lambda e: e.transpose(psA[5][0:16, 1:2], rowA[0:1, 0:16], id_f[0:1, 0:1]),
                  after=[tok_decb, decmms[-1]])
    tok_decc0 = V_(lambda e: e.tensor_copy(s16B[:, 3:4], psA[5][0:16, 0:1]), after=[tok_decT])
    tok_decc = V_(lambda e: e.tensor_add(decbc[:, :], s16B[:, 3:4], psA[5][0:16, 1:2]),
                  after=[tok_decc0])

    tok_vbrow = None
    for rnd in range(2):
        vbT = []
        for c4 in range(4):
            c = rnd * 4 + c4
            vbT.append(T_(lambda e, c=c, c4=c4: e.transpose(
                psA[4][0:1, c4 * 128:(c4 + 1) * 128], vbeC[:, c:c + 1], id_f[:, :]),
                after=[tok_vb, tok_decc] + ([vbT[-1]] if vbT else ([tok_vbrow] if tok_vbrow else []))))
        tok_vbrow = V_(lambda e, rnd=rnd: e.tensor_copy(
            rowsP[0:1, rnd * 512:(rnd + 1) * 512], psA[4][0:1, 0:512]),
            after=[vbT[-1], chain["rc"]])
    chain["rc"] = tok_vbrow
    vb2 = []
    for c4 in range(2):
        mm = []
        for c in range(4):
            mm.append(T_(lambda e, c4=c4, c=c: e.matmul(
                psA[3][:, c * 128:(c + 1) * 128], ones_row[:, :],
                rowsP[0:1, (c4 * 4 + c) * 128:(c4 * 4 + c + 1) * 128], start=True, stop=True),
                after=[tok_vbrow] + ([vb2[-1]] if vb2 else []) + ([mm[-1]] if mm else [])))
        vb2.append(V_(lambda e, c4=c4: e.tensor_copy(
            vbtm[:, c4 * 512:(c4 + 1) * 512], psA[3][:, :]), after=[mm[-1]]))
    tok_vbtm = vb2[-1]

    sc_toks = []
    for wsb, dep in [(wq, tok_qb1), (wk, tok_kb1), (wv, tok_vbtm), (wdec, tok_decc)]:
        ncols = wsb.shape[2]
        for c in range(NDC):
            sc_toks.append(V_(lambda e, wsb=wsb, c=c, ncols=ncols: e.tensor_scalar(
                wsb[:, c, 0:ncols], wsb[:, c, 0:ncols], g1c[:, c:c + 1], None, ALU.mult),
                after=[dep, tok_g1]))
    tok_wsc = sc_toks[-1]

    def ld_clip(eng, sem):
        eng.dma_start(out=rowA[0:1, 20:21],
                      in_=AP(win["L0_ffn_clip_x"], 0, [[1, 1], [1, 1]])).then_inc(sem, 16)
        eng.dma_start(out=rowA[0:1, 21:22],
                      in_=AP(win["L0_ffn_clip_down"], 0, [[1, 1], [1, 1]])).then_inc(sem, 16)
    tok_clipl = DMA_("sync", ld_clip, n=2, after=[chain["rc"]])
    tok_clip = S_(lambda e: e.activation(clipc[0:1, 0:2], rowA[0:1, 20:22], AF.Sigmoid),
                  after=[tok_clipl])
    tok_vone = G_(lambda e: e.memset(
        AP(vtm, 64, [[2 * H * KVE, 128], [KVE, 2 * H], [1, 1]]), 1.0), after=[tok_z])

    bld.flush("p0")
    if PHASE_LIMIT == "p0":
        return nc

    st1 = {"xtm": [], "chunk_tail": [], "b7": [], "xnst": [], "ln": {},
           "tll": {}, "vg": {}, "ln_last": []}
    stq = {}
    hbm_st = {}

    def emit_ldx(t):
        def ld_x(eng, sem, t=t):
            eng.dma_start(out=xtm[0][:, :, :],
                          in_=AP(x_in, t * TILE * D, [[D, 128], [128 * D, NCHT], [1, D]])
                          ).then_inc(sem, 16)
        tok = DMA_("sync", ld_x, n=1, after=st1["xtm"])
        st1["xtm"] = []
        st1["ln"][t] = {"x": tok, "xn_j": [], "x2e_j": []}

    def prep_ln_part(t, j):
        st = st1["ln"][t]
        tok_x = st["x"]
        tok_xn_j = st["xn_j"]
        tok_x2e_j = st["x2e_j"]
        sl = t % 2
        xj = xtm[0][:, j, :]
        cb = (j % 2) * 24
        tm = V_(lambda e, xj=xj, cb=cb: e.tensor_reduce(colA[:, cb:cb + 1], xj, AX.X, ALU.add),
                after=[tok_x] + (st1["ln_last"] if j == 0 else []))
        ts1 = S_(lambda e, xj=xj, cb=cb: e.activation(x2e_tm[:, :], xj, AF.Square,
                                                      accum_out=colA[:, cb + 1:cb + 2]),
                 after=[tok_x] + (st1["ln_last"] if j == 0 else [])
                 + (tok_x2e_j[-1:] if tok_x2e_j else []))

        def colops(e, cb=cb):
            e.tensor_scalar(colA[:, cb + 2:cb + 3], colA[:, cb:cb + 1], 1.0 / D, None, ALU.mult)
            e.tensor_scalar(colA[:, cb + 3:cb + 4], colA[:, cb + 1:cb + 2], 1.0 / D, None, ALU.mult)
            e.drain()
            e.tensor_mul(colA[:, cb + 4:cb + 5], colA[:, cb + 2:cb + 3], colA[:, cb + 2:cb + 3])
            e.drain()
            return e.tensor_sub(colA[:, cb + 5:cb + 6], colA[:, cb + 3:cb + 4], colA[:, cb + 4:cb + 5])
        tv1 = V_(colops, after=[tm, ts1])
        tsq = S_(lambda e, cb=cb: e.activation(colA[:, cb + 6:cb + 7], colA[:, cb + 5:cb + 6],
                                               AF.Sqrt, bias=LN_EPS), after=[tv1])

        def colops2(e, cb=cb):
            e.reciprocal(colA[:, cb + 7:cb + 8], colA[:, cb + 6:cb + 7])
            e.drain()
            e.tensor_mul(colA[:, cb + 8:cb + 9], colA[:, cb + 2:cb + 3], colA[:, cb + 7:cb + 8])
            e.drain()
            return e.tensor_scalar(colA[:, cb + 9:cb + 10], colA[:, cb + 8:cb + 9],
                                   -1.0, None, ALU.mult)
        tv2 = V_(colops2, after=[tsq])
        ta = S_(lambda e, xj=xj, cb=cb: e.activation(xn_tm[:, :], xj, AF.Identity,
                                                     bias=colA[:, cb + 9:cb + 10],
                                                     scale=colA[:, cb + 7:cb + 8]),
                after=[tv2])
        st1["xtm"].append(ta)
        tmx = V_(lambda e, cb=cb: (
            e.tensor_reduce(colA[:, cb + 10:cb + 11], xn_tm[:, :], AX.X, ALU.max,
                            apply_absolute_value=True),
            e.drain(),
            e.tensor_max(mx_col[:, :], mx_col[:, :], colA[:, cb + 10:cb + 11]))[-1],
            after=[ta])
        tm1 = V_(lambda e, cb=cb: e.tensor_reduce(colA[:, cb + 11:cb + 12], xn_tm[:, :],
                                                  AX.X, ALU.add), after=[ta])
        ts2 = S_(lambda e, cb=cb: e.activation(x2e_tm[:, :], xn_tm[:, :], AF.Square,
                                               accum_out=colA[:, cb + 12:cb + 13]),
                 after=[ta])

        def colops3(e, cb=cb):
            e.tensor_scalar(colA[:, cb + 13:cb + 14], colA[:, cb + 11:cb + 12], 1.0 / D, None, ALU.mult)
            e.tensor_scalar(colA[:, cb + 14:cb + 15], colA[:, cb + 12:cb + 13], 1.0 / D, None, ALU.mult)
            e.drain()
            e.tensor_mul(colA[:, cb + 15:cb + 16], colA[:, cb + 13:cb + 14], colA[:, cb + 13:cb + 14])
            e.drain()
            return e.tensor_sub(colA[:, cb + 16:cb + 17], colA[:, cb + 14:cb + 15], colA[:, cb + 15:cb + 16])
        tv3 = V_(colops3, after=[tm1, ts2])
        tsq2 = S_(lambda e, cb=cb: e.activation(colA[:, cb + 17:cb + 18], colA[:, cb + 16:cb + 17],
                                                AF.Sqrt, bias=LN_EPS), after=[tv3])

        def colops4(e, cb=cb):
            e.reciprocal(colA[:, cb + 18:cb + 19], colA[:, cb + 17:cb + 18])
            e.drain()
            e.tensor_mul(colA[:, cb + 19:cb + 20], colA[:, cb + 13:cb + 14], colA[:, cb + 18:cb + 19])
            e.drain()
            return e.tensor_scalar(colA[:, cb + 20:cb + 21], colA[:, cb + 19:cb + 20],
                                   -1.0, None, ALU.mult)
        tv4 = V_(colops4, after=[tsq2])
        ta2 = S_(lambda e, cb=cb: e.activation(x2e_tm[:, :], xn_tm[:, :], AF.Identity,
                                               bias=colA[:, cb + 20:cb + 21],
                                               scale=colA[:, cb + 18:cb + 19]),
                 after=[tv4])
        prevT = []
        for c in range(NDC):
            tT = T_(lambda e, c=c: e.transpose(
                psA[7][:, 0:128], xn_tm[:, c * 128:(c + 1) * 128], id_f[:, :]),
                after=[ta, tmx, ts2] + prevT
                + (st1["b7"] if (j == 0 and c == 0) else []))
            tE = S_(lambda e, c=c, j=j: e.copy(
                xnfm[:, c, 0:128], psA[7][:, 0:128]),
                after=[tT] + (tok_xn_j[-1:] if (c == 0 and tok_xn_j) else [])
                + (st1["xnst"] if (j == 0 and c == 0) else []))
            prevT = [tE]

        def st_xn_j(eng, sem, t=t, j=j):
            eng.dma_start(out=AP(xn_s, t * TILE + j * 128,
                                 [[TC, 128], [128 * TC, NDC], [1, 128]]),
                          in_=xnfm[:, :, :]).then_inc(sem, 16)
        DMA_("sync", st_xn_j, n=1, after=prevT)
        tok_xn_j.append(bld.fence("sync"))
        if "xn_fm" in dbg:
            DMA_("sync", lambda eng, sem, t=t, j=j: eng.dma_start(
                out=AP(dbg["xn_fm"], t * TILE + j * 128,
                       [[TC, 128], [128 * TC, NDC], [1, 128]]),
                in_=xnfm[:, :, :]).then_inc(sem, 16), n=1, after=prevT)
        for c in range(NDC):
            tT = T_(lambda e, c=c: e.transpose(
                psA[7][:, 0:128], x2e_tm[:, c * 128:(c + 1) * 128], id_f[:, :]),
                after=[ta2] + prevT)
            tE = S_(lambda e, c=c, j=j, sl=sl: e.copy(
                x2e[:, c, sl, j * 128:(j + 1) * 128], psA[7][:, 0:128]),
                after=[tT]
                + (st1["chunk_tail"] if (j == 0 and c == 0) else []))
            prevT = [tE]
        tok_x2e_j.append(prevT[0])
        if j == NCHT - 1:
            st1["ln_last"] = [prevT[0]]
            st1["xnst"] = [tok_xn_j[-1]]

    def prep_tail(t, b6gate):
        st = st1["ln"][t]
        sl = t % 2
        x2e_done = st["x2e_j"][-1]
        pcnt = {"i": 0, "first6": True, "first7": True}

        def proj_phi(wsb, bC, b1C, dstP, deps, prev_outs):
            outs = list(prev_outs)
            for co in range(NDC):
                bank = psA[6 + pcnt["i"] % 2]
                use6 = pcnt["i"] % 2 == 0
                pcnt["i"] += 1
                gate6 = []
                if use6 and pcnt["first6"]:
                    gate6 = list(b6gate)
                    pcnt["first6"] = False
                elif not use6 and pcnt["first7"]:
                    gate6 = [x2e_done]
                    pcnt["first7"] = False
                mm = []
                for c in range(NDC):
                    mm.append(T_(lambda e, c=c, co=co, wsb=wsb, sl=sl, bank=bank: e.matmul(
                        bank[:, :], wsb[:, c, co * 128:(co + 1) * 128], x2e[:, c, sl, :],
                        start=(c == 0), stop=(c == NDC - 1)),
                        after=((deps + [tok_wsc]) if (co == 0 and c == 0) else [])
                        + gate6
                        + ([mm[-1]] if mm else ([outs[-2]] if len(outs) >= 2 else []))))
                tv = V_(lambda e, co=co, bC=bC, bank=bank: e.tensor_scalar(
                    tbuf1[:, :], bank[:, :], bC[:, co:co + 1], 0.0, ALU.add, ALU.min),
                    after=[mm[-1]])
                tsx = S_(lambda e: e.activation(tbuf1[:, :], tbuf1[:, :], AF.Exp),
                         after=[tv] + ([outs[-1]] if outs else []))
                tph = V_(lambda e, co=co, dstP=dstP, b1C=b1C, sl=sl, bank=bank:
                         e.scalar_tensor_tensor(
                    dstP[:, co, sl, :], bank[:, :], b1C[:, co:co + 1], tbuf1[:, :],
                    ALU.add, ALU.max), after=[tsx])
                outs.append(tph)
            return outs[len(prev_outs):]
        qt = proj_phi(wq, qbC, qb1C, phiq, [x2e_done], [])
        kt = proj_phi(wk, kbC, kb1C, phik, [x2e_done], qt)
        if "phiq" in dbg:
            DMA_("sync", lambda eng, sem, t=t, sl=sl: eng.dma_start(
                out=AP(dbg["phiq"], t * TILE, [[TC, 128], [128 * TC, NDC], [1, TILE]]),
                in_=phiq[:, :, sl, :]).then_inc(sem, 16), n=1, after=[qt[-1]])
        if "phik" in dbg:
            DMA_("sync", lambda eng, sem, t=t, sl=sl: eng.dma_start(
                out=AP(dbg["phik"], t * TILE, [[TC, 128], [128 * TC, NDC], [1, TILE]]),
                in_=phik[:, :, sl, :]).then_inc(sem, 16), n=1, after=[kt[-1]])

        dmm = []
        for c in range(NDC):
            dmm.append(T_(lambda e, c=c, sl=sl: e.matmul(
                psA[7][0:16, 0:512], wdec[:, c, :], x2e[:, c, sl, :],
                start=(c == 0), stop=(c == NDC - 1)),
                after=(([x2e_done, kt[-1], kt[-2]]) if c == 0 else []) + ([dmm[-1]] if dmm else [])))
        tlam = S_(lambda e, sl=sl: e.activation(lam_t[:, sl, :], psA[7][0:16, 0:512], AF.Sigmoid,
                                                bias=decbc[:, 0:1]), after=[dmm[-1]])
        tlam2 = V_(lambda e, sl=sl: e.tensor_scalar(lam_t[:, sl, :], lam_t[:, sl, :],
                                                    LAM_MIN, None, ALU.max),
                   after=[tlam])
        if "lam_d" in dbg:
            tdl = DMA_("sync", lambda eng, sem, t=t, sl=sl: eng.dma_start(
                out=AP(dbg["lam_d"], t * TILE, [[TC, 16], [1, TILE]]),
                in_=lam_t[:, sl, :]).then_inc(sem, 16), n=1, after=[tlam2])
            tlam2 = tdl
        tll = S_(lambda e, sl=sl: e.activation(llam_t[:, sl, :], lam_t[:, sl, :], AF.Ln),
                 after=[tlam2])
        st1["b7"] = [tlam]
        st1["tll"][t] = tll

    def p1_chunk(t, cc, ch, prev, vgate):
        vslot = cc % 2
        tsl = t % 2
        vtoks = []
        for half in range(2):
            bank = psA[6]
            mm = []
            for c in range(NDC):
                dep = []
                if half == 0 and c == 0:
                    dep = list(prev) + vgate.get(ch - 2, [])
                mm.append(T_(lambda e, c=c, cc=cc, half=half, bank=bank, tsl=tsl: e.matmul(
                    bank[:, :], x2e[:, c, tsl, cc * 128:(cc + 1) * 128],
                    wv[:, c, half * 512:(half + 1) * 512],
                    start=(c == 0), stop=(c == NDC - 1)),
                    after=dep + ([mm[-1]] if mm else (vtoks[-1:] if vtoks else []))))
            ev = V_(lambda e, vslot=vslot, half=half, bank=bank: e.tensor_add(
                AP(vtm, vslot * H * KVE + half * 8 * KVE,
                   [[2 * H * KVE, 128], [KVE, 8], [1, 64]]),
                bank[:, :].rearrange("p (a b) -> p a b", a=8),
                vbtm[:, half * 512:(half + 1) * 512].rearrange("p (a b) -> p a b", a=8)),
                after=[mm[-1]])
            vtoks.append(ev)
        vdone = vtoks[-1]
        if "vtm_d" in dbg:
            DMA_("sync", lambda eng, sem, t=t, cc=cc, vslot=vslot: eng.dma_start(
                out=AP(dbg["vtm_d"], (t * NCHT + cc) * CH * H * KVE,
                       [[H * KVE, 128], [1, H * KVE]]),
                in_=vtm[:, vslot, :]).then_inc(sem, 16), n=1, after=[vdone])
        tlc = V_(lambda e, tsl=tsl: e.tensor_tensor_scan(
            lc_f[:, :], ones16_f[:, 0:CH], llam_t[:, tsl, cc * CH:(cc + 1) * CH],
            0.0, ALU.mult, ALU.add), after=prev)

        def lbk(e):
            e.tensor_copy(lcend_st[:, ch:ch + 1], lc_f[:, CH - 1:CH])
            e.tensor_copy(lpre_st[:, ch:ch + 1], ltot_c[:, :])
            e.drain()
            return e.tensor_add(ltot_c[:, :], ltot_c[:, :], lc_f[:, CH - 1:CH])
        tlbk = V_(lbk, after=[tlc])
        tce = S_(lambda e: e.activation(ctil_f[:, :], lc_f[:, :], AF.Exp), after=[tlc])
        tceb = V_(lambda e: e.tensor_copy(ctil_b[:, :], ctil_f[:, :]), after=[tce])
        tef = V_(lambda e: e.tensor_single_scalar(e_f[:, :], lc_f[:, :],
                                                  lc_f[:, CH - 1:CH], ALU.subtract),
                 after=[tlc])
        tee = S_(lambda e: e.activation(e_f[:, :], e_f[:, :], AF.Exp, scale=-1.0), after=[tef])
        teeb = V_(lambda e: e.tensor_copy(e_b[:, :], e_f[:, :]), after=[tee])
        tLt = T_(lambda e: e.transpose(psA[6][:, 0:16], lc_f[:, :], id_f[0:16, 0:16]),
                 after=[tlc] + prev)
        tCt = T_(lambda e: e.transpose(psA[6][:, 16:32], ctil_f[:, :], id_f[0:16, 0:16]),
                 after=[tce])
        tEt = T_(lambda e: e.transpose(psA[6][:, 32:48], e_f[:, :], id_f[0:16, 0:16]),
                 after=[tee])
        tLe = V_(lambda e: (e.tensor_copy(lctm[:, :], psA[6][:, 0:16]),
                            e.tensor_copy(ctm[:, :], psA[6][:, 16:32]),
                            e.tensor_copy(etm[:, :], psA[6][:, 32:48]))[-1],
                 after=[tLt, tCt, tEt])
        tdc1 = S_(lambda e: e.activation(s16B[:, 1:2], lcend_st[:, ch:ch + 1], AF.Exp),
                  after=[tlbk])
        tdc2 = S_(lambda e: e.activation(s16B[:, 2:3], lpre_st[:, ch:ch + 1], AF.Exp),
                  after=[tlbk])
        tdm1 = V_(lambda e: e.tensor_mul(s16B[:, 8:16], bccol(s16B[:, 1:2], NPAIR),
                                         pair_mask[:, :]), after=[tdc1])
        tdm2 = V_(lambda e: e.tensor_mul(s16A[:, 32:40], bccol(s16B[:, 2:3], NPAIR),
                                         pair_mask[:, :]), after=[tdc2])
        tdb1 = T_(lambda e: e.matmul(psA[6][:, 48:56], sel_half[:, :], s16B[:, 8:16],
                                     start=True, stop=True), after=[tdm1, tLe])
        tdb2 = T_(lambda e: e.matmul(psA[6][:, 56:64], sel_half[:, :], s16A[:, 32:40],
                                     start=True, stop=True), after=[tdm2])
        tdc = V_(lambda e: (e.tensor_copy(declc[:, :], psA[6][:, 48:56]),
                            e.tensor_copy(declp[:, :], psA[6][:, 56:64]))[-1],
                 after=[tdb1, tdb2])

        trh = V_(lambda e: e.tensor_mul(
            drhs[:, :].rearrange("p (a b) -> p a b", a=16),
            bcv(lc_f[:, :], 16), head_mask[:, :].rearrange("p (a b) -> p a b", a=16)),
            after=[tlc])
        dtoks = []
        prevD = [tdc]
        for bh in range(4):
            bank = psA[6]
            tb = T_(lambda e, bh=bh, bank=bank: e.matmul(
                bank[:, 64:576] if False else bank[:, 0:512],
                ones16_b[:, 0:128], drhs[:, bh * 512:(bh + 1) * 512],
                start=True, stop=True), after=[trh] + prevD)
            tsub = V_(lambda e, bh=bh, bank=bank: e.tensor_sub(
                D_sb[:, bh * 512:(bh + 1) * 512],
                AP(lctm, HORDER[bh * 4], [[16, 128], [2, 4], [0, CH]]),
                bank[:, 0:512].rearrange("p (a b) -> p a b", a=4)),
                after=[tb, tLe])
            tab = S_(lambda e, bh=bh: e.activation(D_sb[:, bh * 512:(bh + 1) * 512],
                                                   D_sb[:, bh * 512:(bh + 1) * 512], AF.Abs),
                     after=[tsub])
            tex = S_(lambda e, bh=bh: e.activation(D_sb[:, bh * 512:(bh + 1) * 512],
                                                   D_sb[:, bh * 512:(bh + 1) * 512], AF.Exp,
                                                   scale=-1.0), after=[tab])
            dtoks.append(tex)
            prevD = [tsub]

        if P1_CUT == "lc":
            return [dtoks[-1], tLe, tdc, vdone] + prev
        sc_toks = []
        sc_bank_tok = [None] * 4
        for hh in range(H):
            g, half = hh // 2, hh % 2
            pos = POS[hh]
            bank = psA[2 + pos // 4]
            tk = T_(lambda e, hh=hh, g=g, half=half, bank=bank, pos=pos, tsl=tsl: e.matmul(
                bank[:, (pos % 4) * 128:(pos % 4 + 1) * 128],
                phik[half * 64:(half + 1) * 64, g, tsl, cc * CH:(cc + 1) * CH],
                phiq[half * 64:(half + 1) * 64, g, tsl, cc * CH:(cc + 1) * CH],
                start=True, stop=True),
                after=([dtoks[-1]] + prev) if hh == 0 else [])
            sc_toks.append(tk)
            sc_bank_tok[pos // 4] = tk
        if P1_CUT == "sconly0":
            return [sc_toks[-1], tLe, tdc, dtoks[-1]] + prev
        if P1_CUT == "sconly":
            pvx = []
            for bq in range(4):
                pvx.append(V_(lambda e, bq=bq: e.tensor_copy(
                    scD[:, bq * 512:(bq + 1) * 512], psA[2 + bq][:, :]),
                    after=[sc_toks[bq * 4 + 3]]))
            return [pvx[-1], tLe, tdc, dtoks[-1]] + prev
        sd_toks = []
        for bh in range(4):
            sd_toks.append(V_(lambda e, bh=bh: e.tensor_mul(
                scD[:, bh * 512:(bh + 1) * 512], psA[2 + bh][:, :],
                D_sb[:, bh * 512:(bh + 1) * 512]),
                after=[sc_bank_tok[bh], sc_toks[-1], dtoks[bh]]))
        if P1_CUT == "sd":
            return [sd_toks[-1], tLe, tdc] + prev

        tkf = V_(lambda e, tsl=tsl: e.tensor_copy(
            phikf[:, :].rearrange("p (a b) -> p a b", a=NPAIR),
            phik[:, :, tsl, cc * CH:(cc + 1) * CH]), after=[sc_toks[-1]])
        kT = []
        prevKT = [sc_toks[-1], tkf] + list(prev)
        for gq in range(2):
            tTs = []
            for g4 in range(4):
                g = gq * 4 + g4
                tTs.append(T_(lambda e, g=g, g4=g4: e.transpose(
                    psA[g4 % 2][:, (g4 // 2) * 128:(g4 // 2) * 128 + 128],
                    phikf[:, g * CH:(g + 1) * CH], id_f[:, :]),
                    after=(prevKT if g4 == 0 else [])))
            evs = []
            for g4 in range(4):
                g = gq * 4 + g4
                for half in range(2):
                    hh = 2 * g + half
                    evs.append(V_(lambda e, g4=g4, half=half, hh=hh: (
                        e.tensor_scalar(keB[:, hh * NK:(hh + 1) * NK],
                                        psA[g4 % 2][:, (g4 // 2) * 128 + half * 64:
                                                     (g4 // 2) * 128 + half * 64 + 64],
                                        etm[:, hh:hh + 1], None, ALU.mult),
                        e.tensor_scalar(kcB[:, hh * NK:(hh + 1) * NK],
                                        psA[g4 % 2][:, (g4 // 2) * 128 + half * 64:
                                                     (g4 // 2) * 128 + half * 64 + 64],
                                        ctm[:, hh:hh + 1], None, ALU.mult))[-1],
                        after=([tTs[-1], tLe] if g4 == 0 and half == 0 else [])))
            prevKT = [evs[-1]]
            kT += tTs + evs
        kdone = kT[-1]

        if P1_CUT == "scores":
            return [sd_toks[-1], kdone, tLe, tdc] + prev
        qtk = []
        prevQ = [dtoks[-1], tceb, teeb, tdc]
        for gq in range(2):
            mmq = []
            for g4 in range(4):
                g = gq * 4 + g4
                mmq.append(T_(lambda e, g=g, g4=g4: e.matmul(
                    psA[6][:, g4 * 128:(g4 + 1) * 128],
                    sel_pair_b[:, g * 128:(g + 1) * 128],
                    ctil_b[:, :], start=True, stop=True),
                    after=(prevQ if g4 == 0 else [])))
            tqc = V_(lambda e, gq=gq, sl=ch % 2, tsl=tsl: e.tensor_mul(
                qcB[:, sl, gq * 4 * CH:(gq + 1) * 4 * CH].rearrange(
                    "p (a b) -> p a b", a=4),
                phiq[:, gq * 4:(gq + 1) * 4, tsl, cc * CH:(cc + 1) * CH],
                psA[6][:, 0:512].rearrange("p (a b) -> p a b", a=4)),
                after=[mmq[-1]]
                + ([stq[("qc", ch - 2)]] if (gq == 0 and ("qc", ch - 2) in stq) else []))
            mme = []
            for g4 in range(4):
                g = gq * 4 + g4
                mme.append(T_(lambda e, g=g, g4=g4: e.matmul(
                    psA[6][:, g4 * 128:(g4 + 1) * 128],
                    sel_pair_b[:, g * 128:(g + 1) * 128],
                    e_b[:, :], start=True, stop=True),
                    after=([tqc] if g4 == 0 else [])))
            tqe = V_(lambda e, gq=gq, sl=ch % 2, tsl=tsl: e.tensor_mul(
                qeB[:, sl, gq * 4 * CH:(gq + 1) * 4 * CH].rearrange(
                    "p (a b) -> p a b", a=4),
                phiq[:, gq * 4:(gq + 1) * 4, tsl, cc * CH:(cc + 1) * CH],
                psA[6][:, 0:512].rearrange("p (a b) -> p a b", a=4)),
                after=[mme[-1]])
            prevQ = [tqe]
            qtk += [tqc, tqe]
        qdone = qtk[-1]

        def st_qc(eng, sem, sl=ch % 2):
            eng.dma_start(out=AP(qc_s, ch * 128 * NPAIR * CH, [[NPAIR * CH, 128], [1, NPAIR * CH]]),
                          in_=qcB[:, sl, :]).then_inc(sem, 16)
            eng.dma_start(out=AP(qe_s, ch * 128 * NPAIR * CH, [[NPAIR * CH, 128], [1, NPAIR * CH]]),
                          in_=qeB[:, sl, :]).then_inc(sem, 16)
        DMA_("sync", st_qc, n=2, after=[qdone])
        tok_qst = bld.fence("sync")
        stq[("qc", ch)] = tok_qst

        if P1_CUT == "qc":
            return [qdone, tok_qst, kdone, sd_toks[-1], tdc]
        av_toks = []
        av_bank_tok = [None] * 4
        for hh in range(H):
            g, half = hh // 2, hh % 2
            pos = POS[hh]
            bank = psA[2 + pos // 4]
            m1 = T_(lambda e, hh=hh, bank=bank, pos=pos: e.matmul(
                bank[:, (pos % 4) * KVE:(pos % 4) * KVE + KVE],
                scD[:, pos * CH:(pos + 1) * CH],
                vtm[:, cc % 2, hh * KVE:(hh + 1) * KVE],
                start=True, stop=False),
                after=([sd_toks[pos // 4], qdone, vdone] if hh == 0 else [sd_toks[pos // 4]]))
            m2 = T_(lambda e, hh=hh, g=g, half=half, bank=bank, pos=pos, sl=ch % 2: e.matmul(
                bank[:, (pos % 4) * KVE:(pos % 4) * KVE + KVE],
                qcB[half * 64:(half + 1) * 64, sl, g * CH:(g + 1) * CH],
                Sb16[half * 64:(half + 1) * 64, g * KVE:(g + 1) * KVE],
                start=False, stop=True), after=[m1])
            av_toks.append(m2)
            av_bank_tok[pos // 4] = m2
        pv = []
        for bq in range(4):
            pv.append(V_(lambda e, bq=bq, sl=ch % 2: e.tensor_copy(
                partB[:, sl, bq * 4 * KVE:(bq + 1) * 4 * KVE], psA[2 + bq][:, 0:4 * KVE]),
                after=[av_bank_tok[bq], av_toks[-1]]
                + ([stq[("part", ch - 2)]] if ("part", ch - 2) in stq else [])))

        def st_part(eng, sem, sl=ch % 2):
            eng.dma_start(out=AP(part_s, ch * CH * H * KVE, [[H * KVE, 128], [1, H * KVE]]),
                          in_=partB[:, sl, :]).then_inc(sem, 16)
        DMA_("sync", st_part, n=1, after=pv)
        tok_pst = bld.fence("sync")
        stq[("part", ch)] = tok_pst

        if P1_CUT == "av":
            return [tok_pst, qdone, tok_qst, kdone, tdc, pv[-1]]
        kev_toks = []
        for hh in range(H):
            half = hh % 2
            bank = psA[hh // 8]
            slot = (hh % 8) // 2
            kev_toks.append(T_(lambda e, hh=hh, half=half, bank=bank, slot=slot: e.matmul(
                bank[half * 64:half * 64 + 64, slot * KVE:slot * KVE + KVE],
                keB[:, hh * NK:(hh + 1) * NK],
                vtm[:, cc % 2, hh * KVE:(hh + 1) * KVE],
                start=True, stop=True, tile_position=(0, half * 64)),
                after=([kdone, av_toks[-1]] if hh == 0 else [])))
        sup = []
        for g in range(NPAIR):
            bank = psA[g // 4]
            slot = g % 4
            sup.append(V_(lambda e, g=g, bank=bank, slot=slot: (
                e.tensor_scalar(Tmp_st[:, g * KVE:(g + 1) * KVE],
                                S_st[:, g * KVE:(g + 1) * KVE], declc[:, g:g + 1], None, ALU.mult),
                e.drain(),
                e.tensor_add(S_st[:, g * KVE:(g + 1) * KVE],
                             Tmp_st[:, g * KVE:(g + 1) * KVE],
                             bank[:, slot * KVE:slot * KVE + KVE]),
                e.drain(),
                e.tensor_copy(Sb16[:, g * KVE:(g + 1) * KVE],
                              S_st[:, g * KVE:(g + 1) * KVE]))[-1],
                after=[kev_toks[7 if g < 4 else 15], tdc, av_toks[-1]]))
        kcv_toks = []
        for hh in range(H):
            half = hh % 2
            bank = psA[hh // 8]
            slot = (hh % 8) // 2
            kcv_toks.append(T_(lambda e, hh=hh, half=half, bank=bank, slot=slot: e.matmul(
                bank[half * 64:half * 64 + 64, slot * KVE:slot * KVE + KVE],
                kcB[:, hh * NK:(hh + 1) * NK],
                vtm[:, cc % 2, hh * KVE:(hh + 1) * KVE],
                start=True, stop=True, tile_position=(0, half * 64)),
                after=([sup[-1]] if hh == 0 else [])))
        kce = []
        for g in range(NPAIR):
            bank = psA[g // 4]
            slot = g % 4
            kce.append(V_(lambda e, g=g, ch=ch, bank=bank, slot=slot: (
                e.tensor_copy(kcvs[:, ch % 2, g * KVE:(g + 1) * KVE],
                              bank[:, slot * KVE:slot * KVE + KVE]),
                e.tensor_scalar(Tmp_st[:, g * KVE:(g + 1) * KVE],
                                bank[:, slot * KVE:slot * KVE + KVE],
                                declp[:, g:g + 1], None, ALU.mult),
                e.drain(),
                e.tensor_add(Sb_st[:, g * KVE:(g + 1) * KVE],
                             Sb_st[:, g * KVE:(g + 1) * KVE],
                             Tmp_st[:, g * KVE:(g + 1) * KVE]))[-1],
                after=[kcv_toks[7 if g < 4 else 15], tdc, sup[-1] if g == 0 else kce[-1]]
                + (stq.get(("kcv", ch - 2), []) if g == 0 else [])))

        def st_kcv(eng, sem, ch=ch):
            eng.dma_start(out=AP(kcv_s, ch * 128 * NPAIR * KVE,
                                 [[NPAIR * KVE, 128], [1, NPAIR * KVE]]),
                          in_=kcvs[:, ch % 2, :]).then_inc(sem, 16)
        DMA_("sync", st_kcv, n=1, after=[kce[-1]])
        stq[("kcv", ch)] = [bld.fence("sync")]
        vgate[ch] = [kev_toks[-1], kcv_toks[-1]]
        return [sup[-1], kce[-1], pv[-1], qdone]

    st1["ln_last"] = [tok_wsc, tok_clip, tok_vone]
    emit_ldx(0)
    for j in range(NCHT):
        prep_ln_part(0, j)
    prep_tail(0, [tok_vone])
    for t in range(NT):
        if t + 1 < NT:
            emit_ldx(t + 1)
        prev = [st1["tll"][t]] + st1["chunk_tail"]
        for cc in range(NCHT):
            prev = p1_chunk(t, cc, t * NCHT + cc, prev, st1["vg"])
            if t + 1 < NT:
                prep_ln_part(t + 1, cc)
        if t + 1 < NT:
            prep_tail(t + 1, list(prev))
        st1["chunk_tail"] = list(prev)
    gate = prev
    p1_end = list(gate)
    if "sf_d" in dbg:
        DMA_("sync", lambda eng, sem: eng.dma_start(
            out=dbg["sf_d"].ap(), in_=S_st[:, :]).then_inc(sem, 16), n=1, after=p1_end)
    if "sb_d" in dbg:
        DMA_("sync", lambda eng, sem: eng.dma_start(
            out=dbg["sb_d"].ap(), in_=Sb_st[:, :]).then_inc(sem, 16), n=1, after=p1_end)

    q_p1 = bld.quiesce("sync")
    bld.flush("p1")
    if PHASE_LIMIT == "p1":
        return nc
    es1.close()

    es_x = ExitStack()
    psX = [PS(f"psX{i}", es_x) for i in range(4)]
    rowX = SB("rowX", [1, 192], F32, es_x)
    tmxT = T_(lambda e: e.transpose(psX[0][0:1, 0:128], mx_col[:, :], id_f[:, :]))
    tmxE = V_(lambda e: (e.tensor_copy(rowX[0:1, 0:128], psX[0][0:1, 0:128]),
                         e.memset(rowA[0:1, 31:46], 0.0),
                         e.drain(),
                         e.tensor_reduce(rowA[0:1, 30:31], rowX[0:1, 0:128], AX.X, ALU.max))[-1],
              after=[tmxT])

    def st_ag(eng, sem):
        for half in range(2):
            eng.dma_start(out=AP(ag_in, half * 4160,
                                 [[65, 64], [2 * 4160, NPAIR], [1, 65]]),
                          in_=S_st[half * 64:half * 64 + 64, :].rearrange("p (a b) -> p a b", a=NPAIR)
                          ).then_inc(sem, 16)
            eng.dma_start(out=AP(ag_in, PAY_SEC + half * 4160,
                                 [[65, 64], [2 * 4160, NPAIR], [1, 65]]),
                          in_=Sb_st[half * 64:half * 64 + 64, :].rearrange("p (a b) -> p a b", a=NPAIR)
                          ).then_inc(sem, 16)
        eng.dma_start(out=AP(ag_in, 2 * PAY_SEC, [[1, 16], [1, 1]]),
                      in_=ltot_c[:, :]).then_inc(sem, 16)
        eng.dma_start(out=AP(ag_in, 2 * PAY_SEC + 16, [[1, 16], [1, 1]]),
                      in_=rowA[0:1, 30:46]).then_inc(sem, 16)
    tok_ag_in = DMA_("gpsimd", st_ag, n=6, after=[tmxE])

    tok_cc = bld.g_async(lambda e: e.collective_compute(
        "AllGather", ALU.bypass, replica_groups=[list(range(8))],
        ins=[ag_in.ap().opt()], outs=[ag_out.ap().opt()]),
        after=[tok_ag_in])

    wprep = [SB(f"wprep{i}", [128, IFF], F32, es_x) for i in range(2)]
    wtT = SB("wtT", [128, 2, NIC * 128], BF16, es_x)
    wpc = SB("wpc", [128, 8], F32, es_x)

    blocks = []
    for wname, wT_dst, sc_dst, nrows in [
            ("L0_ffn_w_gate", wgT_s, sg_sc, IFF),
            ("L0_ffn_w_up", wuT_s, su_sc, IFF),
            ("L0_ffn_w_down", wdT_s, sd_sc, D)]:
        for r in range(nrows // 128):
            blocks.append((wname, wT_dst, sc_dst, nrows, r))

    def emit_ld(idx):
        wname, _, _, _, r = blocks[idx]
        ncols = win[wname].shape[1]
        buf = wprep[idx % 2]

        def ldw(eng, sem, r=r, buf=buf, wname=wname, ncols=ncols):
            eng.dma_start(out=buf[:, 0:ncols],
                          in_=AP(win[wname], r * 128 * ncols, [[ncols, 128], [1, ncols]])
                          ).then_inc(sem, 16)
        return DMA_("sync", ldw, n=1, after=pw_st["buf"].get(idx % 2, []))

    pw_st = {"trd": None, "bank": {}, "buf": {}, "wst": {}}
    pend_ld = {0: emit_ld(0), 1: emit_ld(1)}
    pw = []
    for idx, (wname, wT_dst, sc_dst, nrows, r) in enumerate(blocks):
        ncols = win[wname].shape[1]
        nbl = ncols // 128
        buf = wprep[idx % 2]
        tl = pend_ld.pop(idx)
        tmax = V_(lambda e, buf=buf, ncols=ncols: (
            e.tensor_reduce(wpc[:, 0:1], buf[:, 0:ncols], AX.X, ALU.max,
                            apply_absolute_value=True),
            e.drain(),
            e.tensor_scalar(wpc[:, 1:2], wpc[:, 0:1], 1.0 / QMAX, None, ALU.mult),
            e.drain(),
            e.reciprocal(wpc[:, 2:3], wpc[:, 1:2]))[-1],
            after=[tl] + ([pw_st["trd"]] if pw_st["trd"] else []))
        tsc = V_(lambda e, r=r, sc_dst=sc_dst: e.tensor_copy(sc_dst[:, r:r + 1], wpc[:, 1:2]),
                 after=[tmax])
        trd = S_(lambda e, buf=buf, ncols=ncols: e.activation(
            buf[:, 0:ncols], buf[:, 0:ncols], AF.Identity,
            bias=MAGIC, scale=wpc[:, 2:3]), after=[tmax, tsc])
        tcl = V_(lambda e, buf=buf, ncols=ncols: (
            e.tensor_scalar(buf[:, 0:ncols], buf[:, 0:ncols], MAGIC, QMAX,
                            ALU.subtract, ALU.min),
            e.tensor_scalar(buf[:, 0:ncols], buf[:, 0:ncols], -QMAX - 1.0, None, ALU.max))[-1],
            after=[trd])
        pw_st["trd"] = trd
        tT2 = []
        for c in range(nbl):
            bk = c % 4
            bank = psX[bk]
            tt = T_(lambda e, bank=bank, c=c, buf=buf: e.transpose(
                bank[:, 0:128], buf[:, c * 128:(c + 1) * 128], id_f[:, :]),
                after=[tcl] + (([tT2[c - 4]] if c >= 4 else pw_st["bank"].get(bk, []))))
            te = S_(lambda e, c=c, bank=bank, sl=idx % 2: e.copy(
                wtT[:, sl, c * 128:(c + 1) * 128], bank[:, 0:128]),
                after=[tt] + ([pw_st["wst"][idx % 2]] if (c == 0 and idx % 2 in pw_st["wst"]) else []))
            tT2.append(te)
            pw_st["bank"][bk] = [te]
        pw_st["buf"][idx % 2] = [tT2[-1]]
        if idx + 2 < len(blocks):
            pend_ld[idx + 2] = emit_ld(idx + 2)

        def stw(eng, sem, r=r, wT_dst=wT_dst, nbl=nbl, nrows=nrows, sl=idx % 2):
            eng.dma_start(out=AP(wT_dst, r * 128,
                                 [[nrows, 128], [128 * nrows, nbl], [1, 128]]),
                          in_=wtT[:, sl, 0:nbl * 128].rearrange("p (a b) -> p a b", a=nbl)
                          ).then_inc(sem, 16)
        pw = [DMA_("sync", stw, n=1, after=[tT2[-1]])]
        pw_st["wst"][idx % 2] = pw[0]

    tok_ccw = G_(lambda e: e.nop(), after=[tok_cc, pw[-1]])
    q_x1 = bld.quiesce("sync")
    es_x.close()
    bld.flush("x1")
    if PHASE_LIMIT == "x1":
        return nc

    es2 = ExitStack()
    sel_pair_f = SB("sel_pair_f", [16, NPAIR * 128], F32, es2)
    SinC = SB("SinC", [128, NPAIR * KVE], BF16, es2)
    Rb16 = SB("Rb16", [128, NPAIR * KVE], BF16, es2)
    wo = SB("wo", [128, NDC, D], BF16, es2)
    scr2 = SB("scr2", [128, TILE], F32, es2)
    LtG = SB("LtG", [8, 16], F32, es2)
    mG = SB("mG", [8, 1], F32, es2)
    Wf8 = SB("Wf8", [8, 16], F32, es2)
    Wb8 = SB("Wb8", [8, 16], F32, es2)
    WfT = SB("WfT", [16, 8], F32, es2)
    WbT = SB("WbT", [16, 8], F32, es2)
    Wexp = SB("Wexp", [128, NPAIR * 8], F32, es2)
    Wexpb = SB("Wexpb", [128, NPAIR * 8], F32, es2)
    peerb = SB("peerb", [128, NPAIR * KVE], F32, es2)
    xnfm2 = SB("xnfm2", [128, NDC, TILE], F32, es2)
    qcL = SB("qcL", [128, NCHT, NPAIR * CH], BF16, es2)
    qeL = SB("qeL", [128, NCHT, NPAIR * CH], BF16, es2)
    partL = SB("partL", [128, NCHT, H * KVE], BF16, es2)
    totB = SB("totB", [128, H * KVE], F32, es2)
    atm = SB("atm", [128, H * NK], F32, es2)
    attnfm = SB("attnfm", [128, NDC, TILE], BF16, es2)
    x1b = SB("x1b", [128, NDC, 2, TILE], F32, es2)
    kcvL = SB("kcvL", [128, 2, NPAIR * KVE], BF16, es2)
    xqb = SB("xqb", [128, NDC, 2, TILE], BF16, es2)
    psB = [PS(f"psB{i}", es2) for i in range(8)]

    def p2_wo(eng, sem):
        eng.dma_start(out=wo[:, :, :],
                      in_=AP(win["L0_attn_o_w"], 0, [[D, 128], [128 * D, NDC], [1, D]])
                      ).then_inc(sem, 16)
    tok_wo = DMA_("gpsimd", p2_wo, n=1)

    def p2_spf(eng, sem):
        eng.dma_start(out=sel_pair_f[:, :], in_=const_t["sel_pair_f"].ap()).then_inc(sem, 16)
    tok_spf = DMA_("sync", p2_spf, n=1)

    def p2_gather_small(eng, sem):
        eng.dma_start(out=LtG[:, :], in_=AP(ag_out, 2 * PAY_SEC, [[PAY, 8], [1, 16]])
                      ).then_inc(sem, 16)
        eng.dma_start(out=mG[:, :], in_=AP(ag_out, 2 * PAY_SEC + 16, [[PAY, 8], [1, 1]]),
                      allow_slow_non_contiguous=True).then_inc(sem, 16)
    tok_gs = DMA_("sync", p2_gather_small, n=2, after=[tok_ccw])

    tm1T = T_(lambda e: e.transpose(psB[0][0:1, 0:8], mG[:, :], id_f[0:8, 0:8]),
              after=[tok_gs])
    tm1 = V_(lambda e: (e.tensor_copy(rowA[0:1, 40:48], psB[0][0:1, 0:8]),
                        e.drain(),
                        e.tensor_reduce(rowA[0:1, 24:25], rowA[0:1, 40:48], AX.X, ALU.max),
                        e.drain(),
                        e.tensor_mul(rowA[0:1, 25:26], rowA[0:1, 24:25], clipc[0:1, 0:1]),
                        e.drain(),
                        e.tensor_scalar(rowA[0:1, 26:27], rowA[0:1, 25:26], 1.0 / QMAX, None, ALU.mult),
                        e.drain(),
                        e.reciprocal(rowA[0:1, 27:28], rowA[0:1, 26:27]))[-1],
             after=[tm1T])
    if "m1_d" in dbg:
        DMA_("sync", lambda eng, sem: eng.dma_start(
            out=dbg["m1_d"].ap(), in_=rowA[0:1, 24:40]).then_inc(sem, 16), n=1, after=[tm1])
    tb1 = T_(lambda e: (e.matmul(psB[1][:, 0:1], ones_row[:, :], rowA[0:1, 26:27],
                                 start=True, stop=True),
                        e.matmul(psB[1][:, 1:2], ones_row[:, :], rowA[0:1, 27:28],
                                 start=True, stop=True))[-1], after=[tm1])
    tb1e = V_(lambda e: (e.tensor_copy(sc1_col[:, :], psB[1][:, 0:1]),
                         e.tensor_copy(rs1_col[:, :], psB[1][:, 1:2]))[-1], after=[tb1])

    tWf = T_(lambda e: (e.matmul(psB[2][0:8, 0:16], ccpf[:, :], LtG[:, :], start=True, stop=True),
                        e.matmul(psB[2][0:8, 16:32], ccpb[:, :], LtG[:, :], start=True, stop=True))[-1],
             after=[tok_gs, tb1])
    tWfe = S_(lambda e: (e.activation(Wf8[:, :], psB[2][0:8, 0:16], AF.Exp),
                         e.activation(Wb8[:, :], psB[2][0:8, 16:32], AF.Exp))[-1], after=[tWf])
    tWfm = V_(lambda e: (e.tensor_scalar(Wf8[:, :], Wf8[:, :], ccmf[:, 0:1], None, ALU.mult),
                         e.tensor_scalar(Wb8[:, :], Wb8[:, :], ccmb[:, 0:1], None, ALU.mult))[-1],
              after=[tWfe])
    tWT = T_(lambda e: (e.transpose(psB[3][0:16, 0:8], Wf8[:, :], id_f[0:8, 0:8]),
                        e.transpose(psB[3][0:16, 8:16], Wb8[:, :], id_f[0:8, 0:8]))[-1],
             after=[tWfm])
    tWTe = V_(lambda e: (e.tensor_copy(WfT[:, :], psB[3][0:16, 0:8]),
                         e.tensor_copy(WbT[:, :], psB[3][0:16, 8:16]))[-1], after=[tWT])
    wx = []
    for g in range(NPAIR):
        wx.append(T_(lambda e, g=g: (
            e.matmul(psB[4][:, g * 8:(g + 1) * 8], sel_pair_f[:, g * 128:(g + 1) * 128],
                     WfT[:, :], start=True, stop=True),
            e.matmul(psB[4][:, 64 + g * 8:64 + (g + 1) * 8],
                     sel_pair_f[:, g * 128:(g + 1) * 128], WbT[:, :], start=True, stop=True))[-1],
            after=[tWTe, tok_spf] + (wx[-1:] if wx else [])))
    tWx = V_(lambda e: (e.tensor_copy(Wexp[:, :], psB[4][:, 0:64]),
                        e.tensor_copy(Wexpb[:, :], psB[4][:, 64:128]))[-1], after=[wx[-1]])

    comb = [tWx]
    for peer in range(8):
        def ld_peer(eng, sem, peer=peer):
            for half in range(2):
                eng.dma_start(out=peerb[half * 64:half * 64 + 64, :].rearrange(
                    "p (a b) -> p a b", a=NPAIR),
                    in_=AP(ag_out, peer * PAY + half * 4160,
                           [[65, 64], [2 * 4160, NPAIR], [1, 65]])).then_inc(sem, 16)
        tlp = DMA_("sync", ld_peer, n=2, after=comb[-1:])
        ops = []
        for g in range(NPAIR):
            ops.append(V_(lambda e, g=g, peer=peer: (
                e.tensor_scalar(Tmp_st[:, g * KVE:(g + 1) * KVE],
                                peerb[:, g * KVE:(g + 1) * KVE],
                                Wexp[:, g * 8 + peer:g * 8 + peer + 1], None, ALU.mult),
                e.drain(),
                e.tensor_add(Sin_st[:, g * KVE:(g + 1) * KVE],
                             Sin_st[:, g * KVE:(g + 1) * KVE],
                             Tmp_st[:, g * KVE:(g + 1) * KVE]))[-1],
                after=[tlp, tWx]))

        def ld_peerb(eng, sem, peer=peer):
            for half in range(2):
                eng.dma_start(out=peerb[half * 64:half * 64 + 64, :].rearrange(
                    "p (a b) -> p a b", a=NPAIR),
                    in_=AP(ag_out, peer * PAY + PAY_SEC + half * 4160,
                           [[65, 64], [2 * 4160, NPAIR], [1, 65]])).then_inc(sem, 16)
        tlpb = DMA_("sync", ld_peerb, n=2, after=[ops[-1]])
        for g in range(NPAIR):
            ops.append(V_(lambda e, g=g, peer=peer: (
                e.tensor_scalar(Tmp_st[:, g * KVE:(g + 1) * KVE],
                                peerb[:, g * KVE:(g + 1) * KVE],
                                Wexpb[:, g * 8 + peer:g * 8 + peer + 1], None, ALU.mult),
                e.drain(),
                e.tensor_add(Rin_st[:, g * KVE:(g + 1) * KVE],
                             Rin_st[:, g * KVE:(g + 1) * KVE],
                             Tmp_st[:, g * KVE:(g + 1) * KVE]))[-1],
                after=[tlpb]))
        comb = [ops[-1]]
    tok_comb = comb[-1]
    tok_Rinit = V_(lambda e: (e.tensor_copy(R_st[:, :], Rin_st[:, :]),
                              e.tensor_copy(Rb16[:, :], Rin_st[:, :]))[-1], after=[tok_comb])
    if "sin_d" in dbg:
        DMA_("sync", lambda eng, sem: eng.dma_start(
            out=dbg["sin_d"].ap(), in_=Sin_st[:, :]).then_inc(sem, 16), n=1, after=[tok_comb])
    if "rin_d" in dbg:
        DMA_("sync", lambda eng, sem: eng.dma_start(
            out=dbg["rin_d"].ap(), in_=Rin_st[:, :]).then_inc(sem, 16), n=1, after=[tok_comb])

    def p2_tile(t, gate):
        def ld_t(eng, sem):
            eng.dma_start(out=xnfm2[:, :, :],
                          in_=AP(xn_s, t * TILE, [[TC, 128], [128 * TC, NDC], [1, TILE]])
                          ).then_inc(sem, 16)
            eng.dma_start(out=qcL[:, :, :],
                          in_=AP(qc_s, t * NCHT * 128 * NPAIR * CH,
                                 [[NPAIR * CH, 128], [128 * NPAIR * CH, NCHT], [1, NPAIR * CH]])
                          ).then_inc(sem, 16)
            eng.dma_start(out=qeL[:, :, :],
                          in_=AP(qe_s, t * NCHT * 128 * NPAIR * CH,
                                 [[NPAIR * CH, 128], [128 * NPAIR * CH, NCHT], [1, NPAIR * CH]])
                          ).then_inc(sem, 16)
            eng.dma_start(out=partL[:, :, :],
                          in_=AP(part_s, t * NCHT * CH * H * KVE,
                                 [[H * KVE, 128], [128 * H * KVE, NCHT], [1, H * KVE]])
                          ).then_inc(sem, 16)
        tok_l = DMA_("sync", ld_t, n=4, after=gate + [q_p1])

        last = list(gate)
        for cc in range(NCHT - 1, -1, -1):
            ch = t * NCHT + cc
            td1 = S_(lambda e, ch=ch: (
                e.activation(s16B[:, 1:2], lcend_st[:, ch:ch + 1], AF.Exp),
                e.activation(s16B[:, 2:3], lpre_st[:, ch:ch + 1], AF.Exp))[-1], after=last)
            td2 = V_(lambda e: (
                e.tensor_mul(s16B[:, 8:16], bccol(s16B[:, 1:2], NPAIR), pair_mask[:, :]),
                e.tensor_mul(s16A[:, 32:40], bccol(s16B[:, 2:3], NPAIR), pair_mask[:, :]))[-1],
                after=[td1])
            td3 = T_(lambda e: (
                e.matmul(psB[5][:, 0:8], sel_half[:, :], s16B[:, 8:16], start=True, stop=True),
                e.matmul(psB[5][:, 8:16], sel_half[:, :], s16A[:, 32:40], start=True, stop=True))[-1],
                after=[td2] + last)
            td4 = V_(lambda e: (e.tensor_copy(declc[:, :], psB[5][:, 0:8]),
                                e.tensor_copy(declp[:, :], psB[5][:, 8:16]))[-1], after=[td3])
            sic = []
            for g in range(NPAIR):
                sic.append(V_(lambda e, g=g: e.tensor_scalar(
                    SinC[:, g * KVE:(g + 1) * KVE], Sin_st[:, g * KVE:(g + 1) * KVE],
                    declp[:, g:g + 1], None, ALU.mult), after=[td4, tok_Rinit]))
            mmz = []
            mm_bank_tok = [None] * 4
            for hh in range(H):
                g, half = hh // 2, hh % 2
                pos = POS[hh]
                bank = psB[pos // 4]
                ma = T_(lambda e, hh=hh, g=g, half=half, bank=bank, pos=pos, cc=cc: e.matmul(
                    bank[:, (pos % 4) * KVE:(pos % 4) * KVE + KVE],
                    qeL[half * 64:half * 64 + 64, cc, g * CH:(g + 1) * CH],
                    Rb16[half * 64:half * 64 + 64, g * KVE:(g + 1) * KVE],
                    start=True, stop=False),
                    after=([sic[-1], tok_l] + last) if hh == 0 else [])
                mb = T_(lambda e, hh=hh, g=g, half=half, bank=bank, pos=pos, cc=cc: e.matmul(
                    bank[:, (pos % 4) * KVE:(pos % 4) * KVE + KVE],
                    qcL[half * 64:half * 64 + 64, cc, g * CH:(g + 1) * CH],
                    SinC[half * 64:half * 64 + 64, g * KVE:(g + 1) * KVE],
                    start=False, stop=True), after=[ma])
                mmz.append(mb)
                mm_bank_tok[pos // 4] = mb
            pa = []
            for bq in range(4):
                pa.append(V_(lambda e, bq=bq, cc=cc: e.tensor_add(
                    totB[:, bq * 4 * KVE:(bq + 1) * 4 * KVE],
                    psB[bq][:, 0:4 * KVE],
                    partL[:, cc, bq * 4 * KVE:(bq + 1) * 4 * KVE]),
                    after=[mm_bank_tok[bq], mmz[-1], tok_l]))
            dn = []
            for hh in range(H):
                pos = POS[hh]
                dn.append(V_(lambda e, hh=hh, pos=pos: (
                    e.tensor_scalar(colA[:, 40:41], totB[:, pos * KVE + NK:pos * KVE + NK + 1],
                                    EPS_DEN, None, ALU.max),
                    e.drain(),
                    e.reciprocal(colA[:, 41:42], colA[:, 40:41]),
                    e.drain(),
                    e.tensor_scalar(atm[:, hh * NK:(hh + 1) * NK],
                                    totB[:, pos * KVE:pos * KVE + NK],
                                    colA[:, 41:42], None, ALU.mult))[-1],
                    after=[pa[pos // 4]] + (dn[-1:] if dn else [])))
            ru = []
            for g in range(NPAIR):
                ru.append(V_(lambda e, g=g, ch=ch: (
                    e.tensor_scalar(Tmp_st[:, g * KVE:(g + 1) * KVE],
                                    R_st[:, g * KVE:(g + 1) * KVE],
                                    declc[:, g:g + 1], None, ALU.mult),
                    e.drain(),
                    e.tensor_add(R_st[:, g * KVE:(g + 1) * KVE],
                                 Tmp_st[:, g * KVE:(g + 1) * KVE],
                                 kcvL[:, ch % 2, g * KVE:(g + 1) * KVE]),
                    e.drain(),
                    e.tensor_copy(Rb16[:, g * KVE:(g + 1) * KVE],
                                  R_st[:, g * KVE:(g + 1) * KVE]))[-1],
                    after=[td4, tok_Rinit, mmz[-1], st2kl[ch]] + (ru[-1:] if ru else [])))
            if ch - 2 >= 0:
                emit_kl(ch - 2, [ru[-1]])
            at = []
            for g in range(NPAIR):
                bank = psB[6]
                tt = T_(lambda e, g=g, bank=bank: e.transpose(
                    bank[:, (g % 4) * 128:(g % 4) * 128 + 128],
                    atm[:, g * 128:(g + 1) * 128], id_f[:, :]),
                    after=[dn[2 * g + 1]] + ([at[-1]] if at else [mmz[-1]]))
                te = S_(lambda e, g=g, cc=cc, bank=bank: e.copy(
                    attnfm[:, g, cc * CH:(cc + 1) * CH], bank[:, (g % 4) * 128:(g % 4) * 128 + 128]),
                    after=[tt])
                at.append(te)
            last = [ru[-1], at[-1], dn[-1]]
        attn_done = last[1]
        if "attn_d" in dbg:
            DMA_("sync", lambda eng, sem, t=t: eng.dma_start(
                out=AP(dbg["attn_d"], t * TILE, [[TC, 128], [128 * TC, NDC], [1, TILE]]),
                in_=AP(attnfm, 0, [[NDC * TILE, 128], [TILE, NDC], [1, TILE]])).then_inc(sem, 16),
                n=1, after=last)

        oT = []
        for co in range(NDC):
            bank = psB[7]
            mm = []
            for c in range(NDC):
                mm.append(T_(lambda e, c=c, co=co, bank=bank: e.matmul(
                    bank[:, :], wo[:, c, co * 128:(co + 1) * 128], attnfm[:, c, :],
                    start=(c == 0), stop=(c == NDC - 1)),
                    after=(([tok_wo, attn_done] + last) if (co == 0 and c == 0) else [])
                    + ([mm[-1]] if mm else (oT[-1:] if oT else []))))
            ev = V_(lambda e, co=co, bank=bank, sl=t % 2: e.scalar_tensor_tensor(
                x1b[:, co, sl, :], bank[:, :], obC[:, co:co + 1], xnfm2[:, co, :],
                ALU.add, ALU.add), after=[mm[-1], tok_l]
                + ([stq[("x1", t + 2)]] if (co == 0 and ("x1", t + 2) in stq) else []))
            oT.append(ev)

        def st_x1(eng, sem, sl=t % 2):
            eng.dma_start(out=AP(x1_s, t * TILE, [[TC, 128], [128 * TC, NDC], [1, TILE]]),
                          in_=x1b[:, :, sl, :]).then_inc(sem, 16)
        DMA_("sync", st_x1, n=1, after=oT)
        tok_x1st = bld.fence("sync")
        stq[("x1", t)] = tok_x1st
        if "x1_d" in dbg:
            DMA_("sync", lambda eng, sem, t=t: eng.dma_start(
                out=AP(dbg["x1_d"], t * TILE, [[TC, 128], [128 * TC, NDC], [1, TILE]]),
                in_=x1b[:, :, t % 2, :]).then_inc(sem, 16), n=1, after=oT)

        xq_toks = []
        for co in range(NDC):
            t1 = S_(lambda e, co=co: e.activation(scr2[:, :],
                                                  xnfm2[:, co, :], AF.Identity,
                                                  bias=MAGIC, scale=rs1_col[:, 0:1]),
                    after=[tok_l, tb1e] + ([xq_toks[-1]] if xq_toks else []))
            t2 = V_(lambda e, co=co, sl=t % 2: (
                e.tensor_scalar(scr2[:, :], scr2[:, :], MAGIC, QMAX,
                                ALU.subtract, ALU.min),
                e.tensor_scalar(xqb[:, co, sl, :], scr2[:, :], -QMAX - 1.0, None, ALU.max))[-1],
                after=[t1]
                + ([stq[("xq", t + 2)]] if (co == 0 and ("xq", t + 2) in stq) else []))
            xq_toks.append(t2)

        def st_xq(eng, sem, sl=t % 2):
            eng.dma_start(out=AP(xq_s, t * TILE, [[TC, 128], [128 * TC, NDC], [1, TILE]]),
                          in_=xqb[:, :, sl, :]).then_inc(sem, 16)
        DMA_("sync", st_xq, n=1, after=[xq_toks[-1]])
        tok_xqst = bld.fence("sync")
        stq[("xq", t)] = tok_xqst
        if "xq_d" in dbg:
            DMA_("sync", lambda eng, sem, t=t: eng.dma_start(
                out=AP(dbg["xq_d"], t * TILE, [[TC, 128], [128 * TC, NDC], [1, TILE]]),
                in_=xqb[:, :, t % 2, :]).then_inc(sem, 16),
                n=1, after=[xq_toks[-1]])
        return [oT[-1], xq_toks[-1]] + last

    st2kl = {}

    def emit_kl(ch, after):
        def ld_kl(eng, sem, ch=ch):
            eng.dma_start(out=kcvL[:, ch % 2, :],
                          in_=AP(kcv_s, ch * 128 * NPAIR * KVE,
                                 [[NPAIR * KVE, 128], [1, NPAIR * KVE]])).then_inc(sem, 16)
        st2kl[ch] = DMA_("sync", ld_kl, n=1, after=after)

    emit_kl(NCH - 1, [])
    emit_kl(NCH - 2, [])
    gate2 = [tok_Rinit, tok_comb, tb1e]
    for t in range(NT - 1, -1, -1):
        gate2 = p2_tile(t, gate2)
    q_p2 = bld.quiesce("sync")
    bld.flush("p2")
    if PHASE_LIMIT == "p2":
        return nc
    es2.close()
    es_kcv.close()

    es4 = ExitStack()
    wgT = SB("wgT", [128, NDC, IFF], BF16, es4)
    wuT = SB("wuT", [128, NDC, IFF], BF16, es4)
    xqL = [SB(f"xqL{i}", [128, NDC, TILE], BF16, es4) for i in range(2)]
    silb = SB("silb", [128, TILE], F32, es4)
    h2b = [SB(f"h2b{i}", [128, TILE], F32, es4) for i in range(4)]
    sgv = SB("sgv", [128, NIC], F32, es4)
    suv = SB("suv", [128, NIC], F32, es4)
    psC = [PS(f"psC{i}", es4) for i in range(4)]

    def p4_w(eng, sem):
        eng.dma_start(out=wgT[:, :, :],
                      in_=AP(wgT_s, 0, [[IFF, 128], [128 * IFF, NDC], [1, IFF]])).then_inc(sem, 16)
        eng.dma_start(out=wuT[:, :, :],
                      in_=AP(wuT_s, 0, [[IFF, 128], [128 * IFF, NDC], [1, IFF]])).then_inc(sem, 16)
    tok_w4 = DMA_("gpsimd", p4_w, n=2, after=[q_x1])
    tok_sv = V_(lambda e: (e.tensor_scalar(sgv[:, :], sg_sc[:, 0:NIC], sc1_col[:, 0:1], None, ALU.mult),
                           e.tensor_scalar(suv[:, :], su_sc[:, 0:NIC], sc1_col[:, 0:1], None, ALU.mult))[-1])

    def p4_tile(t, gate):
        xq_ = xqL[t % 2]

        def ld_xq(eng, sem):
            eng.dma_start(out=xq_[:, :, :],
                          in_=AP(xq_s, t * TILE, [[TC, 128], [128 * TC, NDC], [1, TILE]])
                          ).then_inc(sem, 16)
        tl = DMA_("sync", ld_xq, n=1, after=gate + [q_p2])
        sil_toks, h2_toks, mx_toks, st_toks, gmm_toks, umm_toks = [], [], [], [], [], []
        for ic in range(NIC):
            h2slot = (t * NIC + ic) % 4
            bg = psC[ic % 2]
            bu = psC[2 + ic % 2]
            mmg = []
            for c in range(NDC):
                dep = []
                if ic == 0 and c == 0:
                    dep = [tok_w4, tl] + gate
                elif c == 0 and ic >= 2:
                    dep = [sil_toks[ic - 2]]
                mmg.append(T_(lambda e, c=c, ic=ic, bg=bg, xq_=xq_: e.matmul(
                    bg[:, :], wgT[:, c, ic * 128:(ic + 1) * 128], xq_[:, c, :],
                    start=(c == 0), stop=(c == NDC - 1)),
                    after=dep + ([mmg[-1]] if mmg else [])))
            gmm_toks.append(mmg[-1])
            mmu = []
            for c in range(NDC):
                dep = [mmg[-1]] if c == 0 else []
                if c == 0 and ic >= 2:
                    dep = dep + [h2_toks[ic - 2]]
                mmu.append(T_(lambda e, c=c, ic=ic, bu=bu, xq_=xq_: e.matmul(
                    bu[:, :], wuT[:, c, ic * 128:(ic + 1) * 128], xq_[:, c, :],
                    start=(c == 0), stop=(c == NDC - 1)),
                    after=dep + ([mmu[-1]] if mmu else [])))
            umm_toks.append(mmu[-1])
            tsil = S_(lambda e, ic=ic, bg=bg: e.activation(
                silb[:, :], bg[:, :], AF.Silu, scale=sgv[:, ic:ic + 1]),
                after=[mmg[-1], tok_sv] + ([h2_toks[-1]] if h2_toks else []))
            sil_toks.append(tsil)
            h2x = h2b[h2slot]
            dep_h2 = [mmu[-1], tsil] + stq.get(("h2s", h2slot), [])
            th2 = V_(lambda e, ic=ic, bu=bu, h2x=h2x: e.scalar_tensor_tensor(
                h2x[:, :], bu[:, :], suv[:, ic:ic + 1], silb[:, :], ALU.mult, ALU.mult),
                after=dep_h2)
            h2_toks.append(th2)
            tmx2 = V_(lambda e, h2x=h2x: (
                e.tensor_reduce(colA[:, 44:45], h2x[:, :], AX.X, ALU.max,
                                apply_absolute_value=True),
                e.drain(),
                e.tensor_max(mh_col[:, :], mh_col[:, :], colA[:, 44:45]))[-1],
                after=[th2])
            mx_toks.append(tmx2)

            def st_h2(eng, sem, ic=ic, t=t, h2x=h2x):
                eng.dma_start(out=AP(h2_s, ic * 128 * TC + t * TILE, [[TC, 128], [1, TILE]]),
                              in_=h2x[:, :]).then_inc(sem, 16)
            DMA_("sync", st_h2, n=1, after=[th2])
            tst = bld.fence("sync")
            stq[("h2s", h2slot)] = [tst, tmx2]
            if "h2_d" in dbg:
                DMA_("sync", lambda eng, sem, ic=ic, t=t, h2x=h2x: eng.dma_start(
                    out=AP(dbg["h2_d"], ic * 128 * TC + t * TILE, [[TC, 128], [1, TILE]]),
                    in_=h2x[:, :]).then_inc(sem, 16), n=1, after=[th2])
            st_toks.append(tst)
        return [st_toks[-1], st_toks[-2], mx_toks[-1], mx_toks[-2], h2_toks[-1]]

    gate4 = [tok_sv]
    for t in range(NT):
        gate4 = p4_tile(t, gate4)
    row4 = SB("row4", [1, 192], F32, es4)
    tm2T = T_(lambda e: e.transpose(psC[0][0:1, 0:128], mh_col[:, :], id_f[:, :]), after=gate4)
    tm2E = V_(lambda e: (e.tensor_copy(row4[0:1, 0:128], psC[0][0:1, 0:128]),
                         e.memset(rowA[0:1, 29:44], 0.0),
                         e.drain(),
                         e.tensor_reduce(rowA[0:1, 28:29], row4[0:1, 0:128], AX.X, ALU.max))[-1],
              after=[tm2T] + gate4)

    def st_ag2(eng, sem):
        eng.dma_start(out=AP(ag2_in, 0, [[1, 16], [1, 1]]), in_=rowA[0:1, 28:44]).then_inc(sem, 16)
    tok_ag2 = DMA_("gpsimd", st_ag2, n=1, after=[tm2E])
    tok_cc2 = bld.g_async(lambda e: e.collective_compute(
        "AllGather", ALU.bypass, replica_groups=[list(range(8))],
        ins=[ag2_in.ap().opt()], outs=[ag2_out.ap().opt()]), after=[tok_ag2])
    q_p4 = bld.quiesce("sync")
    bld.flush("p4")
    if PHASE_LIMIT == "p4":
        return nc
    es4.close()

    es3 = ExitStack()
    wf1 = SB("wf1", [128, NDC, FFN], BF16, es3)
    wf2 = SB("wf2", [128, NFC, D], BF16, es3)
    f1bC = SB("f1bC", [128, NFC], F32, es3)
    x1L = SB("x1L", [128, NDC, TILE], F32, es3)
    x2b = SB("x2b", [128, NDC, TILE], BF16, es3)
    h1b = SB("h1b", [128, NFC, TILE], BF16, es3)
    totc = [SB(f"totc{i}", [128, TILE], F32, es3) for i in range(4)]
    sqs3 = SB("sqs3", [128, TILE], F32, es3)
    row3 = SB("row3", [1, TILE], F32, es3)
    psD = [PS(f"psD{i}", es3) for i in range(8)]

    def p3_w(eng, sem):
        eng.dma_start(out=wf1[:, :, :],
                      in_=AP(win["L0_attn_f1_w"], 0, [[FFN, 128], [128 * FFN, NDC], [1, FFN]])
                      ).then_inc(sem, 16)
        eng.dma_start(out=wf2[:, :, :],
                      in_=AP(win["L0_attn_f2_w"], 0, [[D, 128], [128 * D, NFC], [1, D]])
                      ).then_inc(sem, 16)
    tok_w3 = DMA_("gpsimd", p3_w, n=2)
    fmm = []
    for cp in range(NFC):
        mms = []
        for c in range(NDC):
            mms.append(T_(lambda e, c=c, cp=cp: e.matmul(
                psD[6][:, cp:cp + 1], wf1[:, c, cp * 128:(cp + 1) * 128],
                b2cb[:, c:c + 1], start=(c == 0), stop=(c == NDC - 1)),
                after=([tok_w3] if (cp == 0 and c == 0) else []) + ([mms[-1]] if mms else ([fmm[-1]] if fmm else []))))
        fmm.append(mms[-1])
    f1bc_toks = []
    for rnd in range(4):
        def ldf1b(eng, sem, rnd=rnd):
            eng.dma_start(out=rowsP[0:1, 0:1024],
                          in_=AP(win["L0_attn_f1_b"], rnd * 1024, [[1024, 1], [1, 1024]])
                          ).then_inc(sem, 16)
        tl = DMA_("sync", ldf1b, n=1, after=f1bc_toks[-1:] if f1bc_toks else [fmm[-1]])
        tts = []
        for c in range(8):
            tts.append(T_(lambda e, c=c, rnd=rnd: e.transpose(
                psD[7][:, rnd * 8 + c:rnd * 8 + c + 1],
                rowsP[0:1, c * 128:(c + 1) * 128], id_f[0:1, 0:1]),
                after=[tl, fmm[-1]] + (tts[-1:] if tts else (f1bc_toks[-1:] if f1bc_toks else []))))
        f1bc_toks.append(tts[-1])
    tf1b0 = V_(lambda e: e.tensor_copy(f1bC[:, :], psD[7][:, 0:NFC]),
               after=[f1bc_toks[-1], fmm[-1]])
    tf1b = V_(lambda e: e.tensor_add(f1bC[:, :], f1bC[:, :], psD[6][:, 0:NFC]),
              after=[tf1b0])
    bld.flush("p3head")

    def p3_tile(t, gate):
        def ld_x1(eng, sem):
            eng.dma_start(out=x1L[:, :, :],
                          in_=AP(x1_s, t * TILE, [[TC, 128], [128 * TC, NDC], [1, TILE]])
                          ).then_inc(sem, 16)
        tl = DMA_("sync", ld_x1, n=1, after=gate + [q_p2])
        stat_toks = []
        for c in range(NDC):
            stat_toks.append(T_(lambda e, c=c: e.matmul(
                psD[4][0:1, 0:512], ones_col[:, 0:1], x1L[:, c, :],
                start=(c == 0), stop=(c == NDC - 1)),
                after=([tl] + gate) if c == 0 else []))
        sq_toks = []
        for c in range(NDC):
            tq = S_(lambda e, c=c: e.activation(sqs3[:, :], x1L[:, c, :], AF.Square),
                    after=[tl] + (sq_toks[-1:] if sq_toks else []))
            tqm = T_(lambda e, c=c: e.matmul(
                psD[5][0:1, 0:512], ones_col[:, 0:1], sqs3[:, :],
                start=(c == 0), stop=(c == NDC - 1)),
                after=[tq, stat_toks[-1]] + (sq_toks[-1:] if sq_toks else []))
            sq_toks.append(tqm)
        trow = V_(lambda e: (
            e.tensor_scalar(row3[0:1, 0:512], psD[4][0:1, 0:512], 1.0 / D, None, ALU.mult),
            e.tensor_scalar(rowsP[0:1, 0:512], psD[5][0:1, 0:512], 1.0 / D, None, ALU.mult),
            e.tensor_mul(rowsP[0:1, 512:1024], row3[0:1, 0:512], row3[0:1, 0:512]),
            e.tensor_sub(rowsP[0:1, 0:512], rowsP[0:1, 0:512], rowsP[0:1, 512:1024]))[-1],
            after=[sq_toks[-1], stat_toks[-1]])
        trl = S_(lambda e: (e.activation(rowsP[0:1, 0:512], rowsP[0:1, 0:512], AF.Ln, bias=LN_EPS),
                            e.activation(rowsP[0:1, 0:512], rowsP[0:1, 0:512], AF.Exp, scale=-0.5))[-1],
                 after=[trow])
        trm = V_(lambda e: e.tensor_mul(rowsP[0:1, 512:1024], row3[0:1, 0:512], rowsP[0:1, 0:512]),
                 after=[trl])
        tgq = T_(lambda e: (e.matmul(psD[4][:, :], ones_row[:, :], rowsP[0:1, 0:512],
                                     start=True, stop=True),
                            e.matmul(psD[5][:, :], ones_row[:, :], rowsP[0:1, 512:1024],
                                     start=True, stop=True))[-1], after=[trm])
        x2toks = []
        for c in range(NDC):
            x2toks.append(V_(lambda e, c=c: (
                e.tensor_mul(sqs3[:, :], x1L[:, c, :], psD[4][:, :]),
                e.tensor_sub(x2b[:, c, :], sqs3[:, :], psD[5][:, :]))[-1],
                after=[tgq] + (x2toks[-1:] if x2toks else [])))
        x2done = x2toks[-1]
        h1toks = []
        for fo in range(NFC):
            bank = psD[fo % 2]
            mm = []
            for c in range(NDC):
                mm.append(T_(lambda e, c=c, fo=fo, bank=bank: e.matmul(
                    bank[:, :], wf1[:, c, fo * 128:(fo + 1) * 128], x2b[:, c, :],
                    start=(c == 0), stop=(c == NDC - 1)),
                    after=([x2done] if (fo == 0 and c == 0) else [])
                    + ([mm[-1]] if mm else (h1toks[-2:-1] if len(h1toks) >= 2 else []))))
            te = S_(lambda e, fo=fo, bank=bank: e.activation(
                h1b[:, fo, :], bank[:, :], AF.Gelu, bias=f1bC[:, fo:fo + 1]),
                after=[mm[-1]])
            h1toks.append(te)
        h1done = h1toks[-1]
        f2toks = []
        for co in range(NDC):
            mm = []
            for fc in range(NFC):
                mm.append(T_(lambda e, fc=fc, co=co: e.matmul(
                    psD[co][:, :], wf2[:, fc, co * 128:(co + 1) * 128], h1b[:, fc, :],
                    start=(fc == 0), stop=(fc == NFC - 1)),
                    after=([h1done] if (co == 0 and fc == 0) else [])
                    + ([mm[-1]] if mm else stq.get(("f2bank", co), []))))
            ev = V_(lambda e, co=co: e.scalar_tensor_tensor(
                totc[co % 4][:, :], psD[co][:, :], f2bC[:, co:co + 1], x1L[:, co, :],
                ALU.add, ALU.add),
                after=[mm[-1]] + stq.get(("tots", co % 4), []))
            stq[("f2bank", co)] = [ev]

            def st_tot(eng, sem, co=co, t=t):
                eng.dma_start(out=AP(tot_s, co * 128 * TC + t * TILE, [[TC, 128], [1, TILE]]),
                              in_=totc[co % 4][:, :]).then_inc(sem, 16)
            DMA_("sync", st_tot, n=1, after=[ev])
            td = bld.fence("sync")
            stq[("tots", co % 4)] = [td]
            if "tot_d" in dbg:
                DMA_("sync", lambda eng, sem, co=co, t=t: eng.dma_start(
                    out=AP(dbg["tot_d"], co * 128 * TC + t * TILE, [[TC, 128], [1, TILE]]),
                    in_=totc[co % 4][:, :]).then_inc(sem, 16), n=1, after=[ev])
            f2toks += [ev, td]
        return [f2toks[-1], f2toks[-2], h1done]

    gate3 = [tf1b]
    f1sc2 = []
    for c in range(NDC):
        f1sc2.append(V_(lambda e, c=c: e.tensor_scalar(
            wf1[:, c, :], wf1[:, c, :], g2c[:, c:c + 1], None, ALU.mult),
            after=[tf1b] + (f1sc2[-1:] if f1sc2 else [])))
    gate3 = [f1sc2[-1]]
    for t in range(NT):
        gate3 = p3_tile(t, gate3)
    q_p3 = bld.quiesce("sync")
    bld.flush("p3")
    if PHASE_LIMIT == "p3":
        return nc
    es3.close()

    es5 = ExitStack()
    wdT = SB("wdT", [128, NIC, D], BF16, es5)
    h2L = [SB(f"h2L{i}", [128, TILE], F32, es5) for i in range(4)]
    h2q = [SB(f"h2q{i}", [128, TILE], BF16, es5) for i in range(4)]
    totLs = [SB(f"totL{i}", [128, NDC, TILE], F32, es5) for i in range(2)]
    fintm = SB("fintm", [128, NCHT, D], F32, es5)
    xrl = [SB(f"xrl{i}", [128, NCHT, D], F32, es5) for i in range(2)]
    outb = SB("outb", [128, NCHT, D], F32, es5)
    psE = [PS(f"psE{i}", es5) for i in range(8)]

    def p5_w(eng, sem):
        eng.dma_start(out=wdT[:, :, :],
                      in_=AP(wdT_s, 0, [[D, 128], [128 * D, NIC], [1, D]])).then_inc(sem, 16)
    tok_w5 = DMA_("gpsimd", p5_w, n=1, after=[q_x1])
    tok_cc2w = G_(lambda e: e.nop(), after=[tok_cc2])

    def ld_m2(eng, sem):
        eng.dma_start(out=colA[0:8, 45:46],
                      in_=AP(ag2_out, 0, [[16, 8], [1, 1]]),
                      allow_slow_non_contiguous=True).then_inc(sem, 16)
    tok_m2l = DMA_("sync", ld_m2, n=1, after=[tok_cc2w])
    tm2T2 = T_(lambda e: e.transpose(psE[0][0:1, 0:8], colA[0:8, 45:46], id_f[0:8, 0:8]),
               after=[tok_m2l])
    tm2f = V_(lambda e: (e.tensor_copy(rowA[0:1, 40:48], psE[0][0:1, 0:8]),
                         e.drain(),
                         e.tensor_reduce(rowA[0:1, 29:30], rowA[0:1, 40:48], AX.X, ALU.max),
                         e.drain(),
                         e.tensor_mul(rowA[0:1, 30:31], rowA[0:1, 29:30], clipc[0:1, 1:2]),
                         e.drain(),
                         e.tensor_scalar(rowA[0:1, 31:32], rowA[0:1, 30:31], 1.0 / QMAX, None, ALU.mult),
                         e.drain(),
                         e.reciprocal(rowA[0:1, 33:34], rowA[0:1, 31:32]))[-1],
              after=[tm2T2])
    if "m2_d" in dbg:
        DMA_("sync", lambda eng, sem: eng.dma_start(
            out=dbg["m2_d"].ap(), in_=rowA[0:1, 24:40]).then_inc(sem, 16), n=1, after=[tm2f])
    tb2x = T_(lambda e: (e.matmul(psE[1][:, 0:1], ones_row[:, :], rowA[0:1, 31:32],
                                  start=True, stop=True),
                         e.matmul(psE[1][:, 1:2], ones_row[:, :], rowA[0:1, 33:34],
                                  start=True, stop=True))[-1], after=[tm2f])
    tb2e = V_(lambda e: (e.tensor_copy(sc2_col[:, :], psE[1][:, 0:1]),
                         e.tensor_copy(rs2_col[:, :], psE[1][:, 1:2]))[-1], after=[tb2x])
    tsdv = V_(lambda e: e.tensor_scalar(sd_sc[:, 0:NDC], sd_sc[:, 0:NDC], sc2_col[:, 0:1],
                                        None, ALU.mult), after=[tb2e])

    st5 = {
        "h2slot": [[] for _ in range(4)],
        "totslot": [[] for _ in range(2)],
        "bank": [[] for _ in range(NDC)],
        "fintm": [],
        "outb": [],
        "xr": [[] for _ in range(2)],
    }

    def p5_tile(t, gate):
        tsl = t % 2
        totL = totLs[tsl]

        def ld_tot(eng, sem):
            eng.dma_start(out=totL[:, :, :],
                          in_=AP(tot_s, t * TILE, [[TC, 128], [128 * TC, NDC], [1, TILE]])
                          ).then_inc(sem, 16)
        tlt = DMA_("sync", ld_tot, n=1,
                   after=st5["totslot"][tsl] + (gate if t == 0 else []) + [q_p3])
        xr = xrl[t % 2]

        def ld_x(eng, sem):
            eng.dma_start(out=xr[:, :, :],
                          in_=AP(x_in, t * TILE * D, [[D, 128], [128 * D, NCHT], [1, D]])
                          ).then_inc(sem, 16)
        tlx = DMA_("sync", ld_x, n=1, after=st5["xr"][t % 2] + (gate if t == 0 else []))

        ic_toks = []
        fin = [None] * NDC
        for ic in range(NIC):
            slot = (t * NIC + ic) % 4
            hb = h2L[slot]
            hq = h2q[slot]

            def ld_h2(eng, sem, ic=ic, t=t, hb=hb):
                eng.dma_start(out=hb[:, :],
                              in_=AP(h2_s, ic * 128 * TC + t * TILE, [[TC, 128], [1, TILE]])
                              ).then_inc(sem, 16)
            tlh = DMA_("sync", ld_h2, n=1,
                       after=st5["h2slot"][slot] + [q_p4]
                       + ([tb2e] if t == 0 and ic == 0 else []))
            tq1 = S_(lambda e, hb=hb: e.activation(hb[:, :], hb[:, :], AF.Identity,
                                                   bias=MAGIC, scale=rs2_col[:, 0:1]),
                     after=[tlh, tb2e])
            tq2 = V_(lambda e, hb=hb, hq=hq: (
                e.tensor_scalar(hb[:, :], hb[:, :], MAGIC, QMAX, ALU.subtract, ALU.min),
                e.tensor_scalar(hq[:, :], hb[:, :], -QMAX - 1.0, None, ALU.max))[-1],
                after=[tq1])
            mmd = []
            for co in range(NDC):
                mmd.append(T_(lambda e, co=co, ic=ic, hq=hq: e.matmul(
                    psE[co][:, :], wdT[:, ic, co * 128:(co + 1) * 128], hq[:, :],
                    start=(ic == 0), stop=(ic == NIC - 1)),
                    after=[tq2, tok_w5]
                    + (st5["bank"][co] if ic == 0 else [])
                    + ([mmd[-1]] if mmd else (ic_toks[-1:] if ic_toks else []))))
            st5["h2slot"][slot] = [mmd[-1], tq2]
            ic_toks += [tq2, mmd[-1]]
        for co in range(NDC):
            ev = V_(lambda e, co=co, totL=totL: e.scalar_tensor_tensor(
                totL[:, co, :], psE[co][:, :], sd_sc[:, co:co + 1], totL[:, co, :],
                ALU.mult, ALU.add), after=[ic_toks[-1], tsdv, tlt])
            fin[co] = ev
        ttk = []
        for jj in range(NCHT):
            for co in range(NDC):
                bank = psE[co % 2]
                tt = T_(lambda e, co=co, jj=jj, bank=bank, totL=totL: e.transpose(
                    bank[:, 0:128], totL[:, co, jj * 128:(jj + 1) * 128], id_f[:, :]),
                    after=[fin[co]] + (ttk[-1:] if ttk else [fin[-1]]))
                te = S_(lambda e, co=co, jj=jj, bank=bank: e.copy(
                    fintm[:, jj, co * 128:(co + 1) * 128], bank[:, 0:128]),
                    after=[tt] + st5["fintm"])
                ttk.append(te)
        te_all = ttk
        for co in range(NDC):
            st5["bank"][co] = [ttk[-1]]
        st5["totslot"][tsl] = [ttk[-1]]
        tadd = V_(lambda e: e.tensor_add(outb[:, :, :], fintm[:, :, :], xr[:, :, :]),
                  after=te_all[-1:] + [tlx] + st5["outb"])
        st5["fintm"] = [tadd]
        st5["xr"][t % 2] = [tadd]

        def st_out(eng, sem, t=t):
            eng.dma_start(out=AP(out_t, t * TILE * D, [[D, 128], [128 * D, NCHT], [1, D]]),
                          in_=outb[:, :, :]).then_inc(sem, 16)
        tso = DMA_("sync", st_out, n=1, after=[tadd])
        st5["outb"] = [bld.fence("sync")]
        return [tso, tadd]

    gate5 = [tsdv]
    for t in range(NT):
        gate5 = p5_tile(t, gate5)
    V_(lambda e: e.nop(), after=gate5)
    bld.flush("p5")
    es5.close()
    es_persist.close()
    return nc


def kernel(**inputs):
    x_full = np.ascontiguousarray(inputs["L0_attn_x"])
    nc = build_nc()
    in_maps = []
    for cid in range(NCORE):
        b, t0 = cid // 4, (cid % 4) * TC
        m = {"x_in": np.ascontiguousarray(x_full[b, t0:t0 + TC])}
        for k, v in inputs.items():
            if k == "L0_attn_x":
                continue
            m[k] = np.ascontiguousarray(np.asarray(v))
        my = cid % 4
        pf = np.zeros((8, 8), np.float32)
        pb = np.zeros((8, 8), np.float32)
        mf = np.zeros((8, 1), np.float32)
        mb = np.zeros((8, 1), np.float32)
        grp = [4 * (cid // 4) + i for i in range(4)]
        for i, c2 in enumerate(grp):
            if i < my:
                mf[c2, 0] = 1.0
                for j in range(i + 1, my):
                    pf[grp[j], c2] = 1.0
            if i > my:
                mb[c2, 0] = 1.0
                for j in range(my + 1, i):
                    pb[grp[j], c2] = 1.0
        m["cc_pf"], m["cc_pb"], m["cc_mf"], m["cc_mb"] = pf, pb, mf, mb
        in_maps.append(m)
    res = run_bass_kernel_spmd(nc, in_maps, core_ids=list(range(NCORE)))
    out = np.empty((B, T, D), np.float32)
    for cid in range(NCORE):
        b, t0 = cid // 4, (cid % 4) * TC
        out[b, t0:t0 + TC] = res.results[cid]["out"]
    return out


# version 19
# speedup vs baseline: 2.3412x; 1.0671x over previous
import numpy as np
import ml_dtypes
from contextlib import ExitStack
import concourse.bass as bass
import concourse.mybir as mybir
from concourse.bass_utils import run_bass_kernel_spmd

F32 = mybir.dt.float32
BF16 = mybir.dt.bfloat16
AF = mybir.ActivationFunctionType
ALU = mybir.AluOpType
AX = mybir.AxisListType
AP = bass.AP

B, T, D, H, NK, KVE = 2, 16384, 1024, 16, 64, 65
TC, TILE, CH = 4096, 512, 128
NT = TC // TILE
NCHT = TILE // CH
NCH = TC // CH
NDC = D // 128
NPAIR = H // 2
FFN, IFF = 4096, 2816
NFC = FFN // 128
NIC = IFF // 128
LN_EPS = 1e-5
EPS_DEN = 1e-6
LAM_MIN = 1e-8
QMAX = 127.0
MAGIC = 12582912.0
PAY_SEC = H * NK * KVE
PAY = 2 * PAY_SEC + 32
NCORE = 8
PHASE_LIMIT = None
POS = [(hh % 2) * 8 + hh // 2 for hh in range(16)]
HORDER = [0] * 16
for _hh in range(16):
    HORDER[POS[_hh]] = _hh
P1_CUT = None
LNJS = None
ENGS = ("tensor", "vector", "scalar", "gpsimd", "sync")


class Bld:

    SEMNAMES = ("tensor", "vector", "scalar", "gpsimd", "sync", "gdma", "cc",
                "syncb")
    OWNER = {"tensor": "tensor", "vector": "vector", "scalar": "scalar",
             "gpsimd": "gpsimd", "sync": "sync", "gdma": "gpsimd", "cc": "gpsimd",
             "syncb": "sync"}

    NPOOL = {"sync": 48, "gpsimd": 16}

    def __init__(self, nc):
        self.nc = nc
        self.sems = {s: nc.alloc_semaphore(name=f"sem_{s}") for s in self.SEMNAMES}
        self.count = {s: 0 for s in self.SEMNAMES}
        self.groups = {e: [] for e in ENGS}
        self.pool = {}
        self.pool_count = {}
        for eng, npool in self.NPOOL.items():
            for k in range(npool):
                s = f"dp_{eng}_{k}"
                self.pool[s] = nc.alloc_semaphore(name=s)
                self.pool_count[s] = 0
                self.count[s] = 0
        self.sems.update(self.pool)
        self.pool_idx = {e: 0 for e in self.NPOOL}
        self.prev_flush_counts = {}
        self.last_dma = None

    def _add(self, eng, semname, fn, after, inc, is_async=False):
        self.count[semname] += inc
        self.groups[eng].append((fn, tuple(after), inc, semname))
        return (semname, self.count[semname], is_async)

    def t(self, fn, after=()):
        return self._add("tensor", "tensor", fn, after, 1)

    def v(self, fn, after=()):
        return self._add("vector", "vector", fn, after, 1)

    def s(self, fn, after=()):
        return self._add("scalar", "scalar", fn, after, 1)

    def g(self, fn, after=()):
        return self._add("gpsimd", "gpsimd", fn, after, 1)

    def dma(self, eng, fn, n=1, after=()):
        k = self.pool_idx[eng]
        self.pool_idx[eng] = (k + 1) % self.NPOOL[eng]
        semname = f"dp_{eng}_{k}"
        drain = self.count[semname]

        def wrap(e, sem, fn=fn, drain=drain):
            if drain:
                e.wait_ge(sem, drain)
            fn(e, sem)
        tok = self._add(eng, semname, wrap, after, 16 * n, is_async=True)
        self.last_dma = tok
        return tok

    def g_async(self, fn, after=()):
        return self._add("gpsimd", "cc", fn, after, 1, is_async=True)

    def fence(self, queue="sync"):
        return self.last_dma

    def quiesce(self, semname="sync"):
        return ("syncb", 0, True)

    def flush(self, name=None):
        nc = self.nc
        groups = {e: list(self.groups[e]) for e in ENGS}
        self.groups = {e: [] for e in ENGS}
        if not any(groups.values()):
            return
        qprev = {s: v for s, v in self.prev_flush_counts.items() if v > 0}
        self.prev_flush_counts = {s: self.count[s] for s in self.pool}
        with nc.Block(name) as blk:
            def mk(ename, glist):
                def body(eng):
                    lastw = {}
                    for qs, qv in qprev.items():
                        eng.wait_ge(self.sems[qs], qv)
                        lastw[qs] = qv
                    for fn, after, inc, semname in glist:
                        for tok in after:
                            ps, val, is_async = tok
                            if ename == "tensor" and self.OWNER.get(ps) == "tensor" \
                                    and not is_async:
                                continue
                            if lastw.get(ps, -1) >= val:
                                continue
                            eng.wait_ge(self.sems[ps], val)
                            lastw[ps] = val
                        mysem = self.sems[semname]
                        if inc >= 16:
                            fn(eng, mysem)
                        else:
                            last = fn(eng)
                            (last if last is not None else eng.nop()).then_inc(mysem, 1)
                return body
            for ename in ENGS:
                if groups[ename]:
                    getattr(blk, ename)(mk(ename, groups[ename]))


def _consts():
    c = {}
    c["id_f"] = np.eye(128, dtype=np.float32)
    c["id_b"] = np.eye(128, dtype=ml_dtypes.bfloat16)
    c["ones_row_f"] = np.ones((1, 128), np.float32)
    c["ones_col_f"] = np.ones((128, 1), np.float32)
    c["ones16_b"] = np.ones((16, 128), ml_dtypes.bfloat16)
    c["ones16_f"] = np.ones((16, 128), np.float32)
    sp = np.zeros((16, NPAIR * 128), np.float32)
    for g in range(NPAIR):
        sp[2 * g, g * 128: g * 128 + 64] = 1.0
        sp[2 * g + 1, g * 128 + 64: (g + 1) * 128] = 1.0
    c["sel_pair_b"] = sp.astype(ml_dtypes.bfloat16)
    c["sel_pair_f"] = sp
    sh = np.zeros((16, 128), np.float32)
    for h in range(16):
        sh[h, 0:64] = 1.0 if h % 2 == 0 else 0.0
        sh[h, 64:128] = 1.0 if h % 2 == 1 else 0.0
    c["sel_half_f"] = sh
    pm = np.zeros((16, NPAIR), np.float32)
    for h in range(16):
        pm[h, h // 2] = 1.0
    c["pair_mask_f"] = pm
    hm = np.zeros((16, 16 * 128), np.float32)
    for p_ in range(16):
        hm[HORDER[p_], p_ * 128:(p_ + 1) * 128] = 1.0
    c["head_mask_f"] = hm.astype(ml_dtypes.bfloat16)
    return c


def bcv(ap2d, reps):
    base = ap2d
    a = list(base.ap)
    new = [list(a[0]), [0, reps]] + [list(x) for x in a[1:]]
    return AP(base.tensor, base.offset, new)


def bccol(ap_col, reps):
    a = list(ap_col.ap)
    return AP(ap_col.tensor, ap_col.offset, [list(a[0]), [0, reps]])


def build_nc(dbg_names=()):
    nc = bass.Bass("TRN2", target_bir_lowering=False, detect_race_conditions=False)
    bld = Bld(nc)
    Cst = _consts()
    T_, V_, S_, G_, DMA_ = bld.t, bld.v, bld.s, bld.g, bld.dma

    x_in = nc.dram_tensor("x_in", [TC, D], F32, kind="ExternalInput")
    win = {}
    for nm, shp in [
        ("L0_attn_q_w", [D, D]), ("L0_attn_k_w", [D, D]), ("L0_attn_v_w", [D, D]),
        ("L0_attn_o_w", [D, D]), ("L0_attn_dec_w", [D, H]),
        ("L0_attn_f1_w", [D, FFN]), ("L0_attn_f2_w", [FFN, D]),
        ("L0_ffn_w_gate", [IFF, D]), ("L0_ffn_w_up", [IFF, D]), ("L0_ffn_w_down", [D, IFF]),
        ("L0_attn_q_b", [D]), ("L0_attn_k_b", [D]), ("L0_attn_v_b", [D]),
        ("L0_attn_o_b", [D]), ("L0_attn_dec_b", [H]), ("L0_attn_f1_b", [FFN]),
        ("L0_attn_f2_b", [D]),
        ("L0_attn_ln1_w", [D]), ("L0_attn_ln1_b", [D]),
        ("L0_attn_ln2_w", [D]), ("L0_attn_ln2_b", [D]),
        ("L0_ffn_clip_x", [1]), ("L0_ffn_clip_down", [1]),
        ("cc_pf", [8, 8]), ("cc_pb", [8, 8]), ("cc_mf", [8, 1]), ("cc_mb", [8, 1]),
    ]:
        win[nm] = nc.dram_tensor(nm, shp, F32, kind="ExternalInput")
    out_t = nc.dram_tensor("out", [TC, D], F32, kind="ExternalOutput")

    dbg = {}
    _dbg_shapes = {
        "xn_fm": ([D, TC], F32), "x2e_fm": ([D, TC], BF16), "phiq": ([D, TC], BF16),
        "phik": ([D, TC], BF16),
        "lam_d": ([H, TC], F32), "llam_d": ([H, TC], F32), "vtm_d": ([TC, H * KVE], BF16),
        "sf_d": ([128, NPAIR * KVE], F32), "sb_d": ([128, NPAIR * KVE], F32),
        "sin_d": ([128, NPAIR * KVE], F32), "rin_d": ([128, NPAIR * KVE], F32),
        "m1_d": ([1, 16], F32), "attn_d": ([D, TC], BF16), "x1_d": ([D, TC], F32),
        "tot_d": ([D, TC], F32), "xq_d": ([D, TC], BF16), "h2_d": ([IFF, TC], F32),
        "m2_d": ([1, 16], F32),
    }
    for nmm in dbg_names:
        shp, dt_ = _dbg_shapes[nmm]
        dbg[nmm] = nc.dram_tensor("dbg_" + nmm, shp, dt_, kind="ExternalOutput")

    xn_s = nc.dram_tensor("xn_s", [D, TC], F32, kind="Internal")
    qc_s = nc.dram_tensor("qc_s", [NCH, 128, NPAIR * CH], BF16, kind="Internal")
    qe_s = nc.dram_tensor("qe_s", [NCH, 128, NPAIR * CH], BF16, kind="Internal")
    part_s = nc.dram_tensor("part_s", [NCH, CH, H * KVE], BF16, kind="Internal")
    x1_s = nc.dram_tensor("x1_s", [D, TC], F32, kind="Internal")
    xq_s = nc.dram_tensor("xq_s", [D, TC], BF16, kind="Internal")
    tot_s = nc.dram_tensor("tot_s", [D, TC], F32, kind="Internal")
    h2_s = nc.dram_tensor("h2_s", [IFF, TC], F32, kind="Internal")
    wgT_s = nc.dram_tensor("wgT_s", [D, IFF], BF16, kind="Internal")
    wuT_s = nc.dram_tensor("wuT_s", [D, IFF], BF16, kind="Internal")
    wdT_s = nc.dram_tensor("wdT_s", [IFF, D], BF16, kind="Internal")
    kcv_s = nc.dram_tensor("kcv_s", [NCH, 128, NPAIR * KVE], BF16, kind="Internal")
    ag_in = nc.dram_tensor("ag_in", [1, PAY], F32, kind="Internal")
    ag_out = nc.dram_tensor("ag_out", [8, PAY], F32, kind="Internal")
    ag2_in = nc.dram_tensor("ag2_in", [1, 16], F32, kind="Internal")
    ag2_out = nc.dram_tensor("ag2_out", [8, 16], F32, kind="Internal")

    const_t = {k: nc.inline_tensor(np.ascontiguousarray(v), "c_" + k) for k, v in Cst.items()}

    es_persist = ExitStack()
    es_kcv = ExitStack()
    es_ph = ExitStack()

    def SB(name, shape, dtype, es):
        return es.enter_context(nc.sbuf_tensor(name, shape, dtype))

    def PS(name, es, shape=(128, 512)):
        return es.enter_context(nc.psum_tensor(name, list(shape), F32))

    P = es_persist
    id_f = SB("id_f", [128, 128], F32, P)
    id_b = SB("id_b", [128, 128], BF16, P)
    ones_row = SB("ones_row", [1, 128], F32, P)
    ones_col = SB("ones_col", [128, 1], F32, P)
    ones16_b = SB("ones16_b", [16, 128], BF16, P)
    ones16_f = SB("ones16_f", [16, 128], F32, P)
    sel_pair_b = SB("sel_pair_b", [16, NPAIR * 128], BF16, P)
    sel_half = SB("sel_half", [16, 128], F32, P)
    pair_mask = SB("pair_mask", [16, NPAIR], F32, P)
    ccpf = SB("ccpf", [8, 8], F32, P)
    ccpb = SB("ccpb", [8, 8], F32, P)
    ccmf = SB("ccmf", [8, 1], F32, P)
    ccmb = SB("ccmb", [8, 1], F32, P)

    lcend_st = SB("lcend_st", [16, NCH], F32, P)
    lpre_st = SB("lpre_st", [16, NCH], F32, P)
    ltot_c = SB("ltot_c", [16, 1], F32, P)
    mx_col = SB("mx_col", [128, 1], F32, P)
    mh_col = SB("mh_col", [128, 1], F32, P)
    colA = SB("colA", [128, 64], F32, P)
    rowA = SB("rowA", [1, 64], F32, P)
    s16A = SB("s16A", [16, 64], F32, P)
    rs1_col = SB("rs1_col", [128, 1], F32, P)
    sc1_col = SB("sc1_col", [128, 1], F32, P)
    rs2_col = SB("rs2_col", [128, 1], F32, P)
    sc2_col = SB("sc2_col", [128, 1], F32, P)
    sg_sc = SB("sg_sc", [128, NIC], F32, P)
    su_sc = SB("su_sc", [128, NIC], F32, P)
    sd_sc = SB("sd_sc", [128, NDC], F32, P)
    g2c = SB("g2c", [128, NDC], F32, P)
    b2cb = SB("b2cb", [128, NDC], BF16, P)
    obC = SB("obC", [128, NDC], F32, P)
    f2bC = SB("f2bC", [128, NDC], F32, P)
    clipc = SB("clipc", [1, 2], F32, P)
    rowsP = SB("rowsP", [1, D], F32, P)
    cstT = SB("cstT", [128, 8], F32, P)
    _cvals = [LN_EPS, MAGIC, -1.0, -0.5, 0.0, 1.0]
    for _i, _v in enumerate(_cvals):
        nc.const_aps.register(F32, _v, cstT[:, _i:_i + 1])
    declc = SB("declc", [128, NPAIR], F32, P)
    declp = SB("declp", [128, NPAIR], F32, P)
    s16B = SB("s16B", [16, 16], F32, P)

    kcvs = SB("kcvs", [128, 2, NPAIR * KVE], BF16, es_kcv)
    S_st = SB("S_st", [128, NPAIR * KVE], F32, es_kcv)
    Sb_st = SB("Sb_st", [128, NPAIR * KVE], F32, es_kcv)
    Sin_st = SB("Sin_st", [128, NPAIR * KVE], F32, es_kcv)
    Rin_st = SB("Rin_st", [128, NPAIR * KVE], F32, es_kcv)
    R_st = SB("R_st", [128, NPAIR * KVE], F32, es_kcv)
    Tmp_st = SB("Tmp_st", [128, NPAIR * KVE], F32, es_kcv)

    es1 = ExitStack()
    wq = SB("wq", [128, NDC, D], BF16, es1)
    wk = SB("wk", [128, NDC, D], BF16, es1)
    wv = SB("wv", [128, NDC, D], BF16, es1)
    wdec = SB("wdec", [128, NDC, H], BF16, es1)
    qbC = SB("qbC", [128, NDC], F32, es1)
    kbC = SB("kbC", [128, NDC], F32, es1)
    qb1C = SB("qb1C", [128, NDC], F32, es1)
    kb1C = SB("kb1C", [128, NDC], F32, es1)
    vbtm = SB("vbtm", [128, D], BF16, es1)
    decbc = SB("decbc", [16, 1], F32, es1)
    g1c = SB("g1c", [128, NDC], F32, es1)
    b1cb = SB("b1cb", [128, NDC], BF16, es1)
    vbeC = SB("vbeC", [128, NDC], F32, es1)
    xtm = [SB("xtm0", [128, NCHT, D], F32, es1)]
    xn_tm = SB("xn_tm", [128, D], F32, es1)
    x2e_tm = SB("x2e_tm", [128, D], F32, es1)
    xnfm = SB("xnfm", [128, NDC, CH], F32, es1)
    x2e = SB("x2e", [128, NDC, 2, TILE], BF16, es1)
    phiq = SB("phiq", [128, NDC, 2, TILE], BF16, es1)
    phik = SB("phik", [128, NDC, 2, TILE], BF16, es1)
    tbuf1 = SB("tbuf1", [128, TILE], F32, es1)
    lam_t = SB("lam_t", [16, 2, TILE], F32, es1)
    llam_t = lam_t
    vtm = SB("vtm", [128, 2, H * KVE], BF16, es1)
    lc_f = SB("lc_f", [16, CH], F32, es1)
    ctil_f = SB("ctil_f", [16, CH], F32, es1)
    e_f = SB("e_f", [16, CH], F32, es1)
    lctm = SB("lctm", [128, 16], F32, es1)
    ctil_b = SB("ctil_b", [16, CH], BF16, es1)
    e_b = SB("e_b", [16, CH], BF16, es1)
    ctm = SB("ctm", [128, 16], F32, es1)
    etm = SB("etm", [128, 16], F32, es1)
    drhs = SB("drhs", [16, 16 * CH], BF16, es1)
    D_sb = SB("D_sb", [128, H * CH], BF16, es1)
    scD = SB("scD", [128, H * CH], BF16, es1)
    keB = SB("keB", [128, H * NK], BF16, es1)
    kcB = SB("kcB", [128, H * NK], BF16, es1)
    phikf = SB("phikf", [128, NPAIR * CH], F32, es1)
    head_mask = SB("head_mask", [16, 16 * 128], BF16, es1)
    Sb16 = SB("Sb16", [128, NPAIR * KVE], BF16, es1)
    qcB = SB("qcB", [128, 2, NPAIR * CH], BF16, es1)
    qeB = SB("qeB", [128, 2, NPAIR * CH], BF16, es1)
    partB = SB("partB", [128, 2, H * KVE], BF16, es1)
    psA = [PS(f"psA{i}", es1) for i in range(8)]

    def p0_consts(eng, sem):
        for dst, src in [(id_f, "id_f"), (id_b, "id_b"), (ones_row, "ones_row_f"),
                         (ones_col, "ones_col_f"),
                         (ones16_b, "ones16_b"), (ones16_f, "ones16_f"),
                         (sel_pair_b, "sel_pair_b"),
                         (sel_half, "sel_half_f"),
                         (pair_mask, "pair_mask_f"), (head_mask, "head_mask_f")]:
            eng.dma_start(out=dst[:, :], in_=const_t[src].ap()).then_inc(sem, 16)
        for dst, src in [(ccpf, "cc_pf"), (ccpb, "cc_pb"), (ccmf, "cc_mf"), (ccmb, "cc_mb")]:
            eng.dma_start(out=dst[:, :], in_=win[src].ap()).then_inc(sem, 16)
    tok_c0 = DMA_("sync", p0_consts, n=14)

    def p0_wload(eng, sem):
        for dst, src in [(wq, "L0_attn_q_w"), (wk, "L0_attn_k_w"), (wv, "L0_attn_v_w")]:
            eng.dma_start(out=dst[:, :, :],
                          in_=AP(win[src], 0, [[D, 128], [128 * D, NDC], [1, D]])
                          ).then_inc(sem, 16)
        eng.dma_start(out=wdec[:, :, :],
                      in_=AP(win["L0_attn_dec_w"], 0, [[H, 128], [128 * H, NDC], [1, H]])
                      ).then_inc(sem, 16)
    tok_w0 = DMA_("gpsimd", p0_wload, n=4)

    def _setconsts(e):
        last = None
        for _i, _v in enumerate(_cvals):
            last = e.memset(cstT[:, _i:_i + 1], _v)
        return last
    tok_cst = G_(_setconsts)
    tok_z = G_(lambda e: (e.memset(S_st[:, :], 0.0), e.memset(Sb_st[:, :], 0.0),
                          e.memset(ltot_c[:, :], 0.0), e.memset(mx_col[:, :], 0.0),
                          e.memset(mh_col[:, :], 0.0), e.memset(Sin_st[:, :], 0.0),
                          e.memset(Rin_st[:, :], 0.0), e.memset(Sb16[:, :], 0.0))[-1])

    chain = {"rc": tok_c0}

    def row_to_cols(name, dstC, nchunks, col0=0):
        prev = chain["rc"]

        def ld(eng, sem):
            n = nchunks * 128
            eng.dma_start(out=rowsP[0:1, 0:n],
                          in_=AP(win[name], 0, [[n, 1], [1, n]])).then_inc(sem, 16)
        tl = DMA_("sync", ld, n=1, after=[prev])
        tts = []
        for c in range(nchunks):
            tts.append(T_(lambda e, c=c: e.transpose(psA[7][:, c:c + 1],
                                                     rowsP[0:1, c * 128:(c + 1) * 128],
                                                     id_f[0:1, 0:1]),
                          after=[tl, prev] + ([tts[-1]] if tts else [])))
        tv = V_(lambda e: e.tensor_copy(dstC[:, col0:col0 + nchunks], psA[7][:, 0:nchunks]),
                after=[tts[-1]])
        chain["rc"] = tv
        return tv

    tok_g1 = row_to_cols("L0_attn_ln1_w", g1c, NDC)
    tok_b1 = row_to_cols("L0_attn_ln1_b", colA, NDC)
    tok_b1b = V_(lambda e: e.tensor_copy(b1cb[:, :], colA[:, 0:NDC]), after=[tok_b1])
    tok_g2 = row_to_cols("L0_attn_ln2_w", g2c, NDC)
    tok_b2 = row_to_cols("L0_attn_ln2_b", colA, NDC)
    tok_b2b = V_(lambda e: e.tensor_copy(b2cb[:, :], colA[:, 0:NDC]), after=[tok_b2])
    tok_ob = row_to_cols("L0_attn_o_b", obC, NDC)
    tok_f2bc = row_to_cols("L0_attn_f2_b", f2bC, NDC)

    def fold_bias(wsb, bname, dstC, deps):
        prev = deps
        toks = []
        for cp in range(NDC):
            mms = []
            for c in range(NDC):
                mms.append(T_(lambda e, c=c, cp=cp, wsb=wsb: e.matmul(
                    psA[6][:, cp:cp + 1], wsb[:, c, cp * 128:(cp + 1) * 128],
                    b1cb[:, c:c + 1], start=(c == 0), stop=(c == NDC - 1)),
                    after=(prev if cp == 0 and c == 0 else []) + ([mms[-1]] if mms else ([toks[-1]] if toks else []))))
            toks.append(mms[-1])
        tb = row_to_cols(bname, dstC, NDC)
        return V_(lambda e, dstC=dstC: e.tensor_add(dstC[:, 0:NDC], dstC[:, 0:NDC],
                                                    psA[6][:, 0:NDC]),
                  after=[toks[-1], tb])

    tok_qb = fold_bias(wq, "L0_attn_q_b", qbC, [tok_w0, tok_b1b])
    tok_kb = fold_bias(wk, "L0_attn_k_b", kbC, [tok_qb])
    tok_vb = fold_bias(wv, "L0_attn_v_b", vbeC, [tok_kb])
    tok_qb1 = V_(lambda e: e.tensor_scalar(qb1C[:, :], qbC[:, 0:NDC], 1.0, None, ALU.add),
                 after=[tok_qb])
    tok_kb1 = V_(lambda e: e.tensor_scalar(kb1C[:, :], kbC[:, 0:NDC], 1.0, None, ALU.add),
                 after=[tok_kb])

    decmms = []
    for c in range(NDC):
        decmms.append(T_(lambda e, c=c: e.matmul(
            psA[5][0:16, 0:1], wdec[:, c, :], b1cb[:, c:c + 1],
            start=(c == 0), stop=(c == NDC - 1)),
            after=([tok_w0, tok_b1b, tok_vb] if c == 0 else []) + ([decmms[-1]] if decmms else [])))

    def ld_decb(eng, sem):
        eng.dma_start(out=rowA[0:1, 0:16],
                      in_=AP(win["L0_attn_dec_b"], 0, [[16, 1], [1, 16]])).then_inc(sem, 16)
    tok_decb = DMA_("sync", ld_decb, n=1, after=[chain["rc"]])
    tok_decT = T_(lambda e: e.transpose(psA[5][0:16, 1:2], rowA[0:1, 0:16], id_f[0:1, 0:1]),
                  after=[tok_decb, decmms[-1]])
    tok_decc0 = V_(lambda e: e.tensor_copy(s16B[:, 3:4], psA[5][0:16, 0:1]), after=[tok_decT])
    tok_decc = V_(lambda e: e.tensor_add(decbc[:, :], s16B[:, 3:4], psA[5][0:16, 1:2]),
                  after=[tok_decc0])

    tok_vbrow = None
    for rnd in range(2):
        vbT = []
        for c4 in range(4):
            c = rnd * 4 + c4
            vbT.append(T_(lambda e, c=c, c4=c4: e.transpose(
                psA[4][0:1, c4 * 128:(c4 + 1) * 128], vbeC[:, c:c + 1], id_f[:, :]),
                after=[tok_vb, tok_decc] + ([vbT[-1]] if vbT else ([tok_vbrow] if tok_vbrow else []))))
        tok_vbrow = V_(lambda e, rnd=rnd: e.tensor_copy(
            rowsP[0:1, rnd * 512:(rnd + 1) * 512], psA[4][0:1, 0:512]),
            after=[vbT[-1], chain["rc"]])
    chain["rc"] = tok_vbrow
    vb2 = []
    for c4 in range(2):
        mm = []
        for c in range(4):
            mm.append(T_(lambda e, c4=c4, c=c: e.matmul(
                psA[3][:, c * 128:(c + 1) * 128], ones_row[:, :],
                rowsP[0:1, (c4 * 4 + c) * 128:(c4 * 4 + c + 1) * 128], start=True, stop=True),
                after=[tok_vbrow] + ([vb2[-1]] if vb2 else []) + ([mm[-1]] if mm else [])))
        vb2.append(V_(lambda e, c4=c4: e.tensor_copy(
            vbtm[:, c4 * 512:(c4 + 1) * 512], psA[3][:, :]), after=[mm[-1]]))
    tok_vbtm = vb2[-1]

    sc_toks = []
    for wsb, dep in [(wq, tok_qb1), (wk, tok_kb1), (wv, tok_vbtm), (wdec, tok_decc)]:
        ncols = wsb.shape[2]
        for c in range(NDC):
            sc_toks.append(V_(lambda e, wsb=wsb, c=c, ncols=ncols: e.tensor_scalar(
                wsb[:, c, 0:ncols], wsb[:, c, 0:ncols], g1c[:, c:c + 1], None, ALU.mult),
                after=[dep, tok_g1]))
    tok_wsc = sc_toks[-1]

    def ld_clip(eng, sem):
        eng.dma_start(out=rowA[0:1, 20:21],
                      in_=AP(win["L0_ffn_clip_x"], 0, [[1, 1], [1, 1]])).then_inc(sem, 16)
        eng.dma_start(out=rowA[0:1, 21:22],
                      in_=AP(win["L0_ffn_clip_down"], 0, [[1, 1], [1, 1]])).then_inc(sem, 16)
    tok_clipl = DMA_("sync", ld_clip, n=2, after=[chain["rc"]])
    tok_clip = S_(lambda e: e.activation(clipc[0:1, 0:2], rowA[0:1, 20:22], AF.Sigmoid),
                  after=[tok_clipl])
    tok_vone = G_(lambda e: e.memset(
        AP(vtm, 64, [[2 * H * KVE, 128], [KVE, 2 * H], [1, 1]]), 1.0), after=[tok_z])

    bld.flush("p0")
    if PHASE_LIMIT == "p0":
        return nc

    st1 = {"xtm": [], "chunk_tail": [], "b7": [], "xnst": [], "ln": {},
           "tll": {}, "vg": {}, "ln_last": []}
    stq = {}
    hbm_st = {}

    def emit_ldx(t):
        def ld_x(eng, sem, t=t):
            eng.dma_start(out=xtm[0][:, :, :],
                          in_=AP(x_in, t * TILE * D, [[D, 128], [128 * D, NCHT], [1, D]])
                          ).then_inc(sem, 16)
        tok = DMA_("sync", ld_x, n=1, after=st1["xtm"])
        st1["xtm"] = []
        st1["ln"][t] = {"x": tok, "xn_j": [], "x2e_j": []}

    def prep_ln_part(t, j):
        st = st1["ln"][t]
        tok_x = st["x"]
        tok_xn_j = st["xn_j"]
        tok_x2e_j = st["x2e_j"]
        sl = t % 2
        xj = xtm[0][:, j, :]
        cb = (j % 2) * 24
        tm = V_(lambda e, xj=xj, cb=cb: e.tensor_reduce(colA[:, cb:cb + 1], xj, AX.X, ALU.add),
                after=[tok_x] + (st1["ln_last"] if j == 0 else []))
        ts1 = S_(lambda e, xj=xj, cb=cb: e.activation(x2e_tm[:, :], xj, AF.Square,
                                                      accum_out=colA[:, cb + 1:cb + 2]),
                 after=[tok_x] + (st1["ln_last"] if j == 0 else [])
                 + (tok_x2e_j[-1:] if tok_x2e_j else []))

        def colops(e, cb=cb):
            e.tensor_scalar(colA[:, cb + 2:cb + 3], colA[:, cb:cb + 1], 1.0 / D, None, ALU.mult)
            e.tensor_scalar(colA[:, cb + 3:cb + 4], colA[:, cb + 1:cb + 2], 1.0 / D, None, ALU.mult)
            e.drain()
            e.tensor_mul(colA[:, cb + 4:cb + 5], colA[:, cb + 2:cb + 3], colA[:, cb + 2:cb + 3])
            e.drain()
            return e.tensor_sub(colA[:, cb + 5:cb + 6], colA[:, cb + 3:cb + 4], colA[:, cb + 4:cb + 5])
        tv1 = V_(colops, after=[tm, ts1])
        tsq = S_(lambda e, cb=cb: e.activation(colA[:, cb + 6:cb + 7], colA[:, cb + 5:cb + 6],
                                               AF.Sqrt, bias=LN_EPS), after=[tv1])

        def colops2(e, cb=cb):
            e.reciprocal(colA[:, cb + 7:cb + 8], colA[:, cb + 6:cb + 7])
            e.drain()
            e.tensor_mul(colA[:, cb + 8:cb + 9], colA[:, cb + 2:cb + 3], colA[:, cb + 7:cb + 8])
            e.drain()
            return e.tensor_scalar(colA[:, cb + 9:cb + 10], colA[:, cb + 8:cb + 9],
                                   -1.0, None, ALU.mult)
        tv2 = V_(colops2, after=[tsq])
        ta = S_(lambda e, xj=xj, cb=cb: e.activation(xn_tm[:, :], xj, AF.Identity,
                                                     bias=colA[:, cb + 9:cb + 10],
                                                     scale=colA[:, cb + 7:cb + 8]),
                after=[tv2])
        st1["xtm"].append(ta)
        tmx = V_(lambda e, cb=cb: (
            e.tensor_reduce(colA[:, cb + 10:cb + 11], xn_tm[:, :], AX.X, ALU.max,
                            apply_absolute_value=True),
            e.drain(),
            e.tensor_max(mx_col[:, :], mx_col[:, :], colA[:, cb + 10:cb + 11]))[-1],
            after=[ta])
        tm1 = V_(lambda e, cb=cb: e.tensor_reduce(colA[:, cb + 11:cb + 12], xn_tm[:, :],
                                                  AX.X, ALU.add), after=[ta])
        ts2 = S_(lambda e, cb=cb: e.activation(x2e_tm[:, :], xn_tm[:, :], AF.Square,
                                               accum_out=colA[:, cb + 12:cb + 13]),
                 after=[ta])

        def colops3(e, cb=cb):
            e.tensor_scalar(colA[:, cb + 13:cb + 14], colA[:, cb + 11:cb + 12], 1.0 / D, None, ALU.mult)
            e.tensor_scalar(colA[:, cb + 14:cb + 15], colA[:, cb + 12:cb + 13], 1.0 / D, None, ALU.mult)
            e.drain()
            e.tensor_mul(colA[:, cb + 15:cb + 16], colA[:, cb + 13:cb + 14], colA[:, cb + 13:cb + 14])
            e.drain()
            return e.tensor_sub(colA[:, cb + 16:cb + 17], colA[:, cb + 14:cb + 15], colA[:, cb + 15:cb + 16])
        tv3 = V_(colops3, after=[tm1, ts2])
        tsq2 = S_(lambda e, cb=cb: e.activation(colA[:, cb + 17:cb + 18], colA[:, cb + 16:cb + 17],
                                                AF.Sqrt, bias=LN_EPS), after=[tv3])

        def colops4(e, cb=cb):
            e.reciprocal(colA[:, cb + 18:cb + 19], colA[:, cb + 17:cb + 18])
            e.drain()
            e.tensor_mul(colA[:, cb + 19:cb + 20], colA[:, cb + 13:cb + 14], colA[:, cb + 18:cb + 19])
            e.drain()
            return e.tensor_scalar(colA[:, cb + 20:cb + 21], colA[:, cb + 19:cb + 20],
                                   -1.0, None, ALU.mult)
        tv4 = V_(colops4, after=[tsq2])
        ta2 = S_(lambda e, cb=cb: e.activation(x2e_tm[:, :], xn_tm[:, :], AF.Identity,
                                               bias=colA[:, cb + 20:cb + 21],
                                               scale=colA[:, cb + 18:cb + 19]),
                 after=[tv4])
        prevT = []
        for c in range(NDC):
            tT = T_(lambda e, c=c: e.transpose(
                psA[7][:, 0:128], xn_tm[:, c * 128:(c + 1) * 128], id_f[:, :]),
                after=[ta, tmx, ts2] + prevT
                + (st1["b7"] if (j == 0 and c == 0) else []))
            tE = S_(lambda e, c=c, j=j: e.copy(
                xnfm[:, c, 0:128], psA[7][:, 0:128]),
                after=[tT] + (tok_xn_j[-1:] if (c == 0 and tok_xn_j) else [])
                + (st1["xnst"] if (j == 0 and c == 0) else []))
            prevT = [tE]

        def st_xn_j(eng, sem, t=t, j=j):
            eng.dma_start(out=AP(xn_s, t * TILE + j * 128,
                                 [[TC, 128], [128 * TC, NDC], [1, 128]]),
                          in_=xnfm[:, :, :]).then_inc(sem, 16)
        DMA_("sync", st_xn_j, n=1, after=prevT)
        tok_xn_j.append(bld.fence("sync"))
        if "xn_fm" in dbg:
            DMA_("sync", lambda eng, sem, t=t, j=j: eng.dma_start(
                out=AP(dbg["xn_fm"], t * TILE + j * 128,
                       [[TC, 128], [128 * TC, NDC], [1, 128]]),
                in_=xnfm[:, :, :]).then_inc(sem, 16), n=1, after=prevT)
        for c in range(NDC):
            tT = T_(lambda e, c=c: e.transpose(
                psA[7][:, 0:128], x2e_tm[:, c * 128:(c + 1) * 128], id_f[:, :]),
                after=[ta2] + prevT)
            tE = S_(lambda e, c=c, j=j, sl=sl: e.copy(
                x2e[:, c, sl, j * 128:(j + 1) * 128], psA[7][:, 0:128]),
                after=[tT]
                + (st1["chunk_tail"] if (j == 0 and c == 0) else []))
            prevT = [tE]
        tok_x2e_j.append(prevT[0])
        if j == NCHT - 1:
            st1["ln_last"] = [prevT[0]]
            st1["xnst"] = [tok_xn_j[-1]]

    def prep_tail(t, b6gate):
        st = st1["ln"][t]
        sl = t % 2
        x2e_done = st["x2e_j"][-1]
        pcnt = {"i": 0, "first6": True, "first7": True}

        def proj_phi(wsb, bC, b1C, dstP, deps, prev_outs):
            outs = list(prev_outs)
            for co in range(NDC):
                bank = psA[6 + pcnt["i"] % 2]
                use6 = pcnt["i"] % 2 == 0
                pcnt["i"] += 1
                gate6 = []
                if use6 and pcnt["first6"]:
                    gate6 = list(b6gate)
                    pcnt["first6"] = False
                elif not use6 and pcnt["first7"]:
                    gate6 = [x2e_done]
                    pcnt["first7"] = False
                mm = []
                for c in range(NDC):
                    mm.append(T_(lambda e, c=c, co=co, wsb=wsb, sl=sl, bank=bank: e.matmul(
                        bank[:, :], wsb[:, c, co * 128:(co + 1) * 128], x2e[:, c, sl, :],
                        start=(c == 0), stop=(c == NDC - 1)),
                        after=((deps + [tok_wsc]) if (co == 0 and c == 0) else [])
                        + gate6
                        + ([mm[-1]] if mm else ([outs[-2]] if len(outs) >= 2 else []))))
                tv = V_(lambda e, co=co, bC=bC, bank=bank: e.tensor_scalar(
                    tbuf1[:, :], bank[:, :], bC[:, co:co + 1], 0.0, ALU.add, ALU.min),
                    after=[mm[-1]])
                tsx = S_(lambda e: e.activation(tbuf1[:, :], tbuf1[:, :], AF.Exp),
                         after=[tv] + ([outs[-1]] if outs else []))
                tph = V_(lambda e, co=co, dstP=dstP, b1C=b1C, sl=sl, bank=bank:
                         e.scalar_tensor_tensor(
                    dstP[:, co, sl, :], bank[:, :], b1C[:, co:co + 1], tbuf1[:, :],
                    ALU.add, ALU.max), after=[tsx])
                outs.append(tph)
            return outs[len(prev_outs):]
        qt = proj_phi(wq, qbC, qb1C, phiq, [x2e_done], [])
        kt = proj_phi(wk, kbC, kb1C, phik, [x2e_done], qt)
        if "phiq" in dbg:
            DMA_("sync", lambda eng, sem, t=t, sl=sl: eng.dma_start(
                out=AP(dbg["phiq"], t * TILE, [[TC, 128], [128 * TC, NDC], [1, TILE]]),
                in_=phiq[:, :, sl, :]).then_inc(sem, 16), n=1, after=[qt[-1]])
        if "phik" in dbg:
            DMA_("sync", lambda eng, sem, t=t, sl=sl: eng.dma_start(
                out=AP(dbg["phik"], t * TILE, [[TC, 128], [128 * TC, NDC], [1, TILE]]),
                in_=phik[:, :, sl, :]).then_inc(sem, 16), n=1, after=[kt[-1]])

        dmm = []
        for c in range(NDC):
            dmm.append(T_(lambda e, c=c, sl=sl: e.matmul(
                psA[7][0:16, 0:512], wdec[:, c, :], x2e[:, c, sl, :],
                start=(c == 0), stop=(c == NDC - 1)),
                after=(([x2e_done, kt[-1], kt[-2]]) if c == 0 else []) + ([dmm[-1]] if dmm else [])))
        tlam = S_(lambda e, sl=sl: e.activation(lam_t[:, sl, :], psA[7][0:16, 0:512], AF.Sigmoid,
                                                bias=decbc[:, 0:1]), after=[dmm[-1]])
        tlam2 = V_(lambda e, sl=sl: e.tensor_scalar(lam_t[:, sl, :], lam_t[:, sl, :],
                                                    LAM_MIN, None, ALU.max),
                   after=[tlam])
        if "lam_d" in dbg:
            tdl = DMA_("sync", lambda eng, sem, t=t, sl=sl: eng.dma_start(
                out=AP(dbg["lam_d"], t * TILE, [[TC, 16], [1, TILE]]),
                in_=lam_t[:, sl, :]).then_inc(sem, 16), n=1, after=[tlam2])
            tlam2 = tdl
        tll = S_(lambda e, sl=sl: e.activation(llam_t[:, sl, :], lam_t[:, sl, :], AF.Ln),
                 after=[tlam2])
        st1["b7"] = [tlam]
        st1["tll"][t] = tll

    def p1_chunk(t, cc, ch, prev, vgate):
        vslot = cc % 2
        tsl = t % 2
        vtoks = []
        for half in range(2):
            bank = psA[6]
            mm = []
            for c in range(NDC):
                dep = []
                if half == 0 and c == 0:
                    dep = list(prev) + vgate.get(ch - 2, [])
                mm.append(T_(lambda e, c=c, cc=cc, half=half, bank=bank, tsl=tsl: e.matmul(
                    bank[:, :], x2e[:, c, tsl, cc * 128:(cc + 1) * 128],
                    wv[:, c, half * 512:(half + 1) * 512],
                    start=(c == 0), stop=(c == NDC - 1)),
                    after=dep + ([mm[-1]] if mm else (vtoks[-1:] if vtoks else []))))
            ev = V_(lambda e, vslot=vslot, half=half, bank=bank: e.tensor_add(
                AP(vtm, vslot * H * KVE + half * 8 * KVE,
                   [[2 * H * KVE, 128], [KVE, 8], [1, 64]]),
                bank[:, :].rearrange("p (a b) -> p a b", a=8),
                vbtm[:, half * 512:(half + 1) * 512].rearrange("p (a b) -> p a b", a=8)),
                after=[mm[-1]])
            vtoks.append(ev)
        vdone = vtoks[-1]
        if "vtm_d" in dbg:
            DMA_("sync", lambda eng, sem, t=t, cc=cc, vslot=vslot: eng.dma_start(
                out=AP(dbg["vtm_d"], (t * NCHT + cc) * CH * H * KVE,
                       [[H * KVE, 128], [1, H * KVE]]),
                in_=vtm[:, vslot, :]).then_inc(sem, 16), n=1, after=[vdone])
        tlc = V_(lambda e, tsl=tsl: e.tensor_tensor_scan(
            lc_f[:, :], ones16_f[:, 0:CH], llam_t[:, tsl, cc * CH:(cc + 1) * CH],
            0.0, ALU.mult, ALU.add), after=prev)

        def lbk(e):
            e.tensor_copy(lcend_st[:, ch:ch + 1], lc_f[:, CH - 1:CH])
            e.tensor_copy(lpre_st[:, ch:ch + 1], ltot_c[:, :])
            e.drain()
            return e.tensor_add(ltot_c[:, :], ltot_c[:, :], lc_f[:, CH - 1:CH])
        tlbk = V_(lbk, after=[tlc])
        tce = S_(lambda e: e.activation(ctil_f[:, :], lc_f[:, :], AF.Exp), after=[tlc])
        tceb = V_(lambda e: e.tensor_copy(ctil_b[:, :], ctil_f[:, :]), after=[tce])
        tef = V_(lambda e: e.tensor_single_scalar(e_f[:, :], lc_f[:, :],
                                                  lc_f[:, CH - 1:CH], ALU.subtract),
                 after=[tlc])
        tee = S_(lambda e: e.activation(e_f[:, :], e_f[:, :], AF.Exp, scale=-1.0), after=[tef])
        teeb = V_(lambda e: e.tensor_copy(e_b[:, :], e_f[:, :]), after=[tee])
        tLt = T_(lambda e: e.transpose(psA[6][:, 0:16], lc_f[:, :], id_f[0:16, 0:16]),
                 after=[tlc] + prev)
        tCt = T_(lambda e: e.transpose(psA[6][:, 16:32], ctil_f[:, :], id_f[0:16, 0:16]),
                 after=[tce])
        tEt = T_(lambda e: e.transpose(psA[6][:, 32:48], e_f[:, :], id_f[0:16, 0:16]),
                 after=[tee])
        tLe = V_(lambda e: (e.tensor_copy(lctm[:, :], psA[6][:, 0:16]),
                            e.tensor_copy(ctm[:, :], psA[6][:, 16:32]),
                            e.tensor_copy(etm[:, :], psA[6][:, 32:48]))[-1],
                 after=[tLt, tCt, tEt])
        tdc1 = S_(lambda e: e.activation(s16B[:, 1:2], lcend_st[:, ch:ch + 1], AF.Exp),
                  after=[tlbk])
        tdc2 = S_(lambda e: e.activation(s16B[:, 2:3], lpre_st[:, ch:ch + 1], AF.Exp),
                  after=[tlbk])
        tdm1 = V_(lambda e: e.tensor_mul(s16B[:, 8:16], bccol(s16B[:, 1:2], NPAIR),
                                         pair_mask[:, :]), after=[tdc1])
        tdm2 = V_(lambda e: e.tensor_mul(s16A[:, 32:40], bccol(s16B[:, 2:3], NPAIR),
                                         pair_mask[:, :]), after=[tdc2])
        tdb1 = T_(lambda e: e.matmul(psA[6][:, 48:56], sel_half[:, :], s16B[:, 8:16],
                                     start=True, stop=True), after=[tdm1, tLe])
        tdb2 = T_(lambda e: e.matmul(psA[6][:, 56:64], sel_half[:, :], s16A[:, 32:40],
                                     start=True, stop=True), after=[tdm2])
        tdc = V_(lambda e: (e.tensor_copy(declc[:, :], psA[6][:, 48:56]),
                            e.tensor_copy(declp[:, :], psA[6][:, 56:64]))[-1],
                 after=[tdb1, tdb2])

        trh = V_(lambda e: e.tensor_mul(
            drhs[:, :].rearrange("p (a b) -> p a b", a=16),
            bcv(lc_f[:, :], 16), head_mask[:, :].rearrange("p (a b) -> p a b", a=16)),
            after=[tlc])
        dtoks = []
        prevD = [tdc]
        for bh in range(4):
            bank = psA[6]
            tb = T_(lambda e, bh=bh, bank=bank: e.matmul(
                bank[:, 64:576] if False else bank[:, 0:512],
                ones16_b[:, 0:128], drhs[:, bh * 512:(bh + 1) * 512],
                start=True, stop=True), after=[trh] + prevD)
            tsub = V_(lambda e, bh=bh, bank=bank: e.tensor_sub(
                D_sb[:, bh * 512:(bh + 1) * 512],
                AP(lctm, HORDER[bh * 4], [[16, 128], [2, 4], [0, CH]]),
                bank[:, 0:512].rearrange("p (a b) -> p a b", a=4)),
                after=[tb, tLe])
            tab = S_(lambda e, bh=bh: e.activation(D_sb[:, bh * 512:(bh + 1) * 512],
                                                   D_sb[:, bh * 512:(bh + 1) * 512], AF.Abs),
                     after=[tsub])
            tex = S_(lambda e, bh=bh: e.activation(D_sb[:, bh * 512:(bh + 1) * 512],
                                                   D_sb[:, bh * 512:(bh + 1) * 512], AF.Exp,
                                                   scale=-1.0), after=[tab])
            dtoks.append(tex)
            prevD = [tsub]

        if P1_CUT == "lc":
            return [dtoks[-1], tLe, tdc, vdone] + prev
        sc_toks = []
        sc_bank_tok = [None] * 4
        for hh in range(H):
            g, half = hh // 2, hh % 2
            pos = POS[hh]
            bank = psA[2 + pos // 4]
            tk = T_(lambda e, hh=hh, g=g, half=half, bank=bank, pos=pos, tsl=tsl: e.matmul(
                bank[:, (pos % 4) * 128:(pos % 4 + 1) * 128],
                phik[half * 64:(half + 1) * 64, g, tsl, cc * CH:(cc + 1) * CH],
                phiq[half * 64:(half + 1) * 64, g, tsl, cc * CH:(cc + 1) * CH],
                start=True, stop=True),
                after=([dtoks[-1]] + prev) if hh == 0 else [])
            sc_toks.append(tk)
            sc_bank_tok[pos // 4] = tk
        if P1_CUT == "sconly0":
            return [sc_toks[-1], tLe, tdc, dtoks[-1]] + prev
        if P1_CUT == "sconly":
            pvx = []
            for bq in range(4):
                pvx.append(V_(lambda e, bq=bq: e.tensor_copy(
                    scD[:, bq * 512:(bq + 1) * 512], psA[2 + bq][:, :]),
                    after=[sc_toks[bq * 4 + 3]]))
            return [pvx[-1], tLe, tdc, dtoks[-1]] + prev
        sd_toks = []
        for bh in range(4):
            sd_toks.append(V_(lambda e, bh=bh: e.tensor_mul(
                scD[:, bh * 512:(bh + 1) * 512], psA[2 + bh][:, :],
                D_sb[:, bh * 512:(bh + 1) * 512]),
                after=[sc_bank_tok[bh], sc_toks[-1], dtoks[bh]]))
        if P1_CUT == "sd":
            return [sd_toks[-1], tLe, tdc] + prev

        tkf = V_(lambda e, tsl=tsl: e.tensor_copy(
            phikf[:, :].rearrange("p (a b) -> p a b", a=NPAIR),
            phik[:, :, tsl, cc * CH:(cc + 1) * CH]), after=[sc_toks[-1]])
        kT = []
        prevKT = [sc_toks[-1], tkf] + list(prev)
        for gq in range(2):
            tTs = []
            for g4 in range(4):
                g = gq * 4 + g4
                tTs.append(T_(lambda e, g=g, g4=g4: e.transpose(
                    psA[g4 % 2][:, (g4 // 2) * 128:(g4 // 2) * 128 + 128],
                    phikf[:, g * CH:(g + 1) * CH], id_f[:, :]),
                    after=(prevKT if g4 == 0 else [])))
            evs = []
            for g4 in range(4):
                g = gq * 4 + g4
                for half in range(2):
                    hh = 2 * g + half
                    evs.append(V_(lambda e, g4=g4, half=half, hh=hh: (
                        e.tensor_scalar(keB[:, hh * NK:(hh + 1) * NK],
                                        psA[g4 % 2][:, (g4 // 2) * 128 + half * 64:
                                                     (g4 // 2) * 128 + half * 64 + 64],
                                        etm[:, hh:hh + 1], None, ALU.mult),
                        e.tensor_scalar(kcB[:, hh * NK:(hh + 1) * NK],
                                        psA[g4 % 2][:, (g4 // 2) * 128 + half * 64:
                                                     (g4 // 2) * 128 + half * 64 + 64],
                                        ctm[:, hh:hh + 1], None, ALU.mult))[-1],
                        after=([tTs[-1], tLe] if g4 == 0 and half == 0 else [])))
            prevKT = [evs[-1]]
            kT += tTs + evs
        kdone = kT[-1]

        if P1_CUT == "scores":
            return [sd_toks[-1], kdone, tLe, tdc] + prev
        qtk = []
        prevQ = [dtoks[-1], tceb, teeb, tdc]
        for gq in range(2):
            mmq = []
            for g4 in range(4):
                g = gq * 4 + g4
                mmq.append(T_(lambda e, g=g, g4=g4: e.matmul(
                    psA[6][:, g4 * 128:(g4 + 1) * 128],
                    sel_pair_b[:, g * 128:(g + 1) * 128],
                    ctil_b[:, :], start=True, stop=True),
                    after=(prevQ if g4 == 0 else [])))
            tqc = V_(lambda e, gq=gq, sl=ch % 2, tsl=tsl: e.tensor_mul(
                qcB[:, sl, gq * 4 * CH:(gq + 1) * 4 * CH].rearrange(
                    "p (a b) -> p a b", a=4),
                phiq[:, gq * 4:(gq + 1) * 4, tsl, cc * CH:(cc + 1) * CH],
                psA[6][:, 0:512].rearrange("p (a b) -> p a b", a=4)),
                after=[mmq[-1]]
                + ([stq[("qc", ch - 2)]] if (gq == 0 and ("qc", ch - 2) in stq) else []))
            mme = []
            for g4 in range(4):
                g = gq * 4 + g4
                mme.append(T_(lambda e, g=g, g4=g4: e.matmul(
                    psA[6][:, g4 * 128:(g4 + 1) * 128],
                    sel_pair_b[:, g * 128:(g + 1) * 128],
                    e_b[:, :], start=True, stop=True),
                    after=([tqc] if g4 == 0 else [])))
            tqe = V_(lambda e, gq=gq, sl=ch % 2, tsl=tsl: e.tensor_mul(
                qeB[:, sl, gq * 4 * CH:(gq + 1) * 4 * CH].rearrange(
                    "p (a b) -> p a b", a=4),
                phiq[:, gq * 4:(gq + 1) * 4, tsl, cc * CH:(cc + 1) * CH],
                psA[6][:, 0:512].rearrange("p (a b) -> p a b", a=4)),
                after=[mme[-1]])
            prevQ = [tqe]
            qtk += [tqc, tqe]
        qdone = qtk[-1]

        def st_qc(eng, sem, sl=ch % 2):
            eng.dma_start(out=AP(qc_s, ch * 128 * NPAIR * CH, [[NPAIR * CH, 128], [1, NPAIR * CH]]),
                          in_=qcB[:, sl, :]).then_inc(sem, 16)
            eng.dma_start(out=AP(qe_s, ch * 128 * NPAIR * CH, [[NPAIR * CH, 128], [1, NPAIR * CH]]),
                          in_=qeB[:, sl, :]).then_inc(sem, 16)
        DMA_("sync", st_qc, n=2, after=[qdone])
        tok_qst = bld.fence("sync")
        stq[("qc", ch)] = tok_qst

        if P1_CUT == "qc":
            return [qdone, tok_qst, kdone, sd_toks[-1], tdc]
        av_toks = []
        av_bank_tok = [None] * 4
        for hh in range(H):
            g, half = hh // 2, hh % 2
            pos = POS[hh]
            bank = psA[2 + pos // 4]
            m1 = T_(lambda e, hh=hh, bank=bank, pos=pos: e.matmul(
                bank[:, (pos % 4) * KVE:(pos % 4) * KVE + KVE],
                scD[:, pos * CH:(pos + 1) * CH],
                vtm[:, cc % 2, hh * KVE:(hh + 1) * KVE],
                start=True, stop=False),
                after=([sd_toks[pos // 4], qdone, vdone] if hh == 0 else [sd_toks[pos // 4]]))
            m2 = T_(lambda e, hh=hh, g=g, half=half, bank=bank, pos=pos, sl=ch % 2: e.matmul(
                bank[:, (pos % 4) * KVE:(pos % 4) * KVE + KVE],
                qcB[half * 64:(half + 1) * 64, sl, g * CH:(g + 1) * CH],
                Sb16[half * 64:(half + 1) * 64, g * KVE:(g + 1) * KVE],
                start=False, stop=True), after=[m1])
            av_toks.append(m2)
            av_bank_tok[pos // 4] = m2
        pv = []
        for bq in range(4):
            pv.append(V_(lambda e, bq=bq, sl=ch % 2: e.tensor_copy(
                partB[:, sl, bq * 4 * KVE:(bq + 1) * 4 * KVE], psA[2 + bq][:, 0:4 * KVE]),
                after=[av_bank_tok[bq], av_toks[-1]]
                + ([stq[("part", ch - 2)]] if ("part", ch - 2) in stq else [])))

        def st_part(eng, sem, sl=ch % 2):
            eng.dma_start(out=AP(part_s, ch * CH * H * KVE, [[H * KVE, 128], [1, H * KVE]]),
                          in_=partB[:, sl, :]).then_inc(sem, 16)
        DMA_("sync", st_part, n=1, after=pv)
        tok_pst = bld.fence("sync")
        stq[("part", ch)] = tok_pst

        if P1_CUT == "av":
            return [tok_pst, qdone, tok_qst, kdone, tdc, pv[-1]]
        kev_toks = []
        for hh in range(H):
            half = hh % 2
            bank = psA[hh // 8]
            slot = (hh % 8) // 2
            kev_toks.append(T_(lambda e, hh=hh, half=half, bank=bank, slot=slot: e.matmul(
                bank[half * 64:half * 64 + 64, slot * KVE:slot * KVE + KVE],
                keB[:, hh * NK:(hh + 1) * NK],
                vtm[:, cc % 2, hh * KVE:(hh + 1) * KVE],
                start=True, stop=True, tile_position=(0, half * 64)),
                after=([kdone, av_toks[-1]] if hh == 0 else [])))
        sup = [V_(lambda e: (
            e.tensor_mul(Tmp_st[:, :].rearrange("p (a b) -> p a b", a=NPAIR),
                         S_st[:, :].rearrange("p (a b) -> p a b", a=NPAIR),
                         AP(declc, 0, [[NPAIR, 128], [1, NPAIR], [0, KVE]])),
            e.tensor_add(S_st[:, 0:4 * KVE].rearrange("p (a b) -> p a b", a=4),
                         Tmp_st[:, 0:4 * KVE].rearrange("p (a b) -> p a b", a=4),
                         psA[0][:, 0:4 * KVE].rearrange("p (a b) -> p a b", a=4)),
            e.tensor_add(S_st[:, 4 * KVE:].rearrange("p (a b) -> p a b", a=4),
                         Tmp_st[:, 4 * KVE:].rearrange("p (a b) -> p a b", a=4),
                         psA[1][:, 0:4 * KVE].rearrange("p (a b) -> p a b", a=4)),
            e.tensor_copy(Sb16[:, :], S_st[:, :]))[-1],
            after=[kev_toks[7], kev_toks[15], tdc, av_toks[-1]])]
        kcv_toks = []
        for hh in range(H):
            half = hh % 2
            bank = psA[hh // 8]
            slot = (hh % 8) // 2
            kcv_toks.append(T_(lambda e, hh=hh, half=half, bank=bank, slot=slot: e.matmul(
                bank[half * 64:half * 64 + 64, slot * KVE:slot * KVE + KVE],
                kcB[:, hh * NK:(hh + 1) * NK],
                vtm[:, cc % 2, hh * KVE:(hh + 1) * KVE],
                start=True, stop=True, tile_position=(0, half * 64)),
                after=([sup[-1]] if hh == 0 else [])))
        kce = [V_(lambda e, ch=ch: (
            e.tensor_copy(kcvs[:, ch % 2, 0:4 * KVE].rearrange("p (a b) -> p a b", a=4),
                          psA[0][:, 0:4 * KVE].rearrange("p (a b) -> p a b", a=4)),
            e.tensor_copy(kcvs[:, ch % 2, 4 * KVE:].rearrange("p (a b) -> p a b", a=4),
                          psA[1][:, 0:4 * KVE].rearrange("p (a b) -> p a b", a=4)),
            e.tensor_mul(Tmp_st[:, 0:4 * KVE].rearrange("p (a b) -> p a b", a=4),
                         psA[0][:, 0:4 * KVE].rearrange("p (a b) -> p a b", a=4),
                         AP(declp, 0, [[NPAIR, 128], [1, 4], [0, KVE]])),
            e.tensor_mul(Tmp_st[:, 4 * KVE:].rearrange("p (a b) -> p a b", a=4),
                         psA[1][:, 0:4 * KVE].rearrange("p (a b) -> p a b", a=4),
                         AP(declp, 4, [[NPAIR, 128], [1, 4], [0, KVE]])),
            e.tensor_add(Sb_st[:, :], Sb_st[:, :], Tmp_st[:, :]))[-1],
            after=[kcv_toks[-1], tdc, sup[-1]] + stq.get(("kcv", ch - 2), []))]

        def st_kcv(eng, sem, ch=ch):
            eng.dma_start(out=AP(kcv_s, ch * 128 * NPAIR * KVE,
                                 [[NPAIR * KVE, 128], [1, NPAIR * KVE]]),
                          in_=kcvs[:, ch % 2, :]).then_inc(sem, 16)
        DMA_("sync", st_kcv, n=1, after=[kce[-1]])
        stq[("kcv", ch)] = [bld.fence("sync")]
        vgate[ch] = [kev_toks[-1], kcv_toks[-1]]
        return [sup[-1], kce[-1], pv[-1], qdone]

    st1["ln_last"] = [tok_wsc, tok_clip, tok_vone]
    emit_ldx(0)
    for j in range(NCHT):
        prep_ln_part(0, j)
    prep_tail(0, [tok_vone])
    for t in range(NT):
        if t + 1 < NT:
            emit_ldx(t + 1)
        prev = [st1["tll"][t]] + st1["chunk_tail"]
        for cc in range(NCHT):
            prev = p1_chunk(t, cc, t * NCHT + cc, prev, st1["vg"])
            if t + 1 < NT:
                prep_ln_part(t + 1, cc)
        if t + 1 < NT:
            prep_tail(t + 1, list(prev))
        st1["chunk_tail"] = list(prev)
    gate = prev
    p1_end = list(gate)
    if "sf_d" in dbg:
        DMA_("sync", lambda eng, sem: eng.dma_start(
            out=dbg["sf_d"].ap(), in_=S_st[:, :]).then_inc(sem, 16), n=1, after=p1_end)
    if "sb_d" in dbg:
        DMA_("sync", lambda eng, sem: eng.dma_start(
            out=dbg["sb_d"].ap(), in_=Sb_st[:, :]).then_inc(sem, 16), n=1, after=p1_end)

    q_p1 = bld.quiesce("sync")
    bld.flush("p1")
    if PHASE_LIMIT == "p1":
        return nc
    es1.close()

    es_x = ExitStack()
    psX = [PS(f"psX{i}", es_x) for i in range(4)]
    rowX = SB("rowX", [1, 192], F32, es_x)
    tmxT = T_(lambda e: e.transpose(psX[0][0:1, 0:128], mx_col[:, :], id_f[:, :]))
    tmxE = V_(lambda e: (e.tensor_copy(rowX[0:1, 0:128], psX[0][0:1, 0:128]),
                         e.memset(rowA[0:1, 31:46], 0.0),
                         e.drain(),
                         e.tensor_reduce(rowA[0:1, 30:31], rowX[0:1, 0:128], AX.X, ALU.max))[-1],
              after=[tmxT])

    def st_ag(eng, sem):
        for half in range(2):
            eng.dma_start(out=AP(ag_in, half * 4160,
                                 [[65, 64], [2 * 4160, NPAIR], [1, 65]]),
                          in_=S_st[half * 64:half * 64 + 64, :].rearrange("p (a b) -> p a b", a=NPAIR)
                          ).then_inc(sem, 16)
            eng.dma_start(out=AP(ag_in, PAY_SEC + half * 4160,
                                 [[65, 64], [2 * 4160, NPAIR], [1, 65]]),
                          in_=Sb_st[half * 64:half * 64 + 64, :].rearrange("p (a b) -> p a b", a=NPAIR)
                          ).then_inc(sem, 16)
        eng.dma_start(out=AP(ag_in, 2 * PAY_SEC, [[1, 16], [1, 1]]),
                      in_=ltot_c[:, :]).then_inc(sem, 16)
        eng.dma_start(out=AP(ag_in, 2 * PAY_SEC + 16, [[1, 16], [1, 1]]),
                      in_=rowA[0:1, 30:46]).then_inc(sem, 16)
    tok_ag_in = DMA_("gpsimd", st_ag, n=6, after=[tmxE])

    tok_cc = bld.g_async(lambda e: e.collective_compute(
        "AllGather", ALU.bypass, replica_groups=[list(range(8))],
        ins=[ag_in.ap().opt()], outs=[ag_out.ap().opt()]),
        after=[tok_ag_in])

    wprep = [SB(f"wprep{i}", [128, IFF], F32, es_x) for i in range(2)]
    wtT = SB("wtT", [128, 2, NIC * 128], BF16, es_x)
    wpc = SB("wpc", [128, 8], F32, es_x)

    blocks = []
    for wname, wT_dst, sc_dst, nrows in [
            ("L0_ffn_w_gate", wgT_s, sg_sc, IFF),
            ("L0_ffn_w_up", wuT_s, su_sc, IFF),
            ("L0_ffn_w_down", wdT_s, sd_sc, D)]:
        for r in range(nrows // 128):
            blocks.append((wname, wT_dst, sc_dst, nrows, r))

    def emit_ld(idx):
        wname, _, _, _, r = blocks[idx]
        ncols = win[wname].shape[1]
        buf = wprep[idx % 2]

        def ldw(eng, sem, r=r, buf=buf, wname=wname, ncols=ncols):
            eng.dma_start(out=buf[:, 0:ncols],
                          in_=AP(win[wname], r * 128 * ncols, [[ncols, 128], [1, ncols]])
                          ).then_inc(sem, 16)
        return DMA_("sync", ldw, n=1, after=pw_st["buf"].get(idx % 2, []))

    pw_st = {"trd": None, "bank": {}, "buf": {}, "wst": {}}
    pend_ld = {0: emit_ld(0), 1: emit_ld(1)}
    pw = []
    for idx, (wname, wT_dst, sc_dst, nrows, r) in enumerate(blocks):
        ncols = win[wname].shape[1]
        nbl = ncols // 128
        buf = wprep[idx % 2]
        tl = pend_ld.pop(idx)
        tmax = V_(lambda e, buf=buf, ncols=ncols: (
            e.tensor_reduce(wpc[:, 0:1], buf[:, 0:ncols], AX.X, ALU.max,
                            apply_absolute_value=True),
            e.drain(),
            e.tensor_scalar(wpc[:, 1:2], wpc[:, 0:1], 1.0 / QMAX, None, ALU.mult),
            e.drain(),
            e.reciprocal(wpc[:, 2:3], wpc[:, 1:2]))[-1],
            after=[tl] + ([pw_st["trd"]] if pw_st["trd"] else []))
        tsc = V_(lambda e, r=r, sc_dst=sc_dst: e.tensor_copy(sc_dst[:, r:r + 1], wpc[:, 1:2]),
                 after=[tmax])
        trd = S_(lambda e, buf=buf, ncols=ncols: e.activation(
            buf[:, 0:ncols], buf[:, 0:ncols], AF.Identity,
            bias=MAGIC, scale=wpc[:, 2:3]), after=[tmax, tsc])
        tcl = V_(lambda e, buf=buf, ncols=ncols: (
            e.tensor_scalar(buf[:, 0:ncols], buf[:, 0:ncols], MAGIC, QMAX,
                            ALU.subtract, ALU.min),
            e.tensor_scalar(buf[:, 0:ncols], buf[:, 0:ncols], -QMAX - 1.0, None, ALU.max))[-1],
            after=[trd])
        pw_st["trd"] = trd
        tT2 = []
        for c in range(nbl):
            bk = c % 4
            bank = psX[bk]
            tt = T_(lambda e, bank=bank, c=c, buf=buf: e.transpose(
                bank[:, 0:128], buf[:, c * 128:(c + 1) * 128], id_f[:, :]),
                after=[tcl] + (([tT2[c - 4]] if c >= 4 else pw_st["bank"].get(bk, []))))
            te = S_(lambda e, c=c, bank=bank, sl=idx % 2: e.copy(
                wtT[:, sl, c * 128:(c + 1) * 128], bank[:, 0:128]),
                after=[tt] + ([pw_st["wst"][idx % 2]] if (c == 0 and idx % 2 in pw_st["wst"]) else []))
            tT2.append(te)
            pw_st["bank"][bk] = [te]
        pw_st["buf"][idx % 2] = [tT2[-1]]
        if idx + 2 < len(blocks):
            pend_ld[idx + 2] = emit_ld(idx + 2)

        def stw(eng, sem, r=r, wT_dst=wT_dst, nbl=nbl, nrows=nrows, sl=idx % 2):
            eng.dma_start(out=AP(wT_dst, r * 128,
                                 [[nrows, 128], [128 * nrows, nbl], [1, 128]]),
                          in_=wtT[:, sl, 0:nbl * 128].rearrange("p (a b) -> p a b", a=nbl)
                          ).then_inc(sem, 16)
        pw = [DMA_("sync", stw, n=1, after=[tT2[-1]])]
        pw_st["wst"][idx % 2] = pw[0]

    tok_ccw = G_(lambda e: e.nop(), after=[tok_cc, pw[-1]])
    q_x1 = bld.quiesce("sync")
    es_x.close()
    bld.flush("x1")
    if PHASE_LIMIT == "x1":
        return nc

    es2 = ExitStack()
    sel_pair_f = SB("sel_pair_f", [16, NPAIR * 128], F32, es2)
    SinC = SB("SinC", [128, NPAIR * KVE], BF16, es2)
    Rb16 = SB("Rb16", [128, NPAIR * KVE], BF16, es2)
    wo = SB("wo", [128, NDC, D], BF16, es2)
    scr2 = SB("scr2", [128, TILE], F32, es2)
    LtG = SB("LtG", [8, 16], F32, es2)
    mG = SB("mG", [8, 1], F32, es2)
    Wf8 = SB("Wf8", [8, 16], F32, es2)
    Wb8 = SB("Wb8", [8, 16], F32, es2)
    WfT = SB("WfT", [16, 8], F32, es2)
    WbT = SB("WbT", [16, 8], F32, es2)
    Wexp = SB("Wexp", [128, NPAIR * 8], F32, es2)
    Wexpb = SB("Wexpb", [128, NPAIR * 8], F32, es2)
    peerb = SB("peerb", [128, NPAIR * KVE], F32, es2)
    xnfm2 = SB("xnfm2", [128, NDC, TILE], F32, es2)
    qcL = SB("qcL", [128, NCHT, NPAIR * CH], BF16, es2)
    qeL = SB("qeL", [128, NCHT, NPAIR * CH], BF16, es2)
    partL = SB("partL", [128, NCHT, H * KVE], BF16, es2)
    totB = SB("totB", [128, H * KVE], F32, es2)
    atm = SB("atm", [128, H * NK], F32, es2)
    attnfm = SB("attnfm", [128, NDC, TILE], BF16, es2)
    x1b = SB("x1b", [128, NDC, 2, TILE], F32, es2)
    kcvL = SB("kcvL", [128, 2, NPAIR * KVE], BF16, es2)
    xqb = SB("xqb", [128, NDC, 2, TILE], BF16, es2)
    psB = [PS(f"psB{i}", es2) for i in range(8)]

    def p2_wo(eng, sem):
        eng.dma_start(out=wo[:, :, :],
                      in_=AP(win["L0_attn_o_w"], 0, [[D, 128], [128 * D, NDC], [1, D]])
                      ).then_inc(sem, 16)
    tok_wo = DMA_("gpsimd", p2_wo, n=1)

    def p2_spf(eng, sem):
        eng.dma_start(out=sel_pair_f[:, :], in_=const_t["sel_pair_f"].ap()).then_inc(sem, 16)
    tok_spf = DMA_("sync", p2_spf, n=1)

    def p2_gather_small(eng, sem):
        eng.dma_start(out=LtG[:, :], in_=AP(ag_out, 2 * PAY_SEC, [[PAY, 8], [1, 16]])
                      ).then_inc(sem, 16)
        eng.dma_start(out=mG[:, :], in_=AP(ag_out, 2 * PAY_SEC + 16, [[PAY, 8], [1, 1]]),
                      allow_slow_non_contiguous=True).then_inc(sem, 16)
    tok_gs = DMA_("sync", p2_gather_small, n=2, after=[tok_ccw])

    tm1T = T_(lambda e: e.transpose(psB[0][0:1, 0:8], mG[:, :], id_f[0:8, 0:8]),
              after=[tok_gs])
    tm1 = V_(lambda e: (e.tensor_copy(rowA[0:1, 40:48], psB[0][0:1, 0:8]),
                        e.drain(),
                        e.tensor_reduce(rowA[0:1, 24:25], rowA[0:1, 40:48], AX.X, ALU.max),
                        e.drain(),
                        e.tensor_mul(rowA[0:1, 25:26], rowA[0:1, 24:25], clipc[0:1, 0:1]),
                        e.drain(),
                        e.tensor_scalar(rowA[0:1, 26:27], rowA[0:1, 25:26], 1.0 / QMAX, None, ALU.mult),
                        e.drain(),
                        e.reciprocal(rowA[0:1, 27:28], rowA[0:1, 26:27]))[-1],
             after=[tm1T])
    if "m1_d" in dbg:
        DMA_("sync", lambda eng, sem: eng.dma_start(
            out=dbg["m1_d"].ap(), in_=rowA[0:1, 24:40]).then_inc(sem, 16), n=1, after=[tm1])
    tb1 = T_(lambda e: (e.matmul(psB[1][:, 0:1], ones_row[:, :], rowA[0:1, 26:27],
                                 start=True, stop=True),
                        e.matmul(psB[1][:, 1:2], ones_row[:, :], rowA[0:1, 27:28],
                                 start=True, stop=True))[-1], after=[tm1])
    tb1e = V_(lambda e: (e.tensor_copy(sc1_col[:, :], psB[1][:, 0:1]),
                         e.tensor_copy(rs1_col[:, :], psB[1][:, 1:2]))[-1], after=[tb1])

    tWf = T_(lambda e: (e.matmul(psB[2][0:8, 0:16], ccpf[:, :], LtG[:, :], start=True, stop=True),
                        e.matmul(psB[2][0:8, 16:32], ccpb[:, :], LtG[:, :], start=True, stop=True))[-1],
             after=[tok_gs, tb1])
    tWfe = S_(lambda e: (e.activation(Wf8[:, :], psB[2][0:8, 0:16], AF.Exp),
                         e.activation(Wb8[:, :], psB[2][0:8, 16:32], AF.Exp))[-1], after=[tWf])
    tWfm = V_(lambda e: (e.tensor_scalar(Wf8[:, :], Wf8[:, :], ccmf[:, 0:1], None, ALU.mult),
                         e.tensor_scalar(Wb8[:, :], Wb8[:, :], ccmb[:, 0:1], None, ALU.mult))[-1],
              after=[tWfe])
    tWT = T_(lambda e: (e.transpose(psB[3][0:16, 0:8], Wf8[:, :], id_f[0:8, 0:8]),
                        e.transpose(psB[3][0:16, 8:16], Wb8[:, :], id_f[0:8, 0:8]))[-1],
             after=[tWfm])
    tWTe = V_(lambda e: (e.tensor_copy(WfT[:, :], psB[3][0:16, 0:8]),
                         e.tensor_copy(WbT[:, :], psB[3][0:16, 8:16]))[-1], after=[tWT])
    wx = []
    for g in range(NPAIR):
        wx.append(T_(lambda e, g=g: (
            e.matmul(psB[4][:, g * 8:(g + 1) * 8], sel_pair_f[:, g * 128:(g + 1) * 128],
                     WfT[:, :], start=True, stop=True),
            e.matmul(psB[4][:, 64 + g * 8:64 + (g + 1) * 8],
                     sel_pair_f[:, g * 128:(g + 1) * 128], WbT[:, :], start=True, stop=True))[-1],
            after=[tWTe, tok_spf] + (wx[-1:] if wx else [])))
    tWx = V_(lambda e: (e.tensor_copy(Wexp[:, :], psB[4][:, 0:64]),
                        e.tensor_copy(Wexpb[:, :], psB[4][:, 64:128]))[-1], after=[wx[-1]])

    comb = [tWx]
    for peer in range(8):
        def ld_peer(eng, sem, peer=peer):
            for half in range(2):
                eng.dma_start(out=peerb[half * 64:half * 64 + 64, :].rearrange(
                    "p (a b) -> p a b", a=NPAIR),
                    in_=AP(ag_out, peer * PAY + half * 4160,
                           [[65, 64], [2 * 4160, NPAIR], [1, 65]])).then_inc(sem, 16)
        tlp = DMA_("sync", ld_peer, n=2, after=comb[-1:])
        ops = []
        for g in range(NPAIR):
            ops.append(V_(lambda e, g=g, peer=peer: (
                e.tensor_scalar(Tmp_st[:, g * KVE:(g + 1) * KVE],
                                peerb[:, g * KVE:(g + 1) * KVE],
                                Wexp[:, g * 8 + peer:g * 8 + peer + 1], None, ALU.mult),
                e.drain(),
                e.tensor_add(Sin_st[:, g * KVE:(g + 1) * KVE],
                             Sin_st[:, g * KVE:(g + 1) * KVE],
                             Tmp_st[:, g * KVE:(g + 1) * KVE]))[-1],
                after=[tlp, tWx]))

        def ld_peerb(eng, sem, peer=peer):
            for half in range(2):
                eng.dma_start(out=peerb[half * 64:half * 64 + 64, :].rearrange(
                    "p (a b) -> p a b", a=NPAIR),
                    in_=AP(ag_out, peer * PAY + PAY_SEC + half * 4160,
                           [[65, 64], [2 * 4160, NPAIR], [1, 65]])).then_inc(sem, 16)
        tlpb = DMA_("sync", ld_peerb, n=2, after=[ops[-1]])
        for g in range(NPAIR):
            ops.append(V_(lambda e, g=g, peer=peer: (
                e.tensor_scalar(Tmp_st[:, g * KVE:(g + 1) * KVE],
                                peerb[:, g * KVE:(g + 1) * KVE],
                                Wexpb[:, g * 8 + peer:g * 8 + peer + 1], None, ALU.mult),
                e.drain(),
                e.tensor_add(Rin_st[:, g * KVE:(g + 1) * KVE],
                             Rin_st[:, g * KVE:(g + 1) * KVE],
                             Tmp_st[:, g * KVE:(g + 1) * KVE]))[-1],
                after=[tlpb]))
        comb = [ops[-1]]
    tok_comb = comb[-1]
    tok_Rinit = V_(lambda e: (e.tensor_copy(R_st[:, :], Rin_st[:, :]),
                              e.tensor_copy(Rb16[:, :], Rin_st[:, :]))[-1], after=[tok_comb])
    if "sin_d" in dbg:
        DMA_("sync", lambda eng, sem: eng.dma_start(
            out=dbg["sin_d"].ap(), in_=Sin_st[:, :]).then_inc(sem, 16), n=1, after=[tok_comb])
    if "rin_d" in dbg:
        DMA_("sync", lambda eng, sem: eng.dma_start(
            out=dbg["rin_d"].ap(), in_=Rin_st[:, :]).then_inc(sem, 16), n=1, after=[tok_comb])

    def p2_tile(t, gate):
        def ld_t(eng, sem):
            eng.dma_start(out=xnfm2[:, :, :],
                          in_=AP(xn_s, t * TILE, [[TC, 128], [128 * TC, NDC], [1, TILE]])
                          ).then_inc(sem, 16)
            eng.dma_start(out=qcL[:, :, :],
                          in_=AP(qc_s, t * NCHT * 128 * NPAIR * CH,
                                 [[NPAIR * CH, 128], [128 * NPAIR * CH, NCHT], [1, NPAIR * CH]])
                          ).then_inc(sem, 16)
            eng.dma_start(out=qeL[:, :, :],
                          in_=AP(qe_s, t * NCHT * 128 * NPAIR * CH,
                                 [[NPAIR * CH, 128], [128 * NPAIR * CH, NCHT], [1, NPAIR * CH]])
                          ).then_inc(sem, 16)
            eng.dma_start(out=partL[:, :, :],
                          in_=AP(part_s, t * NCHT * CH * H * KVE,
                                 [[H * KVE, 128], [128 * H * KVE, NCHT], [1, H * KVE]])
                          ).then_inc(sem, 16)
        tok_l = DMA_("sync", ld_t, n=4, after=gate + [q_p1])

        last = list(gate)
        for cc in range(NCHT - 1, -1, -1):
            ch = t * NCHT + cc
            td1 = S_(lambda e, ch=ch: (
                e.activation(s16B[:, 1:2], lcend_st[:, ch:ch + 1], AF.Exp),
                e.activation(s16B[:, 2:3], lpre_st[:, ch:ch + 1], AF.Exp))[-1], after=last)
            td2 = V_(lambda e: (
                e.tensor_mul(s16B[:, 8:16], bccol(s16B[:, 1:2], NPAIR), pair_mask[:, :]),
                e.tensor_mul(s16A[:, 32:40], bccol(s16B[:, 2:3], NPAIR), pair_mask[:, :]))[-1],
                after=[td1])
            td3 = T_(lambda e: (
                e.matmul(psB[5][:, 0:8], sel_half[:, :], s16B[:, 8:16], start=True, stop=True),
                e.matmul(psB[5][:, 8:16], sel_half[:, :], s16A[:, 32:40], start=True, stop=True))[-1],
                after=[td2] + last)
            td4 = V_(lambda e: (e.tensor_copy(declc[:, :], psB[5][:, 0:8]),
                                e.tensor_copy(declp[:, :], psB[5][:, 8:16]))[-1], after=[td3])
            sic = [V_(lambda e: e.tensor_mul(
                SinC[:, :].rearrange("p (a b) -> p a b", a=NPAIR),
                Sin_st[:, :].rearrange("p (a b) -> p a b", a=NPAIR),
                AP(declp, 0, [[NPAIR, 128], [1, NPAIR], [0, KVE]])),
                after=[td4, tok_Rinit])]
            mmz = []
            mm_bank_tok = [None] * 4
            for hh in range(H):
                g, half = hh // 2, hh % 2
                pos = POS[hh]
                bank = psB[pos // 4]
                ma = T_(lambda e, hh=hh, g=g, half=half, bank=bank, pos=pos, cc=cc: e.matmul(
                    bank[:, (pos % 4) * KVE:(pos % 4) * KVE + KVE],
                    qeL[half * 64:half * 64 + 64, cc, g * CH:(g + 1) * CH],
                    Rb16[half * 64:half * 64 + 64, g * KVE:(g + 1) * KVE],
                    start=True, stop=False),
                    after=([sic[-1], tok_l] + last) if hh == 0 else [])
                mb = T_(lambda e, hh=hh, g=g, half=half, bank=bank, pos=pos, cc=cc: e.matmul(
                    bank[:, (pos % 4) * KVE:(pos % 4) * KVE + KVE],
                    qcL[half * 64:half * 64 + 64, cc, g * CH:(g + 1) * CH],
                    SinC[half * 64:half * 64 + 64, g * KVE:(g + 1) * KVE],
                    start=False, stop=True), after=[ma])
                mmz.append(mb)
                mm_bank_tok[pos // 4] = mb
            pa = []
            for bq in range(4):
                pa.append(V_(lambda e, bq=bq, cc=cc: e.tensor_add(
                    totB[:, bq * 4 * KVE:(bq + 1) * 4 * KVE],
                    psB[bq][:, 0:4 * KVE],
                    partL[:, cc, bq * 4 * KVE:(bq + 1) * 4 * KVE]),
                    after=[mm_bank_tok[bq], mmz[-1], tok_l]))
            dn1 = V_(lambda e: e.tensor_scalar(
                AP(colA, 40, [[64, 128], [1, 16], [1, 1]]),
                AP(totB, NK, [[NPAIR * 2 * KVE, 128], [KVE, 16], [1, 1]]),
                EPS_DEN, None, ALU.max), after=pa)
            dn2 = V_(lambda e: e.reciprocal(colA[:, 40:56], colA[:, 40:56]),
                     after=[dn1])
            dn3 = V_(lambda e: e.tensor_mul(
                AP(atm, 0, [[H * NK, 128], [NK, 2], [2 * NK, 8], [1, NK]]),
                AP(totB, 0, [[NPAIR * 2 * KVE, 128], [8 * KVE, 2], [KVE, 8], [1, NK]]),
                AP(colA, 40, [[64, 128], [8, 2], [1, 8], [0, NK]])),
                after=[dn2])
            dn = [dn3, dn3]
            ru = [V_(lambda e, ch=ch: (
                e.tensor_mul(Tmp_st[:, :].rearrange("p (a b) -> p a b", a=NPAIR),
                             R_st[:, :].rearrange("p (a b) -> p a b", a=NPAIR),
                             AP(declc, 0, [[NPAIR, 128], [1, NPAIR], [0, KVE]])),
                e.tensor_add(R_st[:, :], Tmp_st[:, :], kcvL[:, ch % 2, :]),
                e.tensor_copy(Rb16[:, :], R_st[:, :]))[-1],
                after=[td4, tok_Rinit, mmz[-1], st2kl[ch]])]
            if ch - 2 >= 0:
                emit_kl(ch - 2, [ru[-1]])
            at = []
            for g in range(NPAIR):
                bank = psB[6]
                tt = T_(lambda e, g=g, bank=bank: e.transpose(
                    bank[:, (g % 4) * 128:(g % 4) * 128 + 128],
                    atm[:, g * 128:(g + 1) * 128], id_f[:, :]),
                    after=[dn[-1]] + ([at[-1]] if at else []))
                te = S_(lambda e, g=g, cc=cc, bank=bank: e.copy(
                    attnfm[:, g, cc * CH:(cc + 1) * CH], bank[:, (g % 4) * 128:(g % 4) * 128 + 128]),
                    after=[tt])
                at.append(te)
            last = [ru[-1], at[-1], dn[-1]]
        attn_done = last[1]
        if "attn_d" in dbg:
            DMA_("sync", lambda eng, sem, t=t: eng.dma_start(
                out=AP(dbg["attn_d"], t * TILE, [[TC, 128], [128 * TC, NDC], [1, TILE]]),
                in_=AP(attnfm, 0, [[NDC * TILE, 128], [TILE, NDC], [1, TILE]])).then_inc(sem, 16),
                n=1, after=last)

        oT = []
        for co in range(NDC):
            bank = psB[7]
            mm = []
            for c in range(NDC):
                mm.append(T_(lambda e, c=c, co=co, bank=bank: e.matmul(
                    bank[:, :], wo[:, c, co * 128:(co + 1) * 128], attnfm[:, c, :],
                    start=(c == 0), stop=(c == NDC - 1)),
                    after=(([tok_wo, attn_done] + last) if (co == 0 and c == 0) else [])
                    + ([mm[-1]] if mm else (oT[-1:] if oT else []))))
            ev = V_(lambda e, co=co, bank=bank, sl=t % 2: e.scalar_tensor_tensor(
                x1b[:, co, sl, :], bank[:, :], obC[:, co:co + 1], xnfm2[:, co, :],
                ALU.add, ALU.add), after=[mm[-1], tok_l]
                + ([stq[("x1", t + 2)]] if (co == 0 and ("x1", t + 2) in stq) else []))
            oT.append(ev)

        def st_x1(eng, sem, sl=t % 2):
            eng.dma_start(out=AP(x1_s, t * TILE, [[TC, 128], [128 * TC, NDC], [1, TILE]]),
                          in_=x1b[:, :, sl, :]).then_inc(sem, 16)
        DMA_("sync", st_x1, n=1, after=oT)
        tok_x1st = bld.fence("sync")
        stq[("x1", t)] = tok_x1st
        if "x1_d" in dbg:
            DMA_("sync", lambda eng, sem, t=t: eng.dma_start(
                out=AP(dbg["x1_d"], t * TILE, [[TC, 128], [128 * TC, NDC], [1, TILE]]),
                in_=x1b[:, :, t % 2, :]).then_inc(sem, 16), n=1, after=oT)

        xq_toks = []
        for co in range(NDC):
            t1 = S_(lambda e, co=co: e.activation(scr2[:, :],
                                                  xnfm2[:, co, :], AF.Identity,
                                                  bias=MAGIC, scale=rs1_col[:, 0:1]),
                    after=[tok_l, tb1e] + ([xq_toks[-1]] if xq_toks else []))
            t2 = V_(lambda e, co=co, sl=t % 2: (
                e.tensor_scalar(scr2[:, :], scr2[:, :], MAGIC, QMAX,
                                ALU.subtract, ALU.min),
                e.tensor_scalar(xqb[:, co, sl, :], scr2[:, :], -QMAX - 1.0, None, ALU.max))[-1],
                after=[t1]
                + ([stq[("xq", t + 2)]] if (co == 0 and ("xq", t + 2) in stq) else []))
            xq_toks.append(t2)

        def st_xq(eng, sem, sl=t % 2):
            eng.dma_start(out=AP(xq_s, t * TILE, [[TC, 128], [128 * TC, NDC], [1, TILE]]),
                          in_=xqb[:, :, sl, :]).then_inc(sem, 16)
        DMA_("sync", st_xq, n=1, after=[xq_toks[-1]])
        tok_xqst = bld.fence("sync")
        stq[("xq", t)] = tok_xqst
        if "xq_d" in dbg:
            DMA_("sync", lambda eng, sem, t=t: eng.dma_start(
                out=AP(dbg["xq_d"], t * TILE, [[TC, 128], [128 * TC, NDC], [1, TILE]]),
                in_=xqb[:, :, t % 2, :]).then_inc(sem, 16),
                n=1, after=[xq_toks[-1]])
        return [oT[-1], xq_toks[-1]] + last

    st2kl = {}

    def emit_kl(ch, after):
        def ld_kl(eng, sem, ch=ch):
            eng.dma_start(out=kcvL[:, ch % 2, :],
                          in_=AP(kcv_s, ch * 128 * NPAIR * KVE,
                                 [[NPAIR * KVE, 128], [1, NPAIR * KVE]])).then_inc(sem, 16)
        st2kl[ch] = DMA_("sync", ld_kl, n=1, after=after)

    emit_kl(NCH - 1, [])
    emit_kl(NCH - 2, [])
    gate2 = [tok_Rinit, tok_comb, tb1e]
    for t in range(NT - 1, -1, -1):
        gate2 = p2_tile(t, gate2)
    q_p2 = bld.quiesce("sync")
    bld.flush("p2")
    if PHASE_LIMIT == "p2":
        return nc
    es2.close()
    es_kcv.close()

    es4 = ExitStack()
    wgT = SB("wgT", [128, NDC, IFF], BF16, es4)
    wuT = SB("wuT", [128, NDC, IFF], BF16, es4)
    xqL = [SB(f"xqL{i}", [128, NDC, TILE], BF16, es4) for i in range(2)]
    silb = SB("silb", [128, TILE], F32, es4)
    h2b = [SB(f"h2b{i}", [128, TILE], F32, es4) for i in range(4)]
    sgv = SB("sgv", [128, NIC], F32, es4)
    suv = SB("suv", [128, NIC], F32, es4)
    psC = [PS(f"psC{i}", es4) for i in range(4)]

    def p4_w(eng, sem):
        eng.dma_start(out=wgT[:, :, :],
                      in_=AP(wgT_s, 0, [[IFF, 128], [128 * IFF, NDC], [1, IFF]])).then_inc(sem, 16)
        eng.dma_start(out=wuT[:, :, :],
                      in_=AP(wuT_s, 0, [[IFF, 128], [128 * IFF, NDC], [1, IFF]])).then_inc(sem, 16)
    tok_w4 = DMA_("gpsimd", p4_w, n=2, after=[q_x1])
    tok_sv = V_(lambda e: (e.tensor_scalar(sgv[:, :], sg_sc[:, 0:NIC], sc1_col[:, 0:1], None, ALU.mult),
                           e.tensor_scalar(suv[:, :], su_sc[:, 0:NIC], sc1_col[:, 0:1], None, ALU.mult))[-1])

    def p4_tile(t, gate):
        xq_ = xqL[t % 2]

        def ld_xq(eng, sem):
            eng.dma_start(out=xq_[:, :, :],
                          in_=AP(xq_s, t * TILE, [[TC, 128], [128 * TC, NDC], [1, TILE]])
                          ).then_inc(sem, 16)
        tl = DMA_("sync", ld_xq, n=1, after=gate + [q_p2])
        sil_toks, h2_toks, mx_toks, st_toks, gmm_toks, umm_toks = [], [], [], [], [], []
        for ic in range(NIC):
            h2slot = (t * NIC + ic) % 4
            bg = psC[ic % 2]
            bu = psC[2 + ic % 2]
            mmg = []
            for c in range(NDC):
                dep = []
                if ic == 0 and c == 0:
                    dep = [tok_w4, tl] + gate
                elif c == 0 and ic >= 2:
                    dep = [sil_toks[ic - 2]]
                mmg.append(T_(lambda e, c=c, ic=ic, bg=bg, xq_=xq_: e.matmul(
                    bg[:, :], wgT[:, c, ic * 128:(ic + 1) * 128], xq_[:, c, :],
                    start=(c == 0), stop=(c == NDC - 1)),
                    after=dep + ([mmg[-1]] if mmg else [])))
            gmm_toks.append(mmg[-1])
            mmu = []
            for c in range(NDC):
                dep = [mmg[-1]] if c == 0 else []
                if c == 0 and ic >= 2:
                    dep = dep + [h2_toks[ic - 2]]
                mmu.append(T_(lambda e, c=c, ic=ic, bu=bu, xq_=xq_: e.matmul(
                    bu[:, :], wuT[:, c, ic * 128:(ic + 1) * 128], xq_[:, c, :],
                    start=(c == 0), stop=(c == NDC - 1)),
                    after=dep + ([mmu[-1]] if mmu else [])))
            umm_toks.append(mmu[-1])
            tsil = S_(lambda e, ic=ic, bg=bg: e.activation(
                silb[:, :], bg[:, :], AF.Silu, scale=sgv[:, ic:ic + 1]),
                after=[mmg[-1], tok_sv] + ([h2_toks[-1]] if h2_toks else []))
            sil_toks.append(tsil)
            h2x = h2b[h2slot]
            dep_h2 = [mmu[-1], tsil] + stq.get(("h2s", h2slot), [])
            th2 = V_(lambda e, ic=ic, bu=bu, h2x=h2x: e.scalar_tensor_tensor(
                h2x[:, :], bu[:, :], suv[:, ic:ic + 1], silb[:, :], ALU.mult, ALU.mult),
                after=dep_h2)
            h2_toks.append(th2)
            tmx2 = V_(lambda e, h2x=h2x: (
                e.tensor_reduce(colA[:, 44:45], h2x[:, :], AX.X, ALU.max,
                                apply_absolute_value=True),
                e.drain(),
                e.tensor_max(mh_col[:, :], mh_col[:, :], colA[:, 44:45]))[-1],
                after=[th2])
            mx_toks.append(tmx2)

            def st_h2(eng, sem, ic=ic, t=t, h2x=h2x):
                eng.dma_start(out=AP(h2_s, ic * 128 * TC + t * TILE, [[TC, 128], [1, TILE]]),
                              in_=h2x[:, :]).then_inc(sem, 16)
            DMA_("sync", st_h2, n=1, after=[th2])
            tst = bld.fence("sync")
            stq[("h2s", h2slot)] = [tst, tmx2]
            if "h2_d" in dbg:
                DMA_("sync", lambda eng, sem, ic=ic, t=t, h2x=h2x: eng.dma_start(
                    out=AP(dbg["h2_d"], ic * 128 * TC + t * TILE, [[TC, 128], [1, TILE]]),
                    in_=h2x[:, :]).then_inc(sem, 16), n=1, after=[th2])
            st_toks.append(tst)
        return [st_toks[-1], st_toks[-2], mx_toks[-1], mx_toks[-2], h2_toks[-1]]

    gate4 = [tok_sv]
    for t in range(NT):
        gate4 = p4_tile(t, gate4)
    row4 = SB("row4", [1, 192], F32, es4)
    tm2T = T_(lambda e: e.transpose(psC[0][0:1, 0:128], mh_col[:, :], id_f[:, :]), after=gate4)
    tm2E = V_(lambda e: (e.tensor_copy(row4[0:1, 0:128], psC[0][0:1, 0:128]),
                         e.memset(rowA[0:1, 29:44], 0.0),
                         e.drain(),
                         e.tensor_reduce(rowA[0:1, 28:29], row4[0:1, 0:128], AX.X, ALU.max))[-1],
              after=[tm2T] + gate4)

    def st_ag2(eng, sem):
        eng.dma_start(out=AP(ag2_in, 0, [[1, 16], [1, 1]]), in_=rowA[0:1, 28:44]).then_inc(sem, 16)
    tok_ag2 = DMA_("gpsimd", st_ag2, n=1, after=[tm2E])
    tok_cc2 = bld.g_async(lambda e: e.collective_compute(
        "AllGather", ALU.bypass, replica_groups=[list(range(8))],
        ins=[ag2_in.ap().opt()], outs=[ag2_out.ap().opt()]), after=[tok_ag2])
    q_p4 = bld.quiesce("sync")
    bld.flush("p4")
    if PHASE_LIMIT == "p4":
        return nc
    es4.close()

    es3 = ExitStack()
    wf1 = SB("wf1", [128, NDC, FFN], BF16, es3)
    wf2 = SB("wf2", [128, NFC, D], BF16, es3)
    f1bC = SB("f1bC", [128, NFC], F32, es3)
    x1L = SB("x1L", [128, NDC, TILE], F32, es3)
    x2b = SB("x2b", [128, NDC, TILE], BF16, es3)
    h1b = SB("h1b", [128, NFC, TILE], BF16, es3)
    totc = [SB(f"totc{i}", [128, TILE], F32, es3) for i in range(4)]
    sqs3 = SB("sqs3", [128, TILE], F32, es3)
    row3 = SB("row3", [1, TILE], F32, es3)
    psD = [PS(f"psD{i}", es3) for i in range(8)]

    def p3_w(eng, sem):
        eng.dma_start(out=wf1[:, :, :],
                      in_=AP(win["L0_attn_f1_w"], 0, [[FFN, 128], [128 * FFN, NDC], [1, FFN]])
                      ).then_inc(sem, 16)
        eng.dma_start(out=wf2[:, :, :],
                      in_=AP(win["L0_attn_f2_w"], 0, [[D, 128], [128 * D, NFC], [1, D]])
                      ).then_inc(sem, 16)
    tok_w3 = DMA_("gpsimd", p3_w, n=2)
    fmm = []
    for cp in range(NFC):
        mms = []
        for c in range(NDC):
            mms.append(T_(lambda e, c=c, cp=cp: e.matmul(
                psD[6][:, cp:cp + 1], wf1[:, c, cp * 128:(cp + 1) * 128],
                b2cb[:, c:c + 1], start=(c == 0), stop=(c == NDC - 1)),
                after=([tok_w3] if (cp == 0 and c == 0) else []) + ([mms[-1]] if mms else ([fmm[-1]] if fmm else []))))
        fmm.append(mms[-1])
    f1bc_toks = []
    for rnd in range(4):
        def ldf1b(eng, sem, rnd=rnd):
            eng.dma_start(out=rowsP[0:1, 0:1024],
                          in_=AP(win["L0_attn_f1_b"], rnd * 1024, [[1024, 1], [1, 1024]])
                          ).then_inc(sem, 16)
        tl = DMA_("sync", ldf1b, n=1, after=f1bc_toks[-1:] if f1bc_toks else [fmm[-1]])
        tts = []
        for c in range(8):
            tts.append(T_(lambda e, c=c, rnd=rnd: e.transpose(
                psD[7][:, rnd * 8 + c:rnd * 8 + c + 1],
                rowsP[0:1, c * 128:(c + 1) * 128], id_f[0:1, 0:1]),
                after=[tl, fmm[-1]] + (tts[-1:] if tts else (f1bc_toks[-1:] if f1bc_toks else []))))
        f1bc_toks.append(tts[-1])
    tf1b0 = V_(lambda e: e.tensor_copy(f1bC[:, :], psD[7][:, 0:NFC]),
               after=[f1bc_toks[-1], fmm[-1]])
    tf1b = V_(lambda e: e.tensor_add(f1bC[:, :], f1bC[:, :], psD[6][:, 0:NFC]),
              after=[tf1b0])
    bld.flush("p3head")

    def p3_tile(t, gate):
        def ld_x1(eng, sem):
            eng.dma_start(out=x1L[:, :, :],
                          in_=AP(x1_s, t * TILE, [[TC, 128], [128 * TC, NDC], [1, TILE]])
                          ).then_inc(sem, 16)
        tl = DMA_("sync", ld_x1, n=1, after=gate + [q_p2])
        stat_toks = []
        for c in range(NDC):
            stat_toks.append(T_(lambda e, c=c: e.matmul(
                psD[4][0:1, 0:512], ones_col[:, 0:1], x1L[:, c, :],
                start=(c == 0), stop=(c == NDC - 1)),
                after=([tl] + gate) if c == 0 else []))
        sq_toks = []
        for c in range(NDC):
            tq = S_(lambda e, c=c: e.activation(sqs3[:, :], x1L[:, c, :], AF.Square),
                    after=[tl] + (sq_toks[-1:] if sq_toks else []))
            tqm = T_(lambda e, c=c: e.matmul(
                psD[5][0:1, 0:512], ones_col[:, 0:1], sqs3[:, :],
                start=(c == 0), stop=(c == NDC - 1)),
                after=[tq, stat_toks[-1]] + (sq_toks[-1:] if sq_toks else []))
            sq_toks.append(tqm)
        trow = V_(lambda e: (
            e.tensor_scalar(row3[0:1, 0:512], psD[4][0:1, 0:512], 1.0 / D, None, ALU.mult),
            e.tensor_scalar(rowsP[0:1, 0:512], psD[5][0:1, 0:512], 1.0 / D, None, ALU.mult),
            e.tensor_mul(rowsP[0:1, 512:1024], row3[0:1, 0:512], row3[0:1, 0:512]),
            e.tensor_sub(rowsP[0:1, 0:512], rowsP[0:1, 0:512], rowsP[0:1, 512:1024]))[-1],
            after=[sq_toks[-1], stat_toks[-1]])
        trl = S_(lambda e: (e.activation(rowsP[0:1, 0:512], rowsP[0:1, 0:512], AF.Ln, bias=LN_EPS),
                            e.activation(rowsP[0:1, 0:512], rowsP[0:1, 0:512], AF.Exp, scale=-0.5))[-1],
                 after=[trow])
        trm = V_(lambda e: e.tensor_mul(rowsP[0:1, 512:1024], row3[0:1, 0:512], rowsP[0:1, 0:512]),
                 after=[trl])
        tgq = T_(lambda e: (e.matmul(psD[4][:, :], ones_row[:, :], rowsP[0:1, 0:512],
                                     start=True, stop=True),
                            e.matmul(psD[5][:, :], ones_row[:, :], rowsP[0:1, 512:1024],
                                     start=True, stop=True))[-1], after=[trm])
        x2toks = []
        for c in range(NDC):
            x2toks.append(V_(lambda e, c=c: (
                e.tensor_mul(sqs3[:, :], x1L[:, c, :], psD[4][:, :]),
                e.tensor_sub(x2b[:, c, :], sqs3[:, :], psD[5][:, :]))[-1],
                after=[tgq] + (x2toks[-1:] if x2toks else [])))
        x2done = x2toks[-1]
        h1toks = []
        for fo in range(NFC):
            bank = psD[fo % 2]
            mm = []
            for c in range(NDC):
                mm.append(T_(lambda e, c=c, fo=fo, bank=bank: e.matmul(
                    bank[:, :], wf1[:, c, fo * 128:(fo + 1) * 128], x2b[:, c, :],
                    start=(c == 0), stop=(c == NDC - 1)),
                    after=([x2done] if (fo == 0 and c == 0) else [])
                    + ([mm[-1]] if mm else (h1toks[-2:-1] if len(h1toks) >= 2 else []))))
            te = S_(lambda e, fo=fo, bank=bank: e.activation(
                h1b[:, fo, :], bank[:, :], AF.Gelu, bias=f1bC[:, fo:fo + 1]),
                after=[mm[-1]])
            h1toks.append(te)
        h1done = h1toks[-1]
        f2toks = []
        for co in range(NDC):
            mm = []
            for fc in range(NFC):
                mm.append(T_(lambda e, fc=fc, co=co: e.matmul(
                    psD[co][:, :], wf2[:, fc, co * 128:(co + 1) * 128], h1b[:, fc, :],
                    start=(fc == 0), stop=(fc == NFC - 1)),
                    after=([h1done] if (co == 0 and fc == 0) else [])
                    + ([mm[-1]] if mm else stq.get(("f2bank", co), []))))
            ev = V_(lambda e, co=co: e.scalar_tensor_tensor(
                totc[co % 4][:, :], psD[co][:, :], f2bC[:, co:co + 1], x1L[:, co, :],
                ALU.add, ALU.add),
                after=[mm[-1]] + stq.get(("tots", co % 4), []))
            stq[("f2bank", co)] = [ev]

            def st_tot(eng, sem, co=co, t=t):
                eng.dma_start(out=AP(tot_s, co * 128 * TC + t * TILE, [[TC, 128], [1, TILE]]),
                              in_=totc[co % 4][:, :]).then_inc(sem, 16)
            DMA_("sync", st_tot, n=1, after=[ev])
            td = bld.fence("sync")
            stq[("tots", co % 4)] = [td]
            if "tot_d" in dbg:
                DMA_("sync", lambda eng, sem, co=co, t=t: eng.dma_start(
                    out=AP(dbg["tot_d"], co * 128 * TC + t * TILE, [[TC, 128], [1, TILE]]),
                    in_=totc[co % 4][:, :]).then_inc(sem, 16), n=1, after=[ev])
            f2toks += [ev, td]
        return [f2toks[-1], f2toks[-2], h1done]

    gate3 = [tf1b]
    f1sc2 = []
    for c in range(NDC):
        f1sc2.append(V_(lambda e, c=c: e.tensor_scalar(
            wf1[:, c, :], wf1[:, c, :], g2c[:, c:c + 1], None, ALU.mult),
            after=[tf1b] + (f1sc2[-1:] if f1sc2 else [])))
    gate3 = [f1sc2[-1]]
    for t in range(NT):
        gate3 = p3_tile(t, gate3)
    q_p3 = bld.quiesce("sync")
    bld.flush("p3")
    if PHASE_LIMIT == "p3":
        return nc
    es3.close()

    es5 = ExitStack()
    wdT = SB("wdT", [128, NIC, D], BF16, es5)
    h2L = [SB(f"h2L{i}", [128, TILE], F32, es5) for i in range(4)]
    h2q = [SB(f"h2q{i}", [128, TILE], BF16, es5) for i in range(4)]
    totLs = [SB(f"totL{i}", [128, NDC, TILE], F32, es5) for i in range(2)]
    fintm = SB("fintm", [128, NCHT, D], F32, es5)
    xrl = [SB(f"xrl{i}", [128, NCHT, D], F32, es5) for i in range(2)]
    outb = SB("outb", [128, NCHT, D], F32, es5)
    psE = [PS(f"psE{i}", es5) for i in range(8)]

    def p5_w(eng, sem):
        eng.dma_start(out=wdT[:, :, :],
                      in_=AP(wdT_s, 0, [[D, 128], [128 * D, NIC], [1, D]])).then_inc(sem, 16)
    tok_w5 = DMA_("gpsimd", p5_w, n=1, after=[q_x1])
    tok_cc2w = G_(lambda e: e.nop(), after=[tok_cc2])

    def ld_m2(eng, sem):
        eng.dma_start(out=colA[0:8, 45:46],
                      in_=AP(ag2_out, 0, [[16, 8], [1, 1]]),
                      allow_slow_non_contiguous=True).then_inc(sem, 16)
    tok_m2l = DMA_("sync", ld_m2, n=1, after=[tok_cc2w])
    tm2T2 = T_(lambda e: e.transpose(psE[0][0:1, 0:8], colA[0:8, 45:46], id_f[0:8, 0:8]),
               after=[tok_m2l])
    tm2f = V_(lambda e: (e.tensor_copy(rowA[0:1, 40:48], psE[0][0:1, 0:8]),
                         e.drain(),
                         e.tensor_reduce(rowA[0:1, 29:30], rowA[0:1, 40:48], AX.X, ALU.max),
                         e.drain(),
                         e.tensor_mul(rowA[0:1, 30:31], rowA[0:1, 29:30], clipc[0:1, 1:2]),
                         e.drain(),
                         e.tensor_scalar(rowA[0:1, 31:32], rowA[0:1, 30:31], 1.0 / QMAX, None, ALU.mult),
                         e.drain(),
                         e.reciprocal(rowA[0:1, 33:34], rowA[0:1, 31:32]))[-1],
              after=[tm2T2])
    if "m2_d" in dbg:
        DMA_("sync", lambda eng, sem: eng.dma_start(
            out=dbg["m2_d"].ap(), in_=rowA[0:1, 24:40]).then_inc(sem, 16), n=1, after=[tm2f])
    tb2x = T_(lambda e: (e.matmul(psE[1][:, 0:1], ones_row[:, :], rowA[0:1, 31:32],
                                  start=True, stop=True),
                         e.matmul(psE[1][:, 1:2], ones_row[:, :], rowA[0:1, 33:34],
                                  start=True, stop=True))[-1], after=[tm2f])
    tb2e = V_(lambda e: (e.tensor_copy(sc2_col[:, :], psE[1][:, 0:1]),
                         e.tensor_copy(rs2_col[:, :], psE[1][:, 1:2]))[-1], after=[tb2x])
    tsdv = V_(lambda e: e.tensor_scalar(sd_sc[:, 0:NDC], sd_sc[:, 0:NDC], sc2_col[:, 0:1],
                                        None, ALU.mult), after=[tb2e])

    st5 = {
        "h2slot": [[] for _ in range(4)],
        "totslot": [[] for _ in range(2)],
        "bank": [[] for _ in range(NDC)],
        "fintm": [],
        "outb": [],
        "xr": [[] for _ in range(2)],
    }

    def p5_tile(t, gate):
        tsl = t % 2
        totL = totLs[tsl]

        def ld_tot(eng, sem):
            eng.dma_start(out=totL[:, :, :],
                          in_=AP(tot_s, t * TILE, [[TC, 128], [128 * TC, NDC], [1, TILE]])
                          ).then_inc(sem, 16)
        tlt = DMA_("sync", ld_tot, n=1,
                   after=st5["totslot"][tsl] + (gate if t == 0 else []) + [q_p3])
        xr = xrl[t % 2]

        def ld_x(eng, sem):
            eng.dma_start(out=xr[:, :, :],
                          in_=AP(x_in, t * TILE * D, [[D, 128], [128 * D, NCHT], [1, D]])
                          ).then_inc(sem, 16)
        tlx = DMA_("sync", ld_x, n=1, after=st5["xr"][t % 2] + (gate if t == 0 else []))

        ic_toks = []
        fin = [None] * NDC
        for ic in range(NIC):
            slot = (t * NIC + ic) % 4
            hb = h2L[slot]
            hq = h2q[slot]

            def ld_h2(eng, sem, ic=ic, t=t, hb=hb):
                eng.dma_start(out=hb[:, :],
                              in_=AP(h2_s, ic * 128 * TC + t * TILE, [[TC, 128], [1, TILE]])
                              ).then_inc(sem, 16)
            tlh = DMA_("sync", ld_h2, n=1,
                       after=st5["h2slot"][slot] + [q_p4]
                       + ([tb2e] if t == 0 and ic == 0 else []))
            tq1 = S_(lambda e, hb=hb: e.activation(hb[:, :], hb[:, :], AF.Identity,
                                                   bias=MAGIC, scale=rs2_col[:, 0:1]),
                     after=[tlh, tb2e])
            tq2 = V_(lambda e, hb=hb, hq=hq: (
                e.tensor_scalar(hb[:, :], hb[:, :], MAGIC, QMAX, ALU.subtract, ALU.min),
                e.tensor_scalar(hq[:, :], hb[:, :], -QMAX - 1.0, None, ALU.max))[-1],
                after=[tq1])
            mmd = []
            for co in range(NDC):
                mmd.append(T_(lambda e, co=co, ic=ic, hq=hq: e.matmul(
                    psE[co][:, :], wdT[:, ic, co * 128:(co + 1) * 128], hq[:, :],
                    start=(ic == 0), stop=(ic == NIC - 1)),
                    after=[tq2, tok_w5]
                    + (st5["bank"][co] if ic == 0 else [])
                    + ([mmd[-1]] if mmd else (ic_toks[-1:] if ic_toks else []))))
            st5["h2slot"][slot] = [mmd[-1], tq2]
            ic_toks += [tq2, mmd[-1]]
        for co in range(NDC):
            ev = V_(lambda e, co=co, totL=totL: e.scalar_tensor_tensor(
                totL[:, co, :], psE[co][:, :], sd_sc[:, co:co + 1], totL[:, co, :],
                ALU.mult, ALU.add), after=[ic_toks[-1], tsdv, tlt])
            fin[co] = ev
        ttk = []
        for jj in range(NCHT):
            for co in range(NDC):
                bank = psE[co % 2]
                tt = T_(lambda e, co=co, jj=jj, bank=bank, totL=totL: e.transpose(
                    bank[:, 0:128], totL[:, co, jj * 128:(jj + 1) * 128], id_f[:, :]),
                    after=[fin[co]] + (ttk[-1:] if ttk else [fin[-1]]))
                te = S_(lambda e, co=co, jj=jj, bank=bank: e.copy(
                    fintm[:, jj, co * 128:(co + 1) * 128], bank[:, 0:128]),
                    after=[tt] + st5["fintm"])
                ttk.append(te)
        te_all = ttk
        for co in range(NDC):
            st5["bank"][co] = [ttk[-1]]
        st5["totslot"][tsl] = [ttk[-1]]
        tadd = V_(lambda e: e.tensor_add(outb[:, :, :], fintm[:, :, :], xr[:, :, :]),
                  after=te_all[-1:] + [tlx] + st5["outb"])
        st5["fintm"] = [tadd]
        st5["xr"][t % 2] = [tadd]

        def st_out(eng, sem, t=t):
            eng.dma_start(out=AP(out_t, t * TILE * D, [[D, 128], [128 * D, NCHT], [1, D]]),
                          in_=outb[:, :, :]).then_inc(sem, 16)
        tso = DMA_("sync", st_out, n=1, after=[tadd])
        st5["outb"] = [bld.fence("sync")]
        return [tso, tadd]

    gate5 = [tsdv]
    for t in range(NT):
        gate5 = p5_tile(t, gate5)
    V_(lambda e: e.nop(), after=gate5)
    bld.flush("p5")
    es5.close()
    es_persist.close()
    return nc


def kernel(**inputs):
    x_full = np.ascontiguousarray(inputs["L0_attn_x"])
    nc = build_nc()
    in_maps = []
    for cid in range(NCORE):
        b, t0 = cid // 4, (cid % 4) * TC
        m = {"x_in": np.ascontiguousarray(x_full[b, t0:t0 + TC])}
        for k, v in inputs.items():
            if k == "L0_attn_x":
                continue
            m[k] = np.ascontiguousarray(np.asarray(v))
        my = cid % 4
        pf = np.zeros((8, 8), np.float32)
        pb = np.zeros((8, 8), np.float32)
        mf = np.zeros((8, 1), np.float32)
        mb = np.zeros((8, 1), np.float32)
        grp = [4 * (cid // 4) + i for i in range(4)]
        for i, c2 in enumerate(grp):
            if i < my:
                mf[c2, 0] = 1.0
                for j in range(i + 1, my):
                    pf[grp[j], c2] = 1.0
            if i > my:
                mb[c2, 0] = 1.0
                for j in range(my + 1, i):
                    pb[grp[j], c2] = 1.0
        m["cc_pf"], m["cc_pb"], m["cc_mf"], m["cc_mb"] = pf, pb, mf, mb
        in_maps.append(m)
    res = run_bass_kernel_spmd(nc, in_maps, core_ids=list(range(NCORE)))
    out = np.empty((B, T, D), np.float32)
    for cid in range(NCORE):
        b, t0 = cid // 4, (cid % 4) * TC
        out[b, t0:t0 + TC] = res.results[cid]["out"]
    return out
